# version 07
# baseline (speedup 1.0000x reference)
import numpy as np
import ml_dtypes
from contextlib import ExitStack

import concourse.bass as bass
import concourse.mybir as mybir
from concourse.bass_utils import run_bass_kernel_spmd

F32 = mybir.dt.float32
BF16 = mybir.dt.bfloat16
I32 = mybir.dt.int32
AF = mybir.ActivationFunctionType
ALU = mybir.AluOpType
AX = mybir.AxisListType

D = 1024
ECH = D // 128
TOWN = 2048
NKV = 8192
NSU = 16
II = 2816
ICH = II // 128
NEG = -8.0

_CACHE = {}


class Ctr:
    def __init__(self, handle):
        self.h = handle
        self.n = 0

    def inc(self, by=1):
        self.n += by
        return self.n


def build_program(limit=None):
    nc = bass.Bass("TRN2", target_bir_lowering=False)

    xq_in = nc.dram_tensor("xq", [TOWN, D], F32, kind="ExternalInput").ap()
    xkv_in = nc.dram_tensor("xkv", [NKV, D], F32, kind="ExternalInput").ap()
    wqT_in = nc.dram_tensor("wqT", [D, D], F32, kind="ExternalInput").ap()
    wkT_in = nc.dram_tensor("wkT", [D, D], F32, kind="ExternalInput").ap()
    wvT_in = nc.dram_tensor("wvT", [D, D], F32, kind="ExternalInput").ap()
    bq_in = nc.dram_tensor("bq", [D], F32, kind="ExternalInput").ap()
    bk_in = nc.dram_tensor("bk", [D], F32, kind="ExternalInput").ap()
    bv_in = nc.dram_tensor("bv", [D], F32, kind="ExternalInput").ap()
    wg_in = nc.dram_tensor("wg", [II, D], F32, kind="ExternalInput").ap()
    wu_in = nc.dram_tensor("wu", [II, D], F32, kind="ExternalInput").ap()
    wd_in = nc.dram_tensor("wd", [D, II], F32, kind="ExternalInput").ap()
    ramp_in = nc.dram_tensor("ramp", [128, TOWN], F32, kind="ExternalInput").ap()
    tthr_in = nc.dram_tensor("tthr", [128, 64], F32, kind="ExternalInput").ap()
    ident_in = nc.dram_tensor("ident", [128, 128], BF16, kind="ExternalInput").ap()
    out_t = nc.dram_tensor("out", [TOWN, D], F32, kind="ExternalOutput").ap()

    h_scr = nc.dram_tensor("h_scr", [NKV, D], BF16, kind="Internal").ap()
    hq_scr = nc.dram_tensor("hq_scr", [TOWN, D], BF16, kind="Internal").ap()
    xnew_scr = nc.dram_tensor("xnew_scr", [TOWN, D], F32, kind="Internal").ap()
    wgq_scr = nc.dram_tensor("wgq_scr", [II, D], BF16, kind="Internal").ap()
    wuq_scr = nc.dram_tensor("wuq_scr", [II, D], BF16, kind="Internal").ap()
    wdq_scr = nc.dram_tensor("wdq_scr", [D, II], BF16, kind="Internal").ap()

    ops = {k: [] for k in ("sync", "vector", "tensor", "scalar", "gpsimd")}

    def em(eng, fn):
        ops[eng].append(fn)

    def finalize():
        with nc.Block() as block:
            @block.sync
            def _(E):
                for f in ops["sync"]:
                    f(E)

            @block.vector
            def _(E):
                for f in ops["vector"]:
                    f(E)

            @block.tensor
            def _(E):
                for f in ops["tensor"]:
                    f(E)

            @block.scalar
            def _(E):
                for f in ops["scalar"]:
                    f(E)

            @block.gpsimd
            def _(E):
                for f in ops["gpsimd"]:
                    f(E)
        return nc

    pstack = ExitStack()
    bstack = ExitStack()
    P = pstack.enter_context
    B = bstack.enter_context

    ld = Ctr(P(nc.semaphore("ld")))
    sh = Ctr(P(nc.semaphore("sh")))
    so = Ctr(P(nc.semaphore("so")))
    pe = Ctr(P(nc.semaphore("pe")))
    dv = Ctr(P(nc.semaphore("dv")))
    ac = Ctr(P(nc.semaphore("ac")))

    def dma(eng, out, in_, ctr, transpose=False):
        n = ctr.inc(16)
        if transpose:
            em(eng, lambda E, o=out, i=in_, c=ctr.h:
               E.dma_start(out=o, in_=i, transpose=True).then_inc(c, 16))
        else:
            em(eng, lambda E, o=out, i=in_, c=ctr.h:
               E.dma_start(out=o, in_=i).then_inc(c, 16))
        return n

    def wait(eng, ctr, n):
        if n > 0:
            em(eng, lambda E, c=ctr.h, v=n: E.wait_ge(c, v))

    tthr_sb = P(nc.sbuf_tensor("tthr_sb", [128, 64], F32))
    stat_sb = P(nc.sbuf_tensor("stat_sb", [128, 256], F32))
    rstd_all = P(nc.sbuf_tensor("rstd_all", [128, 80], F32))
    negm_all = P(nc.sbuf_tensor("negm_all", [128, 80], F32))
    bqc_sb = P(nc.sbuf_tensor("bqc_sb", [128, ECH], F32))
    bkc_sb = P(nc.sbuf_tensor("bkc_sb", [128, ECH], F32))
    bvb_sb = P(nc.sbuf_tensor("bvb_sb", [128, D], F32))
    ones1_bf = P(nc.sbuf_tensor("ones1_bf", [128, 1], BF16))
    rstdf_sb = P(nc.sbuf_tensor("rstdf_sb", [128, 16], F32))
    negmf_sb = P(nc.sbuf_tensor("negmf_sb", [128, 16], F32))
    rinv_sb = P(nc.sbuf_tensor("rinv_sb", [128, 16], F32))
    brow = P(nc.sbuf_tensor("brow", [1, 3 * D], F32))
    onef = P(nc.sbuf_tensor("onef", [1, 1], F32))
    onerow = P(nc.sbuf_tensor("onerow", [1, 128], F32))
    ident_sb = P(nc.sbuf_tensor("ident_sb", [128, 128], BF16))

    wkT_bf = B(nc.sbuf_tensor("wkT_bf", [128, ECH * D], BF16))
    wvT_bf = B(nc.sbuf_tensor("wvT_bf", [128, ECH * D], BF16))
    q_sb = B(nc.sbuf_tensor("q_sb", [128, ECH * TOWN], BF16))
    o_sb = B(nc.sbuf_tensor("o_sb", [128, 16 * D], F32))
    l_sb = B(nc.sbuf_tensor("l_sb", [128, 16], F32))
    xb = B(nc.sbuf_tensor("xb", [128, D], F32))
    hb = B(nc.sbuf_tensor("hb", [128, 2 * D], BF16))
    hf = B(nc.sbuf_tensor("hf", [128, ECH * 512], BF16))
    k_sb = B(nc.sbuf_tensor("k_sb", [128, ECH * 512], BF16))
    v_sb = B(nc.sbuf_tensor("v_sb", [128, 4 * D], BF16))
    pt_sb = B(nc.sbuf_tensor("pt_sb", [128, 4 * TOWN], BF16))
    ramp_sb = B(nc.sbuf_tensor("ramp_sb", [128, TOWN], F32))
    ev_scr = B(nc.sbuf_tensor("ev_scr", [128, 512], F32))
    es_scr = B(nc.sbuf_tensor("es_scr", [128, 512], F32))

    wstage = o_sb[:, 0:ECH * D]
    wq_bf = pt_sb

    ps_scope = ExitStack()
    pp = ps_scope.enter_context
    ps_pvA = pp(nc.psum_tensor("ps_pvA", [128, D], F32))
    ps_pvB = pp(nc.psum_tensor("ps_pvB", [128, D], F32))
    ps_s0 = pp(nc.psum_tensor("ps_s0", [128, 512], F32))
    ps_s1 = pp(nc.psum_tensor("ps_s1", [128, 512], F32))
    ps_l = pp(nc.psum_tensor("ps_l", [128, 512], F32))
    ps_sq = pp(nc.psum_tensor("ps_sq", [128, 512], F32))

    def _load_act(idx):
        def f(E):
            E.add_instruction(mybir.InstLoadActFuncSet(
                name=nc.get_next_instruction_name(),
                act_func_set_id=idx, ins=[], outs=[]))
        return f

    dma("sync", ramp_sb[:], ramp_in, ld)
    dma("sync", tthr_sb[:], tthr_in, ld)
    dma("sync", ident_sb[:], ident_in, ld)
    dma("sync", brow[0:1, 0:D], bq_in[None, :], ld)
    dma("sync", brow[0:1, D:2 * D], bk_in[None, :], ld)
    n_brow = dma("sync", brow[0:1, 2 * D:3 * D], bv_in[None, :], ld)

    em("vector", lambda E: E.memset(onef[:], 1.0))
    em("vector", lambda E: E.memset(onerow[:], 1.0))
    em("vector", lambda E: E.memset(ones1_bf[:], 1.0).then_inc(dv.h, 1))
    dv_ones = dv.inc()

    dv_wcast = 0
    for name, win, dst in (("q", wqT_in, None), ("k", wkT_in, wkT_bf), ("v", wvT_in, wvT_bf)):
        n_w = dma("sync", wstage.rearrange("p (c e) -> p c e", e=D),
                  win.rearrange("(c p) e -> p c e", p=128), ld)
        wait("vector", ld, n_w)
        if name == "q":
            em("vector", lambda E: E.tensor_copy(out=wq_bf[:, 0:ECH * D], in_=wstage)
               .then_inc(dv.h, 1))
        else:
            em("vector", lambda E, d=dst: E.tensor_copy(out=d[:], in_=wstage)
               .then_inc(dv.h, 1))
        dv_wcast = dv.inc()
        wait("sync", dv, dv_wcast)
    em("vector", lambda E: E.memset(o_sb[:], 0.0))
    em("vector", lambda E: E.memset(l_sb[:], 0.0).then_inc(dv.h, 1))
    dv_zero = dv.inc()

    wait("tensor", ld, n_brow)
    wait("tensor", dv, dv_ones)
    for bi in range(2):
        for ecj in range(ECH):
            em("tensor", lambda E, b=bi, c=ecj: E.matmul(
                out=ps_sq[:, b * 8 + c:b * 8 + c + 1],
                lhsT=brow[0:1, b * D + c * 128:b * D + (c + 1) * 128],
                rhs=onef[0:1, 0:1], start=True, stop=True).then_inc(pe.h, 1))
            n = pe.inc()
            wait("tensor", pe, n)
    pe_bias = pe.n
    wait("scalar", pe, pe_bias)
    em("scalar", lambda E: E.activation(out=bqc_sb[:], in_=ps_sq[:, 0:8],
                                        func=AF.Identity, bias=0.0, scale=np.float32(1.0 / 32.0)))
    em("scalar", lambda E: E.activation(out=bkc_sb[:], in_=ps_sq[:, 8:16],
                                        func=AF.Identity, bias=0.0, scale=1.0)
       .then_inc(ac.h, 1))
    ac_bias = ac.inc()
    for hh in range(2):
        wait("tensor", ac, ac_bias)
        em("tensor", lambda E, h=hh: E.matmul(
            out=ps_sq[:, 0:512],
            lhsT=onerow[0:1, :],
            rhs=brow[0:1, 2 * D + h * 512:2 * D + (h + 1) * 512],
            start=True, stop=True).then_inc(pe.h, 1))
        pe_bv = pe.inc()
        wait("scalar", pe, pe_bv)
        em("scalar", lambda E, h=hh: E.activation(
            out=bvb_sb[:, h * 512:(h + 1) * 512], in_=ps_sq[:, 0:512],
            func=AF.Identity, bias=0.0, scale=1.0).then_inc(ac.h, 1))
        ac_bias = ac.inc()
    ac_bvb = ac_bias

    def ln_tiles(x_ap_list, h_out_list, wait_loads, slot0=0):
        nt = len(x_ap_list)
        wait("scalar", ld, wait_loads)
        wait("scalar", ac, ac_bvb)
        em("scalar", lambda E: E.memzero(stat_sb[:, 8 + slot0:8 + slot0 + nt]))
        em("scalar", lambda E: E.memzero(stat_sb[:, 16 + slot0:16 + slot0 + nt]))
        for i, x in enumerate(x_ap_list):
            s = slot0 + i
            em("scalar", lambda E, x=x, s=s: E.activation(
                out=ps_sq[:], in_=x[:, 0:512], func=AF.Square, bias=0.0, scale=1.0,
                accum_out=stat_sb[:, 8 + s:9 + s]))
            em("scalar", lambda E, x=x, s=s: E.activation(
                out=ps_sq[:], in_=x[:, 512:1024], func=AF.Square, bias=0.0, scale=1.0,
                accum_out=stat_sb[:, 16 + s:17 + s]).then_inc(ac.h, 1))
            ac.inc()
        ac_sq = ac.n
        if limit == "ln0":
            finish_debug([stat_sb[:]])
            return "STOP"
        wait("vector", ld, wait_loads)
        for i, x in enumerate(x_ap_list):
            s = slot0 + i
            last = (i == len(x_ap_list) - 1)
            if last:
                em("vector", lambda E, x=x, s=s: E.reduce_sum(
                    out=stat_sb[:, s:s + 1], in_=x, axis=AX.X).then_inc(dv.h, 1))
                wait("vector", dv, dv.inc())
            else:
                em("vector", lambda E, x=x, s=s: E.reduce_sum(
                    out=stat_sb[:, s:s + 1], in_=x, axis=AX.X))
        wait("vector", ac, ac_sq)
        sl = slice(slot0, slot0 + nt)
        s2a = slice(8 + slot0, 8 + slot0 + nt)
        s2b = slice(16 + slot0, 16 + slot0 + nt)
        mu = slice(24 + slot0, 24 + slot0 + nt)
        var = slice(32 + slot0, 32 + slot0 + nt)
        rstd = slice(40 + slot0, 40 + slot0 + nt)
        ng = slice(48 + slot0, 48 + slot0 + nt)

        def _step(fn):
            em("vector", lambda E, f=fn: f(E).then_inc(dv.h, 1))
            n = dv.inc()
            wait("vector", dv, n)
            return n
        _step(lambda E: E.tensor_scalar(
            out=stat_sb[:, mu], in0=stat_sb[:, sl], scalar1=1.0 / D, scalar2=None,
            op0=ALU.mult))
        _step(lambda E: E.tensor_tensor(
            out=stat_sb[:, var], in0=stat_sb[:, s2a], in1=stat_sb[:, s2b], op=ALU.add))
        _step(lambda E: E.tensor_scalar(
            out=stat_sb[:, var], in0=stat_sb[:, var], scalar1=1.0 / D, scalar2=None,
            op0=ALU.mult))
        _step(lambda E: E.tensor_tensor(
            out=stat_sb[:, s2a], in0=stat_sb[:, mu], in1=stat_sb[:, mu], op=ALU.mult))
        dv_var = _step(lambda E: E.tensor_tensor(
            out=stat_sb[:, var], in0=stat_sb[:, var], in1=stat_sb[:, s2a],
            op=ALU.subtract))
        wait("scalar", dv, dv_var)
        em("scalar", lambda E: E.activation(
            out=stat_sb[:, s2b], in_=stat_sb[:, var], func=AF.Ln, bias=np.float32(1e-5), scale=1.0))
        em("scalar", lambda E: E.activation(
            out=stat_sb[:, rstd], in_=stat_sb[:, s2b], func=AF.Exp, bias=0.0,
            scale=np.float32(-0.5)).then_inc(ac.h, 1))
        ac_rstd = ac.inc()
        wait("vector", ac, ac_rstd)
        _step(lambda E: E.tensor_tensor(
            out=stat_sb[:, ng], in0=stat_sb[:, mu], in1=stat_sb[:, rstd], op=ALU.mult))
        dv_ng = _step(lambda E: E.tensor_scalar(
            out=stat_sb[:, ng], in0=stat_sb[:, ng], scalar1=-1.0, scalar2=None,
            op0=ALU.mult))
        wait("scalar", dv, dv_ng)
        for i, (x, h) in enumerate(zip(x_ap_list, h_out_list)):
            s = slot0 + i
            em("scalar", lambda E, x=x, h=h, s=s: E.activation(
                out=h, in_=x, func=AF.Identity,
                bias=stat_sb[:, 48 + s:49 + s], scale=stat_sb[:, 40 + s:41 + s])
               .then_inc(ac.h, 1))
            ac.inc()
        return ac.n

    dbg_out = nc.dram_tensor("dbg", [128, 16 * D], F32, kind="ExternalOutput").ap() if limit else None

    def finish_debug(ap_list):
        col = 0
        wait("sync", ac, ac.n)
        wait("sync", dv, dv.n)
        wait("sync", pe, pe.n)
        nd = 0
        for ap_ in ap_list:
            ncol = ap_.shape[-1]
            nd = dma("sync", dbg_out[:, col:col + ncol], ap_, so)
            col += ncol
        wait("sync", so, nd)

    if limit == "init":
        finish_debug([bvb_sb[:], bqc_sb[:], bkc_sb[:]])
        return finalize()
    hbf0 = hb[:].bitcast(F32)
    evs0 = ev_scr[:, 0:512]
    evs1 = es_scr[:, 0:512]
    dv_red = {0: 0, 1: 0}
    ac_sq2 = {0: 0, 1: 0}
    dv_chain_prev = 0
    for b8 in range(8):
        for t8 in range(8):
            t = b8 * 8 + t8
            sl = t % 2
            xbuf = xb[:] if sl == 0 else hbf0
            wait("sync", dv, dv_red[sl])
            wait("sync", ac, ac_sq2[sl])
            n_x = dma("sync", xbuf if False else (xb[:] if sl == 0 else hbf0),
                      xkv_in[t * 128:(t + 1) * 128, :], ld)
            xbuf = xb[:] if sl == 0 else hbf0
            wait("vector", ld, n_x)
            em("vector", lambda E, x=xbuf, t=t: E.reduce_sum(
                out=stat_sb[:, t:t + 1], in_=x, axis=AX.X).then_inc(dv.h, 1))
            dv_red[sl] = dv.inc()
            wait("scalar", ld, n_x)
            em("scalar", lambda E, x=xbuf, t=t: E.memzero(stat_sb[:, 64 + t:65 + t]))
            em("scalar", lambda E, x=xbuf, t=t: E.memzero(stat_sb[:, 128 + t:129 + t]))
            em("scalar", lambda E, x=xbuf, t=t: E.activation(
                out=ps_sq[:], in_=x[:, 0:512], func=AF.Square, bias=0.0, scale=1.0,
                accum_out=stat_sb[:, 64 + t:65 + t]))
            em("scalar", lambda E, x=xbuf, t=t: E.activation(
                out=ps_sq[:], in_=x[:, 512:1024], func=AF.Square, bias=0.0, scale=1.0,
                accum_out=stat_sb[:, 128 + t:129 + t]).then_inc(ac.h, 1))
            ac_sq2[sl] = ac.inc()
        c0 = b8 * 8
        wait("vector", ac, ac_sq2[0])
        wait("vector", ac, ac_sq2[1])
        wait("vector", dv, max(dv_red[0], dv_red[1]))

        def _stp(fn):
            em("vector", lambda E, f=fn: f(E).then_inc(dv.h, 1))
            n = dv.inc()
            wait("vector", dv, n)
            return n
        _stp(lambda E, c0=c0: E.tensor_scalar(
            out=stat_sb[:, 192 + c0:200 + c0], in0=stat_sb[:, c0:c0 + 8],
            scalar1=1.0 / D, scalar2=None, op0=ALU.mult))
        _stp(lambda E, c0=c0: E.tensor_tensor(
            out=stat_sb[:, 64 + c0:72 + c0], in0=stat_sb[:, 64 + c0:72 + c0],
            in1=stat_sb[:, 128 + c0:136 + c0], op=ALU.add))
        _stp(lambda E, c0=c0: E.tensor_scalar(
            out=stat_sb[:, 64 + c0:72 + c0], in0=stat_sb[:, 64 + c0:72 + c0],
            scalar1=1.0 / D, scalar2=None, op0=ALU.mult))
        _stp(lambda E, c0=c0: E.tensor_tensor(
            out=stat_sb[:, 128 + c0:136 + c0], in0=stat_sb[:, 192 + c0:200 + c0],
            in1=stat_sb[:, 192 + c0:200 + c0], op=ALU.mult))
        dvv_ = _stp(lambda E, c0=c0: E.tensor_tensor(
            out=stat_sb[:, 64 + c0:72 + c0], in0=stat_sb[:, 64 + c0:72 + c0],
            in1=stat_sb[:, 128 + c0:136 + c0], op=ALU.subtract))
        wait("scalar", dv, dvv_)
        em("scalar", lambda E, c0=c0: E.activation(
            out=stat_sb[:, 128 + c0:136 + c0], in_=stat_sb[:, 64 + c0:72 + c0],
            func=AF.Ln, bias=np.float32(1e-5), scale=1.0))
        em("scalar", lambda E, c0=c0: E.activation(
            out=rstd_all[:, c0:c0 + 8], in_=stat_sb[:, 128 + c0:136 + c0],
            func=AF.Exp, bias=0.0, scale=np.float32(-0.5)).then_inc(ac.h, 1))
        acr_ = ac.inc()
        wait("vector", ac, acr_)
        _stp(lambda E, c0=c0: E.tensor_tensor(
            out=negm_all[:, c0:c0 + 8], in0=stat_sb[:, 192 + c0:200 + c0],
            in1=rstd_all[:, c0:c0 + 8], op=ALU.mult))
        dv_chain_prev = _stp(lambda E, c0=c0: E.tensor_scalar(
            out=negm_all[:, c0:c0 + 8], in0=negm_all[:, c0:c0 + 8],
            scalar1=-1.0, scalar2=None, op0=ALU.mult))
    dv_stats_done = dv.n
    ac_norm = {0: 0, 1: 0}
    sh_hstore_prev = 0
    sh_hstore_pp = 0
    wait("scalar", dv, dv_stats_done)
    for t in range(64):
        sl = t % 2
        wait("sync", ac, ac_norm[sl])
        if sl == 0:
            n_x = dma("sync", xb[:], xkv_in[t * 128:(t + 1) * 128, :], ld)
        else:
            n_x = dma("sync", evs0, xkv_in[t * 128:(t + 1) * 128, 0:512], ld)
            n_x = dma("sync", evs1, xkv_in[t * 128:(t + 1) * 128, 512:1024], ld)
        hsl = t % 2
        wait("scalar", ld, n_x)
        wait("scalar", sh, sh_hstore_pp)
        if sl == 0:
            em("scalar", lambda E, t=t, h=hsl: E.activation(
                out=hb[:, h * D:h * D + D], in_=xb[:], func=AF.Identity,
                bias=negm_all[:, t:t + 1], scale=rstd_all[:, t:t + 1])
               .then_inc(ac.h, 1))
        else:
            em("scalar", lambda E, t=t, h=hsl: E.activation(
                out=hb[:, h * D:h * D + 512], in_=evs0, func=AF.Identity,
                bias=negm_all[:, t:t + 1], scale=rstd_all[:, t:t + 1]))
            em("scalar", lambda E, t=t, h=hsl: E.activation(
                out=hb[:, h * D + 512:h * D + D], in_=evs1, func=AF.Identity,
                bias=negm_all[:, t:t + 1], scale=rstd_all[:, t:t + 1])
               .then_inc(ac.h, 1))
        ac_norm[sl] = ac.inc()
        wait("scalar", ac, ac_norm[sl])
        sh_hstore_pp = sh_hstore_prev
        sh_hstore_prev = dma("scalar", h_scr[t * 128:(t + 1) * 128, :],
                             hb[:, hsl * D:(hsl + 1) * D], sh)
    sh_hdone = sh_hstore_prev

    for b8 in range(2):
        for t8 in range(8):
            t = b8 * 8 + t8
            sl = t % 2
            xbuf = xb[:] if sl == 0 else hbf0
            wait("sync", dv, dv_red[sl])
            wait("sync", ac, ac_sq2[sl])
            wait("sync", sh, sh_hdone if t == 0 else 0)
            n_x = dma("sync", xb[:] if sl == 0 else hbf0,
                      xq_in[t * 128:(t + 1) * 128, :], ld)
            wait("vector", ld, n_x)
            em("vector", lambda E, x=xbuf, t=t: E.reduce_sum(
                out=stat_sb[:, t:t + 1], in_=x, axis=AX.X).then_inc(dv.h, 1))
            dv_red[sl] = dv.inc()
            wait("scalar", ld, n_x)
            em("scalar", lambda E, t=t: E.memzero(stat_sb[:, 64 + t:65 + t]))
            em("scalar", lambda E, t=t: E.memzero(stat_sb[:, 128 + t:129 + t]))
            em("scalar", lambda E, x=xbuf, t=t: E.activation(
                out=ps_sq[:], in_=x[:, 0:512], func=AF.Square, bias=0.0, scale=1.0,
                accum_out=stat_sb[:, 64 + t:65 + t]))
            em("scalar", lambda E, x=xbuf, t=t: E.activation(
                out=ps_sq[:], in_=x[:, 512:1024], func=AF.Square, bias=0.0, scale=1.0,
                accum_out=stat_sb[:, 128 + t:129 + t]).then_inc(ac.h, 1))
            ac_sq2[sl] = ac.inc()
        c0 = b8 * 8
        q0 = 64 + b8 * 8
        wait("vector", ac, ac_sq2[0])
        wait("vector", ac, ac_sq2[1])
        wait("vector", dv, max(dv_red[0], dv_red[1]))

        def _stq(fn):
            em("vector", lambda E, f=fn: f(E).then_inc(dv.h, 1))
            n = dv.inc()
            wait("vector", dv, n)
            return n
        _stq(lambda E, c0=c0: E.tensor_scalar(
            out=stat_sb[:, 192 + c0:200 + c0], in0=stat_sb[:, c0:c0 + 8],
            scalar1=1.0 / D, scalar2=None, op0=ALU.mult))
        _stq(lambda E, c0=c0: E.tensor_tensor(
            out=stat_sb[:, 64 + c0:72 + c0], in0=stat_sb[:, 64 + c0:72 + c0],
            in1=stat_sb[:, 128 + c0:136 + c0], op=ALU.add))
        _stq(lambda E, c0=c0: E.tensor_scalar(
            out=stat_sb[:, 64 + c0:72 + c0], in0=stat_sb[:, 64 + c0:72 + c0],
            scalar1=1.0 / D, scalar2=None, op0=ALU.mult))
        _stq(lambda E, c0=c0: E.tensor_tensor(
            out=stat_sb[:, 128 + c0:136 + c0], in0=stat_sb[:, 192 + c0:200 + c0],
            in1=stat_sb[:, 192 + c0:200 + c0], op=ALU.mult))
        dvv_ = _stq(lambda E, c0=c0: E.tensor_tensor(
            out=stat_sb[:, 64 + c0:72 + c0], in0=stat_sb[:, 64 + c0:72 + c0],
            in1=stat_sb[:, 128 + c0:136 + c0], op=ALU.subtract))
        wait("scalar", dv, dvv_)
        em("scalar", lambda E, c0=c0: E.activation(
            out=stat_sb[:, 128 + c0:136 + c0], in_=stat_sb[:, 64 + c0:72 + c0],
            func=AF.Ln, bias=np.float32(1e-5), scale=1.0))
        em("scalar", lambda E, c0=c0, q0=q0: E.activation(
            out=rstd_all[:, q0:q0 + 8], in_=stat_sb[:, 128 + c0:136 + c0],
            func=AF.Exp, bias=0.0, scale=np.float32(-0.5)).then_inc(ac.h, 1))
        acr_ = ac.inc()
        wait("vector", ac, acr_)
        _stq(lambda E, c0=c0, q0=q0: E.tensor_tensor(
            out=negm_all[:, q0:q0 + 8], in0=stat_sb[:, 192 + c0:200 + c0],
            in1=rstd_all[:, q0:q0 + 8], op=ALU.mult))
        _stq(lambda E, q0=q0: E.tensor_scalar(
            out=negm_all[:, q0:q0 + 8], in0=negm_all[:, q0:q0 + 8],
            scalar1=-1.0, scalar2=None, op0=ALU.mult))
    dv_statsq_done = dv.n
    ac_norm = {0: 0, 1: 0}
    wait("scalar", dv, dv_statsq_done)
    for t in range(16):
        sl = t % 2
        wait("sync", ac, ac_norm[sl])
        if sl == 0:
            n_x = dma("sync", xb[:], xq_in[t * 128:(t + 1) * 128, :], ld)
        else:
            n_x = dma("sync", evs0, xq_in[t * 128:(t + 1) * 128, 0:512], ld)
            n_x = dma("sync", evs1, xq_in[t * 128:(t + 1) * 128, 512:1024], ld)
        hsl = t % 2
        wait("scalar", ld, n_x)
        wait("scalar", sh, sh_hstore_pp)
        if sl == 0:
            em("scalar", lambda E, t=t, h=hsl: E.activation(
                out=hb[:, h * D:h * D + D], in_=xb[:], func=AF.Identity,
                bias=negm_all[:, 64 + t:65 + t], scale=rstd_all[:, 64 + t:65 + t])
               .then_inc(ac.h, 1))
        else:
            em("scalar", lambda E, t=t, h=hsl: E.activation(
                out=hb[:, h * D:h * D + 512], in_=evs0, func=AF.Identity,
                bias=negm_all[:, 64 + t:65 + t], scale=rstd_all[:, 64 + t:65 + t]))
            em("scalar", lambda E, t=t, h=hsl: E.activation(
                out=hb[:, h * D + 512:h * D + D], in_=evs1, func=AF.Identity,
                bias=negm_all[:, 64 + t:65 + t], scale=rstd_all[:, 64 + t:65 + t])
               .then_inc(ac.h, 1))
        ac_norm[sl] = ac.inc()
        wait("scalar", ac, ac_norm[sl])
        sh_hstore_pp = sh_hstore_prev
        sh_hstore_prev = dma("scalar", hq_scr[t * 128:(t + 1) * 128, :],
                             hb[:, hsl * D:(hsl + 1) * D], sh)
    sh_hdone_q = sh_hstore_prev

    pe_hf_prev = 0
    ac_evacA = {0: 0, 1: 0}
    dv_evA = {0: 0, 1: 0}
    wait("sync", sh, sh_hdone_q)
    for g in range(1 if limit in ("A1", "A0") else 4):
        wait("sync", pe, pe_hf_prev)
        n_hf = 0
        for dc in range(ECH):
            n_hf = dma("sync", hf[:, dc * 512:(dc + 1) * 512],
                       hq_scr[g * 512:(g + 1) * 512, dc * 128:(dc + 1) * 128],
                       ld, transpose=True)
        if limit == "hf" and g == 0:
            wait("sync", ld, n_hf)
            finish_debug([hf[:].bitcast(F32)])
            return finalize()
        wait("tensor", ld, n_hf)
        wait("tensor", dv, dv_wcast)
        for ec in range(1 if limit == "A0" else ECH):
            hsel = ec % 2
            wait("tensor", ac, ac_evacA[hsel])
            for dc in range(ECH):
                em("tensor", lambda E, ec=ec, dc=dc, h=hsel: E.matmul(
                    out=ps_pvA[:, h * 512:(h + 1) * 512],
                    lhsT=wq_bf[:, dc * D + ec * 128:dc * D + ec * 128 + 128],
                    rhs=hf[:, dc * 512:(dc + 1) * 512],
                    start=(dc == 0), stop=(dc == ECH - 1)).then_inc(pe.h, 1))
                pe.inc()
            pe_q = pe.n
            wait("scalar", pe, pe_q)
            wait("scalar", dv, max(dv_evA[0], dv_evA[1]))
            em("scalar", lambda E, ec=ec, h=hsel: E.activation(
                out=ev_scr[:, 0:512],
                in_=ps_pvA[:, h * 512:(h + 1) * 512], func=AF.Identity,
                bias=bqc_sb[:, ec:ec + 1], scale=np.float32(1.0 / 32.0)).then_inc(ac.h, 1))
            ac_evacA[hsel] = ac.inc()
            wait("vector", ac, ac_evacA[hsel])
            em("vector", lambda E, ec=ec, g=g, h=hsel: E.tensor_copy(
                out=q_sb[:, ec * TOWN + g * 512:ec * TOWN + (g + 1) * 512],
                in_=ev_scr[:, 0:512]).then_inc(dv.h, 1))
            dv_evA[hsel] = dv.inc()
        pe_hf_prev = pe.n

    if limit in ("A", "A1", "A0"):
        finish_debug([q_sb[:].bitcast(F32)])
        return finalize()
    wsrc = [xb[:, 0:256], xb[:, 256:512]]
    wxn = xb[:, 512:768]
    wtt = xb[:, 768:1024]
    wacc = hb[:, 0:512].bitcast(F32)
    wi1 = hb[:, 512:1024].bitcast(I32)
    wsg = hb[:, 1024:1280]
    wdst = [hb[:, 1280:1536], hb[:, 1536:1792]]
    wsmall = hb[:, 1792:2048].bitcast(F32)
    wbm = wsmall[:, 0:16]
    wsc = wsmall[:, 16:32]
    wrx = wsmall[:, 32:48]
    wchunks = []
    for (w_in_t, w_scr, nrow) in ((wg_in, wgq_scr, II), (wu_in, wuq_scr, II),
                                  (wd_in, wdq_scr, D)):
        ncol = D if nrow == II else II
        for t in range(nrow // 128):
            for coff in range(0, ncol, 256):
                wchunks.append((w_in_t, w_scr, t, coff))
    wpos = {"i": 0}
    wslot = {0: 0, 1: 0}
    wsh = {0: 0, 1: 0}
    wst_state = {"sh_last": 0}

    def emit_wchunks(n):
        for _ in range(n):
            i = wpos["i"]
            if i >= len(wchunks):
                return
            wpos["i"] += 1
            w_in_t, w_scr, t, coff = wchunks[i]
            slot = i % 2
            wait("sync", dv, wslot[slot])
            n_w = dma("sync", wsrc[slot], w_in_t[t * 128:(t + 1) * 128, coff:coff + 256], ld)
            wait("scalar", dv, max(wslot[0], wslot[1]))
            wait("scalar", ld, n_w)
            em("scalar", lambda E, s=slot: E.activation(
                out=wtt[:], in_=wsrc[s], func=AF.Abs, bias=0.0, scale=1.0))
            em("scalar", lambda E, s=slot: E.activation(
                out=wsg[:], in_=wsrc[s], func=AF.Sign, bias=0.0, scale=1.0)
               .then_inc(ac.h, 1))
            acsg = ac.inc()
            wait("vector", ac, acsg)
            wait("vector", sh, wsh[slot])

            def _w(fn):
                em("vector", lambda E, f=fn: f(E).then_inc(dv.h, 1))
                n = dv.inc()
                wait("vector", dv, n)
                return n
            g16 = wtt.rearrange("p (g k) -> p g k", k=16)
            _w(lambda E: E.reduce_max(out=wbm, in_=g16, axis=AX.X))
            _w(lambda E: E.tensor_scalar(out=wsc, in0=wbm, scalar1=1.0 / 6.0,
                                         scalar2=1e-12, op0=ALU.mult, op1=ALU.max))
            _w(lambda E: E.reciprocal(out=wrx, in_=wsc))
            rxb = wrx.unsqueeze(2).broadcast_to([128, 16, 16])
            scb = wsc.unsqueeze(2).broadcast_to([128, 16, 16])
            em("vector", lambda E: E.tensor_tensor(
                out=wxn.rearrange("p (g k) -> p g k", k=16), in0=g16, in1=rxb,
                op=ALU.mult))
            em("vector", lambda E: E.tensor_scalar(
                out=wi1[:], in0=wxn, scalar1=2.0, scalar2=None, op0=ALU.mult))
            em("vector", lambda E: E.tensor_scalar(
                out=wacc[:], in0=wi1[:], scalar1=4.0, scalar2=0.5,
                op0=ALU.min, op1=ALU.mult))
            em("vector", lambda E: E.tensor_scalar(
                out=wi1[:], in0=wxn, scalar1=1.0, scalar2=None, op0=ALU.mult))
            em("vector", lambda E: E.tensor_scalar(
                out=wtt[:], in0=wi1[:], scalar1=2.0, scalar2=2.0,
                op0=ALU.subtract, op1=ALU.min))
            em("vector", lambda E: E.scalar_tensor_tensor(
                out=wacc[:], in0=wtt[:], scalar=0.0, in1=wacc[:],
                op0=ALU.max, op1=ALU.add))
            em("vector", lambda E: E.tensor_scalar(
                out=wi1[:], in0=wxn, scalar1=0.5, scalar2=None, op0=ALU.mult))
            em("vector", lambda E: E.tensor_scalar(
                out=wtt[:], in0=wi1[:], scalar1=2.0, scalar2=4.0,
                op0=ALU.mult, op1=ALU.subtract))
            em("vector", lambda E: E.scalar_tensor_tensor(
                out=wacc[:], in0=wtt[:], scalar=0.0, in1=wacc[:],
                op0=ALU.max, op1=ALU.add))
            em("vector", lambda E: E.tensor_tensor(
                out=wtt.rearrange("p (g k) -> p g k", k=16),
                in0=wacc[:].rearrange("p (g k) -> p g k", k=16), in1=scb,
                op=ALU.mult))
            em("vector", lambda E, s=slot: E.tensor_tensor(
                out=wdst[s], in0=wtt[:], in1=wsg[:], op=ALU.mult).then_inc(dv.h, 1))
            wslot[slot] = dv.inc()
            wait("scalar", dv, wslot[slot])
            wsh[slot] = dma("scalar", w_scr[t * 128:(t + 1) * 128, coff:coff + 256],
                            wdst[slot], sh)
            wst_state["sh_last"] = wsh[slot]

    ac_exp_last = {0: 0, 1: 0}
    dv_mask_b_last = {0: 0, 1: 0}
    ac_kevac = {0: 0, 1: 0}
    dv_vevac = {0: 0, 1: 0}
    dv_oacc = {}
    dv_lacc_prev = 0
    pe_pv_all_prev = 0
    nsu_run = limit if isinstance(limit, int) else NSU
    for su in range(nsu_run):
        kt, half = su // 2, su % 2
        key0 = kt * 1024 + half * 512
        wait("sync", pe, pe_hf_prev)
        n_hf = 0
        for dc in range(ECH):
            n_hf = dma("sync", hf[:, dc * 512:(dc + 1) * 512],
                       h_scr[key0:key0 + 512, dc * 128:(dc + 1) * 128], ld, transpose=True)
        wait("tensor", ld, n_hf)
        wait("tensor", dv, dv_oacc.get(14, 0))
        for ec in range(ECH):
            hsel = ec % 2
            wait("tensor", ac, ac_kevac[hsel])
            wait("tensor", dv, dv_vevac[hsel])
            for dc in range(ECH):
                em("tensor", lambda E, ec=ec, dc=dc, h=hsel: E.matmul(
                    out=ps_pvA[:, h * 512:(h + 1) * 512],
                    lhsT=wkT_bf[:, dc * D + ec * 128:dc * D + ec * 128 + 128],
                    rhs=hf[:, dc * 512:(dc + 1) * 512],
                    start=(dc == 0), stop=(dc == ECH - 1)).then_inc(pe.h, 1))
                pe.inc()
            pe_k = pe.n
            wait("scalar", pe, pe_k)
            wait("scalar", dv, max(dv_vevac[0], dv_vevac[1]))
            em("scalar", lambda E, ec=ec, h=hsel: E.activation(
                out=ev_scr[:, 0:512],
                in_=ps_pvA[:, h * 512:(h + 1) * 512], func=AF.Identity,
                bias=bkc_sb[:, ec:ec + 1], scale=1.0).then_inc(ac.h, 1))
            ac_kevac[hsel] = ac.inc()
            wait("vector", ac, ac_kevac[hsel])
            em("vector", lambda E, ec=ec, h=hsel: E.tensor_copy(
                out=k_sb[:, ec * 512:(ec + 1) * 512],
                in_=ev_scr[:, 0:512]).then_inc(dv.h, 1))
            dv_vevac[hsel] = dv.inc()
        for kc in range(4):
            for vh in range(2):
                hsel = (kc * 2 + vh) % 2
                wait("tensor", ac, ac_kevac[hsel])
                wait("tensor", dv, dv_vevac[hsel])
                for dc in range(ECH):
                    em("tensor", lambda E, kc=kc, vh=vh, dc=dc, h=hsel: E.matmul(
                        out=ps_pvA[:, h * 512:(h + 1) * 512],
                        lhsT=hf[:, dc * 512 + kc * 128:dc * 512 + kc * 128 + 128],
                        rhs=wvT_bf[:, dc * D + vh * 512:dc * D + (vh + 1) * 512],
                        start=(dc == 0), stop=(dc == ECH - 1)).then_inc(pe.h, 1))
                    pe.inc()
                pe_v = pe.n
                wait("scalar", pe, pe_v)
                wait("scalar", dv, max(dv_vevac[0], dv_vevac[1]))
                em("scalar", lambda E, h=hsel: E.activation(
                    out=ev_scr[:, 0:512],
                    in_=ps_pvA[:, h * 512:(h + 1) * 512], func=AF.Identity,
                    bias=0.0, scale=1.0).then_inc(ac.h, 1))
                ac_kevac[hsel] = ac.inc()
                wait("vector", ac, ac_kevac[hsel])
                em("vector", lambda E, kc=kc, vh=vh, h=hsel: E.tensor_tensor(
                    out=v_sb[:, kc * D + vh * 512:kc * D + (vh + 1) * 512],
                    in0=ev_scr[:, 0:512],
                    in1=bvb_sb[:, vh * 512:(vh + 1) * 512], op=ALU.add).then_inc(dv.h, 1))
                dv_vevac[hsel] = dv.inc()
        pe_hf_prev = pe.n

        ac_exp = dict(ac_exp_last)
        pair = 0
        dv_mask_b = dict(dv_mask_b_last)
        for kc in range(4):
            for qh in range(4):
                bank = pair % 2
                ps_s = ps_s0 if bank == 0 else ps_s1
                wait("tensor", ac, ac_exp[bank])
                for ec in range(ECH):
                    em("tensor", lambda E, kc=kc, qh=qh, ec=ec, ps=ps_s: E.matmul(
                        out=ps[:, 0:512],
                        lhsT=k_sb[:, ec * 512 + kc * 128:ec * 512 + kc * 128 + 128],
                        rhs=q_sb[:, ec * TOWN + qh * 512:ec * TOWN + (qh + 1) * 512],
                        start=(ec == 0), stop=(ec == ECH - 1)).then_inc(pe.h, 1))
                    pe.inc()
                pe_s = pe.n
                wait("scalar", pe, pe_s)
                wait("scalar", dv, max(dv_mask_b[0], dv_mask_b[1]))
                em("scalar", lambda E, b=bank, ps=ps_s: E.activation(
                    out=es_scr[:, 0:512],
                    in_=ps[:, 0:512], func=AF.Exp, bias=np.float32(NEG),
                    scale=1.0).then_inc(ac.h, 1))
                ac_exp[bank] = ac.inc()
                wait("vector", ac, ac_exp[bank])
                wait("vector", pe, pe_pv_all_prev)
                em("vector", lambda E, su=su, kc=kc, qh=qh, b=bank: E.scalar_tensor_tensor(
                    out=pt_sb[:, kc * TOWN + qh * 512:kc * TOWN + (qh + 1) * 512],
                    in0=ramp_sb[:, qh * 512:(qh + 1) * 512],
                    scalar=tthr_sb[:, su * 4 + kc:su * 4 + kc + 1],
                    in1=es_scr[:, 0:512],
                    op0=ALU.is_ge, op1=ALU.mult).then_inc(dv.h, 1))
                dv_mask_b[bank] = dv.inc()
                pair += 1
        ac_exp_last = ac_exp
        dv_mask_b_last = dict(dv_mask_b)
        dv_mask_all = dv.n

        dv_oacc_prev = dict(dv_oacc)
        wait("tensor", dv, dv_mask_all)
        wait("tensor", dv, dv_lacc_prev)
        lbase = (su % 2) * 16
        pe_pv = {}
        for qs in range(16):
            buf = qs % 2
            ps_o = ps_pvA if buf == 0 else ps_pvB
            if qs >= 2:
                wait("tensor", dv, dv_oacc[qs - 2])
            elif qs == 1:
                wait("tensor", dv, dv_oacc_prev.get(15, 0))
            for kc in range(4):
                em("tensor", lambda E, qs=qs, kc=kc, ps=ps_o: E.matmul(
                    out=ps[:, 0:512],
                    lhsT=pt_sb[:, kc * TOWN + qs * 128:kc * TOWN + qs * 128 + 128],
                    rhs=v_sb[:, kc * D:kc * D + 512],
                    start=(kc == 0), stop=(kc == 3)).then_inc(pe.h, 1))
                pe.inc()
                em("tensor", lambda E, qs=qs, kc=kc, ps=ps_o: E.matmul(
                    out=ps[:, 512:1024],
                    lhsT=pt_sb[:, kc * TOWN + qs * 128:kc * TOWN + qs * 128 + 128],
                    rhs=v_sb[:, kc * D + 512:kc * D + 1024],
                    start=(kc == 0), stop=(kc == 3)).then_inc(pe.h, 1))
                pe.inc()
                em("tensor", lambda E, qs=qs, kc=kc, lc=lbase + qs: E.matmul(
                    out=ps_l[:, lc:lc + 1],
                    lhsT=pt_sb[:, kc * TOWN + qs * 128:kc * TOWN + qs * 128 + 128],
                    rhs=ones1_bf[:, 0:1],
                    start=(kc == 0), stop=(kc == 3)).then_inc(pe.h, 1))
                pe.inc()
            pe_pv[qs] = pe.n
            wait("vector", pe, pe_pv[qs])
            if su == 0 and qs == 0:
                wait("vector", dv, dv_zero)
            em("vector", lambda E, qs=qs, ps=ps_o: E.tensor_tensor(
                out=o_sb[:, qs * D:(qs + 1) * D], in0=o_sb[:, qs * D:(qs + 1) * D],
                in1=ps[:, 0:1024], op=ALU.add).then_inc(dv.h, 1))
            dv_oacc[qs] = dv.inc()
        pe_pv_all_prev = pe.n
        wait("vector", pe, pe_pv[15])
        em("vector", lambda E, lb=lbase: E.tensor_tensor(
            out=l_sb[:, 0:16], in0=l_sb[:, 0:16], in1=ps_l[:, lb:lb + 16],
            op=ALU.add).then_inc(dv.h, 1))
        dv_lacc_prev = dv.inc()
        emit_wchunks(17)

    if isinstance(limit, int):
        finish_debug([o_sb[:, 0:15 * D], l_sb[:], stat_sb[:]])
        return finalize()
    emit_wchunks(len(wchunks))
    dv_w_final = max(wslot[0], wslot[1])
    sh_w_final = wst_state["sh_last"]

    wait("vector", dv, dv_lacc_prev)
    em("vector", lambda E: E.reciprocal(out=rinv_sb[:], in_=l_sb[:]).then_inc(dv.h, 1))
    dv_rinv = dv.inc()
    wait("vector", dv, dv_rinv)
    hbf = hb[:].bitcast(F32)
    sh_xnew_prev = sh_hstore_prev
    dv_fold = {}
    for r in range(16):
        if r == 0:
            wait("sync", dv, dv_w_final)
            wait("vector", sh, sh_w_final)
        else:
            wait("sync", dv, dv_fold.get(r - 1, 0))
        n_x = dma("sync", xb[:], xq_in[r * 128:(r + 1) * 128, :], ld)
        wait("vector", ld, n_x)
        wait("vector", dv, dv_rinv)
        wait("vector", sh, sh_xnew_prev)
        em("vector", lambda E, r=r: E.scalar_tensor_tensor(
            out=hbf, in0=o_sb[:, r * D:(r + 1) * D],
            scalar=rinv_sb[:, r:r + 1], in1=xb[:],
            op0=ALU.mult, op1=ALU.add).then_inc(dv.h, 1))
        dv_fold[r] = dv.inc()
        if r == 15:
            em("vector", lambda E, r=r: E.reduce_sum(
                out=stat_sb[:, r:r + 1], in_=hbf, axis=AX.X).then_inc(dv.h, 1))
            wait("vector", dv, dv.inc())
        else:
            em("vector", lambda E, r=r: E.reduce_sum(
                out=stat_sb[:, r:r + 1], in_=hbf, axis=AX.X))
        wait("scalar", dv, dv_fold[r])
        em("scalar", lambda E, r=r: E.memzero(stat_sb[:, 16 + r:17 + r]))
        em("scalar", lambda E, r=r: E.memzero(stat_sb[:, 32 + r:33 + r]))
        em("scalar", lambda E, r=r: E.activation(
            out=ps_sq[:], in_=hbf[:, 0:512], func=AF.Square, bias=0.0, scale=1.0,
            accum_out=stat_sb[:, 16 + r:17 + r]))
        em("scalar", lambda E, r=r: E.activation(
            out=ps_sq[:], in_=hbf[:, 512:1024], func=AF.Square, bias=0.0, scale=1.0,
            accum_out=stat_sb[:, 32 + r:33 + r]).then_inc(ac.h, 1))
        ac.inc()
        sh_xnew_prev = dma("scalar", xnew_scr[r * 128:(r + 1) * 128, :], hbf, sh)
    ac_sq_done = ac.n
    wait("vector", ac, ac_sq_done)

    def _stepf(fn):
        em("vector", lambda E, f=fn: f(E).then_inc(dv.h, 1))
        n = dv.inc()
        wait("vector", dv, n)
        return n
    _stepf(lambda E: E.tensor_scalar(
        out=stat_sb[:, 48:64], in0=stat_sb[:, 0:16], scalar1=1.0 / D, scalar2=None,
        op0=ALU.mult))
    _stepf(lambda E: E.tensor_tensor(
        out=stat_sb[:, 16:32], in0=stat_sb[:, 16:32], in1=stat_sb[:, 32:48], op=ALU.add))
    _stepf(lambda E: E.tensor_scalar(
        out=stat_sb[:, 16:32], in0=stat_sb[:, 16:32], scalar1=1.0 / D, scalar2=None,
        op0=ALU.mult))
    _stepf(lambda E: E.tensor_tensor(
        out=stat_sb[:, 32:48], in0=stat_sb[:, 48:64], in1=stat_sb[:, 48:64], op=ALU.mult))
    dv_varf = _stepf(lambda E: E.tensor_tensor(
        out=stat_sb[:, 16:32], in0=stat_sb[:, 16:32], in1=stat_sb[:, 32:48],
        op=ALU.subtract))
    wait("scalar", dv, dv_varf)
    em("scalar", lambda E: E.activation(
        out=stat_sb[:, 32:48], in_=stat_sb[:, 16:32], func=AF.Ln, bias=np.float32(1e-5), scale=1.0))
    em("scalar", lambda E: E.activation(
        out=rstdf_sb[:], in_=stat_sb[:, 32:48], func=AF.Exp, bias=0.0, scale=np.float32(-0.5))
       .then_inc(ac.h, 1))
    ac_rstdf = ac.inc()
    wait("vector", ac, ac_rstdf)
    _stepf(lambda E: E.tensor_tensor(
        out=negmf_sb[:], in0=stat_sb[:, 48:64], in1=rstdf_sb[:], op=ALU.mult))
    dv_negmf = _stepf(lambda E: E.tensor_scalar(
        out=negmf_sb[:], in0=negmf_sb[:], scalar1=-1.0, scalar2=None,
        op0=ALU.mult))
    dv_fold_done = dv.n
    ac_fold_done = ac.n

    if limit == "fold":
        finish_debug([rstdf_sb[:], negmf_sb[:], rinv_sb[:], l_sb[:]])
        return finalize()
    ps_scope.close()
    bstack.close()

    cstack = ExitStack()
    C = cstack.enter_context
    wgq_f = C(nc.sbuf_tensor("wgq_f", [128, ECH * II], BF16))
    wuq_f = C(nc.sbuf_tensor("wuq_f", [128, ECH * II], BF16))
    wdq_f = C(nc.sbuf_tensor("wdq_f", [128, ICH * D], BF16))
    qt_xn = C(nc.sbuf_tensor("qt_xn", [128, D], F32))
    qt_i1 = C(nc.sbuf_tensor("qt_i1", [128, D], I32))
    qt_acc = C(nc.sbuf_tensor("qt_acc", [128, D], F32))
    qt_t = C(nc.sbuf_tensor("qt_t", [128, D], F32))
    qt_sg = C(nc.sbuf_tensor("qt_sg", [128, D], BF16))
    qt_bm = C(nc.sbuf_tensor("qt_bm", [128, 64], F32))
    qt_sc = C(nc.sbuf_tensor("qt_sc", [128, 64], F32))
    qt_rx = C(nc.sbuf_tensor("qt_rx", [128, 64], F32))
    wq_bf_sb = C(nc.sbuf_tensor("wq_bf_sb", [128, 2 * D], BF16))
    wst = C(nc.sbuf_tensor("wst", [128, 2 * D], F32))
    xc = C(nc.sbuf_tensor("xc", [128, 2 * D], F32))
    h2_sb = C(nc.sbuf_tensor("h2_sb", [128, D], F32))
    xq_bf = C(nc.sbuf_tensor("xq_bf", [128, D], BF16))
    xqf_sb = C(nc.sbuf_tensor("xqf_sb", [128, ECH * 128], BF16))
    hq_bf = C(nc.sbuf_tensor("hq_bf", [128, 512], BF16))
    hqf_sb = C(nc.sbuf_tensor("hqf_sb", [128, 8 * 128], BF16))
    sg_sb = C(nc.sbuf_tensor("sg_sb", [128, 512], F32))
    outb_sb = C(nc.sbuf_tensor("outb_sb", [128, D], F32))

    ps_g = C(nc.psum_tensor("ps_g", [128, 512], F32))
    ps_g2 = C(nc.psum_tensor("ps_g2", [128, 512], F32))
    ps_u = C(nc.psum_tensor("ps_u", [128, 512], F32))
    ps_u2 = C(nc.psum_tensor("ps_u2", [128, 512], F32))
    ps_d = C(nc.psum_tensor("ps_d", [128, D], F32))
    ps_tA = C(nc.psum_tensor("ps_tA", [128, 128], BF16))
    ps_tB = C(nc.psum_tensor("ps_tB", [128, 128], BF16))


    qstate = {"dst_dv": 0}

    def quant(src, dst, F, blocks, wait_pe_n=0, wait_src_dv=0, wait_src_ac=0,
              wait_src_ld=0):
        g16 = src.rearrange("p (g k) -> p g k", k=16)
        if wait_pe_n:
            wait("vector", pe, wait_pe_n)
        if wait_src_ac:
            wait("vector", ac, wait_src_ac)
        if wait_src_dv:
            wait("vector", dv, wait_src_dv)
        if wait_src_ld:
            wait("vector", ld, wait_src_ld)
        em("vector", lambda E: E.reduce_max(
            out=qt_bm[:, 0:blocks], in_=g16, axis=AX.X,
            apply_absolute_value=True).then_inc(dv.h, 1))
        dvq = dv.inc()
        wait("vector", dv, dvq)
        em("vector", lambda E: E.tensor_scalar(
            out=qt_sc[:, 0:blocks], in0=qt_bm[:, 0:blocks], scalar1=1.0 / 6.0,
            scalar2=1e-12, op0=ALU.mult, op1=ALU.max).then_inc(dv.h, 1))
        dvq = dv.inc()
        wait("vector", dv, dvq)
        em("vector", lambda E: E.reciprocal(
            out=qt_rx[:, 0:blocks], in_=qt_sc[:, 0:blocks]).then_inc(dv.h, 1))
        dvq = dv.inc()
        wait("vector", dv, dvq)
        rxb = qt_rx[:, 0:blocks].unsqueeze(2).broadcast_to([128, blocks, 16])
        scb = qt_sc[:, 0:blocks].unsqueeze(2).broadcast_to([128, blocks, 16])
        wait("scalar", dv, qstate["dst_dv"])
        if wait_src_dv:
            wait("scalar", dv, wait_src_dv)
        if wait_src_ac:
            wait("scalar", ac, wait_src_ac)
        if wait_src_ld:
            wait("scalar", ld, wait_src_ld)
        em("scalar", lambda E: E.activation(
            out=qt_t[:, 0:F], in_=src, func=AF.Abs, bias=0.0, scale=1.0))
        em("scalar", lambda E: E.activation(
            out=qt_sg[:, 0:F], in_=src, func=AF.Sign, bias=0.0, scale=1.0).then_inc(ac.h, 1))
        acsg = ac.inc()
        wait("vector", ac, acsg)
        em("vector", lambda E: E.tensor_tensor(
            out=qt_xn[:, 0:F].rearrange("p (g k) -> p g k", k=16),
            in0=qt_t[:, 0:F].rearrange("p (g k) -> p g k", k=16), in1=rxb,
            op=ALU.mult).then_inc(dv.h, 1))
        dv.inc()
        em("vector", lambda E: E.tensor_scalar(
            out=qt_i1[:, 0:F], in0=qt_xn[:, 0:F], scalar1=2.0, scalar2=None,
            op0=ALU.mult))
        em("vector", lambda E: E.tensor_scalar(
            out=qt_acc[:, 0:F], in0=qt_i1[:, 0:F], scalar1=4.0, scalar2=0.5,
            op0=ALU.min, op1=ALU.mult))
        em("vector", lambda E: E.tensor_scalar(
            out=qt_i1[:, 0:F], in0=qt_xn[:, 0:F], scalar1=1.0, scalar2=None,
            op0=ALU.mult))
        em("vector", lambda E: E.tensor_scalar(
            out=qt_t[:, 0:F], in0=qt_i1[:, 0:F], scalar1=2.0, scalar2=2.0,
            op0=ALU.subtract, op1=ALU.min))
        em("vector", lambda E: E.scalar_tensor_tensor(
            out=qt_acc[:, 0:F], in0=qt_t[:, 0:F], scalar=0.0, in1=qt_acc[:, 0:F],
            op0=ALU.max, op1=ALU.add))
        em("vector", lambda E: E.tensor_scalar(
            out=qt_i1[:, 0:F], in0=qt_xn[:, 0:F], scalar1=0.5, scalar2=None,
            op0=ALU.mult))
        em("vector", lambda E: E.tensor_scalar(
            out=qt_t[:, 0:F], in0=qt_i1[:, 0:F], scalar1=2.0, scalar2=4.0,
            op0=ALU.mult, op1=ALU.subtract))
        em("vector", lambda E: E.scalar_tensor_tensor(
            out=qt_acc[:, 0:F], in0=qt_t[:, 0:F], scalar=0.0, in1=qt_acc[:, 0:F],
            op0=ALU.max, op1=ALU.add))
        em("vector", lambda E: E.tensor_tensor(
            out=qt_t[:, 0:F].rearrange("p (g k) -> p g k", k=16),
            in0=qt_acc[:, 0:F].rearrange("p (g k) -> p g k", k=16), in1=scb, op=ALU.mult))
        wait("vector", ac, acsg)
        em("vector", lambda E, dst=dst: E.tensor_tensor(
            out=dst, in0=qt_t[:, 0:F], in1=qt_sg[:, 0:F], op=ALU.mult).then_inc(dv.h, 1))
        qstate["dst_dv"] = dv.inc()
        return qstate["dst_dv"]

    wait("sync", sh, sh_w_final)
    for dc in range(ECH):
        dma("sync", wgq_f[:, dc * II:(dc + 1) * II],
            wgq_scr[:, dc * 128:(dc + 1) * 128], ld, transpose=True)
        dma("sync", wuq_f[:, dc * II:(dc + 1) * II],
            wuq_scr[:, dc * 128:(dc + 1) * 128], ld, transpose=True)
    n_wf = 0
    for icc in range(ICH):
        n_wf = dma("sync", wdq_f[:, icc * D:(icc + 1) * D],
                   wdq_scr[:, icc * 128:(icc + 1) * 128], ld, transpose=True)

    NCC = [512, 512, 512, 512, 512, 256]
    dv_out_prev = 0
    pe_xt_prev = 0
    pe_hqt_prev = 0
    dv_tc = {"A": 0, "B": 0}
    for r in range(16):
        xsel = r % 2
        if r >= 2:
            wait("sync", dv, dv_out_prev)
        n_x = dma("sync", xc[:, xsel * D:(xsel + 1) * D],
                  xnew_scr[r * 128:(r + 1) * 128, :], ld)
        wait("scalar", ld, n_x)
        wait("scalar", dv, dv_negmf)
        wait("scalar", dv, qstate["dst_dv"])
        em("scalar", lambda E, r=r, xs=xsel: E.activation(
            out=h2_sb[:], in_=xc[:, xs * D:(xs + 1) * D], func=AF.Identity,
            bias=negmf_sb[:, r:r + 1], scale=rstdf_sb[:, r:r + 1]).then_inc(ac.h, 1))
        ac_h2 = ac.inc()
        dv_xq = quant(h2_sb[:], xq_bf[:], D, 64,
                      wait_pe_n=pe_xt_prev, wait_src_ac=ac_h2)
        if limit == "C0" and r == 0:
            finish_debug([h2_sb[:], xq_bf[:].bitcast(F32), qt_sc[:, 0:64],
                          qt_xn[:, 0:D], qt_acc[:, 0:D]])
            return finalize()
        wait("tensor", dv, dv_xq)
        for dc in range(ECH):
            key = "A" if dc % 2 == 0 else "B"
            pst = ps_tA if dc % 2 == 0 else ps_tB
            wait("tensor", dv, dv_tc[key])
            em("tensor", lambda E, dc=dc, pst=pst: E.transpose(
                out=pst[:], in_=xq_bf[:, dc * 128:(dc + 1) * 128], identity=ident_sb[:])
               .then_inc(pe.h, 1))
            pe_t = pe.inc()
            wait("vector", pe, pe_t)
            em("vector", lambda E, dc=dc, pst=pst: E.tensor_copy(
                out=xqf_sb[:, dc * 128:(dc + 1) * 128], in_=pst[:]).then_inc(dv.h, 1))
            dv_tc[key] = dv.inc()
        pe_xt_prev = pe.n
        ac_xt, dv_xt = 0, dv_tc["B"]
        if limit == "C1" and r == 0:
            finish_debug([xqf_sb[:].bitcast(F32)])
            return finalize()

        off = 0
        icc0 = 0
        ac_sil_prev = {0: 0, 1: 0}
        dv_h_prev = {0: 0, 1: 0}
        dv_h_last = 0
        for ci, csz in enumerate(NCC):
            gsel = ci % 2
            psg = ps_g if gsel == 0 else ps_g2
            psu = ps_u if gsel == 0 else ps_u2
            wait("tensor", ac, ac_sil_prev[gsel])
            wait("tensor", dv, dv_h_prev[gsel])
            if ci == 0:
                wait("tensor", dv, dv_xt)
                wait("tensor", ld, n_wf)
            for dc in range(ECH):
                em("tensor", lambda E, dc=dc, off=off, csz=csz, psg=psg: E.matmul(
                    out=psg[:, 0:csz], lhsT=xqf_sb[:, dc * 128:(dc + 1) * 128],
                    rhs=wgq_f[:, dc * II + off:dc * II + off + csz],
                    start=(dc == 0), stop=(dc == ECH - 1)).then_inc(pe.h, 1))
                pe.inc()
            for dc in range(ECH):
                em("tensor", lambda E, dc=dc, off=off, csz=csz, psu=psu: E.matmul(
                    out=psu[:, 0:csz], lhsT=xqf_sb[:, dc * 128:(dc + 1) * 128],
                    rhs=wuq_f[:, dc * II + off:dc * II + off + csz],
                    start=(dc == 0), stop=(dc == ECH - 1)).then_inc(pe.h, 1))
                pe.inc()
            pe_g = pe.n
            wait("scalar", pe, pe_g)
            wait("scalar", dv, dv_h_last)
            em("scalar", lambda E, csz=csz, psg=psg: E.activation(
                out=sg_sb[:, 0:csz], in_=psg[:, 0:csz], func=AF.Silu, bias=0.0,
                scale=1.0).then_inc(ac.h, 1))
            ac_sil = ac.inc()
            ac_sil_prev[gsel] = ac_sil
            wait("vector", ac, ac_sil)
            wait("vector", pe, pe_g)
            em("vector", lambda E, csz=csz, psu=psu: E.tensor_tensor(
                out=h2_sb[:, 0:csz], in0=sg_sb[:, 0:csz], in1=psu[:, 0:csz],
                op=ALU.mult).then_inc(dv.h, 1))
            dv_h = dv.inc()
            dv_h_prev[gsel] = dv_h
            dv_h_last = dv_h
            dv_hq = quant(h2_sb[:, 0:csz], hq_bf[:, 0:csz], csz, csz // 16,
                          wait_pe_n=pe_hqt_prev, wait_src_dv=dv_h)
            wait("tensor", dv, dv_hq)
            nic = csz // 128
            for j in range(nic):
                icc = icc0 + j
                key = "A" if icc % 2 == 0 else "B"
                pst = ps_tA if icc % 2 == 0 else ps_tB
                wait("tensor", dv, dv_tc[key])
                em("tensor", lambda E, j=j, pst=pst: E.transpose(
                    out=pst[:], in_=hq_bf[:, j * 128:(j + 1) * 128], identity=ident_sb[:])
                   .then_inc(pe.h, 1))
                pe_t = pe.inc()
                wait("vector", pe, pe_t)
                em("vector", lambda E, sl=icc % 8, pst=pst: E.tensor_copy(
                    out=hqf_sb[:, sl * 128:(sl + 1) * 128], in_=pst[:])
                   .then_inc(dv.h, 1))
                dv_tc[key] = dv.inc()
            pe_hqt_prev = pe.n
            wait("tensor", dv, dv_tc["A"])
            wait("tensor", dv, dv_tc["B"])
            if ci == 0 and r >= 1:
                wait("tensor", dv, dv_out_prev)
            for j in range(nic):
                icc = icc0 + j
                for hh2 in range(2):
                    em("tensor", lambda E, icc=icc, sl=icc % 8, hh=hh2: E.matmul(
                        out=ps_d[:, hh * 512:(hh + 1) * 512],
                        lhsT=hqf_sb[:, sl * 128:(sl + 1) * 128],
                        rhs=wdq_f[:, icc * D + hh * 512:icc * D + (hh + 1) * 512],
                        start=(icc == 0), stop=(icc == ICH - 1)).then_inc(pe.h, 1))
                    pe.inc()
            icc0 += nic
            off += csz
        pe_down = pe.n
        wait("vector", pe, pe_down)
        if r >= 1:
            wait("vector", so, r * 16)
        em("vector", lambda E, r=r, xs=xsel: E.tensor_tensor(
            out=outb_sb[:, 0:D], in0=ps_d[:, 0:D],
            in1=xc[:, xs * D:(xs + 1) * D], op=ALU.add).then_inc(dv.h, 1))
        dv_out_prev = dv.inc()
        wait("scalar", dv, dv_out_prev)
        dma("scalar", out_t[r * 128:(r + 1) * 128, :], outb_sb[:, 0:D], so)

    wait("sync", so, so.n)
    return finalize()


def _host_tables(j):
    tt = np.zeros((128, 64), dtype=np.float32)
    p = np.arange(128, dtype=np.float32)
    for su in range(NSU):
        kt, half = su // 2, su % 2
        for kc in range(4):
            t = kt * 1024 + half * 512 + kc * 128 - j * TOWN
            t = max(-128, min(2176, t))
            tt[:, su * 4 + kc] = p + t
    return tt


def kernel(L0_attn_X, L0_attn_Wq, L0_attn_bq, L0_attn_Wk, L0_attn_bk,
           L0_attn_Wv, L0_attn_bv, L0_ffn_w_gate, L0_ffn_w_up, L0_ffn_w_down):
    X = np.ascontiguousarray(np.asarray(L0_attn_X, dtype=np.float32))
    wqT = np.ascontiguousarray(np.asarray(L0_attn_Wq, np.float32).T)
    wkT = np.ascontiguousarray(np.asarray(L0_attn_Wk, np.float32).T)
    wvT = np.ascontiguousarray(np.asarray(L0_attn_Wv, np.float32).T)
    bq = np.ascontiguousarray(np.asarray(L0_attn_bq, np.float32))
    bk = np.ascontiguousarray(np.asarray(L0_attn_bk, np.float32))
    bv = np.ascontiguousarray(np.asarray(L0_attn_bv, np.float32))
    wg = np.ascontiguousarray(np.asarray(L0_ffn_w_gate, np.float32))
    wu = np.ascontiguousarray(np.asarray(L0_ffn_w_up, np.float32))
    wd = np.ascontiguousarray(np.asarray(L0_ffn_w_down, np.float32))

    ramp = np.broadcast_to(np.arange(TOWN, dtype=np.float32)[None, :], (128, TOWN)).copy()
    ident = np.eye(128, dtype=ml_dtypes.bfloat16)

    if "nc" not in _CACHE:
        _CACHE["nc"] = build_program()
    nc = _CACHE["nc"]

    in_maps = []
    for c in range(8):
        b, j = c // 4, c % 4
        in_maps.append({
            "xq": np.ascontiguousarray(X[b, j * TOWN:(j + 1) * TOWN, :]),
            "xkv": np.ascontiguousarray(X[b]),
            "wqT": wqT, "wkT": wkT, "wvT": wvT,
            "bq": bq, "bk": bk, "bv": bv,
            "wg": wg, "wu": wu, "wd": wd,
            "ramp": ramp, "tthr": _host_tables(j), "ident": ident,
        })
    res = run_bass_kernel_spmd(nc, in_maps, core_ids=list(range(8)))
    out = np.empty((2, NKV, D), dtype=np.float32)
    for c in range(8):
        b, j = c // 4, c % 4
        out[b, j * TOWN:(j + 1) * TOWN, :] = np.asarray(res.results[c]["out"])
    return out


# version 13
# speedup vs baseline: 1.0008x; 1.0008x over previous
import numpy as np
import ml_dtypes
from contextlib import ExitStack

import concourse.bass as bass
import concourse.mybir as mybir
from concourse.bass_utils import run_bass_kernel_spmd

F32 = mybir.dt.float32
BF16 = mybir.dt.bfloat16
I32 = mybir.dt.int32
AF = mybir.ActivationFunctionType
ALU = mybir.AluOpType
AX = mybir.AxisListType

D = 1024
ECH = D // 128
TOWN = 2048
NKV = 8192
NSU = 16
II = 2816
ICH = II // 128
NEG = -8.0

_CACHE = {}


class Ctr:
    def __init__(self, handle):
        self.h = handle
        self.n = 0

    def inc(self, by=1):
        self.n += by
        return self.n


def build_program(limit=None):
    nc = bass.Bass("TRN2", target_bir_lowering=False)

    xq_in = nc.dram_tensor("xq", [TOWN, D], F32, kind="ExternalInput").ap()
    xkv_in = nc.dram_tensor("xkv", [NKV, D], F32, kind="ExternalInput").ap()
    wqT_in = nc.dram_tensor("wqT", [D, D], F32, kind="ExternalInput").ap()
    wkT_in = nc.dram_tensor("wkT", [D, D], F32, kind="ExternalInput").ap()
    wvT_in = nc.dram_tensor("wvT", [D, D], F32, kind="ExternalInput").ap()
    bq_in = nc.dram_tensor("bq", [D], F32, kind="ExternalInput").ap()
    bk_in = nc.dram_tensor("bk", [D], F32, kind="ExternalInput").ap()
    bv_in = nc.dram_tensor("bv", [D], F32, kind="ExternalInput").ap()
    wg_in = nc.dram_tensor("wg", [II, D], F32, kind="ExternalInput").ap()
    wu_in = nc.dram_tensor("wu", [II, D], F32, kind="ExternalInput").ap()
    wd_in = nc.dram_tensor("wd", [D, II], F32, kind="ExternalInput").ap()
    ramp_in = nc.dram_tensor("ramp", [128, TOWN], F32, kind="ExternalInput").ap()
    tthr_in = nc.dram_tensor("tthr", [128, 64], F32, kind="ExternalInput").ap()
    ident_in = nc.dram_tensor("ident", [128, 128], BF16, kind="ExternalInput").ap()
    out_t = nc.dram_tensor("out", [TOWN, D], F32, kind="ExternalOutput").ap()

    h_scr = nc.dram_tensor("h_scr", [NKV, D], BF16, kind="Internal").ap()
    hq_scr = nc.dram_tensor("hq_scr", [TOWN, D], BF16, kind="Internal").ap()
    xnew_scr = nc.dram_tensor("xnew_scr", [TOWN, D], F32, kind="Internal").ap()
    wgq_scr = nc.dram_tensor("wgq_scr", [II, D], BF16, kind="Internal").ap()
    wuq_scr = nc.dram_tensor("wuq_scr", [II, D], BF16, kind="Internal").ap()
    wdq_scr = nc.dram_tensor("wdq_scr", [D, II], BF16, kind="Internal").ap()

    ops = {k: [] for k in ("sync", "vector", "tensor", "scalar", "gpsimd")}

    def em(eng, fn):
        ops[eng].append(fn)

    def finalize():
        with nc.Block() as block:
            @block.sync
            def _(E):
                for f in ops["sync"]:
                    f(E)

            @block.vector
            def _(E):
                for f in ops["vector"]:
                    f(E)

            @block.tensor
            def _(E):
                for f in ops["tensor"]:
                    f(E)

            @block.scalar
            def _(E):
                for f in ops["scalar"]:
                    f(E)

            @block.gpsimd
            def _(E):
                for f in ops["gpsimd"]:
                    f(E)
        return nc

    pstack = ExitStack()
    bstack = ExitStack()
    P = pstack.enter_context
    B = bstack.enter_context

    ld = Ctr(P(nc.semaphore("ld")))
    sh = Ctr(P(nc.semaphore("sh")))
    so = Ctr(P(nc.semaphore("so")))
    pe = Ctr(P(nc.semaphore("pe")))
    dv = Ctr(P(nc.semaphore("dv")))
    gp = Ctr(P(nc.semaphore("gp")))
    ac = Ctr(P(nc.semaphore("ac")))

    def dma(eng, out, in_, ctr, transpose=False):
        n = ctr.inc(16)
        if transpose:
            em(eng, lambda E, o=out, i=in_, c=ctr.h:
               E.dma_start(out=o, in_=i, transpose=True).then_inc(c, 16))
        else:
            em(eng, lambda E, o=out, i=in_, c=ctr.h:
               E.dma_start(out=o, in_=i).then_inc(c, 16))
        return n

    def wait(eng, ctr, n):
        if n > 0:
            em(eng, lambda E, c=ctr.h, v=n: E.wait_ge(c, v))

    tthr_sb = P(nc.sbuf_tensor("tthr_sb", [128, 64], F32))
    stat_sb = P(nc.sbuf_tensor("stat_sb", [128, 256], F32))
    rstd_all = P(nc.sbuf_tensor("rstd_all", [128, 80], F32))
    negm_all = P(nc.sbuf_tensor("negm_all", [128, 80], F32))
    bqc_sb = P(nc.sbuf_tensor("bqc_sb", [128, ECH], F32))
    bkc_sb = P(nc.sbuf_tensor("bkc_sb", [128, ECH], F32))
    bvb_sb = P(nc.sbuf_tensor("bvb_sb", [128, D], F32))
    ones1_bf = P(nc.sbuf_tensor("ones1_bf", [128, 1], BF16))
    rstdf_sb = P(nc.sbuf_tensor("rstdf_sb", [128, 16], F32))
    negmf_sb = P(nc.sbuf_tensor("negmf_sb", [128, 16], F32))
    rinv_sb = P(nc.sbuf_tensor("rinv_sb", [128, 16], F32))
    brow = P(nc.sbuf_tensor("brow", [1, 3 * D], F32))
    onef = P(nc.sbuf_tensor("onef", [1, 1], F32))
    onerow = P(nc.sbuf_tensor("onerow", [1, 128], F32))
    ident_sb = P(nc.sbuf_tensor("ident_sb", [128, 128], BF16))

    wkT_bf = B(nc.sbuf_tensor("wkT_bf", [128, ECH * D], BF16))
    wvT_bf = B(nc.sbuf_tensor("wvT_bf", [128, ECH * D], BF16))
    q_sb = B(nc.sbuf_tensor("q_sb", [128, ECH * TOWN], BF16))
    o_sb = B(nc.sbuf_tensor("o_sb", [128, 16 * D], F32))
    l_sb = B(nc.sbuf_tensor("l_sb", [128, 16], F32))
    xb = B(nc.sbuf_tensor("xb", [128, D], F32))
    hb = B(nc.sbuf_tensor("hb", [128, 2 * D], BF16))
    hf = B(nc.sbuf_tensor("hf", [128, ECH * 512], BF16))
    k_sb = B(nc.sbuf_tensor("k_sb", [128, ECH * 512], BF16))
    v_sb = B(nc.sbuf_tensor("v_sb", [128, 4 * D], BF16))
    pt_sb = B(nc.sbuf_tensor("pt_sb", [128, 4 * TOWN], BF16))
    ramp_sb = B(nc.sbuf_tensor("ramp_sb", [128, TOWN], F32))
    ev_scr = B(nc.sbuf_tensor("ev_scr", [128, 512], F32))
    es_scr = B(nc.sbuf_tensor("es_scr", [128, 512], F32))

    wstage = o_sb[:, 0:ECH * D]
    wq_bf = pt_sb

    ps_scope = ExitStack()
    pp = ps_scope.enter_context
    ps_pvA = pp(nc.psum_tensor("ps_pvA", [128, D], F32))
    ps_pvB = pp(nc.psum_tensor("ps_pvB", [128, D], F32))
    ps_s0 = pp(nc.psum_tensor("ps_s0", [128, 512], F32))
    ps_s1 = pp(nc.psum_tensor("ps_s1", [128, 512], F32))
    ps_l = pp(nc.psum_tensor("ps_l", [128, 512], F32))
    ps_sq = pp(nc.psum_tensor("ps_sq", [128, 512], F32))

    def _load_act(idx):
        def f(E):
            E.add_instruction(mybir.InstLoadActFuncSet(
                name=nc.get_next_instruction_name(),
                act_func_set_id=idx, ins=[], outs=[]))
        return f

    dma("sync", ramp_sb[:], ramp_in, ld)
    dma("sync", tthr_sb[:], tthr_in, ld)
    dma("sync", ident_sb[:], ident_in, ld)
    dma("sync", brow[0:1, 0:D], bq_in[None, :], ld)
    dma("sync", brow[0:1, D:2 * D], bk_in[None, :], ld)
    n_brow = dma("sync", brow[0:1, 2 * D:3 * D], bv_in[None, :], ld)

    em("vector", lambda E: E.memset(onef[:], 1.0))
    em("vector", lambda E: E.memset(onerow[:], 1.0))
    em("vector", lambda E: E.memset(ones1_bf[:], 1.0).then_inc(dv.h, 1))
    dv_ones = dv.inc()

    dv_wcast = 0
    for name, win, dst in (("q", wqT_in, None), ("k", wkT_in, wkT_bf), ("v", wvT_in, wvT_bf)):
        n_w = dma("sync", wstage.rearrange("p (c e) -> p c e", e=D),
                  win.rearrange("(c p) e -> p c e", p=128), ld)
        wait("vector", ld, n_w)
        if name == "q":
            em("vector", lambda E: E.tensor_copy(out=wq_bf[:, 0:ECH * D], in_=wstage)
               .then_inc(dv.h, 1))
        else:
            em("vector", lambda E, d=dst: E.tensor_copy(out=d[:], in_=wstage)
               .then_inc(dv.h, 1))
        dv_wcast = dv.inc()
        wait("sync", dv, dv_wcast)
    em("vector", lambda E: E.memset(o_sb[:], 0.0))
    em("vector", lambda E: E.memset(l_sb[:], 0.0).then_inc(dv.h, 1))
    dv_zero = dv.inc()

    wait("tensor", ld, n_brow)
    wait("tensor", dv, dv_ones)
    for bi in range(2):
        for ecj in range(ECH):
            em("tensor", lambda E, b=bi, c=ecj: E.matmul(
                out=ps_sq[:, b * 8 + c:b * 8 + c + 1],
                lhsT=brow[0:1, b * D + c * 128:b * D + (c + 1) * 128],
                rhs=onef[0:1, 0:1], start=True, stop=True).then_inc(pe.h, 1))
            n = pe.inc()
            wait("tensor", pe, n)
    pe_bias = pe.n
    wait("scalar", pe, pe_bias)
    em("scalar", lambda E: E.activation(out=bqc_sb[:], in_=ps_sq[:, 0:8],
                                        func=AF.Identity, bias=0.0, scale=np.float32(1.0 / 32.0)))
    em("scalar", lambda E: E.activation(out=bkc_sb[:], in_=ps_sq[:, 8:16],
                                        func=AF.Identity, bias=0.0, scale=1.0)
       .then_inc(ac.h, 1))
    ac_bias = ac.inc()
    for hh in range(2):
        wait("tensor", ac, ac_bias)
        em("tensor", lambda E, h=hh: E.matmul(
            out=ps_sq[:, 0:512],
            lhsT=onerow[0:1, :],
            rhs=brow[0:1, 2 * D + h * 512:2 * D + (h + 1) * 512],
            start=True, stop=True).then_inc(pe.h, 1))
        pe_bv = pe.inc()
        wait("scalar", pe, pe_bv)
        em("scalar", lambda E, h=hh: E.activation(
            out=bvb_sb[:, h * 512:(h + 1) * 512], in_=ps_sq[:, 0:512],
            func=AF.Identity, bias=0.0, scale=1.0).then_inc(ac.h, 1))
        ac_bias = ac.inc()
    ac_bvb = ac_bias

    def ln_tiles(x_ap_list, h_out_list, wait_loads, slot0=0):
        nt = len(x_ap_list)
        wait("scalar", ld, wait_loads)
        wait("scalar", ac, ac_bvb)
        em("scalar", lambda E: E.memzero(stat_sb[:, 8 + slot0:8 + slot0 + nt]))
        em("scalar", lambda E: E.memzero(stat_sb[:, 16 + slot0:16 + slot0 + nt]))
        for i, x in enumerate(x_ap_list):
            s = slot0 + i
            em("scalar", lambda E, x=x, s=s: E.activation(
                out=ps_sq[:], in_=x[:, 0:512], func=AF.Square, bias=0.0, scale=1.0,
                accum_out=stat_sb[:, 8 + s:9 + s]))
            em("scalar", lambda E, x=x, s=s: E.activation(
                out=ps_sq[:], in_=x[:, 512:1024], func=AF.Square, bias=0.0, scale=1.0,
                accum_out=stat_sb[:, 16 + s:17 + s]).then_inc(ac.h, 1))
            ac.inc()
        ac_sq = ac.n
        if limit == "ln0":
            finish_debug([stat_sb[:]])
            return "STOP"
        wait("vector", ld, wait_loads)
        for i, x in enumerate(x_ap_list):
            s = slot0 + i
            last = (i == len(x_ap_list) - 1)
            if last:
                em("vector", lambda E, x=x, s=s: E.reduce_sum(
                    out=stat_sb[:, s:s + 1], in_=x, axis=AX.X).then_inc(dv.h, 1))
                wait("vector", dv, dv.inc())
            else:
                em("vector", lambda E, x=x, s=s: E.reduce_sum(
                    out=stat_sb[:, s:s + 1], in_=x, axis=AX.X))
        wait("vector", ac, ac_sq)
        sl = slice(slot0, slot0 + nt)
        s2a = slice(8 + slot0, 8 + slot0 + nt)
        s2b = slice(16 + slot0, 16 + slot0 + nt)
        mu = slice(24 + slot0, 24 + slot0 + nt)
        var = slice(32 + slot0, 32 + slot0 + nt)
        rstd = slice(40 + slot0, 40 + slot0 + nt)
        ng = slice(48 + slot0, 48 + slot0 + nt)

        def _step(fn):
            em("vector", lambda E, f=fn: f(E).then_inc(dv.h, 1))
            n = dv.inc()
            wait("vector", dv, n)
            return n
        _step(lambda E: E.tensor_scalar(
            out=stat_sb[:, mu], in0=stat_sb[:, sl], scalar1=1.0 / D, scalar2=None,
            op0=ALU.mult))
        _step(lambda E: E.tensor_tensor(
            out=stat_sb[:, var], in0=stat_sb[:, s2a], in1=stat_sb[:, s2b], op=ALU.add))
        _step(lambda E: E.tensor_scalar(
            out=stat_sb[:, var], in0=stat_sb[:, var], scalar1=1.0 / D, scalar2=None,
            op0=ALU.mult))
        _step(lambda E: E.tensor_tensor(
            out=stat_sb[:, s2a], in0=stat_sb[:, mu], in1=stat_sb[:, mu], op=ALU.mult))
        dv_var = _step(lambda E: E.tensor_tensor(
            out=stat_sb[:, var], in0=stat_sb[:, var], in1=stat_sb[:, s2a],
            op=ALU.subtract))
        wait("scalar", dv, dv_var)
        em("scalar", lambda E: E.activation(
            out=stat_sb[:, s2b], in_=stat_sb[:, var], func=AF.Ln, bias=np.float32(1e-5), scale=1.0))
        em("scalar", lambda E: E.activation(
            out=stat_sb[:, rstd], in_=stat_sb[:, s2b], func=AF.Exp, bias=0.0,
            scale=np.float32(-0.5)).then_inc(ac.h, 1))
        ac_rstd = ac.inc()
        wait("vector", ac, ac_rstd)
        _step(lambda E: E.tensor_tensor(
            out=stat_sb[:, ng], in0=stat_sb[:, mu], in1=stat_sb[:, rstd], op=ALU.mult))
        dv_ng = _step(lambda E: E.tensor_scalar(
            out=stat_sb[:, ng], in0=stat_sb[:, ng], scalar1=-1.0, scalar2=None,
            op0=ALU.mult))
        wait("scalar", dv, dv_ng)
        for i, (x, h) in enumerate(zip(x_ap_list, h_out_list)):
            s = slot0 + i
            em("scalar", lambda E, x=x, h=h, s=s: E.activation(
                out=h, in_=x, func=AF.Identity,
                bias=stat_sb[:, 48 + s:49 + s], scale=stat_sb[:, 40 + s:41 + s])
               .then_inc(ac.h, 1))
            ac.inc()
        return ac.n

    dbg_out = nc.dram_tensor("dbg", [128, 16 * D], F32, kind="ExternalOutput").ap() if limit else None

    def finish_debug(ap_list):
        col = 0
        wait("sync", ac, ac.n)
        wait("sync", dv, dv.n)
        wait("sync", pe, pe.n)
        nd = 0
        for ap_ in ap_list:
            ncol = ap_.shape[-1]
            nd = dma("sync", dbg_out[:, col:col + ncol], ap_, so)
            col += ncol
        wait("sync", so, nd)

    if limit == "init":
        finish_debug([bvb_sb[:], bqc_sb[:], bkc_sb[:]])
        return finalize()
    hbf0 = hb[:].bitcast(F32)
    evs0 = ev_scr[:, 0:512]
    evs1 = es_scr[:, 0:512]
    dv_red = {0: 0, 1: 0}
    ac_sq2 = {0: 0, 1: 0}
    dv_chain_prev = 0
    for b8 in range(8):
        for t8 in range(8):
            t = b8 * 8 + t8
            sl = t % 2
            xbuf = xb[:] if sl == 0 else hbf0
            wait("sync", dv, dv_red[sl])
            wait("sync", ac, ac_sq2[sl])
            n_x = dma("sync", xbuf if False else (xb[:] if sl == 0 else hbf0),
                      xkv_in[t * 128:(t + 1) * 128, :], ld)
            xbuf = xb[:] if sl == 0 else hbf0
            wait("vector", ld, n_x)
            em("vector", lambda E, x=xbuf, t=t: E.reduce_sum(
                out=stat_sb[:, t:t + 1], in_=x, axis=AX.X).then_inc(dv.h, 1))
            dv_red[sl] = dv.inc()
            wait("scalar", ld, n_x)
            em("scalar", lambda E, x=xbuf, t=t: E.memzero(stat_sb[:, 64 + t:65 + t]))
            em("scalar", lambda E, x=xbuf, t=t: E.memzero(stat_sb[:, 128 + t:129 + t]))
            em("scalar", lambda E, x=xbuf, t=t: E.activation(
                out=ps_sq[:], in_=x[:, 0:512], func=AF.Square, bias=0.0, scale=1.0,
                accum_out=stat_sb[:, 64 + t:65 + t]))
            em("scalar", lambda E, x=xbuf, t=t: E.activation(
                out=ps_sq[:], in_=x[:, 512:1024], func=AF.Square, bias=0.0, scale=1.0,
                accum_out=stat_sb[:, 128 + t:129 + t]).then_inc(ac.h, 1))
            ac_sq2[sl] = ac.inc()
        c0 = b8 * 8
        wait("vector", ac, ac_sq2[0])
        wait("vector", ac, ac_sq2[1])
        wait("vector", dv, max(dv_red[0], dv_red[1]))

        def _stp(fn):
            em("vector", lambda E, f=fn: f(E).then_inc(dv.h, 1))
            n = dv.inc()
            wait("vector", dv, n)
            return n
        _stp(lambda E, c0=c0: E.tensor_scalar(
            out=stat_sb[:, 192 + c0:200 + c0], in0=stat_sb[:, c0:c0 + 8],
            scalar1=1.0 / D, scalar2=None, op0=ALU.mult))
        _stp(lambda E, c0=c0: E.tensor_tensor(
            out=stat_sb[:, 64 + c0:72 + c0], in0=stat_sb[:, 64 + c0:72 + c0],
            in1=stat_sb[:, 128 + c0:136 + c0], op=ALU.add))
        _stp(lambda E, c0=c0: E.tensor_scalar(
            out=stat_sb[:, 64 + c0:72 + c0], in0=stat_sb[:, 64 + c0:72 + c0],
            scalar1=1.0 / D, scalar2=None, op0=ALU.mult))
        _stp(lambda E, c0=c0: E.tensor_tensor(
            out=stat_sb[:, 128 + c0:136 + c0], in0=stat_sb[:, 192 + c0:200 + c0],
            in1=stat_sb[:, 192 + c0:200 + c0], op=ALU.mult))
        dvv_ = _stp(lambda E, c0=c0: E.tensor_tensor(
            out=stat_sb[:, 64 + c0:72 + c0], in0=stat_sb[:, 64 + c0:72 + c0],
            in1=stat_sb[:, 128 + c0:136 + c0], op=ALU.subtract))
        wait("scalar", dv, dvv_)
        em("scalar", lambda E, c0=c0: E.activation(
            out=stat_sb[:, 128 + c0:136 + c0], in_=stat_sb[:, 64 + c0:72 + c0],
            func=AF.Ln, bias=np.float32(1e-5), scale=1.0))
        em("scalar", lambda E, c0=c0: E.activation(
            out=rstd_all[:, c0:c0 + 8], in_=stat_sb[:, 128 + c0:136 + c0],
            func=AF.Exp, bias=0.0, scale=np.float32(-0.5)).then_inc(ac.h, 1))
        acr_ = ac.inc()
        wait("vector", ac, acr_)
        _stp(lambda E, c0=c0: E.tensor_tensor(
            out=negm_all[:, c0:c0 + 8], in0=stat_sb[:, 192 + c0:200 + c0],
            in1=rstd_all[:, c0:c0 + 8], op=ALU.mult))
        dv_chain_prev = _stp(lambda E, c0=c0: E.tensor_scalar(
            out=negm_all[:, c0:c0 + 8], in0=negm_all[:, c0:c0 + 8],
            scalar1=-1.0, scalar2=None, op0=ALU.mult))
    dv_stats_done = dv.n
    ac_norm = {0: 0, 1: 0}
    sh_hstore_prev = 0
    sh_hstore_pp = 0
    wait("scalar", dv, dv_stats_done)
    for t in range(64):
        sl = t % 2
        wait("sync", ac, ac_norm[sl])
        if sl == 0:
            n_x = dma("sync", xb[:], xkv_in[t * 128:(t + 1) * 128, :], ld)
        else:
            n_x = dma("sync", evs0, xkv_in[t * 128:(t + 1) * 128, 0:512], ld)
            n_x = dma("sync", evs1, xkv_in[t * 128:(t + 1) * 128, 512:1024], ld)
        hsl = t % 2
        wait("scalar", ld, n_x)
        wait("scalar", sh, sh_hstore_pp)
        if sl == 0:
            em("scalar", lambda E, t=t, h=hsl: E.activation(
                out=hb[:, h * D:h * D + D], in_=xb[:], func=AF.Identity,
                bias=negm_all[:, t:t + 1], scale=rstd_all[:, t:t + 1])
               .then_inc(ac.h, 1))
        else:
            em("scalar", lambda E, t=t, h=hsl: E.activation(
                out=hb[:, h * D:h * D + 512], in_=evs0, func=AF.Identity,
                bias=negm_all[:, t:t + 1], scale=rstd_all[:, t:t + 1]))
            em("scalar", lambda E, t=t, h=hsl: E.activation(
                out=hb[:, h * D + 512:h * D + D], in_=evs1, func=AF.Identity,
                bias=negm_all[:, t:t + 1], scale=rstd_all[:, t:t + 1])
               .then_inc(ac.h, 1))
        ac_norm[sl] = ac.inc()
        wait("scalar", ac, ac_norm[sl])
        sh_hstore_pp = sh_hstore_prev
        sh_hstore_prev = dma("scalar", h_scr[t * 128:(t + 1) * 128, :],
                             hb[:, hsl * D:(hsl + 1) * D], sh)
    sh_hdone = sh_hstore_prev

    for b8 in range(2):
        for t8 in range(8):
            t = b8 * 8 + t8
            sl = t % 2
            xbuf = xb[:] if sl == 0 else hbf0
            wait("sync", dv, dv_red[sl])
            wait("sync", ac, ac_sq2[sl])
            wait("sync", sh, sh_hdone if t == 0 else 0)
            n_x = dma("sync", xb[:] if sl == 0 else hbf0,
                      xq_in[t * 128:(t + 1) * 128, :], ld)
            wait("vector", ld, n_x)
            em("vector", lambda E, x=xbuf, t=t: E.reduce_sum(
                out=stat_sb[:, t:t + 1], in_=x, axis=AX.X).then_inc(dv.h, 1))
            dv_red[sl] = dv.inc()
            wait("scalar", ld, n_x)
            em("scalar", lambda E, t=t: E.memzero(stat_sb[:, 64 + t:65 + t]))
            em("scalar", lambda E, t=t: E.memzero(stat_sb[:, 128 + t:129 + t]))
            em("scalar", lambda E, x=xbuf, t=t: E.activation(
                out=ps_sq[:], in_=x[:, 0:512], func=AF.Square, bias=0.0, scale=1.0,
                accum_out=stat_sb[:, 64 + t:65 + t]))
            em("scalar", lambda E, x=xbuf, t=t: E.activation(
                out=ps_sq[:], in_=x[:, 512:1024], func=AF.Square, bias=0.0, scale=1.0,
                accum_out=stat_sb[:, 128 + t:129 + t]).then_inc(ac.h, 1))
            ac_sq2[sl] = ac.inc()
        c0 = b8 * 8
        q0 = 64 + b8 * 8
        wait("vector", ac, ac_sq2[0])
        wait("vector", ac, ac_sq2[1])
        wait("vector", dv, max(dv_red[0], dv_red[1]))

        def _stq(fn):
            em("vector", lambda E, f=fn: f(E).then_inc(dv.h, 1))
            n = dv.inc()
            wait("vector", dv, n)
            return n
        _stq(lambda E, c0=c0: E.tensor_scalar(
            out=stat_sb[:, 192 + c0:200 + c0], in0=stat_sb[:, c0:c0 + 8],
            scalar1=1.0 / D, scalar2=None, op0=ALU.mult))
        _stq(lambda E, c0=c0: E.tensor_tensor(
            out=stat_sb[:, 64 + c0:72 + c0], in0=stat_sb[:, 64 + c0:72 + c0],
            in1=stat_sb[:, 128 + c0:136 + c0], op=ALU.add))
        _stq(lambda E, c0=c0: E.tensor_scalar(
            out=stat_sb[:, 64 + c0:72 + c0], in0=stat_sb[:, 64 + c0:72 + c0],
            scalar1=1.0 / D, scalar2=None, op0=ALU.mult))
        _stq(lambda E, c0=c0: E.tensor_tensor(
            out=stat_sb[:, 128 + c0:136 + c0], in0=stat_sb[:, 192 + c0:200 + c0],
            in1=stat_sb[:, 192 + c0:200 + c0], op=ALU.mult))
        dvv_ = _stq(lambda E, c0=c0: E.tensor_tensor(
            out=stat_sb[:, 64 + c0:72 + c0], in0=stat_sb[:, 64 + c0:72 + c0],
            in1=stat_sb[:, 128 + c0:136 + c0], op=ALU.subtract))
        wait("scalar", dv, dvv_)
        em("scalar", lambda E, c0=c0: E.activation(
            out=stat_sb[:, 128 + c0:136 + c0], in_=stat_sb[:, 64 + c0:72 + c0],
            func=AF.Ln, bias=np.float32(1e-5), scale=1.0))
        em("scalar", lambda E, c0=c0, q0=q0: E.activation(
            out=rstd_all[:, q0:q0 + 8], in_=stat_sb[:, 128 + c0:136 + c0],
            func=AF.Exp, bias=0.0, scale=np.float32(-0.5)).then_inc(ac.h, 1))
        acr_ = ac.inc()
        wait("vector", ac, acr_)
        _stq(lambda E, c0=c0, q0=q0: E.tensor_tensor(
            out=negm_all[:, q0:q0 + 8], in0=stat_sb[:, 192 + c0:200 + c0],
            in1=rstd_all[:, q0:q0 + 8], op=ALU.mult))
        _stq(lambda E, q0=q0: E.tensor_scalar(
            out=negm_all[:, q0:q0 + 8], in0=negm_all[:, q0:q0 + 8],
            scalar1=-1.0, scalar2=None, op0=ALU.mult))
    dv_statsq_done = dv.n
    ac_norm = {0: 0, 1: 0}
    wait("scalar", dv, dv_statsq_done)
    for t in range(16):
        sl = t % 2
        wait("sync", ac, ac_norm[sl])
        if sl == 0:
            n_x = dma("sync", xb[:], xq_in[t * 128:(t + 1) * 128, :], ld)
        else:
            n_x = dma("sync", evs0, xq_in[t * 128:(t + 1) * 128, 0:512], ld)
            n_x = dma("sync", evs1, xq_in[t * 128:(t + 1) * 128, 512:1024], ld)
        hsl = t % 2
        wait("scalar", ld, n_x)
        wait("scalar", sh, sh_hstore_pp)
        if sl == 0:
            em("scalar", lambda E, t=t, h=hsl: E.activation(
                out=hb[:, h * D:h * D + D], in_=xb[:], func=AF.Identity,
                bias=negm_all[:, 64 + t:65 + t], scale=rstd_all[:, 64 + t:65 + t])
               .then_inc(ac.h, 1))
        else:
            em("scalar", lambda E, t=t, h=hsl: E.activation(
                out=hb[:, h * D:h * D + 512], in_=evs0, func=AF.Identity,
                bias=negm_all[:, 64 + t:65 + t], scale=rstd_all[:, 64 + t:65 + t]))
            em("scalar", lambda E, t=t, h=hsl: E.activation(
                out=hb[:, h * D + 512:h * D + D], in_=evs1, func=AF.Identity,
                bias=negm_all[:, 64 + t:65 + t], scale=rstd_all[:, 64 + t:65 + t])
               .then_inc(ac.h, 1))
        ac_norm[sl] = ac.inc()
        wait("scalar", ac, ac_norm[sl])
        sh_hstore_pp = sh_hstore_prev
        sh_hstore_prev = dma("scalar", hq_scr[t * 128:(t + 1) * 128, :],
                             hb[:, hsl * D:(hsl + 1) * D], sh)
    sh_hdone_q = sh_hstore_prev

    pe_hf_prev = 0
    ac_evacA = {0: 0, 1: 0}
    dv_evA = {0: 0, 1: 0}
    wait("sync", sh, sh_hdone_q)
    for g in range(1 if limit in ("A1", "A0") else 4):
        wait("sync", pe, pe_hf_prev)
        n_hf = 0
        for dc in range(ECH):
            n_hf = dma("sync", hf[:, dc * 512:(dc + 1) * 512],
                       hq_scr[g * 512:(g + 1) * 512, dc * 128:(dc + 1) * 128],
                       ld, transpose=True)
        if limit == "hf" and g == 0:
            wait("sync", ld, n_hf)
            finish_debug([hf[:].bitcast(F32)])
            return finalize()
        wait("tensor", ld, n_hf)
        wait("tensor", dv, dv_wcast)
        for ec in range(1 if limit == "A0" else ECH):
            hsel = ec % 2
            wait("tensor", ac, ac_evacA[hsel])
            for dc in range(ECH):
                em("tensor", lambda E, ec=ec, dc=dc, h=hsel: E.matmul(
                    out=ps_pvA[:, h * 512:(h + 1) * 512],
                    lhsT=wq_bf[:, dc * D + ec * 128:dc * D + ec * 128 + 128],
                    rhs=hf[:, dc * 512:(dc + 1) * 512],
                    start=(dc == 0), stop=(dc == ECH - 1)).then_inc(pe.h, 1))
                pe.inc()
            pe_q = pe.n
            wait("scalar", pe, pe_q)
            wait("scalar", gp, max(dv_evA[0], dv_evA[1]))
            em("scalar", lambda E, ec=ec, h=hsel: E.activation(
                out=ev_scr[:, 0:512],
                in_=ps_pvA[:, h * 512:(h + 1) * 512], func=AF.Identity,
                bias=bqc_sb[:, ec:ec + 1], scale=np.float32(1.0 / 32.0)).then_inc(ac.h, 1))
            ac_evacA[hsel] = ac.inc()
            wait("vector", ac, ac_evacA[hsel])
            em("vector", lambda E, ec=ec, g=g: E.tensor_copy(
                out=q_sb[:, ec * TOWN + g * 512:ec * TOWN + (g + 1) * 512],
                in_=ev_scr[:, 0:512]).then_inc(gp.h, 1))
            dv_evA[hsel] = gp.inc()
        pe_hf_prev = pe.n

    if limit in ("A", "A1", "A0"):
        finish_debug([q_sb[:].bitcast(F32)])
        return finalize()
    wsrc = [xb[:, 0:256], xb[:, 256:512]]
    wxn = xb[:, 512:768]
    wtt = xb[:, 768:1024]
    wacc = hb[:, 0:512].bitcast(F32)
    wi1 = hb[:, 512:1024].bitcast(I32)
    wsg = hb[:, 1024:1280]
    wdst = [hb[:, 1280:1536], hb[:, 1536:1792]]
    wsmall = hb[:, 1792:2048].bitcast(F32)
    wbm = wsmall[:, 0:16]
    wsc = wsmall[:, 16:32]
    wrx = wsmall[:, 32:48]
    wchunks = []
    for (w_in_t, w_scr, nrow) in ((wg_in, wgq_scr, II), (wu_in, wuq_scr, II),
                                  (wd_in, wdq_scr, D)):
        ncol = D if nrow == II else II
        for t in range(nrow // 128):
            for coff in range(0, ncol, 256):
                wchunks.append((w_in_t, w_scr, t, coff))
    wpos = {"i": 0}
    wslot = {0: 0, 1: 0}
    wsh = {0: 0, 1: 0}
    wst_state = {"sh_last": 0}

    def emit_wchunks(n):
        for _ in range(n):
            i = wpos["i"]
            if i >= len(wchunks):
                return
            wpos["i"] += 1
            w_in_t, w_scr, t, coff = wchunks[i]
            slot = i % 2
            wait("sync", dv, wslot[slot])
            n_w = dma("sync", wsrc[slot], w_in_t[t * 128:(t + 1) * 128, coff:coff + 256], ld)
            wait("scalar", dv, max(wslot[0], wslot[1]))
            wait("scalar", ld, n_w)
            em("scalar", lambda E, s=slot: E.activation(
                out=wtt[:], in_=wsrc[s], func=AF.Abs, bias=0.0, scale=1.0))
            em("scalar", lambda E, s=slot: E.activation(
                out=wsg[:], in_=wsrc[s], func=AF.Sign, bias=0.0, scale=1.0)
               .then_inc(ac.h, 1))
            acsg = ac.inc()
            wait("vector", ac, acsg)
            wait("vector", sh, wsh[slot])

            def _w(fn):
                em("vector", lambda E, f=fn: f(E).then_inc(dv.h, 1))
                n = dv.inc()
                wait("vector", dv, n)
                return n
            g16 = wtt.rearrange("p (g k) -> p g k", k=16)
            _w(lambda E: E.reduce_max(out=wbm, in_=g16, axis=AX.X))
            _w(lambda E: E.reciprocal(out=wrx, in_=wbm))
            rxb = wrx.unsqueeze(2).broadcast_to([128, 16, 16])
            scb = wsc.unsqueeze(2).broadcast_to([128, 16, 16])
            em("vector", lambda E: E.tensor_tensor(
                out=wxn.rearrange("p (g k) -> p g k", k=16), in0=g16, in1=rxb,
                op=ALU.mult))
            em("vector", lambda E: E.tensor_scalar(
                out=wi1[:], in0=wxn, scalar1=12.0, scalar2=None, op0=ALU.mult))
            em("vector", lambda E: E.tensor_scalar(
                out=wacc[:], in0=wi1[:], scalar1=4.0, scalar2=0.5,
                op0=ALU.min, op1=ALU.mult))
            em("vector", lambda E: E.tensor_scalar(
                out=wi1[:], in0=wxn, scalar1=6.0, scalar2=None, op0=ALU.mult))
            em("vector", lambda E: E.tensor_scalar(
                out=wtt[:], in0=wi1[:], scalar1=2.0, scalar2=2.0,
                op0=ALU.subtract, op1=ALU.min))
            em("vector", lambda E: E.scalar_tensor_tensor(
                out=wacc[:], in0=wtt[:], scalar=0.0, in1=wacc[:],
                op0=ALU.max, op1=ALU.add))
            em("vector", lambda E: E.tensor_scalar(
                out=wi1[:], in0=wxn, scalar1=3.0, scalar2=None, op0=ALU.mult))
            em("vector", lambda E: E.tensor_scalar(
                out=wsc, in0=wbm, scalar1=1.0 / 6.0, scalar2=1e-12,
                op0=ALU.mult, op1=ALU.max))
            em("vector", lambda E: E.tensor_scalar(
                out=wtt[:], in0=wi1[:], scalar1=2.0, scalar2=4.0,
                op0=ALU.mult, op1=ALU.subtract))
            em("vector", lambda E: E.scalar_tensor_tensor(
                out=wacc[:], in0=wtt[:], scalar=0.0, in1=wacc[:],
                op0=ALU.max, op1=ALU.add))
            em("vector", lambda E: E.tensor_tensor(
                out=wtt.rearrange("p (g k) -> p g k", k=16),
                in0=wacc[:].rearrange("p (g k) -> p g k", k=16), in1=scb,
                op=ALU.mult))
            em("vector", lambda E, s=slot: E.tensor_tensor(
                out=wdst[s], in0=wtt[:], in1=wsg[:], op=ALU.mult).then_inc(dv.h, 1))
            wslot[slot] = dv.inc()
            wait("scalar", dv, wslot[slot])
            wsh[slot] = dma("scalar", w_scr[t * 128:(t + 1) * 128, coff:coff + 256],
                            wdst[slot], sh)
            wst_state["sh_last"] = wsh[slot]

    ac_exp_last = {0: 0, 1: 0}
    dv_mask_b_last = {0: 0, 1: 0}
    ac_kevac = {0: 0, 1: 0}
    dv_vevac = {0: 0, 1: 0}
    dv_oacc = {}
    dv_lacc_prev = 0
    pe_pv_all_prev = 0
    nsu_run = limit if isinstance(limit, int) else NSU
    for su in range(nsu_run):
        kt, half = su // 2, su % 2
        key0 = kt * 1024 + half * 512
        wait("sync", pe, pe_hf_prev)
        n_hf = 0
        for dc in range(ECH):
            n_hf = dma("sync", hf[:, dc * 512:(dc + 1) * 512],
                       h_scr[key0:key0 + 512, dc * 128:(dc + 1) * 128], ld, transpose=True)
        wait("tensor", ld, n_hf)
        wait("tensor", dv, dv_oacc.get(14, 0))
        for ec in range(ECH):
            hsel = ec % 2
            wait("tensor", ac, ac_kevac[hsel])
            wait("tensor", dv, dv_vevac[hsel])
            for dc in range(ECH):
                em("tensor", lambda E, ec=ec, dc=dc, h=hsel: E.matmul(
                    out=ps_pvA[:, h * 512:(h + 1) * 512],
                    lhsT=wkT_bf[:, dc * D + ec * 128:dc * D + ec * 128 + 128],
                    rhs=hf[:, dc * 512:(dc + 1) * 512],
                    start=(dc == 0), stop=(dc == ECH - 1)).then_inc(pe.h, 1))
                pe.inc()
            pe_k = pe.n
            wait("scalar", pe, pe_k)
            wait("scalar", gp, max(dv_vevac[0], dv_vevac[1]))
            em("scalar", lambda E, ec=ec, h=hsel: E.activation(
                out=ev_scr[:, 0:512],
                in_=ps_pvA[:, h * 512:(h + 1) * 512], func=AF.Identity,
                bias=bkc_sb[:, ec:ec + 1], scale=1.0).then_inc(ac.h, 1))
            ac_kevac[hsel] = ac.inc()
            wait("vector", ac, ac_kevac[hsel])
            em("vector", lambda E, ec=ec: E.tensor_copy(
                out=k_sb[:, ec * 512:(ec + 1) * 512],
                in_=ev_scr[:, 0:512]).then_inc(gp.h, 1))
            dv_vevac[hsel] = gp.inc()
        for kc in range(4):
            for vh in range(2):
                hsel = (kc * 2 + vh) % 2
                wait("tensor", ac, ac_kevac[hsel])
                wait("tensor", dv, dv_vevac[hsel])
                for dc in range(ECH):
                    em("tensor", lambda E, kc=kc, vh=vh, dc=dc, h=hsel: E.matmul(
                        out=ps_pvA[:, h * 512:(h + 1) * 512],
                        lhsT=hf[:, dc * 512 + kc * 128:dc * 512 + kc * 128 + 128],
                        rhs=wvT_bf[:, dc * D + vh * 512:dc * D + (vh + 1) * 512],
                        start=(dc == 0), stop=(dc == ECH - 1)).then_inc(pe.h, 1))
                    pe.inc()
                pe_v = pe.n
                wait("scalar", pe, pe_v)
                wait("scalar", gp, max(dv_vevac[0], dv_vevac[1]))
                em("scalar", lambda E, h=hsel: E.activation(
                    out=ev_scr[:, 0:512],
                    in_=ps_pvA[:, h * 512:(h + 1) * 512], func=AF.Identity,
                    bias=0.0, scale=1.0).then_inc(ac.h, 1))
                ac_kevac[hsel] = ac.inc()
                wait("vector", ac, ac_kevac[hsel])
                em("vector", lambda E, kc=kc, vh=vh: E.tensor_tensor(
                    out=v_sb[:, kc * D + vh * 512:kc * D + (vh + 1) * 512],
                    in0=ev_scr[:, 0:512],
                    in1=bvb_sb[:, vh * 512:(vh + 1) * 512], op=ALU.add).then_inc(gp.h, 1))
                dv_vevac[hsel] = gp.inc()
        pe_hf_prev = pe.n

        gp_kv_done = gp.n
        wait("tensor", gp, gp_kv_done)
        ac_exp = dict(ac_exp_last)
        pair = 0
        dv_mask_b = dict(dv_mask_b_last)
        for kc in range(4):
            for qh in range(4):
                bank = pair % 2
                ps_s = ps_s0 if bank == 0 else ps_s1
                wait("tensor", ac, ac_exp[bank])
                for ec in range(ECH):
                    em("tensor", lambda E, kc=kc, qh=qh, ec=ec, ps=ps_s: E.matmul(
                        out=ps[:, 0:512],
                        lhsT=k_sb[:, ec * 512 + kc * 128:ec * 512 + kc * 128 + 128],
                        rhs=q_sb[:, ec * TOWN + qh * 512:ec * TOWN + (qh + 1) * 512],
                        start=(ec == 0), stop=(ec == ECH - 1)).then_inc(pe.h, 1))
                    pe.inc()
                pe_s = pe.n
                wait("scalar", pe, pe_s)
                wait("scalar", gp, max(dv_mask_b[0], dv_mask_b[1]))
                em("scalar", lambda E, b=bank, ps=ps_s: E.activation(
                    out=es_scr[:, 0:512],
                    in_=ps[:, 0:512], func=AF.Exp, bias=np.float32(NEG),
                    scale=1.0).then_inc(ac.h, 1))
                ac_exp[bank] = ac.inc()
                wait("vector", ac, ac_exp[bank])
                wait("vector", pe, pe_pv_all_prev)
                em("vector", lambda E, su=su, kc=kc, qh=qh: E.scalar_tensor_tensor(
                    out=pt_sb[:, kc * TOWN + qh * 512:kc * TOWN + (qh + 1) * 512],
                    in0=ramp_sb[:, qh * 512:(qh + 1) * 512],
                    scalar=tthr_sb[:, su * 4 + kc:su * 4 + kc + 1],
                    in1=es_scr[:, 0:512],
                    op0=ALU.is_ge, op1=ALU.mult).then_inc(gp.h, 1))
                dv_mask_b[bank] = gp.inc()
                pair += 1
        ac_exp_last = ac_exp
        dv_mask_b_last = dict(dv_mask_b)
        gp_mask_all = gp.n

        dv_oacc_prev = dict(dv_oacc)
        wait("tensor", gp, gp_mask_all)
        wait("tensor", dv, dv_lacc_prev)
        lbase = (su % 2) * 16
        pe_pv = {}
        for qs in range(16):
            buf = qs % 2
            ps_o = ps_pvA if buf == 0 else ps_pvB
            if qs >= 2:
                wait("tensor", dv, dv_oacc[qs - 2])
            elif qs == 1:
                wait("tensor", dv, dv_oacc_prev.get(15, 0))
            for kc in range(4):
                em("tensor", lambda E, qs=qs, kc=kc, ps=ps_o: E.matmul(
                    out=ps[:, 0:512],
                    lhsT=pt_sb[:, kc * TOWN + qs * 128:kc * TOWN + qs * 128 + 128],
                    rhs=v_sb[:, kc * D:kc * D + 512],
                    start=(kc == 0), stop=(kc == 3)).then_inc(pe.h, 1))
                pe.inc()
                em("tensor", lambda E, qs=qs, kc=kc, ps=ps_o: E.matmul(
                    out=ps[:, 512:1024],
                    lhsT=pt_sb[:, kc * TOWN + qs * 128:kc * TOWN + qs * 128 + 128],
                    rhs=v_sb[:, kc * D + 512:kc * D + 1024],
                    start=(kc == 0), stop=(kc == 3)).then_inc(pe.h, 1))
                pe.inc()
                em("tensor", lambda E, qs=qs, kc=kc, lc=lbase + qs: E.matmul(
                    out=ps_l[:, lc:lc + 1],
                    lhsT=pt_sb[:, kc * TOWN + qs * 128:kc * TOWN + qs * 128 + 128],
                    rhs=ones1_bf[:, 0:1],
                    start=(kc == 0), stop=(kc == 3)).then_inc(pe.h, 1))
                pe.inc()
            pe_pv[qs] = pe.n
            wait("vector", pe, pe_pv[qs])
            if su == 0 and qs == 0:
                wait("vector", dv, dv_zero)
            em("vector", lambda E, qs=qs, ps=ps_o: E.tensor_tensor(
                out=o_sb[:, qs * D:(qs + 1) * D], in0=o_sb[:, qs * D:(qs + 1) * D],
                in1=ps[:, 0:1024], op=ALU.add).then_inc(dv.h, 1))
            dv_oacc[qs] = dv.inc()
        pe_pv_all_prev = pe.n
        wait("vector", pe, pe_pv[15])
        em("vector", lambda E, lb=lbase: E.tensor_tensor(
            out=l_sb[:, 0:16], in0=l_sb[:, 0:16], in1=ps_l[:, lb:lb + 16],
            op=ALU.add).then_inc(dv.h, 1))
        dv_lacc_prev = dv.inc()
        emit_wchunks(17)

    if isinstance(limit, int):
        finish_debug([o_sb[:, 0:15 * D], l_sb[:], stat_sb[:]])
        return finalize()
    emit_wchunks(len(wchunks))
    dv_w_final = max(wslot[0], wslot[1])
    sh_w_final = wst_state["sh_last"]

    wait("vector", dv, dv_lacc_prev)
    em("vector", lambda E: E.reciprocal(out=rinv_sb[:], in_=l_sb[:]).then_inc(dv.h, 1))
    dv_rinv = dv.inc()
    wait("vector", dv, dv_rinv)
    hbf = hb[:].bitcast(F32)
    sh_xnew_prev = sh_hstore_prev
    dv_fold = {}
    for r in range(16):
        if r == 0:
            wait("sync", dv, dv_w_final)
            wait("vector", sh, sh_w_final)
        else:
            wait("sync", dv, dv_fold.get(r - 1, 0))
        n_x = dma("sync", xb[:], xq_in[r * 128:(r + 1) * 128, :], ld)
        wait("vector", ld, n_x)
        wait("vector", dv, dv_rinv)
        wait("vector", sh, sh_xnew_prev)
        em("vector", lambda E, r=r: E.scalar_tensor_tensor(
            out=hbf, in0=o_sb[:, r * D:(r + 1) * D],
            scalar=rinv_sb[:, r:r + 1], in1=xb[:],
            op0=ALU.mult, op1=ALU.add).then_inc(dv.h, 1))
        dv_fold[r] = dv.inc()
        if r == 15:
            em("vector", lambda E, r=r: E.reduce_sum(
                out=stat_sb[:, r:r + 1], in_=hbf, axis=AX.X).then_inc(dv.h, 1))
            wait("vector", dv, dv.inc())
        else:
            em("vector", lambda E, r=r: E.reduce_sum(
                out=stat_sb[:, r:r + 1], in_=hbf, axis=AX.X))
        wait("scalar", dv, dv_fold[r])
        em("scalar", lambda E, r=r: E.memzero(stat_sb[:, 16 + r:17 + r]))
        em("scalar", lambda E, r=r: E.memzero(stat_sb[:, 32 + r:33 + r]))
        em("scalar", lambda E, r=r: E.activation(
            out=ps_sq[:], in_=hbf[:, 0:512], func=AF.Square, bias=0.0, scale=1.0,
            accum_out=stat_sb[:, 16 + r:17 + r]))
        em("scalar", lambda E, r=r: E.activation(
            out=ps_sq[:], in_=hbf[:, 512:1024], func=AF.Square, bias=0.0, scale=1.0,
            accum_out=stat_sb[:, 32 + r:33 + r]).then_inc(ac.h, 1))
        ac.inc()
        sh_xnew_prev = dma("scalar", xnew_scr[r * 128:(r + 1) * 128, :], hbf, sh)
    ac_sq_done = ac.n
    wait("vector", ac, ac_sq_done)

    def _stepf(fn):
        em("vector", lambda E, f=fn: f(E).then_inc(dv.h, 1))
        n = dv.inc()
        wait("vector", dv, n)
        return n
    _stepf(lambda E: E.tensor_scalar(
        out=stat_sb[:, 48:64], in0=stat_sb[:, 0:16], scalar1=1.0 / D, scalar2=None,
        op0=ALU.mult))
    _stepf(lambda E: E.tensor_tensor(
        out=stat_sb[:, 16:32], in0=stat_sb[:, 16:32], in1=stat_sb[:, 32:48], op=ALU.add))
    _stepf(lambda E: E.tensor_scalar(
        out=stat_sb[:, 16:32], in0=stat_sb[:, 16:32], scalar1=1.0 / D, scalar2=None,
        op0=ALU.mult))
    _stepf(lambda E: E.tensor_tensor(
        out=stat_sb[:, 32:48], in0=stat_sb[:, 48:64], in1=stat_sb[:, 48:64], op=ALU.mult))
    dv_varf = _stepf(lambda E: E.tensor_tensor(
        out=stat_sb[:, 16:32], in0=stat_sb[:, 16:32], in1=stat_sb[:, 32:48],
        op=ALU.subtract))
    wait("scalar", dv, dv_varf)
    em("scalar", lambda E: E.activation(
        out=stat_sb[:, 32:48], in_=stat_sb[:, 16:32], func=AF.Ln, bias=np.float32(1e-5), scale=1.0))
    em("scalar", lambda E: E.activation(
        out=rstdf_sb[:], in_=stat_sb[:, 32:48], func=AF.Exp, bias=0.0, scale=np.float32(-0.5))
       .then_inc(ac.h, 1))
    ac_rstdf = ac.inc()
    wait("vector", ac, ac_rstdf)
    _stepf(lambda E: E.tensor_tensor(
        out=negmf_sb[:], in0=stat_sb[:, 48:64], in1=rstdf_sb[:], op=ALU.mult))
    dv_negmf = _stepf(lambda E: E.tensor_scalar(
        out=negmf_sb[:], in0=negmf_sb[:], scalar1=-1.0, scalar2=None,
        op0=ALU.mult))
    dv_fold_done = dv.n
    ac_fold_done = ac.n

    if limit == "fold":
        finish_debug([rstdf_sb[:], negmf_sb[:], rinv_sb[:], l_sb[:]])
        return finalize()
    ps_scope.close()
    bstack.close()

    cstack = ExitStack()
    C = cstack.enter_context
    wgq_f = C(nc.sbuf_tensor("wgq_f", [128, ECH * II], BF16))
    wuq_f = C(nc.sbuf_tensor("wuq_f", [128, ECH * II], BF16))
    wdq_f = C(nc.sbuf_tensor("wdq_f", [128, ICH * D], BF16))
    qt_xn = C(nc.sbuf_tensor("qt_xn", [128, D], F32))
    qt_i1 = C(nc.sbuf_tensor("qt_i1", [128, D], I32))
    qt_acc = C(nc.sbuf_tensor("qt_acc", [128, D], F32))
    qt_t = C(nc.sbuf_tensor("qt_t", [128, D], F32))
    qt_sg = C(nc.sbuf_tensor("qt_sg", [128, D], BF16))
    qt_bm = C(nc.sbuf_tensor("qt_bm", [128, 64], F32))
    qt_sc = C(nc.sbuf_tensor("qt_sc", [128, 64], F32))
    qt_rx = C(nc.sbuf_tensor("qt_rx", [128, 64], F32))
    wq_bf_sb = C(nc.sbuf_tensor("wq_bf_sb", [128, 2 * D], BF16))
    wst = C(nc.sbuf_tensor("wst", [128, 2 * D], F32))
    xc = C(nc.sbuf_tensor("xc", [128, 2 * D], F32))
    h2_sb = C(nc.sbuf_tensor("h2_sb", [128, D], F32))
    xq_bf = C(nc.sbuf_tensor("xq_bf", [128, D], BF16))
    xqf_sb = C(nc.sbuf_tensor("xqf_sb", [128, ECH * 128], BF16))
    hq_bf = C(nc.sbuf_tensor("hq_bf", [128, 512], BF16))
    hqf_sb = C(nc.sbuf_tensor("hqf_sb", [128, 8 * 128], BF16))
    sg_sb = C(nc.sbuf_tensor("sg_sb", [128, 512], F32))
    outb_sb = C(nc.sbuf_tensor("outb_sb", [128, D], F32))

    ps_g = C(nc.psum_tensor("ps_g", [128, 512], F32))
    ps_g2 = C(nc.psum_tensor("ps_g2", [128, 512], F32))
    ps_u = C(nc.psum_tensor("ps_u", [128, 512], F32))
    ps_u2 = C(nc.psum_tensor("ps_u2", [128, 512], F32))
    ps_d = C(nc.psum_tensor("ps_d", [128, D], F32))
    ps_tA = C(nc.psum_tensor("ps_tA", [128, 128], BF16))
    ps_tB = C(nc.psum_tensor("ps_tB", [128, 128], BF16))


    qstate = {"dst_dv": 0}

    def quant(src, dst, F, blocks, wait_pe_n=0, wait_src_dv=0, wait_src_ac=0,
              wait_src_ld=0):
        g16 = src.rearrange("p (g k) -> p g k", k=16)
        if wait_pe_n:
            wait("vector", pe, wait_pe_n)
        if wait_src_ac:
            wait("vector", ac, wait_src_ac)
        if wait_src_dv:
            wait("vector", dv, wait_src_dv)
        if wait_src_ld:
            wait("vector", ld, wait_src_ld)
        em("vector", lambda E: E.reduce_max(
            out=qt_bm[:, 0:blocks], in_=g16, axis=AX.X,
            apply_absolute_value=True).then_inc(dv.h, 1))
        dvq = dv.inc()
        wait("vector", dv, dvq)
        em("vector", lambda E: E.tensor_scalar(
            out=qt_sc[:, 0:blocks], in0=qt_bm[:, 0:blocks], scalar1=1.0 / 6.0,
            scalar2=1e-12, op0=ALU.mult, op1=ALU.max).then_inc(dv.h, 1))
        dvq = dv.inc()
        wait("vector", dv, dvq)
        em("vector", lambda E: E.reciprocal(
            out=qt_rx[:, 0:blocks], in_=qt_sc[:, 0:blocks]).then_inc(dv.h, 1))
        dvq = dv.inc()
        wait("vector", dv, dvq)
        rxb = qt_rx[:, 0:blocks].unsqueeze(2).broadcast_to([128, blocks, 16])
        scb = qt_sc[:, 0:blocks].unsqueeze(2).broadcast_to([128, blocks, 16])
        wait("scalar", dv, qstate["dst_dv"])
        if wait_src_dv:
            wait("scalar", dv, wait_src_dv)
        if wait_src_ac:
            wait("scalar", ac, wait_src_ac)
        if wait_src_ld:
            wait("scalar", ld, wait_src_ld)
        em("scalar", lambda E: E.activation(
            out=qt_t[:, 0:F], in_=src, func=AF.Abs, bias=0.0, scale=1.0))
        em("scalar", lambda E: E.activation(
            out=qt_sg[:, 0:F], in_=src, func=AF.Sign, bias=0.0, scale=1.0).then_inc(ac.h, 1))
        acsg = ac.inc()
        wait("vector", ac, acsg)
        em("vector", lambda E: E.tensor_tensor(
            out=qt_xn[:, 0:F].rearrange("p (g k) -> p g k", k=16),
            in0=qt_t[:, 0:F].rearrange("p (g k) -> p g k", k=16), in1=rxb,
            op=ALU.mult).then_inc(dv.h, 1))
        dv.inc()
        em("vector", lambda E: E.tensor_scalar(
            out=qt_i1[:, 0:F], in0=qt_xn[:, 0:F], scalar1=2.0, scalar2=None,
            op0=ALU.mult))
        em("vector", lambda E: E.tensor_scalar(
            out=qt_acc[:, 0:F], in0=qt_i1[:, 0:F], scalar1=4.0, scalar2=0.5,
            op0=ALU.min, op1=ALU.mult))
        em("vector", lambda E: E.tensor_scalar(
            out=qt_i1[:, 0:F], in0=qt_xn[:, 0:F], scalar1=1.0, scalar2=None,
            op0=ALU.mult))
        em("vector", lambda E: E.tensor_scalar(
            out=qt_t[:, 0:F], in0=qt_i1[:, 0:F], scalar1=2.0, scalar2=2.0,
            op0=ALU.subtract, op1=ALU.min))
        em("vector", lambda E: E.scalar_tensor_tensor(
            out=qt_acc[:, 0:F], in0=qt_t[:, 0:F], scalar=0.0, in1=qt_acc[:, 0:F],
            op0=ALU.max, op1=ALU.add))
        em("vector", lambda E: E.tensor_scalar(
            out=qt_i1[:, 0:F], in0=qt_xn[:, 0:F], scalar1=0.5, scalar2=None,
            op0=ALU.mult))
        em("vector", lambda E: E.tensor_scalar(
            out=qt_t[:, 0:F], in0=qt_i1[:, 0:F], scalar1=2.0, scalar2=4.0,
            op0=ALU.mult, op1=ALU.subtract))
        em("vector", lambda E: E.scalar_tensor_tensor(
            out=qt_acc[:, 0:F], in0=qt_t[:, 0:F], scalar=0.0, in1=qt_acc[:, 0:F],
            op0=ALU.max, op1=ALU.add))
        em("vector", lambda E: E.tensor_tensor(
            out=qt_t[:, 0:F].rearrange("p (g k) -> p g k", k=16),
            in0=qt_acc[:, 0:F].rearrange("p (g k) -> p g k", k=16), in1=scb, op=ALU.mult))
        wait("vector", ac, acsg)
        em("vector", lambda E, dst=dst: E.tensor_tensor(
            out=dst, in0=qt_t[:, 0:F], in1=qt_sg[:, 0:F], op=ALU.mult).then_inc(dv.h, 1))
        qstate["dst_dv"] = dv.inc()
        return qstate["dst_dv"]

    wait("sync", sh, sh_w_final)
    for dc in range(ECH):
        dma("sync", wgq_f[:, dc * II:(dc + 1) * II],
            wgq_scr[:, dc * 128:(dc + 1) * 128], ld, transpose=True)
        dma("sync", wuq_f[:, dc * II:(dc + 1) * II],
            wuq_scr[:, dc * 128:(dc + 1) * 128], ld, transpose=True)
    n_wf = 0
    for icc in range(ICH):
        n_wf = dma("sync", wdq_f[:, icc * D:(icc + 1) * D],
                   wdq_scr[:, icc * 128:(icc + 1) * 128], ld, transpose=True)

    NCC = [512, 512, 512, 512, 512, 256]
    dv_out_prev = 0
    pe_xt_prev = 0
    pe_hqt_prev = 0
    dv_tc = {"A": 0, "B": 0}
    for r in range(16):
        xsel = r % 2
        if r >= 2:
            wait("sync", dv, dv_out_prev)
        n_x = dma("sync", xc[:, xsel * D:(xsel + 1) * D],
                  xnew_scr[r * 128:(r + 1) * 128, :], ld)
        wait("scalar", ld, n_x)
        wait("scalar", dv, dv_negmf)
        wait("scalar", dv, qstate["dst_dv"])
        em("scalar", lambda E, r=r, xs=xsel: E.activation(
            out=h2_sb[:], in_=xc[:, xs * D:(xs + 1) * D], func=AF.Identity,
            bias=negmf_sb[:, r:r + 1], scale=rstdf_sb[:, r:r + 1]).then_inc(ac.h, 1))
        ac_h2 = ac.inc()
        dv_xq = quant(h2_sb[:], xq_bf[:], D, 64,
                      wait_pe_n=pe_xt_prev, wait_src_ac=ac_h2)
        if limit == "C0" and r == 0:
            finish_debug([h2_sb[:], xq_bf[:].bitcast(F32), qt_sc[:, 0:64],
                          qt_xn[:, 0:D], qt_acc[:, 0:D]])
            return finalize()
        wait("tensor", dv, dv_xq)
        for dc in range(ECH):
            key = "A" if dc % 2 == 0 else "B"
            pst = ps_tA if dc % 2 == 0 else ps_tB
            wait("tensor", dv, dv_tc[key])
            em("tensor", lambda E, dc=dc, pst=pst: E.transpose(
                out=pst[:], in_=xq_bf[:, dc * 128:(dc + 1) * 128], identity=ident_sb[:])
               .then_inc(pe.h, 1))
            pe_t = pe.inc()
            wait("vector", pe, pe_t)
            em("vector", lambda E, dc=dc, pst=pst: E.tensor_copy(
                out=xqf_sb[:, dc * 128:(dc + 1) * 128], in_=pst[:]).then_inc(dv.h, 1))
            dv_tc[key] = dv.inc()
        pe_xt_prev = pe.n
        ac_xt, dv_xt = 0, dv_tc["B"]
        if limit == "C1" and r == 0:
            finish_debug([xqf_sb[:].bitcast(F32)])
            return finalize()

        off = 0
        icc0 = 0
        ac_sil_prev = {0: 0, 1: 0}
        dv_h_prev = {0: 0, 1: 0}
        dv_h_last = 0
        for ci, csz in enumerate(NCC):
            gsel = ci % 2
            psg = ps_g if gsel == 0 else ps_g2
            psu = ps_u if gsel == 0 else ps_u2
            wait("tensor", ac, ac_sil_prev[gsel])
            wait("tensor", dv, dv_h_prev[gsel])
            if ci == 0:
                wait("tensor", dv, dv_xt)
                wait("tensor", ld, n_wf)
            for dc in range(ECH):
                em("tensor", lambda E, dc=dc, off=off, csz=csz, psg=psg: E.matmul(
                    out=psg[:, 0:csz], lhsT=xqf_sb[:, dc * 128:(dc + 1) * 128],
                    rhs=wgq_f[:, dc * II + off:dc * II + off + csz],
                    start=(dc == 0), stop=(dc == ECH - 1)).then_inc(pe.h, 1))
                pe.inc()
            for dc in range(ECH):
                em("tensor", lambda E, dc=dc, off=off, csz=csz, psu=psu: E.matmul(
                    out=psu[:, 0:csz], lhsT=xqf_sb[:, dc * 128:(dc + 1) * 128],
                    rhs=wuq_f[:, dc * II + off:dc * II + off + csz],
                    start=(dc == 0), stop=(dc == ECH - 1)).then_inc(pe.h, 1))
                pe.inc()
            pe_g = pe.n
            wait("scalar", pe, pe_g)
            wait("scalar", dv, dv_h_last)
            em("scalar", lambda E, csz=csz, psg=psg: E.activation(
                out=sg_sb[:, 0:csz], in_=psg[:, 0:csz], func=AF.Silu, bias=0.0,
                scale=1.0).then_inc(ac.h, 1))
            ac_sil = ac.inc()
            ac_sil_prev[gsel] = ac_sil
            wait("vector", ac, ac_sil)
            wait("vector", pe, pe_g)
            em("vector", lambda E, csz=csz, psu=psu: E.tensor_tensor(
                out=h2_sb[:, 0:csz], in0=sg_sb[:, 0:csz], in1=psu[:, 0:csz],
                op=ALU.mult).then_inc(dv.h, 1))
            dv_h = dv.inc()
            dv_h_prev[gsel] = dv_h
            dv_h_last = dv_h
            dv_hq = quant(h2_sb[:, 0:csz], hq_bf[:, 0:csz], csz, csz // 16,
                          wait_pe_n=pe_hqt_prev, wait_src_dv=dv_h)
            wait("tensor", dv, dv_hq)
            nic = csz // 128
            for j in range(nic):
                icc = icc0 + j
                key = "A" if icc % 2 == 0 else "B"
                pst = ps_tA if icc % 2 == 0 else ps_tB
                wait("tensor", dv, dv_tc[key])
                em("tensor", lambda E, j=j, pst=pst: E.transpose(
                    out=pst[:], in_=hq_bf[:, j * 128:(j + 1) * 128], identity=ident_sb[:])
                   .then_inc(pe.h, 1))
                pe_t = pe.inc()
                wait("vector", pe, pe_t)
                em("vector", lambda E, sl=icc % 8, pst=pst: E.tensor_copy(
                    out=hqf_sb[:, sl * 128:(sl + 1) * 128], in_=pst[:])
                   .then_inc(dv.h, 1))
                dv_tc[key] = dv.inc()
            pe_hqt_prev = pe.n
            wait("tensor", dv, dv_tc["A"])
            wait("tensor", dv, dv_tc["B"])
            if ci == 0 and r >= 1:
                wait("tensor", dv, dv_out_prev)
            for j in range(nic):
                icc = icc0 + j
                for hh2 in range(2):
                    em("tensor", lambda E, icc=icc, sl=icc % 8, hh=hh2: E.matmul(
                        out=ps_d[:, hh * 512:(hh + 1) * 512],
                        lhsT=hqf_sb[:, sl * 128:(sl + 1) * 128],
                        rhs=wdq_f[:, icc * D + hh * 512:icc * D + (hh + 1) * 512],
                        start=(icc == 0), stop=(icc == ICH - 1)).then_inc(pe.h, 1))
                    pe.inc()
            icc0 += nic
            off += csz
        pe_down = pe.n
        wait("vector", pe, pe_down)
        if r >= 1:
            wait("vector", so, r * 16)
        em("vector", lambda E, r=r, xs=xsel: E.tensor_tensor(
            out=outb_sb[:, 0:D], in0=ps_d[:, 0:D],
            in1=xc[:, xs * D:(xs + 1) * D], op=ALU.add).then_inc(dv.h, 1))
        dv_out_prev = dv.inc()
        wait("scalar", dv, dv_out_prev)
        dma("scalar", out_t[r * 128:(r + 1) * 128, :], outb_sb[:, 0:D], so)

    wait("sync", so, so.n)
    return finalize()


def _host_tables(j):
    tt = np.zeros((128, 64), dtype=np.float32)
    p = np.arange(128, dtype=np.float32)
    for su in range(NSU):
        kt, half = su // 2, su % 2
        for kc in range(4):
            t = kt * 1024 + half * 512 + kc * 128 - j * TOWN
            t = max(-128, min(2176, t))
            tt[:, su * 4 + kc] = p + t
    return tt


def kernel(L0_attn_X, L0_attn_Wq, L0_attn_bq, L0_attn_Wk, L0_attn_bk,
           L0_attn_Wv, L0_attn_bv, L0_ffn_w_gate, L0_ffn_w_up, L0_ffn_w_down):
    X = np.ascontiguousarray(np.asarray(L0_attn_X, dtype=np.float32))
    wqT = np.ascontiguousarray(np.asarray(L0_attn_Wq, np.float32).T)
    wkT = np.ascontiguousarray(np.asarray(L0_attn_Wk, np.float32).T)
    wvT = np.ascontiguousarray(np.asarray(L0_attn_Wv, np.float32).T)
    bq = np.ascontiguousarray(np.asarray(L0_attn_bq, np.float32))
    bk = np.ascontiguousarray(np.asarray(L0_attn_bk, np.float32))
    bv = np.ascontiguousarray(np.asarray(L0_attn_bv, np.float32))
    wg = np.ascontiguousarray(np.asarray(L0_ffn_w_gate, np.float32))
    wu = np.ascontiguousarray(np.asarray(L0_ffn_w_up, np.float32))
    wd = np.ascontiguousarray(np.asarray(L0_ffn_w_down, np.float32))

    ramp = np.broadcast_to(np.arange(TOWN, dtype=np.float32)[None, :], (128, TOWN)).copy()
    ident = np.eye(128, dtype=ml_dtypes.bfloat16)

    if "nc" not in _CACHE:
        _CACHE["nc"] = build_program()
    nc = _CACHE["nc"]

    in_maps = []
    for c in range(8):
        b, j = c // 4, c % 4
        in_maps.append({
            "xq": np.ascontiguousarray(X[b, j * TOWN:(j + 1) * TOWN, :]),
            "xkv": np.ascontiguousarray(X[b]),
            "wqT": wqT, "wkT": wkT, "wvT": wvT,
            "bq": bq, "bk": bk, "bv": bv,
            "wg": wg, "wu": wu, "wd": wd,
            "ramp": ramp, "tthr": _host_tables(j), "ident": ident,
        })
    res = run_bass_kernel_spmd(nc, in_maps, core_ids=list(range(8)))
    out = np.empty((2, NKV, D), dtype=np.float32)
    for c in range(8):
        b, j = c // 4, c % 4
        out[b, j * TOWN:(j + 1) * TOWN, :] = np.asarray(res.results[c]["out"])
    return out


# version 14
# speedup vs baseline: 1.1467x; 1.1458x over previous
import numpy as np
import ml_dtypes
from contextlib import ExitStack

import concourse.bass as bass
import concourse.mybir as mybir
from concourse.bass_utils import run_bass_kernel_spmd

F32 = mybir.dt.float32
BF16 = mybir.dt.bfloat16
I32 = mybir.dt.int32
AF = mybir.ActivationFunctionType
ALU = mybir.AluOpType
AX = mybir.AxisListType

D = 1024
ECH = D // 128
TOWN = 2048
NKV = 8192
NSU = 16
II = 2816
ICH = II // 128
NEG = -8.0

_CACHE = {}


class Ctr:
    def __init__(self, handle):
        self.h = handle
        self.n = 0

    def inc(self, by=1):
        self.n += by
        return self.n


def build_program(limit=None):
    nc = bass.Bass("TRN2", target_bir_lowering=False)

    xq_in = nc.dram_tensor("xq", [TOWN, D], F32, kind="ExternalInput").ap()
    xkv_in = nc.dram_tensor("xkv", [NKV, D], F32, kind="ExternalInput").ap()
    wqT_in = nc.dram_tensor("wqT", [D, D], F32, kind="ExternalInput").ap()
    wkT_in = nc.dram_tensor("wkT", [D, D], F32, kind="ExternalInput").ap()
    wvT_in = nc.dram_tensor("wvT", [D, D], F32, kind="ExternalInput").ap()
    bq_in = nc.dram_tensor("bq", [D], F32, kind="ExternalInput").ap()
    bk_in = nc.dram_tensor("bk", [D], F32, kind="ExternalInput").ap()
    bv_in = nc.dram_tensor("bv", [D], F32, kind="ExternalInput").ap()
    wg_in = nc.dram_tensor("wg", [II, D], F32, kind="ExternalInput").ap()
    wu_in = nc.dram_tensor("wu", [II, D], F32, kind="ExternalInput").ap()
    wd_in = nc.dram_tensor("wd", [D, II], F32, kind="ExternalInput").ap()
    ramp_in = nc.dram_tensor("ramp", [128, TOWN], F32, kind="ExternalInput").ap()
    tthr_in = nc.dram_tensor("tthr", [128, 64], F32, kind="ExternalInput").ap()
    ident_in = nc.dram_tensor("ident", [128, 128], BF16, kind="ExternalInput").ap()
    out_t = nc.dram_tensor("out", [TOWN, D], F32, kind="ExternalOutput").ap()

    h_scr = nc.dram_tensor("h_scr", [NKV, D], BF16, kind="Internal").ap()
    hq_scr = nc.dram_tensor("hq_scr", [TOWN, D], BF16, kind="Internal").ap()
    xnew_scr = nc.dram_tensor("xnew_scr", [TOWN, D], F32, kind="Internal").ap()
    wgq_scr = nc.dram_tensor("wgq_scr", [II, D], BF16, kind="Internal").ap()
    wuq_scr = nc.dram_tensor("wuq_scr", [II, D], BF16, kind="Internal").ap()
    wdq_scr = nc.dram_tensor("wdq_scr", [D, II], BF16, kind="Internal").ap()

    ops = {k: [] for k in ("sync", "vector", "tensor", "scalar", "gpsimd")}

    def em(eng, fn):
        ops[eng].append(fn)

    def finalize():
        with nc.Block() as block:
            @block.sync
            def _(E):
                for f in ops["sync"]:
                    f(E)

            @block.vector
            def _(E):
                for f in ops["vector"]:
                    f(E)

            @block.tensor
            def _(E):
                for f in ops["tensor"]:
                    f(E)

            @block.scalar
            def _(E):
                for f in ops["scalar"]:
                    f(E)

            @block.gpsimd
            def _(E):
                for f in ops["gpsimd"]:
                    f(E)
        return nc

    pstack = ExitStack()
    bstack = ExitStack()
    P = pstack.enter_context
    B = bstack.enter_context

    ld = Ctr(P(nc.semaphore("ld")))
    sh = Ctr(P(nc.semaphore("sh")))
    so = Ctr(P(nc.semaphore("so")))
    pe = Ctr(P(nc.semaphore("pe")))
    dv = Ctr(P(nc.semaphore("dv")))
    gp = Ctr(P(nc.semaphore("gp")))
    ac = Ctr(P(nc.semaphore("ac")))

    def dma(eng, out, in_, ctr, transpose=False):
        n = ctr.inc(16)
        if transpose:
            em(eng, lambda E, o=out, i=in_, c=ctr.h:
               E.dma_start(out=o, in_=i, transpose=True).then_inc(c, 16))
        else:
            em(eng, lambda E, o=out, i=in_, c=ctr.h:
               E.dma_start(out=o, in_=i).then_inc(c, 16))
        return n

    def wait(eng, ctr, n):
        if n > 0:
            em(eng, lambda E, c=ctr.h, v=n: E.wait_ge(c, v))

    tthr_sb = P(nc.sbuf_tensor("tthr_sb", [128, 64], F32))
    stat_sb = P(nc.sbuf_tensor("stat_sb", [128, 256], F32))
    rstd_all = P(nc.sbuf_tensor("rstd_all", [128, 80], F32))
    negm_all = P(nc.sbuf_tensor("negm_all", [128, 80], F32))
    bqc_sb = P(nc.sbuf_tensor("bqc_sb", [128, ECH], F32))
    bkc_sb = P(nc.sbuf_tensor("bkc_sb", [128, ECH], F32))
    bvb_sb = P(nc.sbuf_tensor("bvb_sb", [128, D], F32))
    ones1_bf = P(nc.sbuf_tensor("ones1_bf", [128, 1], BF16))
    rstdf_sb = P(nc.sbuf_tensor("rstdf_sb", [128, 16], F32))
    negmf_sb = P(nc.sbuf_tensor("negmf_sb", [128, 16], F32))
    rinv_sb = P(nc.sbuf_tensor("rinv_sb", [128, 16], F32))
    brow = P(nc.sbuf_tensor("brow", [1, 3 * D], F32))
    onef = P(nc.sbuf_tensor("onef", [1, 1], F32))
    onerow = P(nc.sbuf_tensor("onerow", [1, 128], F32))
    ident_sb = P(nc.sbuf_tensor("ident_sb", [128, 128], BF16))

    wkT_bf = B(nc.sbuf_tensor("wkT_bf", [128, ECH * D], BF16))
    wvT_bf = B(nc.sbuf_tensor("wvT_bf", [128, ECH * D], BF16))
    q_sb = B(nc.sbuf_tensor("q_sb", [128, ECH * TOWN], BF16))
    o_sb = B(nc.sbuf_tensor("o_sb", [128, 16 * D], F32))
    l_sb = B(nc.sbuf_tensor("l_sb", [128, 16], F32))
    xb = B(nc.sbuf_tensor("xb", [128, D], F32))
    hb = B(nc.sbuf_tensor("hb", [128, 2 * D], BF16))
    hf = B(nc.sbuf_tensor("hf", [128, ECH * 512], BF16))
    k_sb = B(nc.sbuf_tensor("k_sb", [128, ECH * 512], BF16))
    v_sb = B(nc.sbuf_tensor("v_sb", [128, 4 * D], BF16))
    pt_sb = B(nc.sbuf_tensor("pt_sb", [128, 4 * TOWN], BF16))
    ramp_sb = B(nc.sbuf_tensor("ramp_sb", [128, TOWN], F32))
    ev_scr = B(nc.sbuf_tensor("ev_scr", [128, 512], F32))
    es_scr = B(nc.sbuf_tensor("es_scr", [128, 512], F32))

    wstage = o_sb[:, 0:ECH * D]
    wq_bf = pt_sb

    ps_scope = ExitStack()
    pp = ps_scope.enter_context
    ps_pvA = pp(nc.psum_tensor("ps_pvA", [128, D], F32))
    ps_pvB = pp(nc.psum_tensor("ps_pvB", [128, D], F32))
    ps_s0 = pp(nc.psum_tensor("ps_s0", [128, 512], F32))
    ps_s1 = pp(nc.psum_tensor("ps_s1", [128, 512], F32))
    ps_l = pp(nc.psum_tensor("ps_l", [128, 512], F32))
    ps_sq = pp(nc.psum_tensor("ps_sq", [128, 512], F32))

    def _load_act(idx):
        def f(E):
            E.add_instruction(mybir.InstLoadActFuncSet(
                name=nc.get_next_instruction_name(),
                act_func_set_id=idx, ins=[], outs=[]))
        return f

    dma("sync", ramp_sb[:], ramp_in, ld)
    dma("sync", tthr_sb[:], tthr_in, ld)
    dma("sync", ident_sb[:], ident_in, ld)
    dma("sync", brow[0:1, 0:D], bq_in[None, :], ld)
    dma("sync", brow[0:1, D:2 * D], bk_in[None, :], ld)
    n_brow = dma("sync", brow[0:1, 2 * D:3 * D], bv_in[None, :], ld)

    em("vector", lambda E: E.memset(onef[:], 1.0))
    em("vector", lambda E: E.memset(onerow[:], 1.0))
    em("vector", lambda E: E.memset(ones1_bf[:], 1.0).then_inc(dv.h, 1))
    dv_ones = dv.inc()

    dv_wcast = 0
    for name, win, dst in (("q", wqT_in, None), ("k", wkT_in, wkT_bf), ("v", wvT_in, wvT_bf)):
        n_w = dma("sync", wstage.rearrange("p (c e) -> p c e", e=D),
                  win.rearrange("(c p) e -> p c e", p=128), ld)
        wait("vector", ld, n_w)
        if name == "q":
            em("vector", lambda E: E.tensor_copy(out=wq_bf[:, 0:ECH * D], in_=wstage)
               .then_inc(dv.h, 1))
        else:
            em("vector", lambda E, d=dst: E.tensor_copy(out=d[:], in_=wstage)
               .then_inc(dv.h, 1))
        dv_wcast = dv.inc()
        wait("sync", dv, dv_wcast)
    em("vector", lambda E: E.memset(o_sb[:], 0.0))
    em("vector", lambda E: E.memset(l_sb[:], 0.0).then_inc(dv.h, 1))
    dv_zero = dv.inc()

    wait("tensor", ld, n_brow)
    wait("tensor", dv, dv_ones)
    for bi in range(2):
        for ecj in range(ECH):
            em("tensor", lambda E, b=bi, c=ecj: E.matmul(
                out=ps_sq[:, b * 8 + c:b * 8 + c + 1],
                lhsT=brow[0:1, b * D + c * 128:b * D + (c + 1) * 128],
                rhs=onef[0:1, 0:1], start=True, stop=True).then_inc(pe.h, 1))
            n = pe.inc()
            wait("tensor", pe, n)
    pe_bias = pe.n
    wait("scalar", pe, pe_bias)
    em("scalar", lambda E: E.activation(out=bqc_sb[:], in_=ps_sq[:, 0:8],
                                        func=AF.Identity, bias=0.0, scale=np.float32(1.0 / 32.0)))
    em("scalar", lambda E: E.activation(out=bkc_sb[:], in_=ps_sq[:, 8:16],
                                        func=AF.Identity, bias=0.0, scale=1.0)
       .then_inc(ac.h, 1))
    ac_bias = ac.inc()
    for hh in range(2):
        wait("tensor", ac, ac_bias)
        em("tensor", lambda E, h=hh: E.matmul(
            out=ps_sq[:, 0:512],
            lhsT=onerow[0:1, :],
            rhs=brow[0:1, 2 * D + h * 512:2 * D + (h + 1) * 512],
            start=True, stop=True).then_inc(pe.h, 1))
        pe_bv = pe.inc()
        wait("scalar", pe, pe_bv)
        em("scalar", lambda E, h=hh: E.activation(
            out=bvb_sb[:, h * 512:(h + 1) * 512], in_=ps_sq[:, 0:512],
            func=AF.Identity, bias=0.0, scale=1.0).then_inc(ac.h, 1))
        ac_bias = ac.inc()
    ac_bvb = ac_bias

    def ln_tiles(x_ap_list, h_out_list, wait_loads, slot0=0):
        nt = len(x_ap_list)
        wait("scalar", ld, wait_loads)
        wait("scalar", ac, ac_bvb)
        em("scalar", lambda E: E.memzero(stat_sb[:, 8 + slot0:8 + slot0 + nt]))
        em("scalar", lambda E: E.memzero(stat_sb[:, 16 + slot0:16 + slot0 + nt]))
        for i, x in enumerate(x_ap_list):
            s = slot0 + i
            em("scalar", lambda E, x=x, s=s: E.activation(
                out=ps_sq[:], in_=x[:, 0:512], func=AF.Square, bias=0.0, scale=1.0,
                accum_out=stat_sb[:, 8 + s:9 + s]))
            em("scalar", lambda E, x=x, s=s: E.activation(
                out=ps_sq[:], in_=x[:, 512:1024], func=AF.Square, bias=0.0, scale=1.0,
                accum_out=stat_sb[:, 16 + s:17 + s]).then_inc(ac.h, 1))
            ac.inc()
        ac_sq = ac.n
        if limit == "ln0":
            finish_debug([stat_sb[:]])
            return "STOP"
        wait("vector", ld, wait_loads)
        for i, x in enumerate(x_ap_list):
            s = slot0 + i
            last = (i == len(x_ap_list) - 1)
            if last:
                em("vector", lambda E, x=x, s=s: E.reduce_sum(
                    out=stat_sb[:, s:s + 1], in_=x, axis=AX.X).then_inc(dv.h, 1))
                wait("vector", dv, dv.inc())
            else:
                em("vector", lambda E, x=x, s=s: E.reduce_sum(
                    out=stat_sb[:, s:s + 1], in_=x, axis=AX.X))
        wait("vector", ac, ac_sq)
        sl = slice(slot0, slot0 + nt)
        s2a = slice(8 + slot0, 8 + slot0 + nt)
        s2b = slice(16 + slot0, 16 + slot0 + nt)
        mu = slice(24 + slot0, 24 + slot0 + nt)
        var = slice(32 + slot0, 32 + slot0 + nt)
        rstd = slice(40 + slot0, 40 + slot0 + nt)
        ng = slice(48 + slot0, 48 + slot0 + nt)

        def _step(fn):
            em("vector", lambda E, f=fn: f(E).then_inc(dv.h, 1))
            n = dv.inc()
            wait("vector", dv, n)
            return n
        _step(lambda E: E.tensor_scalar(
            out=stat_sb[:, mu], in0=stat_sb[:, sl], scalar1=1.0 / D, scalar2=None,
            op0=ALU.mult))
        _step(lambda E: E.tensor_tensor(
            out=stat_sb[:, var], in0=stat_sb[:, s2a], in1=stat_sb[:, s2b], op=ALU.add))
        _step(lambda E: E.tensor_scalar(
            out=stat_sb[:, var], in0=stat_sb[:, var], scalar1=1.0 / D, scalar2=None,
            op0=ALU.mult))
        _step(lambda E: E.tensor_tensor(
            out=stat_sb[:, s2a], in0=stat_sb[:, mu], in1=stat_sb[:, mu], op=ALU.mult))
        dv_var = _step(lambda E: E.tensor_tensor(
            out=stat_sb[:, var], in0=stat_sb[:, var], in1=stat_sb[:, s2a],
            op=ALU.subtract))
        wait("scalar", dv, dv_var)
        em("scalar", lambda E: E.activation(
            out=stat_sb[:, s2b], in_=stat_sb[:, var], func=AF.Ln, bias=np.float32(1e-5), scale=1.0))
        em("scalar", lambda E: E.activation(
            out=stat_sb[:, rstd], in_=stat_sb[:, s2b], func=AF.Exp, bias=0.0,
            scale=np.float32(-0.5)).then_inc(ac.h, 1))
        ac_rstd = ac.inc()
        wait("vector", ac, ac_rstd)
        _step(lambda E: E.tensor_tensor(
            out=stat_sb[:, ng], in0=stat_sb[:, mu], in1=stat_sb[:, rstd], op=ALU.mult))
        dv_ng = _step(lambda E: E.tensor_scalar(
            out=stat_sb[:, ng], in0=stat_sb[:, ng], scalar1=-1.0, scalar2=None,
            op0=ALU.mult))
        wait("scalar", dv, dv_ng)
        for i, (x, h) in enumerate(zip(x_ap_list, h_out_list)):
            s = slot0 + i
            em("scalar", lambda E, x=x, h=h, s=s: E.activation(
                out=h, in_=x, func=AF.Identity,
                bias=stat_sb[:, 48 + s:49 + s], scale=stat_sb[:, 40 + s:41 + s])
               .then_inc(ac.h, 1))
            ac.inc()
        return ac.n

    dbg_out = nc.dram_tensor("dbg", [128, 16 * D], F32, kind="ExternalOutput").ap() if limit else None

    def finish_debug(ap_list):
        col = 0
        wait("sync", ac, ac.n)
        wait("sync", dv, dv.n)
        wait("sync", pe, pe.n)
        nd = 0
        for ap_ in ap_list:
            ncol = ap_.shape[-1]
            nd = dma("sync", dbg_out[:, col:col + ncol], ap_, so)
            col += ncol
        wait("sync", so, nd)

    if limit == "init":
        finish_debug([bvb_sb[:], bqc_sb[:], bkc_sb[:]])
        return finalize()
    hbf0 = hb[:].bitcast(F32)
    evs0 = ev_scr[:, 0:512]
    evs1 = es_scr[:, 0:512]
    dv_red = {0: 0, 1: 0}
    ac_sq2 = {0: 0, 1: 0}
    dv_chain_prev = 0
    for b8 in range(8):
        for t8 in range(8):
            t = b8 * 8 + t8
            sl = t % 2
            xbuf = xb[:] if sl == 0 else hbf0
            wait("sync", dv, dv_red[sl])
            wait("sync", ac, ac_sq2[sl])
            n_x = dma("sync", xbuf if False else (xb[:] if sl == 0 else hbf0),
                      xkv_in[t * 128:(t + 1) * 128, :], ld)
            xbuf = xb[:] if sl == 0 else hbf0
            wait("vector", ld, n_x)
            em("vector", lambda E, x=xbuf, t=t: E.reduce_sum(
                out=stat_sb[:, t:t + 1], in_=x, axis=AX.X).then_inc(dv.h, 1))
            dv_red[sl] = dv.inc()
            wait("scalar", ld, n_x)
            em("scalar", lambda E, x=xbuf, t=t: E.memzero(stat_sb[:, 64 + t:65 + t]))
            em("scalar", lambda E, x=xbuf, t=t: E.memzero(stat_sb[:, 128 + t:129 + t]))
            em("scalar", lambda E, x=xbuf, t=t: E.activation(
                out=ps_sq[:], in_=x[:, 0:512], func=AF.Square, bias=0.0, scale=1.0,
                accum_out=stat_sb[:, 64 + t:65 + t]))
            em("scalar", lambda E, x=xbuf, t=t: E.activation(
                out=ps_sq[:], in_=x[:, 512:1024], func=AF.Square, bias=0.0, scale=1.0,
                accum_out=stat_sb[:, 128 + t:129 + t]).then_inc(ac.h, 1))
            ac_sq2[sl] = ac.inc()
        c0 = b8 * 8
        wait("vector", ac, ac_sq2[0])
        wait("vector", ac, ac_sq2[1])
        wait("vector", dv, max(dv_red[0], dv_red[1]))

        def _stp(fn):
            em("vector", lambda E, f=fn: f(E).then_inc(dv.h, 1))
            n = dv.inc()
            wait("vector", dv, n)
            return n
        _stp(lambda E, c0=c0: E.tensor_scalar(
            out=stat_sb[:, 192 + c0:200 + c0], in0=stat_sb[:, c0:c0 + 8],
            scalar1=1.0 / D, scalar2=None, op0=ALU.mult))
        _stp(lambda E, c0=c0: E.tensor_tensor(
            out=stat_sb[:, 64 + c0:72 + c0], in0=stat_sb[:, 64 + c0:72 + c0],
            in1=stat_sb[:, 128 + c0:136 + c0], op=ALU.add))
        _stp(lambda E, c0=c0: E.tensor_scalar(
            out=stat_sb[:, 64 + c0:72 + c0], in0=stat_sb[:, 64 + c0:72 + c0],
            scalar1=1.0 / D, scalar2=None, op0=ALU.mult))
        _stp(lambda E, c0=c0: E.tensor_tensor(
            out=stat_sb[:, 128 + c0:136 + c0], in0=stat_sb[:, 192 + c0:200 + c0],
            in1=stat_sb[:, 192 + c0:200 + c0], op=ALU.mult))
        dvv_ = _stp(lambda E, c0=c0: E.tensor_tensor(
            out=stat_sb[:, 64 + c0:72 + c0], in0=stat_sb[:, 64 + c0:72 + c0],
            in1=stat_sb[:, 128 + c0:136 + c0], op=ALU.subtract))
        wait("scalar", dv, dvv_)
        em("scalar", lambda E, c0=c0: E.activation(
            out=stat_sb[:, 128 + c0:136 + c0], in_=stat_sb[:, 64 + c0:72 + c0],
            func=AF.Ln, bias=np.float32(1e-5), scale=1.0))
        em("scalar", lambda E, c0=c0: E.activation(
            out=rstd_all[:, c0:c0 + 8], in_=stat_sb[:, 128 + c0:136 + c0],
            func=AF.Exp, bias=0.0, scale=np.float32(-0.5)).then_inc(ac.h, 1))
        acr_ = ac.inc()
        wait("vector", ac, acr_)
        _stp(lambda E, c0=c0: E.tensor_tensor(
            out=negm_all[:, c0:c0 + 8], in0=stat_sb[:, 192 + c0:200 + c0],
            in1=rstd_all[:, c0:c0 + 8], op=ALU.mult))
        dv_chain_prev = _stp(lambda E, c0=c0: E.tensor_scalar(
            out=negm_all[:, c0:c0 + 8], in0=negm_all[:, c0:c0 + 8],
            scalar1=-1.0, scalar2=None, op0=ALU.mult))
    dv_stats_done = dv.n
    ac_norm = {0: 0, 1: 0}
    sh_hstore_prev = 0
    sh_hstore_pp = 0
    wait("scalar", dv, dv_stats_done)
    for t in range(64):
        sl = t % 2
        wait("sync", ac, ac_norm[sl])
        if sl == 0:
            n_x = dma("sync", xb[:], xkv_in[t * 128:(t + 1) * 128, :], ld)
        else:
            n_x = dma("sync", evs0, xkv_in[t * 128:(t + 1) * 128, 0:512], ld)
            n_x = dma("sync", evs1, xkv_in[t * 128:(t + 1) * 128, 512:1024], ld)
        hsl = t % 2
        wait("scalar", ld, n_x)
        wait("scalar", sh, sh_hstore_pp)
        if sl == 0:
            em("scalar", lambda E, t=t, h=hsl: E.activation(
                out=hb[:, h * D:h * D + D], in_=xb[:], func=AF.Identity,
                bias=negm_all[:, t:t + 1], scale=rstd_all[:, t:t + 1])
               .then_inc(ac.h, 1))
        else:
            em("scalar", lambda E, t=t, h=hsl: E.activation(
                out=hb[:, h * D:h * D + 512], in_=evs0, func=AF.Identity,
                bias=negm_all[:, t:t + 1], scale=rstd_all[:, t:t + 1]))
            em("scalar", lambda E, t=t, h=hsl: E.activation(
                out=hb[:, h * D + 512:h * D + D], in_=evs1, func=AF.Identity,
                bias=negm_all[:, t:t + 1], scale=rstd_all[:, t:t + 1])
               .then_inc(ac.h, 1))
        ac_norm[sl] = ac.inc()
        wait("scalar", ac, ac_norm[sl])
        sh_hstore_pp = sh_hstore_prev
        sh_hstore_prev = dma("scalar", h_scr[t * 128:(t + 1) * 128, :],
                             hb[:, hsl * D:(hsl + 1) * D], sh)
    sh_hdone = sh_hstore_prev

    for b8 in range(2):
        for t8 in range(8):
            t = b8 * 8 + t8
            sl = t % 2
            xbuf = xb[:] if sl == 0 else hbf0
            wait("sync", dv, dv_red[sl])
            wait("sync", ac, ac_sq2[sl])
            wait("sync", sh, sh_hdone if t == 0 else 0)
            n_x = dma("sync", xb[:] if sl == 0 else hbf0,
                      xq_in[t * 128:(t + 1) * 128, :], ld)
            wait("vector", ld, n_x)
            em("vector", lambda E, x=xbuf, t=t: E.reduce_sum(
                out=stat_sb[:, t:t + 1], in_=x, axis=AX.X).then_inc(dv.h, 1))
            dv_red[sl] = dv.inc()
            wait("scalar", ld, n_x)
            em("scalar", lambda E, t=t: E.memzero(stat_sb[:, 64 + t:65 + t]))
            em("scalar", lambda E, t=t: E.memzero(stat_sb[:, 128 + t:129 + t]))
            em("scalar", lambda E, x=xbuf, t=t: E.activation(
                out=ps_sq[:], in_=x[:, 0:512], func=AF.Square, bias=0.0, scale=1.0,
                accum_out=stat_sb[:, 64 + t:65 + t]))
            em("scalar", lambda E, x=xbuf, t=t: E.activation(
                out=ps_sq[:], in_=x[:, 512:1024], func=AF.Square, bias=0.0, scale=1.0,
                accum_out=stat_sb[:, 128 + t:129 + t]).then_inc(ac.h, 1))
            ac_sq2[sl] = ac.inc()
        c0 = b8 * 8
        q0 = 64 + b8 * 8
        wait("vector", ac, ac_sq2[0])
        wait("vector", ac, ac_sq2[1])
        wait("vector", dv, max(dv_red[0], dv_red[1]))

        def _stq(fn):
            em("vector", lambda E, f=fn: f(E).then_inc(dv.h, 1))
            n = dv.inc()
            wait("vector", dv, n)
            return n
        _stq(lambda E, c0=c0: E.tensor_scalar(
            out=stat_sb[:, 192 + c0:200 + c0], in0=stat_sb[:, c0:c0 + 8],
            scalar1=1.0 / D, scalar2=None, op0=ALU.mult))
        _stq(lambda E, c0=c0: E.tensor_tensor(
            out=stat_sb[:, 64 + c0:72 + c0], in0=stat_sb[:, 64 + c0:72 + c0],
            in1=stat_sb[:, 128 + c0:136 + c0], op=ALU.add))
        _stq(lambda E, c0=c0: E.tensor_scalar(
            out=stat_sb[:, 64 + c0:72 + c0], in0=stat_sb[:, 64 + c0:72 + c0],
            scalar1=1.0 / D, scalar2=None, op0=ALU.mult))
        _stq(lambda E, c0=c0: E.tensor_tensor(
            out=stat_sb[:, 128 + c0:136 + c0], in0=stat_sb[:, 192 + c0:200 + c0],
            in1=stat_sb[:, 192 + c0:200 + c0], op=ALU.mult))
        dvv_ = _stq(lambda E, c0=c0: E.tensor_tensor(
            out=stat_sb[:, 64 + c0:72 + c0], in0=stat_sb[:, 64 + c0:72 + c0],
            in1=stat_sb[:, 128 + c0:136 + c0], op=ALU.subtract))
        wait("scalar", dv, dvv_)
        em("scalar", lambda E, c0=c0: E.activation(
            out=stat_sb[:, 128 + c0:136 + c0], in_=stat_sb[:, 64 + c0:72 + c0],
            func=AF.Ln, bias=np.float32(1e-5), scale=1.0))
        em("scalar", lambda E, c0=c0, q0=q0: E.activation(
            out=rstd_all[:, q0:q0 + 8], in_=stat_sb[:, 128 + c0:136 + c0],
            func=AF.Exp, bias=0.0, scale=np.float32(-0.5)).then_inc(ac.h, 1))
        acr_ = ac.inc()
        wait("vector", ac, acr_)
        _stq(lambda E, c0=c0, q0=q0: E.tensor_tensor(
            out=negm_all[:, q0:q0 + 8], in0=stat_sb[:, 192 + c0:200 + c0],
            in1=rstd_all[:, q0:q0 + 8], op=ALU.mult))
        _stq(lambda E, q0=q0: E.tensor_scalar(
            out=negm_all[:, q0:q0 + 8], in0=negm_all[:, q0:q0 + 8],
            scalar1=-1.0, scalar2=None, op0=ALU.mult))
    dv_statsq_done = dv.n
    ac_norm = {0: 0, 1: 0}
    wait("scalar", dv, dv_statsq_done)
    for t in range(16):
        sl = t % 2
        wait("sync", ac, ac_norm[sl])
        if sl == 0:
            n_x = dma("sync", xb[:], xq_in[t * 128:(t + 1) * 128, :], ld)
        else:
            n_x = dma("sync", evs0, xq_in[t * 128:(t + 1) * 128, 0:512], ld)
            n_x = dma("sync", evs1, xq_in[t * 128:(t + 1) * 128, 512:1024], ld)
        hsl = t % 2
        wait("scalar", ld, n_x)
        wait("scalar", sh, sh_hstore_pp)
        if sl == 0:
            em("scalar", lambda E, t=t, h=hsl: E.activation(
                out=hb[:, h * D:h * D + D], in_=xb[:], func=AF.Identity,
                bias=negm_all[:, 64 + t:65 + t], scale=rstd_all[:, 64 + t:65 + t])
               .then_inc(ac.h, 1))
        else:
            em("scalar", lambda E, t=t, h=hsl: E.activation(
                out=hb[:, h * D:h * D + 512], in_=evs0, func=AF.Identity,
                bias=negm_all[:, 64 + t:65 + t], scale=rstd_all[:, 64 + t:65 + t]))
            em("scalar", lambda E, t=t, h=hsl: E.activation(
                out=hb[:, h * D + 512:h * D + D], in_=evs1, func=AF.Identity,
                bias=negm_all[:, 64 + t:65 + t], scale=rstd_all[:, 64 + t:65 + t])
               .then_inc(ac.h, 1))
        ac_norm[sl] = ac.inc()
        wait("scalar", ac, ac_norm[sl])
        sh_hstore_pp = sh_hstore_prev
        sh_hstore_prev = dma("scalar", hq_scr[t * 128:(t + 1) * 128, :],
                             hb[:, hsl * D:(hsl + 1) * D], sh)
    sh_hdone_q = sh_hstore_prev

    pe_hf_prev = 0
    ac_evacA = {0: 0, 1: 0}
    dv_evA = {0: 0, 1: 0}
    wait("sync", sh, sh_hdone_q)
    for g in range(1 if limit in ("A1", "A0") else 4):
        wait("sync", pe, pe_hf_prev)
        n_hf = 0
        for dc in range(ECH):
            n_hf = dma("sync", hf[:, dc * 512:(dc + 1) * 512],
                       hq_scr[g * 512:(g + 1) * 512, dc * 128:(dc + 1) * 128],
                       ld, transpose=True)
        if limit == "hf" and g == 0:
            wait("sync", ld, n_hf)
            finish_debug([hf[:].bitcast(F32)])
            return finalize()
        wait("tensor", ld, n_hf)
        wait("tensor", dv, dv_wcast)
        for ec in range(1 if limit == "A0" else ECH):
            hsel = ec % 2
            wait("tensor", ac, ac_evacA[hsel])
            for dc in range(ECH):
                em("tensor", lambda E, ec=ec, dc=dc, h=hsel: E.matmul(
                    out=ps_pvA[:, h * 512:(h + 1) * 512],
                    lhsT=wq_bf[:, dc * D + ec * 128:dc * D + ec * 128 + 128],
                    rhs=hf[:, dc * 512:(dc + 1) * 512],
                    start=(dc == 0), stop=(dc == ECH - 1)).then_inc(pe.h, 1))
                pe.inc()
            pe_q = pe.n
            wait("scalar", pe, pe_q)
            wait("scalar", gp, max(dv_evA[0], dv_evA[1]))
            em("scalar", lambda E, ec=ec, h=hsel: E.activation(
                out=ev_scr[:, 0:512],
                in_=ps_pvA[:, h * 512:(h + 1) * 512], func=AF.Identity,
                bias=bqc_sb[:, ec:ec + 1], scale=np.float32(1.0 / 32.0)).then_inc(ac.h, 1))
            ac_evacA[hsel] = ac.inc()
            wait("vector", ac, ac_evacA[hsel])
            em("vector", lambda E, ec=ec, g=g: E.tensor_copy(
                out=q_sb[:, ec * TOWN + g * 512:ec * TOWN + (g + 1) * 512],
                in_=ev_scr[:, 0:512]).then_inc(gp.h, 1))
            dv_evA[hsel] = gp.inc()
        pe_hf_prev = pe.n

    if limit in ("A", "A1", "A0"):
        finish_debug([q_sb[:].bitcast(F32)])
        return finalize()
    wsrc = [xb[:, 0:256], xb[:, 256:512]]
    wxn = xb[:, 512:768]
    wtt = xb[:, 768:1024]
    wacc = hb[:, 0:512].bitcast(F32)
    wi1 = hb[:, 512:1024].bitcast(I32)
    wsg = hb[:, 1024:1280]
    wdst = [hb[:, 1280:1536], hb[:, 1536:1792]]
    wsmall = hb[:, 1792:2048].bitcast(F32)
    wbm = wsmall[:, 0:16]
    wsc = wsmall[:, 16:32]
    wrx = wsmall[:, 32:48]
    wchunks = []
    for (w_in_t, w_scr, nrow) in ((wg_in, wgq_scr, II), (wu_in, wuq_scr, II),
                                  (wd_in, wdq_scr, D)):
        ncol = D if nrow == II else II
        for t in range(nrow // 128):
            for coff in range(0, ncol, 256):
                wchunks.append((w_in_t, w_scr, t, coff))
    wpos = {"i": 0}
    wslot = {0: 0, 1: 0}
    wsh = {0: 0, 1: 0}
    wst_state = {"sh_last": 0}

    def emit_wchunks(n):
        for _ in range(n):
            i = wpos["i"]
            if i >= len(wchunks):
                return
            wpos["i"] += 1
            w_in_t, w_scr, t, coff = wchunks[i]
            slot = i % 2
            wait("sync", dv, wslot[slot])
            n_w = dma("sync", wsrc[slot], w_in_t[t * 128:(t + 1) * 128, coff:coff + 256], ld)
            wait("scalar", dv, max(wslot[0], wslot[1]))
            wait("scalar", ld, n_w)
            em("scalar", lambda E, s=slot: E.activation(
                out=wtt[:], in_=wsrc[s], func=AF.Abs, bias=0.0, scale=1.0))
            em("scalar", lambda E, s=slot: E.activation(
                out=wsg[:], in_=wsrc[s], func=AF.Sign, bias=0.0, scale=1.0)
               .then_inc(ac.h, 1))
            acsg = ac.inc()
            wait("vector", ac, acsg)
            wait("vector", sh, wsh[slot])

            def _w(fn):
                em("vector", lambda E, f=fn: f(E).then_inc(dv.h, 1))
                n = dv.inc()
                wait("vector", dv, n)
                return n
            g16 = wtt.rearrange("p (g k) -> p g k", k=16)
            _w(lambda E: E.reduce_max(out=wbm, in_=g16, axis=AX.X))
            _w(lambda E: E.reciprocal(out=wrx, in_=wbm))
            rxb = wrx.unsqueeze(2).broadcast_to([128, 16, 16])
            scb = wsc.unsqueeze(2).broadcast_to([128, 16, 16])
            em("vector", lambda E: E.tensor_tensor(
                out=wxn.rearrange("p (g k) -> p g k", k=16), in0=g16, in1=rxb,
                op=ALU.mult))
            em("vector", lambda E: E.tensor_scalar(
                out=wi1[:], in0=wxn, scalar1=12.0, scalar2=None, op0=ALU.mult))
            em("vector", lambda E: E.tensor_scalar(
                out=wacc[:], in0=wi1[:], scalar1=4.0, scalar2=0.5,
                op0=ALU.min, op1=ALU.mult))
            em("vector", lambda E: E.tensor_scalar(
                out=wi1[:], in0=wxn, scalar1=6.0, scalar2=None, op0=ALU.mult))
            em("vector", lambda E: E.tensor_scalar(
                out=wtt[:], in0=wi1[:], scalar1=2.0, scalar2=2.0,
                op0=ALU.subtract, op1=ALU.min))
            em("vector", lambda E: E.scalar_tensor_tensor(
                out=wacc[:], in0=wtt[:], scalar=0.0, in1=wacc[:],
                op0=ALU.max, op1=ALU.add))
            em("vector", lambda E: E.tensor_scalar(
                out=wi1[:], in0=wxn, scalar1=3.0, scalar2=None, op0=ALU.mult))
            em("vector", lambda E: E.tensor_scalar(
                out=wsc, in0=wbm, scalar1=1.0 / 6.0, scalar2=1e-12,
                op0=ALU.mult, op1=ALU.max))
            em("vector", lambda E: E.tensor_scalar(
                out=wtt[:], in0=wi1[:], scalar1=2.0, scalar2=4.0,
                op0=ALU.mult, op1=ALU.subtract))
            em("vector", lambda E: E.scalar_tensor_tensor(
                out=wacc[:], in0=wtt[:], scalar=0.0, in1=wacc[:],
                op0=ALU.max, op1=ALU.add))
            em("vector", lambda E: E.tensor_tensor(
                out=wtt.rearrange("p (g k) -> p g k", k=16),
                in0=wacc[:].rearrange("p (g k) -> p g k", k=16), in1=scb,
                op=ALU.mult))
            em("vector", lambda E, s=slot: E.tensor_tensor(
                out=wdst[s], in0=wtt[:], in1=wsg[:], op=ALU.mult).then_inc(dv.h, 1))
            wslot[slot] = dv.inc()
            wait("scalar", dv, wslot[slot])
            wsh[slot] = dma("scalar", w_scr[t * 128:(t + 1) * 128, coff:coff + 256],
                            wdst[slot], sh)
            wst_state["sh_last"] = wsh[slot]

    ac_exp_last = {0: 0, 1: 0}
    dv_mask_b_last = {0: 0, 1: 0}
    ac_kevac = {0: 0, 1: 0}
    dv_vevac = {0: 0, 1: 0}
    dv_oacc = {}
    dv_lacc_prev = 0
    pe_pv_all_prev = 0
    nsu_run = limit if isinstance(limit, int) else NSU
    for su in range(nsu_run):
        kt, half = su // 2, su % 2
        key0 = kt * 1024 + half * 512
        wait("sync", pe, pe_hf_prev)
        n_hf = 0
        for dc in range(ECH):
            n_hf = dma("sync", hf[:, dc * 512:(dc + 1) * 512],
                       h_scr[key0:key0 + 512, dc * 128:(dc + 1) * 128], ld, transpose=True)
        wait("tensor", ld, n_hf)
        wait("tensor", dv, dv_oacc.get(14, 0))
        for ec in range(ECH):
            hsel = ec % 2
            wait("tensor", ac, ac_kevac[hsel])
            wait("tensor", dv, dv_vevac[hsel])
            for dc in range(ECH):
                em("tensor", lambda E, ec=ec, dc=dc, h=hsel: E.matmul(
                    out=ps_pvA[:, h * 512:(h + 1) * 512],
                    lhsT=wkT_bf[:, dc * D + ec * 128:dc * D + ec * 128 + 128],
                    rhs=hf[:, dc * 512:(dc + 1) * 512],
                    start=(dc == 0), stop=(dc == ECH - 1)).then_inc(pe.h, 1))
                pe.inc()
            pe_k = pe.n
            wait("scalar", pe, pe_k)
            wait("scalar", gp, max(dv_vevac[0], dv_vevac[1]))
            em("scalar", lambda E, ec=ec, h=hsel: E.activation(
                out=ev_scr[:, 0:512],
                in_=ps_pvA[:, h * 512:(h + 1) * 512], func=AF.Identity,
                bias=bkc_sb[:, ec:ec + 1], scale=1.0).then_inc(ac.h, 1))
            ac_kevac[hsel] = ac.inc()
            wait("vector", ac, ac_kevac[hsel])
            em("vector", lambda E, ec=ec: E.tensor_copy(
                out=k_sb[:, ec * 512:(ec + 1) * 512],
                in_=ev_scr[:, 0:512]).then_inc(gp.h, 1))
            dv_vevac[hsel] = gp.inc()
        for kc in range(4):
            for vh in range(2):
                hsel = (kc * 2 + vh) % 2
                wait("tensor", ac, ac_kevac[hsel])
                wait("tensor", dv, dv_vevac[hsel])
                for dc in range(ECH):
                    em("tensor", lambda E, kc=kc, vh=vh, dc=dc, h=hsel: E.matmul(
                        out=ps_pvA[:, h * 512:(h + 1) * 512],
                        lhsT=hf[:, dc * 512 + kc * 128:dc * 512 + kc * 128 + 128],
                        rhs=wvT_bf[:, dc * D + vh * 512:dc * D + (vh + 1) * 512],
                        start=(dc == 0), stop=(dc == ECH - 1)).then_inc(pe.h, 1))
                    pe.inc()
                pe_v = pe.n
                wait("scalar", pe, pe_v)
                wait("scalar", gp, max(dv_vevac[0], dv_vevac[1]))
                em("scalar", lambda E, h=hsel: E.activation(
                    out=ev_scr[:, 0:512],
                    in_=ps_pvA[:, h * 512:(h + 1) * 512], func=AF.Identity,
                    bias=0.0, scale=1.0).then_inc(ac.h, 1))
                ac_kevac[hsel] = ac.inc()
                wait("vector", ac, ac_kevac[hsel])
                em("vector", lambda E, kc=kc, vh=vh: E.tensor_tensor(
                    out=v_sb[:, kc * D + vh * 512:kc * D + (vh + 1) * 512],
                    in0=ev_scr[:, 0:512],
                    in1=bvb_sb[:, vh * 512:(vh + 1) * 512], op=ALU.add).then_inc(gp.h, 1))
                dv_vevac[hsel] = gp.inc()
        pe_hf_prev = pe.n

        gp_kv_done = gp.n
        wait("tensor", gp, gp_kv_done)
        ac_exp = dict(ac_exp_last)
        pair = 0
        dv_mask_b = dict(dv_mask_b_last)
        for kc in range(4):
            for qh in range(4):
                bank = pair % 2
                ps_s = ps_s0 if bank == 0 else ps_s1
                wait("tensor", ac, ac_exp[bank])
                for ec in range(ECH):
                    em("tensor", lambda E, kc=kc, qh=qh, ec=ec, ps=ps_s: E.matmul(
                        out=ps[:, 0:512],
                        lhsT=k_sb[:, ec * 512 + kc * 128:ec * 512 + kc * 128 + 128],
                        rhs=q_sb[:, ec * TOWN + qh * 512:ec * TOWN + (qh + 1) * 512],
                        start=(ec == 0), stop=(ec == ECH - 1)).then_inc(pe.h, 1))
                    pe.inc()
                pe_s = pe.n
                wait("scalar", pe, pe_s)
                wait("scalar", gp, max(dv_mask_b[0], dv_mask_b[1]))
                em("scalar", lambda E, b=bank, ps=ps_s: E.activation(
                    out=es_scr[:, 0:512],
                    in_=ps[:, 0:512], func=AF.Exp, bias=np.float32(NEG),
                    scale=1.0).then_inc(ac.h, 1))
                ac_exp[bank] = ac.inc()
                wait("vector", ac, ac_exp[bank])
                wait("vector", pe, pe_pv_all_prev)
                em("vector", lambda E, su=su, kc=kc, qh=qh: E.scalar_tensor_tensor(
                    out=pt_sb[:, kc * TOWN + qh * 512:kc * TOWN + (qh + 1) * 512],
                    in0=ramp_sb[:, qh * 512:(qh + 1) * 512],
                    scalar=tthr_sb[:, su * 4 + kc:su * 4 + kc + 1],
                    in1=es_scr[:, 0:512],
                    op0=ALU.is_ge, op1=ALU.mult).then_inc(gp.h, 1))
                dv_mask_b[bank] = gp.inc()
                pair += 1
        ac_exp_last = ac_exp
        dv_mask_b_last = dict(dv_mask_b)
        gp_mask_all = gp.n

        dv_oacc_prev = dict(dv_oacc)
        wait("tensor", gp, gp_mask_all)
        wait("tensor", dv, dv_lacc_prev)
        lbase = (su % 2) * 16
        pe_pv = {}
        for qs in range(16):
            buf = qs % 2
            ps_o = ps_pvA if buf == 0 else ps_pvB
            if qs >= 2:
                wait("tensor", dv, dv_oacc[qs - 2])
            elif qs == 1:
                wait("tensor", dv, dv_oacc_prev.get(15, 0))
            for kc in range(4):
                em("tensor", lambda E, qs=qs, kc=kc, ps=ps_o: E.matmul(
                    out=ps[:, 0:512],
                    lhsT=pt_sb[:, kc * TOWN + qs * 128:kc * TOWN + qs * 128 + 128],
                    rhs=v_sb[:, kc * D:kc * D + 512],
                    start=(kc == 0), stop=(kc == 3)).then_inc(pe.h, 1))
                pe.inc()
                em("tensor", lambda E, qs=qs, kc=kc, ps=ps_o: E.matmul(
                    out=ps[:, 512:1024],
                    lhsT=pt_sb[:, kc * TOWN + qs * 128:kc * TOWN + qs * 128 + 128],
                    rhs=v_sb[:, kc * D + 512:kc * D + 1024],
                    start=(kc == 0), stop=(kc == 3)).then_inc(pe.h, 1))
                pe.inc()
                em("tensor", lambda E, qs=qs, kc=kc, lc=lbase + qs: E.matmul(
                    out=ps_l[:, lc:lc + 1],
                    lhsT=pt_sb[:, kc * TOWN + qs * 128:kc * TOWN + qs * 128 + 128],
                    rhs=ones1_bf[:, 0:1],
                    start=(kc == 0), stop=(kc == 3)).then_inc(pe.h, 1))
                pe.inc()
            pe_pv[qs] = pe.n
            wait("vector", pe, pe_pv[qs])
            if su == 0 and qs == 0:
                wait("vector", dv, dv_zero)
            em("vector", lambda E, qs=qs, ps=ps_o: E.tensor_tensor(
                out=o_sb[:, qs * D:(qs + 1) * D], in0=o_sb[:, qs * D:(qs + 1) * D],
                in1=ps[:, 0:1024], op=ALU.add).then_inc(dv.h, 1))
            dv_oacc[qs] = dv.inc()
        pe_pv_all_prev = pe.n
        wait("vector", pe, pe_pv[15])
        em("vector", lambda E, lb=lbase: E.tensor_tensor(
            out=l_sb[:, 0:16], in0=l_sb[:, 0:16], in1=ps_l[:, lb:lb + 16],
            op=ALU.add).then_inc(dv.h, 1))
        dv_lacc_prev = dv.inc()
        emit_wchunks(17)

    if isinstance(limit, int):
        finish_debug([o_sb[:, 0:15 * D], l_sb[:], stat_sb[:]])
        return finalize()
    emit_wchunks(len(wchunks))
    dv_w_final = max(wslot[0], wslot[1])
    sh_w_final = wst_state["sh_last"]

    wait("vector", dv, dv_lacc_prev)
    em("vector", lambda E: E.reciprocal(out=rinv_sb[:], in_=l_sb[:]).then_inc(dv.h, 1))
    dv_rinv = dv.inc()
    wait("vector", dv, dv_rinv)
    hbf = hb[:].bitcast(F32)
    sh_xnew_prev = sh_hstore_prev
    dv_fold = {}
    for r in range(16):
        if r == 0:
            wait("sync", dv, dv_w_final)
            wait("vector", sh, sh_w_final)
        else:
            wait("sync", dv, dv_fold.get(r - 1, 0))
        n_x = dma("sync", xb[:], xq_in[r * 128:(r + 1) * 128, :], ld)
        wait("vector", ld, n_x)
        wait("vector", dv, dv_rinv)
        wait("vector", sh, sh_xnew_prev)
        em("vector", lambda E, r=r: E.scalar_tensor_tensor(
            out=hbf, in0=o_sb[:, r * D:(r + 1) * D],
            scalar=rinv_sb[:, r:r + 1], in1=xb[:],
            op0=ALU.mult, op1=ALU.add).then_inc(dv.h, 1))
        dv_fold[r] = dv.inc()
        if r == 15:
            em("vector", lambda E, r=r: E.reduce_sum(
                out=stat_sb[:, r:r + 1], in_=hbf, axis=AX.X).then_inc(dv.h, 1))
            wait("vector", dv, dv.inc())
        else:
            em("vector", lambda E, r=r: E.reduce_sum(
                out=stat_sb[:, r:r + 1], in_=hbf, axis=AX.X))
        wait("scalar", dv, dv_fold[r])
        em("scalar", lambda E, r=r: E.memzero(stat_sb[:, 16 + r:17 + r]))
        em("scalar", lambda E, r=r: E.memzero(stat_sb[:, 32 + r:33 + r]))
        em("scalar", lambda E, r=r: E.activation(
            out=ps_sq[:], in_=hbf[:, 0:512], func=AF.Square, bias=0.0, scale=1.0,
            accum_out=stat_sb[:, 16 + r:17 + r]))
        em("scalar", lambda E, r=r: E.activation(
            out=ps_sq[:], in_=hbf[:, 512:1024], func=AF.Square, bias=0.0, scale=1.0,
            accum_out=stat_sb[:, 32 + r:33 + r]).then_inc(ac.h, 1))
        ac.inc()
        sh_xnew_prev = dma("scalar", xnew_scr[r * 128:(r + 1) * 128, :], hbf, sh)
    ac_sq_done = ac.n
    wait("vector", ac, ac_sq_done)

    def _stepf(fn):
        em("vector", lambda E, f=fn: f(E).then_inc(dv.h, 1))
        n = dv.inc()
        wait("vector", dv, n)
        return n
    _stepf(lambda E: E.tensor_scalar(
        out=stat_sb[:, 48:64], in0=stat_sb[:, 0:16], scalar1=1.0 / D, scalar2=None,
        op0=ALU.mult))
    _stepf(lambda E: E.tensor_tensor(
        out=stat_sb[:, 16:32], in0=stat_sb[:, 16:32], in1=stat_sb[:, 32:48], op=ALU.add))
    _stepf(lambda E: E.tensor_scalar(
        out=stat_sb[:, 16:32], in0=stat_sb[:, 16:32], scalar1=1.0 / D, scalar2=None,
        op0=ALU.mult))
    _stepf(lambda E: E.tensor_tensor(
        out=stat_sb[:, 32:48], in0=stat_sb[:, 48:64], in1=stat_sb[:, 48:64], op=ALU.mult))
    dv_varf = _stepf(lambda E: E.tensor_tensor(
        out=stat_sb[:, 16:32], in0=stat_sb[:, 16:32], in1=stat_sb[:, 32:48],
        op=ALU.subtract))
    wait("scalar", dv, dv_varf)
    em("scalar", lambda E: E.activation(
        out=stat_sb[:, 32:48], in_=stat_sb[:, 16:32], func=AF.Ln, bias=np.float32(1e-5), scale=1.0))
    em("scalar", lambda E: E.activation(
        out=rstdf_sb[:], in_=stat_sb[:, 32:48], func=AF.Exp, bias=0.0, scale=np.float32(-0.5))
       .then_inc(ac.h, 1))
    ac_rstdf = ac.inc()
    wait("vector", ac, ac_rstdf)
    _stepf(lambda E: E.tensor_tensor(
        out=negmf_sb[:], in0=stat_sb[:, 48:64], in1=rstdf_sb[:], op=ALU.mult))
    dv_negmf = _stepf(lambda E: E.tensor_scalar(
        out=negmf_sb[:], in0=negmf_sb[:], scalar1=-1.0, scalar2=None,
        op0=ALU.mult))
    dv_fold_done = dv.n
    ac_fold_done = ac.n

    if limit == "fold":
        finish_debug([rstdf_sb[:], negmf_sb[:], rinv_sb[:], l_sb[:]])
        return finalize()
    ps_scope.close()
    bstack.close()

    cstack = ExitStack()
    C = cstack.enter_context
    wgq_f = C(nc.sbuf_tensor("wgq_f", [128, ECH * II], BF16))
    wuq_f = C(nc.sbuf_tensor("wuq_f", [128, ECH * II], BF16))
    wdq_f = C(nc.sbuf_tensor("wdq_f", [128, ICH * D], BF16))
    qt_xn = C(nc.sbuf_tensor("qt_xn", [128, D], F32))
    qt_i1 = C(nc.sbuf_tensor("qt_i1", [128, D], I32))
    qt_acc = C(nc.sbuf_tensor("qt_acc", [128, D], F32))
    qt_t = C(nc.sbuf_tensor("qt_t", [128, D], F32))
    qt_sg = C(nc.sbuf_tensor("qt_sg", [128, D], BF16))
    qt_bm = C(nc.sbuf_tensor("qt_bm", [128, 64], F32))
    qt_sc = C(nc.sbuf_tensor("qt_sc", [128, 64], F32))
    qt_rx = C(nc.sbuf_tensor("qt_rx", [128, 64], F32))
    wq_bf_sb = C(nc.sbuf_tensor("wq_bf_sb", [128, 2 * D], BF16))
    wst = C(nc.sbuf_tensor("wst", [128, 2 * D], F32))
    xc = C(nc.sbuf_tensor("xc", [128, 2 * D], F32))
    h2_sb = C(nc.sbuf_tensor("h2_sb", [128, D], F32))
    xq_bf = C(nc.sbuf_tensor("xq_bf", [128, D], BF16))
    xqf_sb = C(nc.sbuf_tensor("xqf_sb", [128, ECH * 128], BF16))
    hq_bf = C(nc.sbuf_tensor("hq_bf", [128, 512], BF16))
    hqf_sb = C(nc.sbuf_tensor("hqf_sb", [128, 8 * 128], BF16))
    sg_sb = C(nc.sbuf_tensor("sg_sb", [128, 512], F32))
    outb_sb = C(nc.sbuf_tensor("outb_sb", [128, D], F32))

    ps_g = C(nc.psum_tensor("ps_g", [128, 512], F32))
    ps_g2 = C(nc.psum_tensor("ps_g2", [128, 512], F32))
    ps_u = C(nc.psum_tensor("ps_u", [128, 512], F32))
    ps_u2 = C(nc.psum_tensor("ps_u2", [128, 512], F32))
    ps_d = C(nc.psum_tensor("ps_d", [128, D], F32))
    ps_tA = C(nc.psum_tensor("ps_tA", [128, 128], BF16))
    ps_tB = C(nc.psum_tensor("ps_tB", [128, 128], BF16))


    qstate = {"dst_dv": 0}

    def quant(src, dst, F, blocks, wait_pe_n=0, wait_src_dv=0, wait_src_ac=0,
              wait_src_ld=0):
        g16 = src.rearrange("p (g k) -> p g k", k=16)
        if wait_pe_n:
            wait("vector", pe, wait_pe_n)
        if wait_src_ac:
            wait("vector", ac, wait_src_ac)
        if wait_src_dv:
            wait("vector", dv, wait_src_dv)
        if wait_src_ld:
            wait("vector", ld, wait_src_ld)
        em("vector", lambda E: E.reduce_max(
            out=qt_bm[:, 0:blocks], in_=g16, axis=AX.X,
            apply_absolute_value=True).then_inc(dv.h, 1))
        dvq = dv.inc()
        wait("vector", dv, dvq)
        em("vector", lambda E: E.tensor_scalar(
            out=qt_sc[:, 0:blocks], in0=qt_bm[:, 0:blocks], scalar1=1.0 / 6.0,
            scalar2=1e-12, op0=ALU.mult, op1=ALU.max).then_inc(dv.h, 1))
        dvq = dv.inc()
        wait("vector", dv, dvq)
        em("vector", lambda E: E.reciprocal(
            out=qt_rx[:, 0:blocks], in_=qt_sc[:, 0:blocks]).then_inc(dv.h, 1))
        dvq = dv.inc()
        wait("vector", dv, dvq)
        rxb = qt_rx[:, 0:blocks].unsqueeze(2).broadcast_to([128, blocks, 16])
        scb = qt_sc[:, 0:blocks].unsqueeze(2).broadcast_to([128, blocks, 16])
        wait("scalar", dv, qstate["dst_dv"])
        if wait_src_dv:
            wait("scalar", dv, wait_src_dv)
        if wait_src_ac:
            wait("scalar", ac, wait_src_ac)
        if wait_src_ld:
            wait("scalar", ld, wait_src_ld)
        em("scalar", lambda E: E.activation(
            out=qt_t[:, 0:F], in_=src, func=AF.Abs, bias=0.0, scale=1.0))
        em("scalar", lambda E: E.activation(
            out=qt_sg[:, 0:F], in_=src, func=AF.Sign, bias=0.0, scale=1.0).then_inc(ac.h, 1))
        acsg = ac.inc()
        wait("vector", ac, acsg)
        em("vector", lambda E: E.tensor_tensor(
            out=qt_xn[:, 0:F].rearrange("p (g k) -> p g k", k=16),
            in0=qt_t[:, 0:F].rearrange("p (g k) -> p g k", k=16), in1=rxb,
            op=ALU.mult).then_inc(dv.h, 1))
        dv.inc()
        em("vector", lambda E: E.tensor_scalar(
            out=qt_i1[:, 0:F], in0=qt_xn[:, 0:F], scalar1=2.0, scalar2=None,
            op0=ALU.mult))
        em("vector", lambda E: E.tensor_scalar(
            out=qt_acc[:, 0:F], in0=qt_i1[:, 0:F], scalar1=4.0, scalar2=0.5,
            op0=ALU.min, op1=ALU.mult))
        em("vector", lambda E: E.tensor_scalar(
            out=qt_i1[:, 0:F], in0=qt_xn[:, 0:F], scalar1=1.0, scalar2=None,
            op0=ALU.mult))
        em("vector", lambda E: E.tensor_scalar(
            out=qt_t[:, 0:F], in0=qt_i1[:, 0:F], scalar1=2.0, scalar2=2.0,
            op0=ALU.subtract, op1=ALU.min))
        em("vector", lambda E: E.scalar_tensor_tensor(
            out=qt_acc[:, 0:F], in0=qt_t[:, 0:F], scalar=0.0, in1=qt_acc[:, 0:F],
            op0=ALU.max, op1=ALU.add))
        em("vector", lambda E: E.tensor_scalar(
            out=qt_i1[:, 0:F], in0=qt_xn[:, 0:F], scalar1=0.5, scalar2=None,
            op0=ALU.mult))
        em("vector", lambda E: E.tensor_scalar(
            out=qt_t[:, 0:F], in0=qt_i1[:, 0:F], scalar1=2.0, scalar2=4.0,
            op0=ALU.mult, op1=ALU.subtract))
        em("vector", lambda E: E.scalar_tensor_tensor(
            out=qt_acc[:, 0:F], in0=qt_t[:, 0:F], scalar=0.0, in1=qt_acc[:, 0:F],
            op0=ALU.max, op1=ALU.add))
        em("vector", lambda E: E.tensor_tensor(
            out=qt_t[:, 0:F].rearrange("p (g k) -> p g k", k=16),
            in0=qt_acc[:, 0:F].rearrange("p (g k) -> p g k", k=16), in1=scb, op=ALU.mult))
        wait("vector", ac, acsg)
        em("vector", lambda E, dst=dst: E.tensor_tensor(
            out=dst, in0=qt_t[:, 0:F], in1=qt_sg[:, 0:F], op=ALU.mult).then_inc(dv.h, 1))
        qstate["dst_dv"] = dv.inc()
        return qstate["dst_dv"]

    wait("sync", sh, sh_w_final)
    for dc in range(ECH):
        dma("sync", wgq_f[:, dc * II:(dc + 1) * II],
            wgq_scr[:, dc * 128:(dc + 1) * 128], ld, transpose=True)
        dma("sync", wuq_f[:, dc * II:(dc + 1) * II],
            wuq_scr[:, dc * 128:(dc + 1) * 128], ld, transpose=True)
    n_wf = 0
    for icc in range(ICH):
        n_wf = dma("sync", wdq_f[:, icc * D:(icc + 1) * D],
                   wdq_scr[:, icc * 128:(icc + 1) * 128], ld, transpose=True)

    NCC = [512, 512, 512, 512, 512, 256]
    dv_out_prev = 0
    pe_xt_prev = 0
    pe_hqt_prev = 0
    dv_tc = {"A": 0, "B": 0}
    h2x = wst
    ac_h2 = {}
    dv_xq = {}
    n_xc = {}

    def emit_ln_tile(r):
        if r >= 16:
            return
        xsel = r % 2
        if r >= 2:
            wait("sync", dv, dv_out_prev)
        n_xc[r] = dma("sync", xc[:, xsel * D:(xsel + 1) * D],
                      xnew_scr[r * 128:(r + 1) * 128, :], ld)
        wait("scalar", ld, n_xc[r])
        wait("scalar", dv, dv_negmf)
        em("scalar", lambda E, r=r, xs=xsel: E.activation(
            out=h2x[:, xs * D:(xs + 1) * D], in_=xc[:, xs * D:(xs + 1) * D],
            func=AF.Identity,
            bias=negmf_sb[:, r:r + 1], scale=rstdf_sb[:, r:r + 1]).then_inc(ac.h, 1))
        ac_h2[r] = ac.inc()

    def emit_xquant_tile(r):
        if r >= 16:
            return
        xsel = r % 2
        dv_xq[r] = quant(h2x[:, xsel * D:(xsel + 1) * D], xq_bf[:], D, 64,
                         wait_pe_n=pe_xt_prev, wait_src_ac=ac_h2[r])

    emit_ln_tile(0)
    emit_xquant_tile(0)
    for r in range(16):
        xsel = r % 2
        wait("tensor", dv, dv_xq[r])
        for dc in range(ECH):
            key = "A" if dc % 2 == 0 else "B"
            pst = ps_tA if dc % 2 == 0 else ps_tB
            wait("tensor", dv, dv_tc[key])
            em("tensor", lambda E, dc=dc, pst=pst: E.transpose(
                out=pst[:], in_=xq_bf[:, dc * 128:(dc + 1) * 128], identity=ident_sb[:])
               .then_inc(pe.h, 1))
            pe_t = pe.inc()
            wait("vector", pe, pe_t)
            em("vector", lambda E, dc=dc, pst=pst: E.tensor_copy(
                out=xqf_sb[:, dc * 128:(dc + 1) * 128], in_=pst[:]).then_inc(dv.h, 1))
            dv_tc[key] = dv.inc()
        pe_xt_prev = pe.n
        dv_xt = dv_tc["B"]

        def emit_gu(ci):
            csz = NCC[ci]
            off = 512 * ci
            gsel = ci % 2
            psg = ps_g if gsel == 0 else ps_g2
            psu = ps_u if gsel == 0 else ps_u2
            wait("tensor", ac, ac_sil_prev[gsel])
            wait("tensor", dv, dv_h_prev[gsel])
            if ci == 0:
                wait("tensor", dv, dv_xt)
                wait("tensor", ld, n_wf)
            for dc in range(ECH):
                em("tensor", lambda E, dc=dc, off=off, csz=csz, psg=psg: E.matmul(
                    out=psg[:, 0:csz], lhsT=xqf_sb[:, dc * 128:(dc + 1) * 128],
                    rhs=wgq_f[:, dc * II + off:dc * II + off + csz],
                    start=(dc == 0), stop=(dc == ECH - 1)).then_inc(pe.h, 1))
                pe.inc()
            for dc in range(ECH):
                em("tensor", lambda E, dc=dc, off=off, csz=csz, psu=psu: E.matmul(
                    out=psu[:, 0:csz], lhsT=xqf_sb[:, dc * 128:(dc + 1) * 128],
                    rhs=wuq_f[:, dc * II + off:dc * II + off + csz],
                    start=(dc == 0), stop=(dc == ECH - 1)).then_inc(pe.h, 1))
                pe.inc()
            return pe.n

        ac_sil_prev = {0: 0, 1: 0}
        dv_h_prev = {0: 0, 1: 0}
        dv_h_last = 0
        icc0 = 0
        pe_g = {0: emit_gu(0)}
        for ci, csz in enumerate(NCC):
            gsel = ci % 2
            psg = ps_g if gsel == 0 else ps_g2
            psu = ps_u if gsel == 0 else ps_u2
            wait("scalar", pe, pe_g[ci])
            wait("scalar", dv, dv_h_last)
            em("scalar", lambda E, csz=csz, psg=psg: E.activation(
                out=sg_sb[:, 0:csz], in_=psg[:, 0:csz], func=AF.Silu, bias=0.0,
                scale=1.0).then_inc(ac.h, 1))
            ac_sil = ac.inc()
            ac_sil_prev[gsel] = ac_sil
            wait("vector", ac, ac_sil)
            wait("vector", pe, pe_g[ci])
            em("vector", lambda E, csz=csz, psu=psu: E.tensor_tensor(
                out=h2_sb[:, 0:csz], in0=sg_sb[:, 0:csz], in1=psu[:, 0:csz],
                op=ALU.mult).then_inc(dv.h, 1))
            dv_h = dv.inc()
            dv_h_prev[gsel] = dv_h
            dv_h_last = dv_h
            dv_hq = quant(h2_sb[:, 0:csz], hq_bf[:, 0:csz], csz, csz // 16,
                          wait_pe_n=pe_hqt_prev, wait_src_dv=dv_h)
            if ci == 1:
                emit_ln_tile(r + 1)
            if ci == 2:
                emit_xquant_tile(r + 1)
            if ci + 1 < len(NCC):
                pe_g[ci + 1] = emit_gu(ci + 1)
            wait("tensor", dv, dv_hq)
            nic = csz // 128
            for j in range(nic):
                icc = icc0 + j
                key = "A" if icc % 2 == 0 else "B"
                pst = ps_tA if icc % 2 == 0 else ps_tB
                wait("tensor", dv, dv_tc[key])
                em("tensor", lambda E, j=j, pst=pst: E.transpose(
                    out=pst[:], in_=hq_bf[:, j * 128:(j + 1) * 128], identity=ident_sb[:])
                   .then_inc(pe.h, 1))
                pe_t = pe.inc()
                wait("vector", pe, pe_t)
                em("vector", lambda E, sl=icc % 8, pst=pst: E.tensor_copy(
                    out=hqf_sb[:, sl * 128:(sl + 1) * 128], in_=pst[:])
                   .then_inc(dv.h, 1))
                dv_tc[key] = dv.inc()
            pe_hqt_prev = pe.n
            wait("tensor", dv, dv_tc["A"])
            wait("tensor", dv, dv_tc["B"])
            if ci == 0 and r >= 1:
                wait("tensor", dv, dv_out_prev)
            for j in range(nic):
                icc = icc0 + j
                for hh2 in range(2):
                    em("tensor", lambda E, icc=icc, sl=icc % 8, hh=hh2: E.matmul(
                        out=ps_d[:, hh * 512:(hh + 1) * 512],
                        lhsT=hqf_sb[:, sl * 128:(sl + 1) * 128],
                        rhs=wdq_f[:, icc * D + hh * 512:icc * D + (hh + 1) * 512],
                        start=(icc == 0), stop=(icc == ICH - 1)).then_inc(pe.h, 1))
                    pe.inc()
            icc0 += nic
        pe_down = pe.n
        wait("vector", pe, pe_down)
        if r >= 1:
            wait("vector", so, r * 16)
        em("vector", lambda E, r=r, xs=xsel: E.tensor_tensor(
            out=outb_sb[:, 0:D], in0=ps_d[:, 0:D],
            in1=xc[:, xs * D:(xs + 1) * D], op=ALU.add).then_inc(dv.h, 1))
        dv_out_prev = dv.inc()
        wait("scalar", dv, dv_out_prev)
        dma("scalar", out_t[r * 128:(r + 1) * 128, :], outb_sb[:, 0:D], so)

    wait("sync", so, so.n)
    return finalize()


def _host_tables(j):
    tt = np.zeros((128, 64), dtype=np.float32)
    p = np.arange(128, dtype=np.float32)
    for su in range(NSU):
        kt, half = su // 2, su % 2
        for kc in range(4):
            t = kt * 1024 + half * 512 + kc * 128 - j * TOWN
            t = max(-128, min(2176, t))
            tt[:, su * 4 + kc] = p + t
    return tt


def kernel(L0_attn_X, L0_attn_Wq, L0_attn_bq, L0_attn_Wk, L0_attn_bk,
           L0_attn_Wv, L0_attn_bv, L0_ffn_w_gate, L0_ffn_w_up, L0_ffn_w_down):
    X = np.ascontiguousarray(np.asarray(L0_attn_X, dtype=np.float32))
    wqT = np.ascontiguousarray(np.asarray(L0_attn_Wq, np.float32).T)
    wkT = np.ascontiguousarray(np.asarray(L0_attn_Wk, np.float32).T)
    wvT = np.ascontiguousarray(np.asarray(L0_attn_Wv, np.float32).T)
    bq = np.ascontiguousarray(np.asarray(L0_attn_bq, np.float32))
    bk = np.ascontiguousarray(np.asarray(L0_attn_bk, np.float32))
    bv = np.ascontiguousarray(np.asarray(L0_attn_bv, np.float32))
    wg = np.ascontiguousarray(np.asarray(L0_ffn_w_gate, np.float32))
    wu = np.ascontiguousarray(np.asarray(L0_ffn_w_up, np.float32))
    wd = np.ascontiguousarray(np.asarray(L0_ffn_w_down, np.float32))

    ramp = np.broadcast_to(np.arange(TOWN, dtype=np.float32)[None, :], (128, TOWN)).copy()
    ident = np.eye(128, dtype=ml_dtypes.bfloat16)

    if "nc" not in _CACHE:
        _CACHE["nc"] = build_program()
    nc = _CACHE["nc"]

    in_maps = []
    for c in range(8):
        b, j = c // 4, c % 4
        in_maps.append({
            "xq": np.ascontiguousarray(X[b, j * TOWN:(j + 1) * TOWN, :]),
            "xkv": np.ascontiguousarray(X[b]),
            "wqT": wqT, "wkT": wkT, "wvT": wvT,
            "bq": bq, "bk": bk, "bv": bv,
            "wg": wg, "wu": wu, "wd": wd,
            "ramp": ramp, "tthr": _host_tables(j), "ident": ident,
        })
    res = run_bass_kernel_spmd(nc, in_maps, core_ids=list(range(8)))
    out = np.empty((2, NKV, D), dtype=np.float32)
    for c in range(8):
        b, j = c // 4, c % 4
        out[b, j * TOWN:(j + 1) * TOWN, :] = np.asarray(res.results[c]["out"])
    return out


# version 15
# speedup vs baseline: 1.1530x; 1.0054x over previous
import numpy as np
import ml_dtypes
from contextlib import ExitStack

import concourse.bass as bass
import concourse.mybir as mybir
from concourse.bass_utils import run_bass_kernel_spmd

F32 = mybir.dt.float32
BF16 = mybir.dt.bfloat16
I32 = mybir.dt.int32
AF = mybir.ActivationFunctionType
ALU = mybir.AluOpType
AX = mybir.AxisListType

D = 1024
ECH = D // 128
TOWN = 2048
NKV = 8192
NSU = 16
II = 2816
ICH = II // 128
NEG = -8.0

_CACHE = {}


class Ctr:
    def __init__(self, handle):
        self.h = handle
        self.n = 0

    def inc(self, by=1):
        self.n += by
        return self.n


def build_program(limit=None):
    nc = bass.Bass("TRN2", target_bir_lowering=False)

    xq_in = nc.dram_tensor("xq", [TOWN, D], F32, kind="ExternalInput").ap()
    xkv_in = nc.dram_tensor("xkv", [NKV, D], F32, kind="ExternalInput").ap()
    wqT_in = nc.dram_tensor("wqT", [D, D], F32, kind="ExternalInput").ap()
    wkT_in = nc.dram_tensor("wkT", [D, D], F32, kind="ExternalInput").ap()
    wvT_in = nc.dram_tensor("wvT", [D, D], F32, kind="ExternalInput").ap()
    bq_in = nc.dram_tensor("bq", [D], F32, kind="ExternalInput").ap()
    bk_in = nc.dram_tensor("bk", [D], F32, kind="ExternalInput").ap()
    bv_in = nc.dram_tensor("bv", [D], F32, kind="ExternalInput").ap()
    wg_in = nc.dram_tensor("wg", [II, D], F32, kind="ExternalInput").ap()
    wu_in = nc.dram_tensor("wu", [II, D], F32, kind="ExternalInput").ap()
    wd_in = nc.dram_tensor("wd", [D, II], F32, kind="ExternalInput").ap()
    ramp_in = nc.dram_tensor("ramp", [128, TOWN], F32, kind="ExternalInput").ap()
    tthr_in = nc.dram_tensor("tthr", [128, 64], F32, kind="ExternalInput").ap()
    ident_in = nc.dram_tensor("ident", [128, 128], BF16, kind="ExternalInput").ap()
    out_t = nc.dram_tensor("out", [TOWN, D], F32, kind="ExternalOutput").ap()

    h_scr = nc.dram_tensor("h_scr", [NKV, D], BF16, kind="Internal").ap()
    hq_scr = nc.dram_tensor("hq_scr", [TOWN, D], BF16, kind="Internal").ap()
    xnew_scr = nc.dram_tensor("xnew_scr", [TOWN, D], F32, kind="Internal").ap()
    wgq_scr = nc.dram_tensor("wgq_scr", [II, D], BF16, kind="Internal").ap()
    wuq_scr = nc.dram_tensor("wuq_scr", [II, D], BF16, kind="Internal").ap()
    wdq_scr = nc.dram_tensor("wdq_scr", [D, II], BF16, kind="Internal").ap()

    ops = {k: [] for k in ("sync", "vector", "tensor", "scalar", "gpsimd")}

    def em(eng, fn):
        ops[eng].append(fn)

    def finalize():
        with nc.Block() as block:
            @block.sync
            def _(E):
                for f in ops["sync"]:
                    f(E)

            @block.vector
            def _(E):
                for f in ops["vector"]:
                    f(E)

            @block.tensor
            def _(E):
                for f in ops["tensor"]:
                    f(E)

            @block.scalar
            def _(E):
                for f in ops["scalar"]:
                    f(E)

            @block.gpsimd
            def _(E):
                for f in ops["gpsimd"]:
                    f(E)
        return nc

    pstack = ExitStack()
    bstack = ExitStack()
    P = pstack.enter_context
    B = bstack.enter_context

    ld = Ctr(P(nc.semaphore("ld")))
    sh = Ctr(P(nc.semaphore("sh")))
    so = Ctr(P(nc.semaphore("so")))
    pe = Ctr(P(nc.semaphore("pe")))
    dv = Ctr(P(nc.semaphore("dv")))
    gp = Ctr(P(nc.semaphore("gp")))
    ac = Ctr(P(nc.semaphore("ac")))

    def dma(eng, out, in_, ctr, transpose=False):
        n = ctr.inc(16)
        if transpose:
            em(eng, lambda E, o=out, i=in_, c=ctr.h:
               E.dma_start(out=o, in_=i, transpose=True).then_inc(c, 16))
        else:
            em(eng, lambda E, o=out, i=in_, c=ctr.h:
               E.dma_start(out=o, in_=i).then_inc(c, 16))
        return n

    def wait(eng, ctr, n):
        if n > 0:
            em(eng, lambda E, c=ctr.h, v=n: E.wait_ge(c, v))

    tthr_sb = P(nc.sbuf_tensor("tthr_sb", [128, 64], F32))
    stat_sb = P(nc.sbuf_tensor("stat_sb", [128, 256], F32))
    rstd_all = P(nc.sbuf_tensor("rstd_all", [128, 80], F32))
    negm_all = P(nc.sbuf_tensor("negm_all", [128, 80], F32))
    bqc_sb = P(nc.sbuf_tensor("bqc_sb", [128, ECH], F32))
    bkc_sb = P(nc.sbuf_tensor("bkc_sb", [128, ECH], F32))
    bvb_sb = P(nc.sbuf_tensor("bvb_sb", [128, D], F32))
    ones1_bf = P(nc.sbuf_tensor("ones1_bf", [128, 1], BF16))
    rstdf_sb = P(nc.sbuf_tensor("rstdf_sb", [128, 16], F32))
    negmf_sb = P(nc.sbuf_tensor("negmf_sb", [128, 16], F32))
    rinv_sb = P(nc.sbuf_tensor("rinv_sb", [128, 16], F32))
    brow = P(nc.sbuf_tensor("brow", [1, 3 * D], F32))
    onef = P(nc.sbuf_tensor("onef", [1, 1], F32))
    onerow = P(nc.sbuf_tensor("onerow", [1, 128], F32))
    ident_sb = P(nc.sbuf_tensor("ident_sb", [128, 128], BF16))

    wkT_bf = B(nc.sbuf_tensor("wkT_bf", [128, ECH * D], BF16))
    wvT_bf = B(nc.sbuf_tensor("wvT_bf", [128, ECH * D], BF16))
    q_sb = B(nc.sbuf_tensor("q_sb", [128, ECH * TOWN], BF16))
    o_sb = B(nc.sbuf_tensor("o_sb", [128, 16 * D], F32))
    l_sb = B(nc.sbuf_tensor("l_sb", [128, 16], F32))
    xb = B(nc.sbuf_tensor("xb", [128, D], F32))
    hb = B(nc.sbuf_tensor("hb", [128, 2 * D], BF16))
    hf = B(nc.sbuf_tensor("hf", [128, ECH * 512], BF16))
    k_sb = B(nc.sbuf_tensor("k_sb", [128, ECH * 512], BF16))
    v_sb = B(nc.sbuf_tensor("v_sb", [128, 4 * D], BF16))
    pt_sb = B(nc.sbuf_tensor("pt_sb", [128, 4 * TOWN], BF16))
    ramp_sb = B(nc.sbuf_tensor("ramp_sb", [128, TOWN], F32))
    ev_scr = B(nc.sbuf_tensor("ev_scr", [128, 512], F32))
    es_scr = B(nc.sbuf_tensor("es_scr", [128, 512], F32))

    wstage = o_sb[:, 0:ECH * D]
    wq_bf = pt_sb

    ps_scope = ExitStack()
    pp = ps_scope.enter_context
    ps_pvA = pp(nc.psum_tensor("ps_pvA", [128, D], F32))
    ps_pvB = pp(nc.psum_tensor("ps_pvB", [128, D], F32))
    ps_s0 = pp(nc.psum_tensor("ps_s0", [128, 512], F32))
    ps_s1 = pp(nc.psum_tensor("ps_s1", [128, 512], F32))
    ps_l = pp(nc.psum_tensor("ps_l", [128, 512], F32))
    ps_sq = pp(nc.psum_tensor("ps_sq", [128, 512], F32))

    def _load_act(idx):
        def f(E):
            E.add_instruction(mybir.InstLoadActFuncSet(
                name=nc.get_next_instruction_name(),
                act_func_set_id=idx, ins=[], outs=[]))
        return f

    dma("sync", ramp_sb[:], ramp_in, ld)
    dma("sync", tthr_sb[:], tthr_in, ld)
    dma("sync", ident_sb[:], ident_in, ld)
    dma("sync", brow[0:1, 0:D], bq_in[None, :], ld)
    dma("sync", brow[0:1, D:2 * D], bk_in[None, :], ld)
    n_brow = dma("sync", brow[0:1, 2 * D:3 * D], bv_in[None, :], ld)

    em("vector", lambda E: E.memset(onef[:], 1.0))
    em("vector", lambda E: E.memset(onerow[:], 1.0))
    em("vector", lambda E: E.memset(ones1_bf[:], 1.0).then_inc(dv.h, 1))
    dv_ones = dv.inc()

    dv_wcast = 0
    for name, win, dst in (("q", wqT_in, None), ("k", wkT_in, wkT_bf), ("v", wvT_in, wvT_bf)):
        n_w = dma("sync", wstage.rearrange("p (c e) -> p c e", e=D),
                  win.rearrange("(c p) e -> p c e", p=128), ld)
        wait("vector", ld, n_w)
        if name == "q":
            em("vector", lambda E: E.tensor_copy(out=wq_bf[:, 0:ECH * D], in_=wstage)
               .then_inc(dv.h, 1))
        else:
            em("vector", lambda E, d=dst: E.tensor_copy(out=d[:], in_=wstage)
               .then_inc(dv.h, 1))
        dv_wcast = dv.inc()
        wait("sync", dv, dv_wcast)
    em("vector", lambda E: E.memset(o_sb[:], 0.0))
    em("vector", lambda E: E.memset(l_sb[:], 0.0).then_inc(dv.h, 1))
    dv_zero = dv.inc()

    wait("tensor", ld, n_brow)
    wait("tensor", dv, dv_ones)
    for bi in range(2):
        for ecj in range(ECH):
            em("tensor", lambda E, b=bi, c=ecj: E.matmul(
                out=ps_sq[:, b * 8 + c:b * 8 + c + 1],
                lhsT=brow[0:1, b * D + c * 128:b * D + (c + 1) * 128],
                rhs=onef[0:1, 0:1], start=True, stop=True).then_inc(pe.h, 1))
            n = pe.inc()
            wait("tensor", pe, n)
    pe_bias = pe.n
    wait("scalar", pe, pe_bias)
    em("scalar", lambda E: E.activation(out=bqc_sb[:], in_=ps_sq[:, 0:8],
                                        func=AF.Identity, bias=0.0, scale=np.float32(1.0 / 32.0)))
    em("scalar", lambda E: E.activation(out=bkc_sb[:], in_=ps_sq[:, 8:16],
                                        func=AF.Identity, bias=0.0, scale=1.0)
       .then_inc(ac.h, 1))
    ac_bias = ac.inc()
    for hh in range(2):
        wait("tensor", ac, ac_bias)
        em("tensor", lambda E, h=hh: E.matmul(
            out=ps_sq[:, 0:512],
            lhsT=onerow[0:1, :],
            rhs=brow[0:1, 2 * D + h * 512:2 * D + (h + 1) * 512],
            start=True, stop=True).then_inc(pe.h, 1))
        pe_bv = pe.inc()
        wait("scalar", pe, pe_bv)
        em("scalar", lambda E, h=hh: E.activation(
            out=bvb_sb[:, h * 512:(h + 1) * 512], in_=ps_sq[:, 0:512],
            func=AF.Identity, bias=0.0, scale=1.0).then_inc(ac.h, 1))
        ac_bias = ac.inc()
    ac_bvb = ac_bias

    def ln_tiles(x_ap_list, h_out_list, wait_loads, slot0=0):
        nt = len(x_ap_list)
        wait("scalar", ld, wait_loads)
        wait("scalar", ac, ac_bvb)
        em("scalar", lambda E: E.memzero(stat_sb[:, 8 + slot0:8 + slot0 + nt]))
        em("scalar", lambda E: E.memzero(stat_sb[:, 16 + slot0:16 + slot0 + nt]))
        for i, x in enumerate(x_ap_list):
            s = slot0 + i
            em("scalar", lambda E, x=x, s=s: E.activation(
                out=ps_sq[:], in_=x[:, 0:512], func=AF.Square, bias=0.0, scale=1.0,
                accum_out=stat_sb[:, 8 + s:9 + s]))
            em("scalar", lambda E, x=x, s=s: E.activation(
                out=ps_sq[:], in_=x[:, 512:1024], func=AF.Square, bias=0.0, scale=1.0,
                accum_out=stat_sb[:, 16 + s:17 + s]).then_inc(ac.h, 1))
            ac.inc()
        ac_sq = ac.n
        if limit == "ln0":
            finish_debug([stat_sb[:]])
            return "STOP"
        wait("vector", ld, wait_loads)
        for i, x in enumerate(x_ap_list):
            s = slot0 + i
            last = (i == len(x_ap_list) - 1)
            if last:
                em("vector", lambda E, x=x, s=s: E.reduce_sum(
                    out=stat_sb[:, s:s + 1], in_=x, axis=AX.X).then_inc(dv.h, 1))
                wait("vector", dv, dv.inc())
            else:
                em("vector", lambda E, x=x, s=s: E.reduce_sum(
                    out=stat_sb[:, s:s + 1], in_=x, axis=AX.X))
        wait("vector", ac, ac_sq)
        sl = slice(slot0, slot0 + nt)
        s2a = slice(8 + slot0, 8 + slot0 + nt)
        s2b = slice(16 + slot0, 16 + slot0 + nt)
        mu = slice(24 + slot0, 24 + slot0 + nt)
        var = slice(32 + slot0, 32 + slot0 + nt)
        rstd = slice(40 + slot0, 40 + slot0 + nt)
        ng = slice(48 + slot0, 48 + slot0 + nt)

        def _step(fn):
            em("vector", lambda E, f=fn: f(E).then_inc(dv.h, 1))
            n = dv.inc()
            wait("vector", dv, n)
            return n
        _step(lambda E: E.tensor_scalar(
            out=stat_sb[:, mu], in0=stat_sb[:, sl], scalar1=1.0 / D, scalar2=None,
            op0=ALU.mult))
        _step(lambda E: E.tensor_tensor(
            out=stat_sb[:, var], in0=stat_sb[:, s2a], in1=stat_sb[:, s2b], op=ALU.add))
        _step(lambda E: E.tensor_scalar(
            out=stat_sb[:, var], in0=stat_sb[:, var], scalar1=1.0 / D, scalar2=None,
            op0=ALU.mult))
        _step(lambda E: E.tensor_tensor(
            out=stat_sb[:, s2a], in0=stat_sb[:, mu], in1=stat_sb[:, mu], op=ALU.mult))
        dv_var = _step(lambda E: E.tensor_tensor(
            out=stat_sb[:, var], in0=stat_sb[:, var], in1=stat_sb[:, s2a],
            op=ALU.subtract))
        wait("scalar", dv, dv_var)
        em("scalar", lambda E: E.activation(
            out=stat_sb[:, s2b], in_=stat_sb[:, var], func=AF.Ln, bias=np.float32(1e-5), scale=1.0))
        em("scalar", lambda E: E.activation(
            out=stat_sb[:, rstd], in_=stat_sb[:, s2b], func=AF.Exp, bias=0.0,
            scale=np.float32(-0.5)).then_inc(ac.h, 1))
        ac_rstd = ac.inc()
        wait("vector", ac, ac_rstd)
        _step(lambda E: E.tensor_tensor(
            out=stat_sb[:, ng], in0=stat_sb[:, mu], in1=stat_sb[:, rstd], op=ALU.mult))
        dv_ng = _step(lambda E: E.tensor_scalar(
            out=stat_sb[:, ng], in0=stat_sb[:, ng], scalar1=-1.0, scalar2=None,
            op0=ALU.mult))
        wait("scalar", dv, dv_ng)
        for i, (x, h) in enumerate(zip(x_ap_list, h_out_list)):
            s = slot0 + i
            em("scalar", lambda E, x=x, h=h, s=s: E.activation(
                out=h, in_=x, func=AF.Identity,
                bias=stat_sb[:, 48 + s:49 + s], scale=stat_sb[:, 40 + s:41 + s])
               .then_inc(ac.h, 1))
            ac.inc()
        return ac.n

    dbg_out = nc.dram_tensor("dbg", [128, 16 * D], F32, kind="ExternalOutput").ap() if limit else None

    def finish_debug(ap_list):
        col = 0
        wait("sync", ac, ac.n)
        wait("sync", dv, dv.n)
        wait("sync", pe, pe.n)
        nd = 0
        for ap_ in ap_list:
            ncol = ap_.shape[-1]
            nd = dma("sync", dbg_out[:, col:col + ncol], ap_, so)
            col += ncol
        wait("sync", so, nd)

    if limit == "init":
        finish_debug([bvb_sb[:], bqc_sb[:], bkc_sb[:]])
        return finalize()
    hbf0 = hb[:].bitcast(F32)
    evs0 = ev_scr[:, 0:512]
    evs1 = es_scr[:, 0:512]
    dv_red = {0: 0, 1: 0}
    ac_sq2 = {0: 0, 1: 0}
    dv_chain_prev = 0
    for b8 in range(8):
        for t8 in range(8):
            t = b8 * 8 + t8
            sl = t % 2
            xbuf = xb[:] if sl == 0 else hbf0
            wait("sync", dv, dv_red[sl])
            wait("sync", ac, ac_sq2[sl])
            n_x = dma("sync", xbuf if False else (xb[:] if sl == 0 else hbf0),
                      xkv_in[t * 128:(t + 1) * 128, :], ld)
            xbuf = xb[:] if sl == 0 else hbf0
            wait("vector", ld, n_x)
            em("vector", lambda E, x=xbuf, t=t: E.reduce_sum(
                out=stat_sb[:, t:t + 1], in_=x, axis=AX.X).then_inc(dv.h, 1))
            dv_red[sl] = dv.inc()
            wait("scalar", ld, n_x)
            em("scalar", lambda E, x=xbuf, t=t: E.memzero(stat_sb[:, 64 + t:65 + t]))
            em("scalar", lambda E, x=xbuf, t=t: E.memzero(stat_sb[:, 128 + t:129 + t]))
            em("scalar", lambda E, x=xbuf, t=t: E.activation(
                out=ps_sq[:], in_=x[:, 0:512], func=AF.Square, bias=0.0, scale=1.0,
                accum_out=stat_sb[:, 64 + t:65 + t]))
            em("scalar", lambda E, x=xbuf, t=t: E.activation(
                out=ps_sq[:], in_=x[:, 512:1024], func=AF.Square, bias=0.0, scale=1.0,
                accum_out=stat_sb[:, 128 + t:129 + t]).then_inc(ac.h, 1))
            ac_sq2[sl] = ac.inc()
        c0 = b8 * 8
        wait("vector", ac, ac_sq2[0])
        wait("vector", ac, ac_sq2[1])
        wait("vector", dv, max(dv_red[0], dv_red[1]))

        def _stp(fn):
            em("vector", lambda E, f=fn: f(E).then_inc(dv.h, 1))
            n = dv.inc()
            wait("vector", dv, n)
            return n
        _stp(lambda E, c0=c0: E.tensor_scalar(
            out=stat_sb[:, 192 + c0:200 + c0], in0=stat_sb[:, c0:c0 + 8],
            scalar1=1.0 / D, scalar2=None, op0=ALU.mult))
        _stp(lambda E, c0=c0: E.tensor_tensor(
            out=stat_sb[:, 64 + c0:72 + c0], in0=stat_sb[:, 64 + c0:72 + c0],
            in1=stat_sb[:, 128 + c0:136 + c0], op=ALU.add))
        _stp(lambda E, c0=c0: E.tensor_scalar(
            out=stat_sb[:, 64 + c0:72 + c0], in0=stat_sb[:, 64 + c0:72 + c0],
            scalar1=1.0 / D, scalar2=None, op0=ALU.mult))
        _stp(lambda E, c0=c0: E.tensor_tensor(
            out=stat_sb[:, 128 + c0:136 + c0], in0=stat_sb[:, 192 + c0:200 + c0],
            in1=stat_sb[:, 192 + c0:200 + c0], op=ALU.mult))
        dvv_ = _stp(lambda E, c0=c0: E.tensor_tensor(
            out=stat_sb[:, 64 + c0:72 + c0], in0=stat_sb[:, 64 + c0:72 + c0],
            in1=stat_sb[:, 128 + c0:136 + c0], op=ALU.subtract))
        wait("scalar", dv, dvv_)
        em("scalar", lambda E, c0=c0: E.activation(
            out=stat_sb[:, 128 + c0:136 + c0], in_=stat_sb[:, 64 + c0:72 + c0],
            func=AF.Ln, bias=np.float32(1e-5), scale=1.0))
        em("scalar", lambda E, c0=c0: E.activation(
            out=rstd_all[:, c0:c0 + 8], in_=stat_sb[:, 128 + c0:136 + c0],
            func=AF.Exp, bias=0.0, scale=np.float32(-0.5)).then_inc(ac.h, 1))
        acr_ = ac.inc()
        wait("vector", ac, acr_)
        _stp(lambda E, c0=c0: E.tensor_tensor(
            out=negm_all[:, c0:c0 + 8], in0=stat_sb[:, 192 + c0:200 + c0],
            in1=rstd_all[:, c0:c0 + 8], op=ALU.mult))
        dv_chain_prev = _stp(lambda E, c0=c0: E.tensor_scalar(
            out=negm_all[:, c0:c0 + 8], in0=negm_all[:, c0:c0 + 8],
            scalar1=-1.0, scalar2=None, op0=ALU.mult))
    dv_stats_done = dv.n
    ac_norm = {0: 0, 1: 0}
    sh_hstore_prev = 0
    sh_hstore_pp = 0
    wait("scalar", dv, dv_stats_done)
    for t in range(64):
        sl = t % 2
        wait("sync", ac, ac_norm[sl])
        if sl == 0:
            n_x = dma("sync", xb[:], xkv_in[t * 128:(t + 1) * 128, :], ld)
        else:
            n_x = dma("sync", evs0, xkv_in[t * 128:(t + 1) * 128, 0:512], ld)
            n_x = dma("sync", evs1, xkv_in[t * 128:(t + 1) * 128, 512:1024], ld)
        hsl = t % 2
        wait("scalar", ld, n_x)
        wait("scalar", sh, sh_hstore_pp)
        if sl == 0:
            em("scalar", lambda E, t=t, h=hsl: E.activation(
                out=hb[:, h * D:h * D + D], in_=xb[:], func=AF.Identity,
                bias=negm_all[:, t:t + 1], scale=rstd_all[:, t:t + 1])
               .then_inc(ac.h, 1))
        else:
            em("scalar", lambda E, t=t, h=hsl: E.activation(
                out=hb[:, h * D:h * D + 512], in_=evs0, func=AF.Identity,
                bias=negm_all[:, t:t + 1], scale=rstd_all[:, t:t + 1]))
            em("scalar", lambda E, t=t, h=hsl: E.activation(
                out=hb[:, h * D + 512:h * D + D], in_=evs1, func=AF.Identity,
                bias=negm_all[:, t:t + 1], scale=rstd_all[:, t:t + 1])
               .then_inc(ac.h, 1))
        ac_norm[sl] = ac.inc()
        wait("scalar", ac, ac_norm[sl])
        sh_hstore_pp = sh_hstore_prev
        sh_hstore_prev = dma("scalar", h_scr[t * 128:(t + 1) * 128, :],
                             hb[:, hsl * D:(hsl + 1) * D], sh)
    sh_hdone = sh_hstore_prev

    for b8 in range(2):
        for t8 in range(8):
            t = b8 * 8 + t8
            sl = t % 2
            xbuf = xb[:] if sl == 0 else hbf0
            wait("sync", dv, dv_red[sl])
            wait("sync", ac, ac_sq2[sl])
            wait("sync", sh, sh_hdone if t == 0 else 0)
            n_x = dma("sync", xb[:] if sl == 0 else hbf0,
                      xq_in[t * 128:(t + 1) * 128, :], ld)
            wait("vector", ld, n_x)
            em("vector", lambda E, x=xbuf, t=t: E.reduce_sum(
                out=stat_sb[:, t:t + 1], in_=x, axis=AX.X).then_inc(dv.h, 1))
            dv_red[sl] = dv.inc()
            wait("scalar", ld, n_x)
            em("scalar", lambda E, t=t: E.memzero(stat_sb[:, 64 + t:65 + t]))
            em("scalar", lambda E, t=t: E.memzero(stat_sb[:, 128 + t:129 + t]))
            em("scalar", lambda E, x=xbuf, t=t: E.activation(
                out=ps_sq[:], in_=x[:, 0:512], func=AF.Square, bias=0.0, scale=1.0,
                accum_out=stat_sb[:, 64 + t:65 + t]))
            em("scalar", lambda E, x=xbuf, t=t: E.activation(
                out=ps_sq[:], in_=x[:, 512:1024], func=AF.Square, bias=0.0, scale=1.0,
                accum_out=stat_sb[:, 128 + t:129 + t]).then_inc(ac.h, 1))
            ac_sq2[sl] = ac.inc()
        c0 = b8 * 8
        q0 = 64 + b8 * 8
        wait("vector", ac, ac_sq2[0])
        wait("vector", ac, ac_sq2[1])
        wait("vector", dv, max(dv_red[0], dv_red[1]))

        def _stq(fn):
            em("vector", lambda E, f=fn: f(E).then_inc(dv.h, 1))
            n = dv.inc()
            wait("vector", dv, n)
            return n
        _stq(lambda E, c0=c0: E.tensor_scalar(
            out=stat_sb[:, 192 + c0:200 + c0], in0=stat_sb[:, c0:c0 + 8],
            scalar1=1.0 / D, scalar2=None, op0=ALU.mult))
        _stq(lambda E, c0=c0: E.tensor_tensor(
            out=stat_sb[:, 64 + c0:72 + c0], in0=stat_sb[:, 64 + c0:72 + c0],
            in1=stat_sb[:, 128 + c0:136 + c0], op=ALU.add))
        _stq(lambda E, c0=c0: E.tensor_scalar(
            out=stat_sb[:, 64 + c0:72 + c0], in0=stat_sb[:, 64 + c0:72 + c0],
            scalar1=1.0 / D, scalar2=None, op0=ALU.mult))
        _stq(lambda E, c0=c0: E.tensor_tensor(
            out=stat_sb[:, 128 + c0:136 + c0], in0=stat_sb[:, 192 + c0:200 + c0],
            in1=stat_sb[:, 192 + c0:200 + c0], op=ALU.mult))
        dvv_ = _stq(lambda E, c0=c0: E.tensor_tensor(
            out=stat_sb[:, 64 + c0:72 + c0], in0=stat_sb[:, 64 + c0:72 + c0],
            in1=stat_sb[:, 128 + c0:136 + c0], op=ALU.subtract))
        wait("scalar", dv, dvv_)
        em("scalar", lambda E, c0=c0: E.activation(
            out=stat_sb[:, 128 + c0:136 + c0], in_=stat_sb[:, 64 + c0:72 + c0],
            func=AF.Ln, bias=np.float32(1e-5), scale=1.0))
        em("scalar", lambda E, c0=c0, q0=q0: E.activation(
            out=rstd_all[:, q0:q0 + 8], in_=stat_sb[:, 128 + c0:136 + c0],
            func=AF.Exp, bias=0.0, scale=np.float32(-0.5)).then_inc(ac.h, 1))
        acr_ = ac.inc()
        wait("vector", ac, acr_)
        _stq(lambda E, c0=c0, q0=q0: E.tensor_tensor(
            out=negm_all[:, q0:q0 + 8], in0=stat_sb[:, 192 + c0:200 + c0],
            in1=rstd_all[:, q0:q0 + 8], op=ALU.mult))
        _stq(lambda E, q0=q0: E.tensor_scalar(
            out=negm_all[:, q0:q0 + 8], in0=negm_all[:, q0:q0 + 8],
            scalar1=-1.0, scalar2=None, op0=ALU.mult))
    dv_statsq_done = dv.n
    ac_norm = {0: 0, 1: 0}
    wait("scalar", dv, dv_statsq_done)
    for t in range(16):
        sl = t % 2
        wait("sync", ac, ac_norm[sl])
        if sl == 0:
            n_x = dma("sync", xb[:], xq_in[t * 128:(t + 1) * 128, :], ld)
        else:
            n_x = dma("sync", evs0, xq_in[t * 128:(t + 1) * 128, 0:512], ld)
            n_x = dma("sync", evs1, xq_in[t * 128:(t + 1) * 128, 512:1024], ld)
        hsl = t % 2
        wait("scalar", ld, n_x)
        wait("scalar", sh, sh_hstore_pp)
        if sl == 0:
            em("scalar", lambda E, t=t, h=hsl: E.activation(
                out=hb[:, h * D:h * D + D], in_=xb[:], func=AF.Identity,
                bias=negm_all[:, 64 + t:65 + t], scale=rstd_all[:, 64 + t:65 + t])
               .then_inc(ac.h, 1))
        else:
            em("scalar", lambda E, t=t, h=hsl: E.activation(
                out=hb[:, h * D:h * D + 512], in_=evs0, func=AF.Identity,
                bias=negm_all[:, 64 + t:65 + t], scale=rstd_all[:, 64 + t:65 + t]))
            em("scalar", lambda E, t=t, h=hsl: E.activation(
                out=hb[:, h * D + 512:h * D + D], in_=evs1, func=AF.Identity,
                bias=negm_all[:, 64 + t:65 + t], scale=rstd_all[:, 64 + t:65 + t])
               .then_inc(ac.h, 1))
        ac_norm[sl] = ac.inc()
        wait("scalar", ac, ac_norm[sl])
        sh_hstore_pp = sh_hstore_prev
        sh_hstore_prev = dma("scalar", hq_scr[t * 128:(t + 1) * 128, :],
                             hb[:, hsl * D:(hsl + 1) * D], sh)
    sh_hdone_q = sh_hstore_prev

    pe_hf_prev = 0
    ac_evacA = {0: 0, 1: 0}
    dv_evA = {0: 0, 1: 0}
    wait("sync", sh, sh_hdone_q)
    for g in range(1 if limit in ("A1", "A0") else 4):
        wait("sync", pe, pe_hf_prev)
        n_hf = 0
        for dc in range(ECH):
            n_hf = dma("sync", hf[:, dc * 512:(dc + 1) * 512],
                       hq_scr[g * 512:(g + 1) * 512, dc * 128:(dc + 1) * 128],
                       ld, transpose=True)
        if limit == "hf" and g == 0:
            wait("sync", ld, n_hf)
            finish_debug([hf[:].bitcast(F32)])
            return finalize()
        wait("tensor", ld, n_hf)
        wait("tensor", dv, dv_wcast)
        for ec in range(1 if limit == "A0" else ECH):
            hsel = ec % 2
            wait("tensor", ac, ac_evacA[hsel])
            for dc in range(ECH):
                em("tensor", lambda E, ec=ec, dc=dc, h=hsel: E.matmul(
                    out=ps_pvA[:, h * 512:(h + 1) * 512],
                    lhsT=wq_bf[:, dc * D + ec * 128:dc * D + ec * 128 + 128],
                    rhs=hf[:, dc * 512:(dc + 1) * 512],
                    start=(dc == 0), stop=(dc == ECH - 1)).then_inc(pe.h, 1))
                pe.inc()
            pe_q = pe.n
            wait("scalar", pe, pe_q)
            wait("scalar", gp, max(dv_evA[0], dv_evA[1]))
            em("scalar", lambda E, ec=ec, h=hsel: E.activation(
                out=ev_scr[:, 0:512],
                in_=ps_pvA[:, h * 512:(h + 1) * 512], func=AF.Identity,
                bias=bqc_sb[:, ec:ec + 1], scale=np.float32(1.0 / 32.0)).then_inc(ac.h, 1))
            ac_evacA[hsel] = ac.inc()
            wait("vector", ac, ac_evacA[hsel])
            em("vector", lambda E, ec=ec, g=g: E.tensor_copy(
                out=q_sb[:, ec * TOWN + g * 512:ec * TOWN + (g + 1) * 512],
                in_=ev_scr[:, 0:512]).then_inc(gp.h, 1))
            dv_evA[hsel] = gp.inc()
        pe_hf_prev = pe.n

    if limit in ("A", "A1", "A0"):
        finish_debug([q_sb[:].bitcast(F32)])
        return finalize()
    wsrc = [xb[:, 0:256], xb[:, 256:512]]
    wxn = xb[:, 512:768]
    wtt = xb[:, 768:1024]
    wacc = hb[:, 0:512].bitcast(F32)
    wi1 = hb[:, 512:1024].bitcast(I32)
    wsg = hb[:, 1024:1280]
    wdst = [hb[:, 1280:1536], hb[:, 1536:1792]]
    wsmall = hb[:, 1792:2048].bitcast(F32)
    wbm = wsmall[:, 0:16]
    wsc = wsmall[:, 16:32]
    wrx = wsmall[:, 32:48]
    wchunks = []
    for (w_in_t, w_scr, nrow) in ((wg_in, wgq_scr, II), (wu_in, wuq_scr, II),
                                  (wd_in, wdq_scr, D)):
        ncol = D if nrow == II else II
        for t in range(nrow // 128):
            for coff in range(0, ncol, 256):
                wchunks.append((w_in_t, w_scr, t, coff))
    wpos = {"i": 0}
    wslot = {0: 0, 1: 0}
    wsh = {0: 0, 1: 0}
    wst_state = {"sh_last": 0}

    def emit_wchunks(n):
        for _ in range(n):
            i = wpos["i"]
            if i >= len(wchunks):
                return
            wpos["i"] += 1
            w_in_t, w_scr, t, coff = wchunks[i]
            slot = i % 2
            wait("sync", dv, wslot[slot])
            n_w = dma("sync", wsrc[slot], w_in_t[t * 128:(t + 1) * 128, coff:coff + 256], ld)
            wait("scalar", dv, max(wslot[0], wslot[1]))
            wait("scalar", ld, n_w)
            em("scalar", lambda E, s=slot: E.activation(
                out=wtt[:], in_=wsrc[s], func=AF.Abs, bias=0.0, scale=1.0))
            em("scalar", lambda E, s=slot: E.activation(
                out=wsg[:], in_=wsrc[s], func=AF.Sign, bias=0.0, scale=1.0)
               .then_inc(ac.h, 1))
            acsg = ac.inc()
            wait("vector", ac, acsg)
            wait("vector", sh, wsh[slot])

            def _w(fn):
                em("vector", lambda E, f=fn: f(E).then_inc(dv.h, 1))
                n = dv.inc()
                wait("vector", dv, n)
                return n
            g16 = wtt.rearrange("p (g k) -> p g k", k=16)
            _w(lambda E: E.reduce_max(out=wbm, in_=g16, axis=AX.X))
            _w(lambda E: E.reciprocal(out=wrx, in_=wbm))
            rxb = wrx.unsqueeze(2).broadcast_to([128, 16, 16])
            scb = wsc.unsqueeze(2).broadcast_to([128, 16, 16])
            em("vector", lambda E: E.tensor_tensor(
                out=wxn.rearrange("p (g k) -> p g k", k=16), in0=g16, in1=rxb,
                op=ALU.mult))
            em("vector", lambda E: E.tensor_scalar(
                out=wi1[:], in0=wxn, scalar1=12.0, scalar2=None, op0=ALU.mult))
            em("vector", lambda E: E.tensor_scalar(
                out=wacc[:], in0=wi1[:], scalar1=4.0, scalar2=0.5,
                op0=ALU.min, op1=ALU.mult))
            em("vector", lambda E: E.tensor_scalar(
                out=wi1[:], in0=wxn, scalar1=6.0, scalar2=None, op0=ALU.mult))
            em("vector", lambda E: E.tensor_scalar(
                out=wtt[:], in0=wi1[:], scalar1=2.0, scalar2=2.0,
                op0=ALU.subtract, op1=ALU.min))
            em("vector", lambda E: E.scalar_tensor_tensor(
                out=wacc[:], in0=wtt[:], scalar=0.0, in1=wacc[:],
                op0=ALU.max, op1=ALU.add))
            em("vector", lambda E: E.tensor_scalar(
                out=wi1[:], in0=wxn, scalar1=3.0, scalar2=None, op0=ALU.mult))
            em("vector", lambda E: E.tensor_scalar(
                out=wsc, in0=wbm, scalar1=1.0 / 6.0, scalar2=1e-12,
                op0=ALU.mult, op1=ALU.max))
            em("vector", lambda E: E.tensor_scalar(
                out=wtt[:], in0=wi1[:], scalar1=2.0, scalar2=4.0,
                op0=ALU.mult, op1=ALU.subtract))
            em("vector", lambda E: E.scalar_tensor_tensor(
                out=wacc[:], in0=wtt[:], scalar=0.0, in1=wacc[:],
                op0=ALU.max, op1=ALU.add))
            em("vector", lambda E: E.tensor_tensor(
                out=wtt.rearrange("p (g k) -> p g k", k=16),
                in0=wacc[:].rearrange("p (g k) -> p g k", k=16), in1=scb,
                op=ALU.mult))
            em("vector", lambda E, s=slot: E.tensor_tensor(
                out=wdst[s], in0=wtt[:], in1=wsg[:], op=ALU.mult).then_inc(dv.h, 1))
            wslot[slot] = dv.inc()
            wait("scalar", dv, wslot[slot])
            wsh[slot] = dma("scalar", w_scr[t * 128:(t + 1) * 128, coff:coff + 256],
                            wdst[slot], sh)
            wst_state["sh_last"] = wsh[slot]

    ac_exp_last = {0: 0, 1: 0}
    dv_mask_b_last = {0: 0, 1: 0}
    ac_kevac = {0: 0, 1: 0}
    dv_vevac = {0: 0, 1: 0}
    dv_oacc = {}
    dv_lacc_prev = 0
    pe_pv_all_prev = 0
    nsu_run = limit if isinstance(limit, int) else NSU
    for su in range(nsu_run):
        kt, half = su // 2, su % 2
        key0 = kt * 1024 + half * 512
        wait("sync", pe, pe_hf_prev)
        n_hf = 0
        for dc in range(ECH):
            n_hf = dma("sync", hf[:, dc * 512:(dc + 1) * 512],
                       h_scr[key0:key0 + 512, dc * 128:(dc + 1) * 128], ld, transpose=True)
        if su > 0:
            emit_wchunks(17)
        wait("tensor", ld, n_hf)
        wait("tensor", dv, dv_oacc.get(14, 0))
        for ec in range(ECH):
            hsel = ec % 2
            wait("tensor", ac, ac_kevac[hsel])
            wait("tensor", dv, dv_vevac[hsel])
            for dc in range(ECH):
                em("tensor", lambda E, ec=ec, dc=dc, h=hsel: E.matmul(
                    out=ps_pvA[:, h * 512:(h + 1) * 512],
                    lhsT=wkT_bf[:, dc * D + ec * 128:dc * D + ec * 128 + 128],
                    rhs=hf[:, dc * 512:(dc + 1) * 512],
                    start=(dc == 0), stop=(dc == ECH - 1)).then_inc(pe.h, 1))
                pe.inc()
            pe_k = pe.n
            wait("scalar", pe, pe_k)
            wait("scalar", gp, max(dv_vevac[0], dv_vevac[1]))
            em("scalar", lambda E, ec=ec, h=hsel: E.activation(
                out=ev_scr[:, 0:512],
                in_=ps_pvA[:, h * 512:(h + 1) * 512], func=AF.Identity,
                bias=bkc_sb[:, ec:ec + 1], scale=1.0).then_inc(ac.h, 1))
            ac_kevac[hsel] = ac.inc()
            wait("vector", ac, ac_kevac[hsel])
            em("vector", lambda E, ec=ec: E.tensor_copy(
                out=k_sb[:, ec * 512:(ec + 1) * 512],
                in_=ev_scr[:, 0:512]).then_inc(gp.h, 1))
            dv_vevac[hsel] = gp.inc()
        for kc in range(4):
            for vh in range(2):
                hsel = (kc * 2 + vh) % 2
                wait("tensor", ac, ac_kevac[hsel])
                wait("tensor", dv, dv_vevac[hsel])
                for dc in range(ECH):
                    em("tensor", lambda E, kc=kc, vh=vh, dc=dc, h=hsel: E.matmul(
                        out=ps_pvA[:, h * 512:(h + 1) * 512],
                        lhsT=hf[:, dc * 512 + kc * 128:dc * 512 + kc * 128 + 128],
                        rhs=wvT_bf[:, dc * D + vh * 512:dc * D + (vh + 1) * 512],
                        start=(dc == 0), stop=(dc == ECH - 1)).then_inc(pe.h, 1))
                    pe.inc()
                pe_v = pe.n
                wait("scalar", pe, pe_v)
                wait("scalar", gp, max(dv_vevac[0], dv_vevac[1]))
                em("scalar", lambda E, h=hsel: E.activation(
                    out=ev_scr[:, 0:512],
                    in_=ps_pvA[:, h * 512:(h + 1) * 512], func=AF.Identity,
                    bias=0.0, scale=1.0).then_inc(ac.h, 1))
                ac_kevac[hsel] = ac.inc()
                wait("vector", ac, ac_kevac[hsel])
                em("vector", lambda E, kc=kc, vh=vh: E.tensor_tensor(
                    out=v_sb[:, kc * D + vh * 512:kc * D + (vh + 1) * 512],
                    in0=ev_scr[:, 0:512],
                    in1=bvb_sb[:, vh * 512:(vh + 1) * 512], op=ALU.add).then_inc(gp.h, 1))
                dv_vevac[hsel] = gp.inc()
        pe_hf_prev = pe.n

        gp_kv_done = gp.n
        wait("tensor", gp, gp_kv_done)
        ac_exp = dict(ac_exp_last)
        pair = 0
        dv_mask_b = dict(dv_mask_b_last)
        for kc in range(4):
            for qh in range(4):
                bank = pair % 2
                ps_s = ps_s0 if bank == 0 else ps_s1
                wait("tensor", ac, ac_exp[bank])
                for ec in range(ECH):
                    em("tensor", lambda E, kc=kc, qh=qh, ec=ec, ps=ps_s: E.matmul(
                        out=ps[:, 0:512],
                        lhsT=k_sb[:, ec * 512 + kc * 128:ec * 512 + kc * 128 + 128],
                        rhs=q_sb[:, ec * TOWN + qh * 512:ec * TOWN + (qh + 1) * 512],
                        start=(ec == 0), stop=(ec == ECH - 1)).then_inc(pe.h, 1))
                    pe.inc()
                pe_s = pe.n
                wait("scalar", pe, pe_s)
                wait("scalar", gp, max(dv_mask_b[0], dv_mask_b[1]))
                em("scalar", lambda E, b=bank, ps=ps_s: E.activation(
                    out=es_scr[:, 0:512],
                    in_=ps[:, 0:512], func=AF.Exp, bias=np.float32(NEG),
                    scale=1.0).then_inc(ac.h, 1))
                ac_exp[bank] = ac.inc()
                wait("vector", ac, ac_exp[bank])
                wait("vector", pe, pe_pv_all_prev)
                em("vector", lambda E, su=su, kc=kc, qh=qh: E.scalar_tensor_tensor(
                    out=pt_sb[:, kc * TOWN + qh * 512:kc * TOWN + (qh + 1) * 512],
                    in0=ramp_sb[:, qh * 512:(qh + 1) * 512],
                    scalar=tthr_sb[:, su * 4 + kc:su * 4 + kc + 1],
                    in1=es_scr[:, 0:512],
                    op0=ALU.is_ge, op1=ALU.mult).then_inc(gp.h, 1))
                dv_mask_b[bank] = gp.inc()
                pair += 1
        ac_exp_last = ac_exp
        dv_mask_b_last = dict(dv_mask_b)
        gp_mask_all = gp.n

        dv_oacc_prev = dict(dv_oacc)
        wait("tensor", gp, gp_mask_all)
        wait("tensor", dv, dv_lacc_prev)
        lbase = (su % 2) * 16
        pe_pv = {}
        for qs in range(16):
            buf = qs % 2
            ps_o = ps_pvA if buf == 0 else ps_pvB
            if qs >= 2:
                wait("tensor", dv, dv_oacc[qs - 2])
            elif qs == 1:
                wait("tensor", dv, dv_oacc_prev.get(15, 0))
            for kc in range(4):
                em("tensor", lambda E, qs=qs, kc=kc, ps=ps_o: E.matmul(
                    out=ps[:, 0:512],
                    lhsT=pt_sb[:, kc * TOWN + qs * 128:kc * TOWN + qs * 128 + 128],
                    rhs=v_sb[:, kc * D:kc * D + 512],
                    start=(kc == 0), stop=(kc == 3)).then_inc(pe.h, 1))
                pe.inc()
                em("tensor", lambda E, qs=qs, kc=kc, ps=ps_o: E.matmul(
                    out=ps[:, 512:1024],
                    lhsT=pt_sb[:, kc * TOWN + qs * 128:kc * TOWN + qs * 128 + 128],
                    rhs=v_sb[:, kc * D + 512:kc * D + 1024],
                    start=(kc == 0), stop=(kc == 3)).then_inc(pe.h, 1))
                pe.inc()
                em("tensor", lambda E, qs=qs, kc=kc, lc=lbase + qs: E.matmul(
                    out=ps_l[:, lc:lc + 1],
                    lhsT=pt_sb[:, kc * TOWN + qs * 128:kc * TOWN + qs * 128 + 128],
                    rhs=ones1_bf[:, 0:1],
                    start=(kc == 0), stop=(kc == 3)).then_inc(pe.h, 1))
                pe.inc()
            pe_pv[qs] = pe.n
            wait("vector", pe, pe_pv[qs])
            if su == 0 and qs == 0:
                wait("vector", dv, dv_zero)
            em("vector", lambda E, qs=qs, ps=ps_o: E.tensor_tensor(
                out=o_sb[:, qs * D:(qs + 1) * D], in0=o_sb[:, qs * D:(qs + 1) * D],
                in1=ps[:, 0:1024], op=ALU.add).then_inc(dv.h, 1))
            dv_oacc[qs] = dv.inc()
        pe_pv_all_prev = pe.n
        wait("vector", pe, pe_pv[15])
        em("vector", lambda E, lb=lbase: E.tensor_tensor(
            out=l_sb[:, 0:16], in0=l_sb[:, 0:16], in1=ps_l[:, lb:lb + 16],
            op=ALU.add).then_inc(dv.h, 1))
        dv_lacc_prev = dv.inc()

    if isinstance(limit, int):
        finish_debug([o_sb[:, 0:15 * D], l_sb[:], stat_sb[:]])
        return finalize()
    emit_wchunks(len(wchunks))
    dv_w_final = max(wslot[0], wslot[1])
    sh_w_final = wst_state["sh_last"]

    wait("vector", dv, dv_lacc_prev)
    em("vector", lambda E: E.reciprocal(out=rinv_sb[:], in_=l_sb[:]).then_inc(dv.h, 1))
    dv_rinv = dv.inc()
    wait("vector", dv, dv_rinv)
    hbf = hb[:].bitcast(F32)
    sh_xnew_prev = sh_hstore_prev
    dv_fold = {}
    for r in range(16):
        if r == 0:
            wait("sync", dv, dv_w_final)
            wait("vector", sh, sh_w_final)
        else:
            wait("sync", dv, dv_fold.get(r - 1, 0))
        n_x = dma("sync", xb[:], xq_in[r * 128:(r + 1) * 128, :], ld)
        wait("vector", ld, n_x)
        wait("vector", dv, dv_rinv)
        wait("vector", sh, sh_xnew_prev)
        em("vector", lambda E, r=r: E.scalar_tensor_tensor(
            out=hbf, in0=o_sb[:, r * D:(r + 1) * D],
            scalar=rinv_sb[:, r:r + 1], in1=xb[:],
            op0=ALU.mult, op1=ALU.add).then_inc(dv.h, 1))
        dv_fold[r] = dv.inc()
        if r == 15:
            em("vector", lambda E, r=r: E.reduce_sum(
                out=stat_sb[:, r:r + 1], in_=hbf, axis=AX.X).then_inc(dv.h, 1))
            wait("vector", dv, dv.inc())
        else:
            em("vector", lambda E, r=r: E.reduce_sum(
                out=stat_sb[:, r:r + 1], in_=hbf, axis=AX.X))
        wait("scalar", dv, dv_fold[r])
        em("scalar", lambda E, r=r: E.memzero(stat_sb[:, 16 + r:17 + r]))
        em("scalar", lambda E, r=r: E.memzero(stat_sb[:, 32 + r:33 + r]))
        em("scalar", lambda E, r=r: E.activation(
            out=ps_sq[:], in_=hbf[:, 0:512], func=AF.Square, bias=0.0, scale=1.0,
            accum_out=stat_sb[:, 16 + r:17 + r]))
        em("scalar", lambda E, r=r: E.activation(
            out=ps_sq[:], in_=hbf[:, 512:1024], func=AF.Square, bias=0.0, scale=1.0,
            accum_out=stat_sb[:, 32 + r:33 + r]).then_inc(ac.h, 1))
        ac.inc()
        sh_xnew_prev = dma("scalar", xnew_scr[r * 128:(r + 1) * 128, :], hbf, sh)
    ac_sq_done = ac.n
    wait("vector", ac, ac_sq_done)

    def _stepf(fn):
        em("vector", lambda E, f=fn: f(E).then_inc(dv.h, 1))
        n = dv.inc()
        wait("vector", dv, n)
        return n
    _stepf(lambda E: E.tensor_scalar(
        out=stat_sb[:, 48:64], in0=stat_sb[:, 0:16], scalar1=1.0 / D, scalar2=None,
        op0=ALU.mult))
    _stepf(lambda E: E.tensor_tensor(
        out=stat_sb[:, 16:32], in0=stat_sb[:, 16:32], in1=stat_sb[:, 32:48], op=ALU.add))
    _stepf(lambda E: E.tensor_scalar(
        out=stat_sb[:, 16:32], in0=stat_sb[:, 16:32], scalar1=1.0 / D, scalar2=None,
        op0=ALU.mult))
    _stepf(lambda E: E.tensor_tensor(
        out=stat_sb[:, 32:48], in0=stat_sb[:, 48:64], in1=stat_sb[:, 48:64], op=ALU.mult))
    dv_varf = _stepf(lambda E: E.tensor_tensor(
        out=stat_sb[:, 16:32], in0=stat_sb[:, 16:32], in1=stat_sb[:, 32:48],
        op=ALU.subtract))
    wait("scalar", dv, dv_varf)
    em("scalar", lambda E: E.activation(
        out=stat_sb[:, 32:48], in_=stat_sb[:, 16:32], func=AF.Ln, bias=np.float32(1e-5), scale=1.0))
    em("scalar", lambda E: E.activation(
        out=rstdf_sb[:], in_=stat_sb[:, 32:48], func=AF.Exp, bias=0.0, scale=np.float32(-0.5))
       .then_inc(ac.h, 1))
    ac_rstdf = ac.inc()
    wait("vector", ac, ac_rstdf)
    _stepf(lambda E: E.tensor_tensor(
        out=negmf_sb[:], in0=stat_sb[:, 48:64], in1=rstdf_sb[:], op=ALU.mult))
    dv_negmf = _stepf(lambda E: E.tensor_scalar(
        out=negmf_sb[:], in0=negmf_sb[:], scalar1=-1.0, scalar2=None,
        op0=ALU.mult))
    dv_fold_done = dv.n
    ac_fold_done = ac.n

    if limit == "fold":
        finish_debug([rstdf_sb[:], negmf_sb[:], rinv_sb[:], l_sb[:]])
        return finalize()
    ps_scope.close()
    bstack.close()

    cstack = ExitStack()
    C = cstack.enter_context
    wgq_f = C(nc.sbuf_tensor("wgq_f", [128, ECH * II], BF16))
    wuq_f = C(nc.sbuf_tensor("wuq_f", [128, ECH * II], BF16))
    wdq_f = C(nc.sbuf_tensor("wdq_f", [128, ICH * D], BF16))
    qt_xn = C(nc.sbuf_tensor("qt_xn", [128, D], F32))
    qt_i1 = C(nc.sbuf_tensor("qt_i1", [128, D], I32))
    qt_acc = C(nc.sbuf_tensor("qt_acc", [128, D], F32))
    qt_t = C(nc.sbuf_tensor("qt_t", [128, D], F32))
    qt_sg = C(nc.sbuf_tensor("qt_sg", [128, D], BF16))
    qt_bm = C(nc.sbuf_tensor("qt_bm", [128, 64], F32))
    qt_sc = C(nc.sbuf_tensor("qt_sc", [128, 64], F32))
    qt_rx = C(nc.sbuf_tensor("qt_rx", [128, 64], F32))
    wq_bf_sb = C(nc.sbuf_tensor("wq_bf_sb", [128, 2 * D], BF16))
    wst = C(nc.sbuf_tensor("wst", [128, 2 * D], F32))
    xc = C(nc.sbuf_tensor("xc", [128, 2 * D], F32))
    h2_sb = C(nc.sbuf_tensor("h2_sb", [128, D], F32))
    xq_bf = C(nc.sbuf_tensor("xq_bf", [128, D], BF16))
    xqf_sb = C(nc.sbuf_tensor("xqf_sb", [128, ECH * 128], BF16))
    hq_bf = C(nc.sbuf_tensor("hq_bf", [128, 512], BF16))
    hqf_sb = C(nc.sbuf_tensor("hqf_sb", [128, 8 * 128], BF16))
    sg_sb = C(nc.sbuf_tensor("sg_sb", [128, 512], F32))
    outb_sb = C(nc.sbuf_tensor("outb_sb", [128, D], F32))

    ps_g = C(nc.psum_tensor("ps_g", [128, 512], F32))
    ps_g2 = C(nc.psum_tensor("ps_g2", [128, 512], F32))
    ps_u = C(nc.psum_tensor("ps_u", [128, 512], F32))
    ps_u2 = C(nc.psum_tensor("ps_u2", [128, 512], F32))
    ps_d = C(nc.psum_tensor("ps_d", [128, D], F32))
    ps_tA = C(nc.psum_tensor("ps_tA", [128, 128], BF16))
    ps_tB = C(nc.psum_tensor("ps_tB", [128, 128], BF16))


    qstate = {"dst_dv": 0}

    def quant(src, dst, F, blocks, wait_pe_n=0, wait_src_dv=0, wait_src_ac=0,
              wait_src_ld=0):
        g16 = src.rearrange("p (g k) -> p g k", k=16)
        if wait_pe_n:
            wait("vector", pe, wait_pe_n)
        if wait_src_ac:
            wait("vector", ac, wait_src_ac)
        if wait_src_dv:
            wait("vector", dv, wait_src_dv)
        if wait_src_ld:
            wait("vector", ld, wait_src_ld)
        em("vector", lambda E: E.reduce_max(
            out=qt_bm[:, 0:blocks], in_=g16, axis=AX.X,
            apply_absolute_value=True).then_inc(dv.h, 1))
        dvq = dv.inc()
        wait("vector", dv, dvq)
        em("vector", lambda E: E.tensor_scalar(
            out=qt_sc[:, 0:blocks], in0=qt_bm[:, 0:blocks], scalar1=1.0 / 6.0,
            scalar2=1e-12, op0=ALU.mult, op1=ALU.max).then_inc(dv.h, 1))
        dvq = dv.inc()
        wait("vector", dv, dvq)
        em("vector", lambda E: E.reciprocal(
            out=qt_rx[:, 0:blocks], in_=qt_sc[:, 0:blocks]).then_inc(dv.h, 1))
        dvq = dv.inc()
        wait("vector", dv, dvq)
        rxb = qt_rx[:, 0:blocks].unsqueeze(2).broadcast_to([128, blocks, 16])
        scb = qt_sc[:, 0:blocks].unsqueeze(2).broadcast_to([128, blocks, 16])
        wait("scalar", dv, qstate["dst_dv"])
        if wait_src_dv:
            wait("scalar", dv, wait_src_dv)
        if wait_src_ac:
            wait("scalar", ac, wait_src_ac)
        if wait_src_ld:
            wait("scalar", ld, wait_src_ld)
        em("scalar", lambda E: E.activation(
            out=qt_t[:, 0:F], in_=src, func=AF.Abs, bias=0.0, scale=1.0))
        em("scalar", lambda E: E.activation(
            out=qt_sg[:, 0:F], in_=src, func=AF.Sign, bias=0.0, scale=1.0).then_inc(ac.h, 1))
        acsg = ac.inc()
        wait("vector", ac, acsg)
        em("vector", lambda E: E.tensor_tensor(
            out=qt_xn[:, 0:F].rearrange("p (g k) -> p g k", k=16),
            in0=qt_t[:, 0:F].rearrange("p (g k) -> p g k", k=16), in1=rxb,
            op=ALU.mult).then_inc(dv.h, 1))
        dv.inc()
        em("vector", lambda E: E.tensor_scalar(
            out=qt_i1[:, 0:F], in0=qt_xn[:, 0:F], scalar1=2.0, scalar2=None,
            op0=ALU.mult))
        em("vector", lambda E: E.tensor_scalar(
            out=qt_acc[:, 0:F], in0=qt_i1[:, 0:F], scalar1=4.0, scalar2=0.5,
            op0=ALU.min, op1=ALU.mult))
        em("vector", lambda E: E.tensor_scalar(
            out=qt_i1[:, 0:F], in0=qt_xn[:, 0:F], scalar1=1.0, scalar2=None,
            op0=ALU.mult))
        em("vector", lambda E: E.tensor_scalar(
            out=qt_t[:, 0:F], in0=qt_i1[:, 0:F], scalar1=2.0, scalar2=2.0,
            op0=ALU.subtract, op1=ALU.min))
        em("vector", lambda E: E.scalar_tensor_tensor(
            out=qt_acc[:, 0:F], in0=qt_t[:, 0:F], scalar=0.0, in1=qt_acc[:, 0:F],
            op0=ALU.max, op1=ALU.add))
        em("vector", lambda E: E.tensor_scalar(
            out=qt_i1[:, 0:F], in0=qt_xn[:, 0:F], scalar1=0.5, scalar2=None,
            op0=ALU.mult))
        em("vector", lambda E: E.tensor_scalar(
            out=qt_t[:, 0:F], in0=qt_i1[:, 0:F], scalar1=2.0, scalar2=4.0,
            op0=ALU.mult, op1=ALU.subtract))
        em("vector", lambda E: E.scalar_tensor_tensor(
            out=qt_acc[:, 0:F], in0=qt_t[:, 0:F], scalar=0.0, in1=qt_acc[:, 0:F],
            op0=ALU.max, op1=ALU.add))
        em("vector", lambda E: E.tensor_tensor(
            out=qt_t[:, 0:F].rearrange("p (g k) -> p g k", k=16),
            in0=qt_acc[:, 0:F].rearrange("p (g k) -> p g k", k=16), in1=scb, op=ALU.mult))
        wait("vector", ac, acsg)
        em("vector", lambda E, dst=dst: E.tensor_tensor(
            out=dst, in0=qt_t[:, 0:F], in1=qt_sg[:, 0:F], op=ALU.mult).then_inc(dv.h, 1))
        qstate["dst_dv"] = dv.inc()
        return qstate["dst_dv"]

    wait("sync", sh, sh_w_final)
    for dc in range(ECH):
        dma("sync", wgq_f[:, dc * II:(dc + 1) * II],
            wgq_scr[:, dc * 128:(dc + 1) * 128], ld, transpose=True)
        dma("sync", wuq_f[:, dc * II:(dc + 1) * II],
            wuq_scr[:, dc * 128:(dc + 1) * 128], ld, transpose=True)
    n_wf = 0
    for icc in range(ICH):
        n_wf = dma("sync", wdq_f[:, icc * D:(icc + 1) * D],
                   wdq_scr[:, icc * 128:(icc + 1) * 128], ld, transpose=True)

    NCC = [512, 512, 512, 512, 512, 256]
    dv_out_prev = 0
    pe_xt_prev = 0
    pe_hqt_prev = 0
    dv_tc = {"A": 0, "B": 0}
    h2x = wst
    ac_h2 = {}
    dv_xq = {}
    n_xc = {}

    def emit_ln_tile(r):
        if r >= 16:
            return
        xsel = r % 2
        if r >= 2:
            wait("sync", dv, dv_out_prev)
        n_xc[r] = dma("sync", xc[:, xsel * D:(xsel + 1) * D],
                      xnew_scr[r * 128:(r + 1) * 128, :], ld)
        wait("scalar", ld, n_xc[r])
        wait("scalar", dv, dv_negmf)
        em("scalar", lambda E, r=r, xs=xsel: E.activation(
            out=h2x[:, xs * D:(xs + 1) * D], in_=xc[:, xs * D:(xs + 1) * D],
            func=AF.Identity,
            bias=negmf_sb[:, r:r + 1], scale=rstdf_sb[:, r:r + 1]).then_inc(ac.h, 1))
        ac_h2[r] = ac.inc()

    def emit_xquant_tile(r):
        if r >= 16:
            return
        xsel = r % 2
        dv_xq[r] = quant(h2x[:, xsel * D:(xsel + 1) * D], xq_bf[:], D, 64,
                         wait_pe_n=pe_xt_prev, wait_src_ac=ac_h2[r])

    emit_ln_tile(0)
    emit_xquant_tile(0)
    for r in range(16):
        xsel = r % 2
        wait("tensor", dv, dv_xq[r])
        for dc in range(ECH):
            key = "A" if dc % 2 == 0 else "B"
            pst = ps_tA if dc % 2 == 0 else ps_tB
            wait("tensor", dv, dv_tc[key])
            em("tensor", lambda E, dc=dc, pst=pst: E.transpose(
                out=pst[:], in_=xq_bf[:, dc * 128:(dc + 1) * 128], identity=ident_sb[:])
               .then_inc(pe.h, 1))
            pe_t = pe.inc()
            wait("vector", pe, pe_t)
            em("vector", lambda E, dc=dc, pst=pst: E.tensor_copy(
                out=xqf_sb[:, dc * 128:(dc + 1) * 128], in_=pst[:]).then_inc(dv.h, 1))
            dv_tc[key] = dv.inc()
        pe_xt_prev = pe.n
        dv_xt = dv_tc["B"]

        def emit_gu(ci):
            csz = NCC[ci]
            off = 512 * ci
            gsel = ci % 2
            psg = ps_g if gsel == 0 else ps_g2
            psu = ps_u if gsel == 0 else ps_u2
            wait("tensor", ac, ac_sil_prev[gsel])
            wait("tensor", dv, dv_h_prev[gsel])
            if ci == 0:
                wait("tensor", dv, dv_xt)
                wait("tensor", ld, n_wf)
            for dc in range(ECH):
                em("tensor", lambda E, dc=dc, off=off, csz=csz, psg=psg: E.matmul(
                    out=psg[:, 0:csz], lhsT=xqf_sb[:, dc * 128:(dc + 1) * 128],
                    rhs=wgq_f[:, dc * II + off:dc * II + off + csz],
                    start=(dc == 0), stop=(dc == ECH - 1)).then_inc(pe.h, 1))
                pe.inc()
            for dc in range(ECH):
                em("tensor", lambda E, dc=dc, off=off, csz=csz, psu=psu: E.matmul(
                    out=psu[:, 0:csz], lhsT=xqf_sb[:, dc * 128:(dc + 1) * 128],
                    rhs=wuq_f[:, dc * II + off:dc * II + off + csz],
                    start=(dc == 0), stop=(dc == ECH - 1)).then_inc(pe.h, 1))
                pe.inc()
            return pe.n

        ac_sil_prev = {0: 0, 1: 0}
        dv_h_prev = {0: 0, 1: 0}
        dv_h_last = 0
        icc0 = 0
        pe_g = {0: emit_gu(0)}
        for ci, csz in enumerate(NCC):
            gsel = ci % 2
            psg = ps_g if gsel == 0 else ps_g2
            psu = ps_u if gsel == 0 else ps_u2
            wait("scalar", pe, pe_g[ci])
            wait("scalar", dv, dv_h_last)
            em("scalar", lambda E, csz=csz, psg=psg: E.activation(
                out=sg_sb[:, 0:csz], in_=psg[:, 0:csz], func=AF.Silu, bias=0.0,
                scale=1.0).then_inc(ac.h, 1))
            ac_sil = ac.inc()
            ac_sil_prev[gsel] = ac_sil
            wait("vector", ac, ac_sil)
            wait("vector", pe, pe_g[ci])
            em("vector", lambda E, csz=csz, psu=psu: E.tensor_tensor(
                out=h2_sb[:, 0:csz], in0=sg_sb[:, 0:csz], in1=psu[:, 0:csz],
                op=ALU.mult).then_inc(dv.h, 1))
            dv_h = dv.inc()
            dv_h_prev[gsel] = dv_h
            dv_h_last = dv_h
            dv_hq = quant(h2_sb[:, 0:csz], hq_bf[:, 0:csz], csz, csz // 16,
                          wait_pe_n=pe_hqt_prev, wait_src_dv=dv_h)
            if ci == 1:
                emit_ln_tile(r + 1)
            if ci == 2:
                emit_xquant_tile(r + 1)
            if ci + 1 < len(NCC):
                pe_g[ci + 1] = emit_gu(ci + 1)
            wait("tensor", dv, dv_hq)
            nic = csz // 128
            for j in range(nic):
                icc = icc0 + j
                key = "A" if icc % 2 == 0 else "B"
                pst = ps_tA if icc % 2 == 0 else ps_tB
                wait("tensor", dv, dv_tc[key])
                em("tensor", lambda E, j=j, pst=pst: E.transpose(
                    out=pst[:], in_=hq_bf[:, j * 128:(j + 1) * 128], identity=ident_sb[:])
                   .then_inc(pe.h, 1))
                pe_t = pe.inc()
                wait("vector", pe, pe_t)
                em("vector", lambda E, sl=icc % 8, pst=pst: E.tensor_copy(
                    out=hqf_sb[:, sl * 128:(sl + 1) * 128], in_=pst[:])
                   .then_inc(dv.h, 1))
                dv_tc[key] = dv.inc()
            pe_hqt_prev = pe.n
            wait("tensor", dv, dv_tc["A"])
            wait("tensor", dv, dv_tc["B"])
            if ci == 0 and r >= 1:
                wait("tensor", dv, dv_out_prev)
            for j in range(nic):
                icc = icc0 + j
                for hh2 in range(2):
                    em("tensor", lambda E, icc=icc, sl=icc % 8, hh=hh2: E.matmul(
                        out=ps_d[:, hh * 512:(hh + 1) * 512],
                        lhsT=hqf_sb[:, sl * 128:(sl + 1) * 128],
                        rhs=wdq_f[:, icc * D + hh * 512:icc * D + (hh + 1) * 512],
                        start=(icc == 0), stop=(icc == ICH - 1)).then_inc(pe.h, 1))
                    pe.inc()
            icc0 += nic
        pe_down = pe.n
        wait("vector", pe, pe_down)
        if r >= 1:
            wait("vector", so, r * 16)
        em("vector", lambda E, r=r, xs=xsel: E.tensor_tensor(
            out=outb_sb[:, 0:D], in0=ps_d[:, 0:D],
            in1=xc[:, xs * D:(xs + 1) * D], op=ALU.add).then_inc(dv.h, 1))
        dv_out_prev = dv.inc()
        wait("scalar", dv, dv_out_prev)
        dma("scalar", out_t[r * 128:(r + 1) * 128, :], outb_sb[:, 0:D], so)

    wait("sync", so, so.n)
    return finalize()


def _host_tables(j):
    tt = np.zeros((128, 64), dtype=np.float32)
    p = np.arange(128, dtype=np.float32)
    for su in range(NSU):
        kt, half = su // 2, su % 2
        for kc in range(4):
            t = kt * 1024 + half * 512 + kc * 128 - j * TOWN
            t = max(-128, min(2176, t))
            tt[:, su * 4 + kc] = p + t
    return tt


def kernel(L0_attn_X, L0_attn_Wq, L0_attn_bq, L0_attn_Wk, L0_attn_bk,
           L0_attn_Wv, L0_attn_bv, L0_ffn_w_gate, L0_ffn_w_up, L0_ffn_w_down):
    X = np.ascontiguousarray(np.asarray(L0_attn_X, dtype=np.float32))
    wqT = np.ascontiguousarray(np.asarray(L0_attn_Wq, np.float32).T)
    wkT = np.ascontiguousarray(np.asarray(L0_attn_Wk, np.float32).T)
    wvT = np.ascontiguousarray(np.asarray(L0_attn_Wv, np.float32).T)
    bq = np.ascontiguousarray(np.asarray(L0_attn_bq, np.float32))
    bk = np.ascontiguousarray(np.asarray(L0_attn_bk, np.float32))
    bv = np.ascontiguousarray(np.asarray(L0_attn_bv, np.float32))
    wg = np.ascontiguousarray(np.asarray(L0_ffn_w_gate, np.float32))
    wu = np.ascontiguousarray(np.asarray(L0_ffn_w_up, np.float32))
    wd = np.ascontiguousarray(np.asarray(L0_ffn_w_down, np.float32))

    ramp = np.broadcast_to(np.arange(TOWN, dtype=np.float32)[None, :], (128, TOWN)).copy()
    ident = np.eye(128, dtype=ml_dtypes.bfloat16)

    if "nc" not in _CACHE:
        _CACHE["nc"] = build_program()
    nc = _CACHE["nc"]

    in_maps = []
    for c in range(8):
        b, j = c // 4, c % 4
        in_maps.append({
            "xq": np.ascontiguousarray(X[b, j * TOWN:(j + 1) * TOWN, :]),
            "xkv": np.ascontiguousarray(X[b]),
            "wqT": wqT, "wkT": wkT, "wvT": wvT,
            "bq": bq, "bk": bk, "bv": bv,
            "wg": wg, "wu": wu, "wd": wd,
            "ramp": ramp, "tthr": _host_tables(j), "ident": ident,
        })
    res = run_bass_kernel_spmd(nc, in_maps, core_ids=list(range(8)))
    out = np.empty((2, NKV, D), dtype=np.float32)
    for c in range(8):
        b, j = c // 4, c % 4
        out[b, j * TOWN:(j + 1) * TOWN, :] = np.asarray(res.results[c]["out"])
    return out


# version 17
# speedup vs baseline: 1.1572x; 1.0036x over previous
import numpy as np
import ml_dtypes
from contextlib import ExitStack

import concourse.bass as bass
import concourse.mybir as mybir
from concourse.bass_utils import run_bass_kernel_spmd

F32 = mybir.dt.float32
BF16 = mybir.dt.bfloat16
I32 = mybir.dt.int32
AF = mybir.ActivationFunctionType
ALU = mybir.AluOpType
AX = mybir.AxisListType

D = 1024
ECH = D // 128
TOWN = 2048
NKV = 8192
NSU = 16
II = 2816
ICH = II // 128
NEG = -8.0

_CACHE = {}


class Ctr:
    def __init__(self, handle):
        self.h = handle
        self.n = 0

    def inc(self, by=1):
        self.n += by
        return self.n


def build_program(limit=None):
    nc = bass.Bass("TRN2", target_bir_lowering=False)

    xq_in = nc.dram_tensor("xq", [TOWN, D], F32, kind="ExternalInput").ap()
    xkv_in = nc.dram_tensor("xkv", [NKV, D], F32, kind="ExternalInput").ap()
    wqT_in = nc.dram_tensor("wqT", [D, D], F32, kind="ExternalInput").ap()
    wkT_in = nc.dram_tensor("wkT", [D, D], F32, kind="ExternalInput").ap()
    wvT_in = nc.dram_tensor("wvT", [D, D], F32, kind="ExternalInput").ap()
    bq_in = nc.dram_tensor("bq", [D], F32, kind="ExternalInput").ap()
    bk_in = nc.dram_tensor("bk", [D], F32, kind="ExternalInput").ap()
    bv_in = nc.dram_tensor("bv", [D], F32, kind="ExternalInput").ap()
    wg_in = nc.dram_tensor("wg", [II, D], F32, kind="ExternalInput").ap()
    wu_in = nc.dram_tensor("wu", [II, D], F32, kind="ExternalInput").ap()
    wd_in = nc.dram_tensor("wd", [D, II], F32, kind="ExternalInput").ap()
    ramp_in = nc.dram_tensor("ramp", [128, TOWN], F32, kind="ExternalInput").ap()
    tthr_in = nc.dram_tensor("tthr", [128, 64], F32, kind="ExternalInput").ap()
    ident_in = nc.dram_tensor("ident", [128, 128], BF16, kind="ExternalInput").ap()
    out_t = nc.dram_tensor("out", [TOWN, D], F32, kind="ExternalOutput").ap()

    h_scr = nc.dram_tensor("h_scr", [NKV, D], BF16, kind="Internal").ap()
    hq_scr = nc.dram_tensor("hq_scr", [TOWN, D], BF16, kind="Internal").ap()
    xnew_scr = nc.dram_tensor("xnew_scr", [TOWN, D], F32, kind="Internal").ap()
    wq_mine = nc.dram_tensor("wq_mine", [1152, D], BF16, kind="Internal").ap()
    wq_all1d = nc.dram_tensor("wq_all", [9216 * D], BF16, kind="Internal").ap()
    wq_all = wq_all1d.rearrange("(r c) -> r c", c=D)
    wgq_scr = wq_all[0:II, :]
    wuq_scr = wq_all[II:2 * II, :]
    wdq_scr = wq_all1d[2 * II * D:(2 * II + D) * II + 0].rearrange(
        "(r c) -> r c", c=II) if False else         wq_all1d[2 * II * D:2 * II * D + D * II].rearrange("(r c) -> r c", c=II)

    ops = {k: [] for k in ("sync", "vector", "tensor", "scalar", "gpsimd")}

    def em(eng, fn):
        ops[eng].append(fn)

    def finalize():
        with nc.Block() as block:
            @block.sync
            def _(E):
                for f in ops["sync"]:
                    f(E)

            @block.vector
            def _(E):
                for f in ops["vector"]:
                    f(E)

            @block.tensor
            def _(E):
                for f in ops["tensor"]:
                    f(E)

            @block.scalar
            def _(E):
                for f in ops["scalar"]:
                    f(E)

            @block.gpsimd
            def _(E):
                for f in ops["gpsimd"]:
                    f(E)
        return nc

    pstack = ExitStack()
    bstack = ExitStack()
    P = pstack.enter_context
    B = bstack.enter_context

    ld = Ctr(P(nc.semaphore("ld")))
    sh = Ctr(P(nc.semaphore("sh")))
    so = Ctr(P(nc.semaphore("so")))
    pe = Ctr(P(nc.semaphore("pe")))
    dv = Ctr(P(nc.semaphore("dv")))
    gp = Ctr(P(nc.semaphore("gp")))
    wcol = Ctr(P(nc.semaphore("wcol")))
    ac = Ctr(P(nc.semaphore("ac")))

    def dma(eng, out, in_, ctr, transpose=False):
        n = ctr.inc(16)
        if transpose:
            em(eng, lambda E, o=out, i=in_, c=ctr.h:
               E.dma_start(out=o, in_=i, transpose=True).then_inc(c, 16))
        else:
            em(eng, lambda E, o=out, i=in_, c=ctr.h:
               E.dma_start(out=o, in_=i).then_inc(c, 16))
        return n

    def wait(eng, ctr, n):
        if n > 0:
            em(eng, lambda E, c=ctr.h, v=n: E.wait_ge(c, v))

    tthr_sb = P(nc.sbuf_tensor("tthr_sb", [128, 64], F32))
    stat_sb = P(nc.sbuf_tensor("stat_sb", [128, 256], F32))
    rstd_all = P(nc.sbuf_tensor("rstd_all", [128, 80], F32))
    negm_all = P(nc.sbuf_tensor("negm_all", [128, 80], F32))
    bqc_sb = P(nc.sbuf_tensor("bqc_sb", [128, ECH], F32))
    bkc_sb = P(nc.sbuf_tensor("bkc_sb", [128, ECH], F32))
    bvb_sb = P(nc.sbuf_tensor("bvb_sb", [128, D], F32))
    ones1_bf = P(nc.sbuf_tensor("ones1_bf", [128, 1], BF16))
    rstdf_sb = P(nc.sbuf_tensor("rstdf_sb", [128, 16], F32))
    negmf_sb = P(nc.sbuf_tensor("negmf_sb", [128, 16], F32))
    rinv_sb = P(nc.sbuf_tensor("rinv_sb", [128, 16], F32))
    brow = P(nc.sbuf_tensor("brow", [1, 3 * D], F32))
    onef = P(nc.sbuf_tensor("onef", [1, 1], F32))
    onerow = P(nc.sbuf_tensor("onerow", [1, 128], F32))
    ident_sb = P(nc.sbuf_tensor("ident_sb", [128, 128], BF16))

    wkT_bf = B(nc.sbuf_tensor("wkT_bf", [128, ECH * D], BF16))
    wvT_bf = B(nc.sbuf_tensor("wvT_bf", [128, ECH * D], BF16))
    q_sb = B(nc.sbuf_tensor("q_sb", [128, ECH * TOWN], BF16))
    o_sb = B(nc.sbuf_tensor("o_sb", [128, 16 * D], F32))
    l_sb = B(nc.sbuf_tensor("l_sb", [128, 16], F32))
    xb = B(nc.sbuf_tensor("xb", [128, D], F32))
    hb = B(nc.sbuf_tensor("hb", [128, 2 * D], BF16))
    hf = B(nc.sbuf_tensor("hf", [128, ECH * 512], BF16))
    k_sb = B(nc.sbuf_tensor("k_sb", [128, ECH * 512], BF16))
    v_sb = B(nc.sbuf_tensor("v_sb", [128, 4 * D], BF16))
    pt_sb = B(nc.sbuf_tensor("pt_sb", [128, 4 * TOWN], BF16))
    ramp_sb = B(nc.sbuf_tensor("ramp_sb", [128, TOWN], F32))
    ev_scr = B(nc.sbuf_tensor("ev_scr", [128, 512], F32))
    es_scr = B(nc.sbuf_tensor("es_scr", [128, 512], F32))

    wstage = o_sb[:, 0:ECH * D]
    wq_bf = pt_sb

    ps_scope = ExitStack()
    pp = ps_scope.enter_context
    ps_pvA = pp(nc.psum_tensor("ps_pvA", [128, D], F32))
    ps_pvB = pp(nc.psum_tensor("ps_pvB", [128, D], F32))
    ps_s0 = pp(nc.psum_tensor("ps_s0", [128, 512], F32))
    ps_s1 = pp(nc.psum_tensor("ps_s1", [128, 512], F32))
    ps_l = pp(nc.psum_tensor("ps_l", [128, 512], F32))
    ps_sq = pp(nc.psum_tensor("ps_sq", [128, 512], F32))

    def _load_act(idx):
        def f(E):
            E.add_instruction(mybir.InstLoadActFuncSet(
                name=nc.get_next_instruction_name(),
                act_func_set_id=idx, ins=[], outs=[]))
        return f

    dma("sync", ramp_sb[:], ramp_in, ld)
    dma("sync", tthr_sb[:], tthr_in, ld)
    dma("sync", ident_sb[:], ident_in, ld)
    dma("sync", brow[0:1, 0:D], bq_in[None, :], ld)
    dma("sync", brow[0:1, D:2 * D], bk_in[None, :], ld)
    n_brow = dma("sync", brow[0:1, 2 * D:3 * D], bv_in[None, :], ld)

    em("vector", lambda E: E.memset(onef[:], 1.0))
    em("vector", lambda E: E.memset(onerow[:], 1.0))
    em("vector", lambda E: E.memset(ones1_bf[:], 1.0).then_inc(dv.h, 1))
    dv_ones = dv.inc()

    dv_wcast = 0
    for name, win, dst in (("q", wqT_in, None), ("k", wkT_in, wkT_bf), ("v", wvT_in, wvT_bf)):
        n_w = dma("sync", wstage.rearrange("p (c e) -> p c e", e=D),
                  win.rearrange("(c p) e -> p c e", p=128), ld)
        wait("vector", ld, n_w)
        if name == "q":
            em("vector", lambda E: E.tensor_copy(out=wq_bf[:, 0:ECH * D], in_=wstage)
               .then_inc(dv.h, 1))
        else:
            em("vector", lambda E, d=dst: E.tensor_copy(out=d[:], in_=wstage)
               .then_inc(dv.h, 1))
        dv_wcast = dv.inc()
        wait("sync", dv, dv_wcast)
    em("vector", lambda E: E.memset(o_sb[:], 0.0))
    em("vector", lambda E: E.memset(l_sb[:], 0.0).then_inc(dv.h, 1))
    dv_zero = dv.inc()

    wait("tensor", ld, n_brow)
    wait("tensor", dv, dv_ones)
    for bi in range(2):
        for ecj in range(ECH):
            em("tensor", lambda E, b=bi, c=ecj: E.matmul(
                out=ps_sq[:, b * 8 + c:b * 8 + c + 1],
                lhsT=brow[0:1, b * D + c * 128:b * D + (c + 1) * 128],
                rhs=onef[0:1, 0:1], start=True, stop=True).then_inc(pe.h, 1))
            n = pe.inc()
            wait("tensor", pe, n)
    pe_bias = pe.n
    wait("scalar", pe, pe_bias)
    em("scalar", lambda E: E.activation(out=bqc_sb[:], in_=ps_sq[:, 0:8],
                                        func=AF.Identity, bias=0.0, scale=np.float32(1.0 / 32.0)))
    em("scalar", lambda E: E.activation(out=bkc_sb[:], in_=ps_sq[:, 8:16],
                                        func=AF.Identity, bias=0.0, scale=1.0)
       .then_inc(ac.h, 1))
    ac_bias = ac.inc()
    for hh in range(2):
        wait("tensor", ac, ac_bias)
        em("tensor", lambda E, h=hh: E.matmul(
            out=ps_sq[:, 0:512],
            lhsT=onerow[0:1, :],
            rhs=brow[0:1, 2 * D + h * 512:2 * D + (h + 1) * 512],
            start=True, stop=True).then_inc(pe.h, 1))
        pe_bv = pe.inc()
        wait("scalar", pe, pe_bv)
        em("scalar", lambda E, h=hh: E.activation(
            out=bvb_sb[:, h * 512:(h + 1) * 512], in_=ps_sq[:, 0:512],
            func=AF.Identity, bias=0.0, scale=1.0).then_inc(ac.h, 1))
        ac_bias = ac.inc()
    ac_bvb = ac_bias

    def ln_tiles(x_ap_list, h_out_list, wait_loads, slot0=0):
        nt = len(x_ap_list)
        wait("scalar", ld, wait_loads)
        wait("scalar", ac, ac_bvb)
        em("scalar", lambda E: E.memzero(stat_sb[:, 8 + slot0:8 + slot0 + nt]))
        em("scalar", lambda E: E.memzero(stat_sb[:, 16 + slot0:16 + slot0 + nt]))
        for i, x in enumerate(x_ap_list):
            s = slot0 + i
            em("scalar", lambda E, x=x, s=s: E.activation(
                out=ps_sq[:], in_=x[:, 0:512], func=AF.Square, bias=0.0, scale=1.0,
                accum_out=stat_sb[:, 8 + s:9 + s]))
            em("scalar", lambda E, x=x, s=s: E.activation(
                out=ps_sq[:], in_=x[:, 512:1024], func=AF.Square, bias=0.0, scale=1.0,
                accum_out=stat_sb[:, 16 + s:17 + s]).then_inc(ac.h, 1))
            ac.inc()
        ac_sq = ac.n
        if limit == "ln0":
            finish_debug([stat_sb[:]])
            return "STOP"
        wait("vector", ld, wait_loads)
        for i, x in enumerate(x_ap_list):
            s = slot0 + i
            last = (i == len(x_ap_list) - 1)
            if last:
                em("vector", lambda E, x=x, s=s: E.reduce_sum(
                    out=stat_sb[:, s:s + 1], in_=x, axis=AX.X).then_inc(dv.h, 1))
                wait("vector", dv, dv.inc())
            else:
                em("vector", lambda E, x=x, s=s: E.reduce_sum(
                    out=stat_sb[:, s:s + 1], in_=x, axis=AX.X))
        wait("vector", ac, ac_sq)
        sl = slice(slot0, slot0 + nt)
        s2a = slice(8 + slot0, 8 + slot0 + nt)
        s2b = slice(16 + slot0, 16 + slot0 + nt)
        mu = slice(24 + slot0, 24 + slot0 + nt)
        var = slice(32 + slot0, 32 + slot0 + nt)
        rstd = slice(40 + slot0, 40 + slot0 + nt)
        ng = slice(48 + slot0, 48 + slot0 + nt)

        def _step(fn):
            em("vector", lambda E, f=fn: f(E).then_inc(dv.h, 1))
            n = dv.inc()
            wait("vector", dv, n)
            return n
        _step(lambda E: E.tensor_scalar(
            out=stat_sb[:, mu], in0=stat_sb[:, sl], scalar1=1.0 / D, scalar2=None,
            op0=ALU.mult))
        _step(lambda E: E.tensor_tensor(
            out=stat_sb[:, var], in0=stat_sb[:, s2a], in1=stat_sb[:, s2b], op=ALU.add))
        _step(lambda E: E.tensor_scalar(
            out=stat_sb[:, var], in0=stat_sb[:, var], scalar1=1.0 / D, scalar2=None,
            op0=ALU.mult))
        _step(lambda E: E.tensor_tensor(
            out=stat_sb[:, s2a], in0=stat_sb[:, mu], in1=stat_sb[:, mu], op=ALU.mult))
        dv_var = _step(lambda E: E.tensor_tensor(
            out=stat_sb[:, var], in0=stat_sb[:, var], in1=stat_sb[:, s2a],
            op=ALU.subtract))
        wait("scalar", dv, dv_var)
        em("scalar", lambda E: E.activation(
            out=stat_sb[:, s2b], in_=stat_sb[:, var], func=AF.Ln, bias=np.float32(1e-5), scale=1.0))
        em("scalar", lambda E: E.activation(
            out=stat_sb[:, rstd], in_=stat_sb[:, s2b], func=AF.Exp, bias=0.0,
            scale=np.float32(-0.5)).then_inc(ac.h, 1))
        ac_rstd = ac.inc()
        wait("vector", ac, ac_rstd)
        _step(lambda E: E.tensor_tensor(
            out=stat_sb[:, ng], in0=stat_sb[:, mu], in1=stat_sb[:, rstd], op=ALU.mult))
        dv_ng = _step(lambda E: E.tensor_scalar(
            out=stat_sb[:, ng], in0=stat_sb[:, ng], scalar1=-1.0, scalar2=None,
            op0=ALU.mult))
        wait("scalar", dv, dv_ng)
        for i, (x, h) in enumerate(zip(x_ap_list, h_out_list)):
            s = slot0 + i
            em("scalar", lambda E, x=x, h=h, s=s: E.activation(
                out=h, in_=x, func=AF.Identity,
                bias=stat_sb[:, 48 + s:49 + s], scale=stat_sb[:, 40 + s:41 + s])
               .then_inc(ac.h, 1))
            ac.inc()
        return ac.n

    dbg_out = nc.dram_tensor("dbg", [128, 16 * D], F32, kind="ExternalOutput").ap() if limit else None

    def finish_debug(ap_list):
        col = 0
        wait("sync", ac, ac.n)
        wait("sync", dv, dv.n)
        wait("sync", pe, pe.n)
        nd = 0
        for ap_ in ap_list:
            ncol = ap_.shape[-1]
            nd = dma("sync", dbg_out[:, col:col + ncol], ap_, so)
            col += ncol
        wait("sync", so, nd)

    if limit == "init":
        finish_debug([bvb_sb[:], bqc_sb[:], bkc_sb[:]])
        return finalize()
    hbf0 = hb[:].bitcast(F32)
    evs0 = ev_scr[:, 0:512]
    evs1 = es_scr[:, 0:512]
    dv_red = {0: 0, 1: 0}
    ac_sq2 = {0: 0, 1: 0}
    for b8 in range(2):
        for t8 in range(8):
            t = b8 * 8 + t8
            sl = t % 2
            xbuf = xb[:] if sl == 0 else hbf0
            wait("sync", dv, dv_red[sl])
            wait("sync", ac, ac_sq2[sl])
            n_x = dma("sync", xb[:] if sl == 0 else hbf0,
                      xq_in[t * 128:(t + 1) * 128, :], ld)
            wait("vector", ld, n_x)
            em("vector", lambda E, x=xbuf, t=t: E.reduce_sum(
                out=stat_sb[:, t:t + 1], in_=x, axis=AX.X).then_inc(dv.h, 1))
            dv_red[sl] = dv.inc()
            wait("scalar", ld, n_x)
            em("scalar", lambda E, t=t: E.memzero(stat_sb[:, 64 + t:65 + t]))
            em("scalar", lambda E, t=t: E.memzero(stat_sb[:, 128 + t:129 + t]))
            em("scalar", lambda E, x=xbuf, t=t: E.activation(
                out=ps_sq[:], in_=x[:, 0:512], func=AF.Square, bias=0.0, scale=1.0,
                accum_out=stat_sb[:, 64 + t:65 + t]))
            em("scalar", lambda E, x=xbuf, t=t: E.activation(
                out=ps_sq[:], in_=x[:, 512:1024], func=AF.Square, bias=0.0, scale=1.0,
                accum_out=stat_sb[:, 128 + t:129 + t]).then_inc(ac.h, 1))
            ac_sq2[sl] = ac.inc()
        c0 = b8 * 8
        q0 = 64 + b8 * 8
        wait("vector", ac, ac_sq2[0])
        wait("vector", ac, ac_sq2[1])
        wait("vector", dv, max(dv_red[0], dv_red[1]))

        def _stq(fn):
            em("vector", lambda E, f=fn: f(E).then_inc(dv.h, 1))
            n = dv.inc()
            wait("vector", dv, n)
            return n
        _stq(lambda E, c0=c0: E.tensor_scalar(
            out=stat_sb[:, 192 + c0:200 + c0], in0=stat_sb[:, c0:c0 + 8],
            scalar1=1.0 / D, scalar2=None, op0=ALU.mult))
        _stq(lambda E, c0=c0: E.tensor_tensor(
            out=stat_sb[:, 64 + c0:72 + c0], in0=stat_sb[:, 64 + c0:72 + c0],
            in1=stat_sb[:, 128 + c0:136 + c0], op=ALU.add))
        _stq(lambda E, c0=c0: E.tensor_scalar(
            out=stat_sb[:, 64 + c0:72 + c0], in0=stat_sb[:, 64 + c0:72 + c0],
            scalar1=1.0 / D, scalar2=None, op0=ALU.mult))
        _stq(lambda E, c0=c0: E.tensor_tensor(
            out=stat_sb[:, 128 + c0:136 + c0], in0=stat_sb[:, 192 + c0:200 + c0],
            in1=stat_sb[:, 192 + c0:200 + c0], op=ALU.mult))
        dvv_ = _stq(lambda E, c0=c0: E.tensor_tensor(
            out=stat_sb[:, 64 + c0:72 + c0], in0=stat_sb[:, 64 + c0:72 + c0],
            in1=stat_sb[:, 128 + c0:136 + c0], op=ALU.subtract))
        wait("scalar", dv, dvv_)
        em("scalar", lambda E, c0=c0: E.activation(
            out=stat_sb[:, 128 + c0:136 + c0], in_=stat_sb[:, 64 + c0:72 + c0],
            func=AF.Ln, bias=np.float32(1e-5), scale=1.0))
        em("scalar", lambda E, c0=c0, q0=q0: E.activation(
            out=rstd_all[:, q0:q0 + 8], in_=stat_sb[:, 128 + c0:136 + c0],
            func=AF.Exp, bias=0.0, scale=np.float32(-0.5)).then_inc(ac.h, 1))
        acr_ = ac.inc()
        wait("vector", ac, acr_)
        _stq(lambda E, c0=c0, q0=q0: E.tensor_tensor(
            out=negm_all[:, q0:q0 + 8], in0=stat_sb[:, 192 + c0:200 + c0],
            in1=rstd_all[:, q0:q0 + 8], op=ALU.mult))
        _stq(lambda E, q0=q0: E.tensor_scalar(
            out=negm_all[:, q0:q0 + 8], in0=negm_all[:, q0:q0 + 8],
            scalar1=-1.0, scalar2=None, op0=ALU.mult))
    dv_statsq_done = dv.n
    ac_norm = {0: 0, 1: 0}
    sh_hstore_prev = 0
    sh_hstore_pp = 0
    wait("scalar", dv, dv_statsq_done)
    for t in range(16):
        sl = t % 2
        wait("sync", ac, ac_norm[sl])
        if sl == 0:
            n_x = dma("sync", xb[:], xq_in[t * 128:(t + 1) * 128, :], ld)
        else:
            n_x = dma("sync", evs0, xq_in[t * 128:(t + 1) * 128, 0:512], ld)
            n_x = dma("sync", evs1, xq_in[t * 128:(t + 1) * 128, 512:1024], ld)
        hsl = t % 2
        wait("scalar", ld, n_x)
        wait("scalar", sh, sh_hstore_pp)
        if sl == 0:
            em("scalar", lambda E, t=t, h=hsl: E.activation(
                out=hb[:, h * D:h * D + D], in_=xb[:], func=AF.Identity,
                bias=negm_all[:, 64 + t:65 + t], scale=rstd_all[:, 64 + t:65 + t])
               .then_inc(ac.h, 1))
        else:
            em("scalar", lambda E, t=t, h=hsl: E.activation(
                out=hb[:, h * D:h * D + 512], in_=evs0, func=AF.Identity,
                bias=negm_all[:, 64 + t:65 + t], scale=rstd_all[:, 64 + t:65 + t]))
            em("scalar", lambda E, t=t, h=hsl: E.activation(
                out=hb[:, h * D + 512:h * D + D], in_=evs1, func=AF.Identity,
                bias=negm_all[:, 64 + t:65 + t], scale=rstd_all[:, 64 + t:65 + t])
               .then_inc(ac.h, 1))
        ac_norm[sl] = ac.inc()
        wait("scalar", ac, ac_norm[sl])
        sh_hstore_pp = sh_hstore_prev
        sh_hstore_prev = dma("scalar", hq_scr[t * 128:(t + 1) * 128, :],
                             hb[:, hsl * D:(hsl + 1) * D], sh)
    sh_hdone_q = sh_hstore_prev

    pe_hf_prev = 0
    ac_evacA = {0: 0, 1: 0}
    dv_evA = {0: 0, 1: 0}
    wait("sync", sh, sh_hdone_q)
    for g in range(1 if limit in ("A1", "A0") else 4):
        wait("sync", pe, pe_hf_prev)
        n_hf = 0
        for dc in range(ECH):
            n_hf = dma("sync", hf[:, dc * 512:(dc + 1) * 512],
                       hq_scr[g * 512:(g + 1) * 512, dc * 128:(dc + 1) * 128],
                       ld, transpose=True)
        if limit == "hf" and g == 0:
            wait("sync", ld, n_hf)
            finish_debug([hf[:].bitcast(F32)])
            return finalize()
        wait("tensor", ld, n_hf)
        wait("tensor", dv, dv_wcast)
        for ec in range(1 if limit == "A0" else ECH):
            hsel = ec % 2
            wait("tensor", ac, ac_evacA[hsel])
            for dc in range(ECH):
                em("tensor", lambda E, ec=ec, dc=dc, h=hsel: E.matmul(
                    out=ps_pvA[:, h * 512:(h + 1) * 512],
                    lhsT=wq_bf[:, dc * D + ec * 128:dc * D + ec * 128 + 128],
                    rhs=hf[:, dc * 512:(dc + 1) * 512],
                    start=(dc == 0), stop=(dc == ECH - 1)).then_inc(pe.h, 1))
                pe.inc()
            pe_q = pe.n
            wait("scalar", pe, pe_q)
            wait("scalar", gp, max(dv_evA[0], dv_evA[1]))
            em("scalar", lambda E, ec=ec, h=hsel: E.activation(
                out=ev_scr[:, 0:512],
                in_=ps_pvA[:, h * 512:(h + 1) * 512], func=AF.Identity,
                bias=bqc_sb[:, ec:ec + 1], scale=np.float32(1.0 / 32.0)).then_inc(ac.h, 1))
            ac_evacA[hsel] = ac.inc()
            wait("vector", ac, ac_evacA[hsel])
            em("vector", lambda E, ec=ec, g=g: E.tensor_copy(
                out=q_sb[:, ec * TOWN + g * 512:ec * TOWN + (g + 1) * 512],
                in_=ev_scr[:, 0:512]).then_inc(gp.h, 1))
            dv_evA[hsel] = gp.inc()
        pe_hf_prev = pe.n

    if limit in ("A", "A1", "A0"):
        finish_debug([q_sb[:].bitcast(F32)])
        return finalize()
    dv_chain_prev = 0
    for b8 in range(8):
        for t8 in range(8):
            t = b8 * 8 + t8
            sl = t % 2
            xbuf = xb[:] if sl == 0 else hbf0
            wait("sync", dv, dv_red[sl])
            wait("sync", ac, ac_sq2[sl])
            n_x = dma("sync", xbuf if False else (xb[:] if sl == 0 else hbf0),
                      xkv_in[t * 128:(t + 1) * 128, :], ld)
            xbuf = xb[:] if sl == 0 else hbf0
            wait("vector", ld, n_x)
            em("vector", lambda E, x=xbuf, t=t: E.reduce_sum(
                out=stat_sb[:, t:t + 1], in_=x, axis=AX.X).then_inc(dv.h, 1))
            dv_red[sl] = dv.inc()
            wait("scalar", ld, n_x)
            em("scalar", lambda E, x=xbuf, t=t: E.memzero(stat_sb[:, 64 + t:65 + t]))
            em("scalar", lambda E, x=xbuf, t=t: E.memzero(stat_sb[:, 128 + t:129 + t]))
            em("scalar", lambda E, x=xbuf, t=t: E.activation(
                out=ps_sq[:], in_=x[:, 0:512], func=AF.Square, bias=0.0, scale=1.0,
                accum_out=stat_sb[:, 64 + t:65 + t]))
            em("scalar", lambda E, x=xbuf, t=t: E.activation(
                out=ps_sq[:], in_=x[:, 512:1024], func=AF.Square, bias=0.0, scale=1.0,
                accum_out=stat_sb[:, 128 + t:129 + t]).then_inc(ac.h, 1))
            ac_sq2[sl] = ac.inc()
        c0 = b8 * 8
        wait("vector", ac, ac_sq2[0])
        wait("vector", ac, ac_sq2[1])
        wait("vector", dv, max(dv_red[0], dv_red[1]))

        def _stp(fn):
            em("vector", lambda E, f=fn: f(E).then_inc(dv.h, 1))
            n = dv.inc()
            wait("vector", dv, n)
            return n
        _stp(lambda E, c0=c0: E.tensor_scalar(
            out=stat_sb[:, 192 + c0:200 + c0], in0=stat_sb[:, c0:c0 + 8],
            scalar1=1.0 / D, scalar2=None, op0=ALU.mult))
        _stp(lambda E, c0=c0: E.tensor_tensor(
            out=stat_sb[:, 64 + c0:72 + c0], in0=stat_sb[:, 64 + c0:72 + c0],
            in1=stat_sb[:, 128 + c0:136 + c0], op=ALU.add))
        _stp(lambda E, c0=c0: E.tensor_scalar(
            out=stat_sb[:, 64 + c0:72 + c0], in0=stat_sb[:, 64 + c0:72 + c0],
            scalar1=1.0 / D, scalar2=None, op0=ALU.mult))
        _stp(lambda E, c0=c0: E.tensor_tensor(
            out=stat_sb[:, 128 + c0:136 + c0], in0=stat_sb[:, 192 + c0:200 + c0],
            in1=stat_sb[:, 192 + c0:200 + c0], op=ALU.mult))
        dvv_ = _stp(lambda E, c0=c0: E.tensor_tensor(
            out=stat_sb[:, 64 + c0:72 + c0], in0=stat_sb[:, 64 + c0:72 + c0],
            in1=stat_sb[:, 128 + c0:136 + c0], op=ALU.subtract))
        wait("scalar", dv, dvv_)
        em("scalar", lambda E, c0=c0: E.activation(
            out=stat_sb[:, 128 + c0:136 + c0], in_=stat_sb[:, 64 + c0:72 + c0],
            func=AF.Ln, bias=np.float32(1e-5), scale=1.0))
        em("scalar", lambda E, c0=c0: E.activation(
            out=rstd_all[:, c0:c0 + 8], in_=stat_sb[:, 128 + c0:136 + c0],
            func=AF.Exp, bias=0.0, scale=np.float32(-0.5)).then_inc(ac.h, 1))
        acr_ = ac.inc()
        wait("vector", ac, acr_)
        _stp(lambda E, c0=c0: E.tensor_tensor(
            out=negm_all[:, c0:c0 + 8], in0=stat_sb[:, 192 + c0:200 + c0],
            in1=rstd_all[:, c0:c0 + 8], op=ALU.mult))
        dv_chain_prev = _stp(lambda E, c0=c0: E.tensor_scalar(
            out=negm_all[:, c0:c0 + 8], in0=negm_all[:, c0:c0 + 8],
            scalar1=-1.0, scalar2=None, op0=ALU.mult))
    dv_stats_done = dv.n
    ac_norm = {0: 0, 1: 0}
    wait("scalar", dv, dv_stats_done)
    wait("sync", gp, max(dv_evA[0], dv_evA[1]))
    sh_tile = []
    for t in range(64):
        sl = t % 2
        wait("sync", ac, ac_norm[sl])
        if sl == 0:
            n_x = dma("sync", xb[:], xkv_in[t * 128:(t + 1) * 128, :], ld)
        else:
            n_x = dma("sync", evs0, xkv_in[t * 128:(t + 1) * 128, 0:512], ld)
            n_x = dma("sync", evs1, xkv_in[t * 128:(t + 1) * 128, 512:1024], ld)
        hsl = t % 2
        wait("scalar", ld, n_x)
        wait("scalar", sh, sh_hstore_pp)
        if sl == 0:
            em("scalar", lambda E, t=t, h=hsl: E.activation(
                out=hb[:, h * D:h * D + D], in_=xb[:], func=AF.Identity,
                bias=negm_all[:, t:t + 1], scale=rstd_all[:, t:t + 1])
               .then_inc(ac.h, 1))
        else:
            em("scalar", lambda E, t=t, h=hsl: E.activation(
                out=hb[:, h * D:h * D + 512], in_=evs0, func=AF.Identity,
                bias=negm_all[:, t:t + 1], scale=rstd_all[:, t:t + 1]))
            em("scalar", lambda E, t=t, h=hsl: E.activation(
                out=hb[:, h * D + 512:h * D + D], in_=evs1, func=AF.Identity,
                bias=negm_all[:, t:t + 1], scale=rstd_all[:, t:t + 1])
               .then_inc(ac.h, 1))
        ac_norm[sl] = ac.inc()
        wait("scalar", ac, ac_norm[sl])
        sh_hstore_pp = sh_hstore_prev
        sh_hstore_prev = dma("scalar", h_scr[t * 128:(t + 1) * 128, :],
                             hb[:, hsl * D:(hsl + 1) * D], sh)
        sh_tile.append(sh_hstore_prev)
    sh_hdone = sh_hstore_prev

    wsrc = [xb[:, 0:256], xb[:, 256:512]]
    wxn = xb[:, 512:768]
    wtt = xb[:, 768:1024]
    wacc = hb[:, 0:512].bitcast(F32)
    wi1 = hb[:, 512:1024].bitcast(I32)
    wsg = hb[:, 1024:1280]
    wdst = [hb[:, 1280:1536], hb[:, 1536:1792]]
    wsmall = hb[:, 1792:2048].bitcast(F32)
    wbm = wsmall[:, 0:16]
    wsc = wsmall[:, 16:32]
    wrx = wsmall[:, 32:48]
    wchunks = []
    for (w_in_t, w_scr, nrow) in ((wg_in, wgq_scr, II), (wu_in, wuq_scr, II),
                                  (wd_in, wdq_scr, D)):
        ncol = D if nrow == II else II
        for t in range(nrow // 128):
            for coff in range(0, ncol, 256):
                wchunks.append((w_in_t, w_scr, t, coff))
    wpos = {"i": 0}
    wslot = {0: 0, 1: 0}
    wsh = {0: 0, 1: 0}
    wst_state = {"sh_last": 0}

    def emit_wchunks(n):
        for _ in range(n):
            i = wpos["i"]
            if i >= len(wchunks):
                return
            wpos["i"] += 1
            w_in_t, w_scr, t, coff = wchunks[i]
            slot = i % 2
            wait("sync", dv, wslot[slot])
            n_w = dma("sync", wsrc[slot], w_in_t[t * 128:(t + 1) * 128, coff:coff + 256], ld)
            wait("scalar", dv, max(wslot[0], wslot[1]))
            wait("scalar", ld, n_w)
            em("scalar", lambda E, s=slot: E.activation(
                out=wtt[:], in_=wsrc[s], func=AF.Abs, bias=0.0, scale=1.0))
            em("scalar", lambda E, s=slot: E.activation(
                out=wsg[:], in_=wsrc[s], func=AF.Sign, bias=0.0, scale=1.0)
               .then_inc(ac.h, 1))
            acsg = ac.inc()
            wait("vector", ac, acsg)
            wait("vector", sh, wsh[slot])

            def _w(fn):
                em("vector", lambda E, f=fn: f(E).then_inc(dv.h, 1))
                n = dv.inc()
                wait("vector", dv, n)
                return n
            g16 = wtt.rearrange("p (g k) -> p g k", k=16)
            _w(lambda E: E.reduce_max(out=wbm, in_=g16, axis=AX.X))
            _w(lambda E: E.reciprocal(out=wrx, in_=wbm))
            rxb = wrx.unsqueeze(2).broadcast_to([128, 16, 16])
            scb = wsc.unsqueeze(2).broadcast_to([128, 16, 16])
            em("vector", lambda E: E.tensor_tensor(
                out=wxn.rearrange("p (g k) -> p g k", k=16), in0=g16, in1=rxb,
                op=ALU.mult))
            em("vector", lambda E: E.tensor_scalar(
                out=wi1[:], in0=wxn, scalar1=12.0, scalar2=None, op0=ALU.mult))
            em("vector", lambda E: E.tensor_scalar(
                out=wacc[:], in0=wi1[:], scalar1=4.0, scalar2=0.5,
                op0=ALU.min, op1=ALU.mult))
            em("vector", lambda E: E.tensor_scalar(
                out=wi1[:], in0=wxn, scalar1=6.0, scalar2=None, op0=ALU.mult))
            em("vector", lambda E: E.tensor_scalar(
                out=wtt[:], in0=wi1[:], scalar1=2.0, scalar2=2.0,
                op0=ALU.subtract, op1=ALU.min))
            em("vector", lambda E: E.scalar_tensor_tensor(
                out=wacc[:], in0=wtt[:], scalar=0.0, in1=wacc[:],
                op0=ALU.max, op1=ALU.add))
            em("vector", lambda E: E.tensor_scalar(
                out=wi1[:], in0=wxn, scalar1=3.0, scalar2=None, op0=ALU.mult))
            em("vector", lambda E: E.tensor_scalar(
                out=wsc, in0=wbm, scalar1=1.0 / 6.0, scalar2=1e-12,
                op0=ALU.mult, op1=ALU.max))
            em("vector", lambda E: E.tensor_scalar(
                out=wtt[:], in0=wi1[:], scalar1=2.0, scalar2=4.0,
                op0=ALU.mult, op1=ALU.subtract))
            em("vector", lambda E: E.scalar_tensor_tensor(
                out=wacc[:], in0=wtt[:], scalar=0.0, in1=wacc[:],
                op0=ALU.max, op1=ALU.add))
            em("vector", lambda E: E.tensor_tensor(
                out=wtt.rearrange("p (g k) -> p g k", k=16),
                in0=wacc[:].rearrange("p (g k) -> p g k", k=16), in1=scb,
                op=ALU.mult))
            em("vector", lambda E, s=slot: E.tensor_tensor(
                out=wdst[s], in0=wtt[:], in1=wsg[:], op=ALU.mult).then_inc(dv.h, 1))
            wslot[slot] = dv.inc()
            wait("scalar", dv, wslot[slot])
            wsh[slot] = dma("scalar", w_scr[t * 128:(t + 1) * 128, coff:coff + 256],
                            wdst[slot], sh)
            wst_state["sh_last"] = wsh[slot]

    ac_exp_last = {0: 0, 1: 0}
    dv_mask_b_last = {0: 0, 1: 0}
    ac_kevac = {0: 0, 1: 0}
    dv_vevac = {0: 0, 1: 0}
    dv_oacc = {}
    dv_lacc_prev = 0
    pe_pv_all_prev = 0
    nsu_run = limit if isinstance(limit, int) else NSU
    for su in range(nsu_run):
        kt, half = su // 2, su % 2
        key0 = kt * 1024 + half * 512
        wait("sync", pe, pe_hf_prev)
        wait("sync", sh, sh_tile[key0 // 128 + 3])
        n_hf = 0
        for dc in range(ECH):
            n_hf = dma("sync", hf[:, dc * 512:(dc + 1) * 512],
                       h_scr[key0:key0 + 512, dc * 128:(dc + 1) * 128], ld, transpose=True)
        if su > 0:
            emit_wchunks(17)
        wait("tensor", ld, n_hf)
        wait("tensor", dv, dv_oacc.get(14, 0))
        for ec in range(ECH):
            hsel = ec % 2
            wait("tensor", ac, ac_kevac[hsel])
            wait("tensor", dv, dv_vevac[hsel])
            for dc in range(ECH):
                em("tensor", lambda E, ec=ec, dc=dc, h=hsel: E.matmul(
                    out=ps_pvA[:, h * 512:(h + 1) * 512],
                    lhsT=wkT_bf[:, dc * D + ec * 128:dc * D + ec * 128 + 128],
                    rhs=hf[:, dc * 512:(dc + 1) * 512],
                    start=(dc == 0), stop=(dc == ECH - 1)).then_inc(pe.h, 1))
                pe.inc()
            pe_k = pe.n
            wait("scalar", pe, pe_k)
            wait("scalar", gp, max(dv_vevac[0], dv_vevac[1]))
            em("scalar", lambda E, ec=ec, h=hsel: E.activation(
                out=ev_scr[:, 0:512],
                in_=ps_pvA[:, h * 512:(h + 1) * 512], func=AF.Identity,
                bias=bkc_sb[:, ec:ec + 1], scale=1.0).then_inc(ac.h, 1))
            ac_kevac[hsel] = ac.inc()
            wait("vector", ac, ac_kevac[hsel])
            em("vector", lambda E, ec=ec: E.tensor_copy(
                out=k_sb[:, ec * 512:(ec + 1) * 512],
                in_=ev_scr[:, 0:512]).then_inc(gp.h, 1))
            dv_vevac[hsel] = gp.inc()
        for kc in range(4):
            for vh in range(2):
                hsel = (kc * 2 + vh) % 2
                wait("tensor", ac, ac_kevac[hsel])
                wait("tensor", dv, dv_vevac[hsel])
                for dc in range(ECH):
                    em("tensor", lambda E, kc=kc, vh=vh, dc=dc, h=hsel: E.matmul(
                        out=ps_pvA[:, h * 512:(h + 1) * 512],
                        lhsT=hf[:, dc * 512 + kc * 128:dc * 512 + kc * 128 + 128],
                        rhs=wvT_bf[:, dc * D + vh * 512:dc * D + (vh + 1) * 512],
                        start=(dc == 0), stop=(dc == ECH - 1)).then_inc(pe.h, 1))
                    pe.inc()
                pe_v = pe.n
                wait("scalar", pe, pe_v)
                wait("scalar", gp, max(dv_vevac[0], dv_vevac[1]))
                em("scalar", lambda E, h=hsel: E.activation(
                    out=ev_scr[:, 0:512],
                    in_=ps_pvA[:, h * 512:(h + 1) * 512], func=AF.Identity,
                    bias=0.0, scale=1.0).then_inc(ac.h, 1))
                ac_kevac[hsel] = ac.inc()
                wait("vector", ac, ac_kevac[hsel])
                em("vector", lambda E, kc=kc, vh=vh: E.tensor_tensor(
                    out=v_sb[:, kc * D + vh * 512:kc * D + (vh + 1) * 512],
                    in0=ev_scr[:, 0:512],
                    in1=bvb_sb[:, vh * 512:(vh + 1) * 512], op=ALU.add).then_inc(gp.h, 1))
                dv_vevac[hsel] = gp.inc()
        pe_hf_prev = pe.n

        gp_kv_done = gp.n
        wait("tensor", gp, gp_kv_done)
        ac_exp = dict(ac_exp_last)
        pair = 0
        dv_mask_b = dict(dv_mask_b_last)
        for kc in range(4):
            for qh in range(4):
                bank = pair % 2
                ps_s = ps_s0 if bank == 0 else ps_s1
                wait("tensor", ac, ac_exp[bank])
                for ec in range(ECH):
                    em("tensor", lambda E, kc=kc, qh=qh, ec=ec, ps=ps_s: E.matmul(
                        out=ps[:, 0:512],
                        lhsT=k_sb[:, ec * 512 + kc * 128:ec * 512 + kc * 128 + 128],
                        rhs=q_sb[:, ec * TOWN + qh * 512:ec * TOWN + (qh + 1) * 512],
                        start=(ec == 0), stop=(ec == ECH - 1)).then_inc(pe.h, 1))
                    pe.inc()
                pe_s = pe.n
                wait("scalar", pe, pe_s)
                wait("scalar", gp, max(dv_mask_b[0], dv_mask_b[1]))
                em("scalar", lambda E, b=bank, ps=ps_s: E.activation(
                    out=es_scr[:, 0:512],
                    in_=ps[:, 0:512], func=AF.Exp, bias=np.float32(NEG),
                    scale=1.0).then_inc(ac.h, 1))
                ac_exp[bank] = ac.inc()
                wait("vector", ac, ac_exp[bank])
                wait("vector", pe, pe_pv_all_prev)
                em("vector", lambda E, su=su, kc=kc, qh=qh: E.scalar_tensor_tensor(
                    out=pt_sb[:, kc * TOWN + qh * 512:kc * TOWN + (qh + 1) * 512],
                    in0=ramp_sb[:, qh * 512:(qh + 1) * 512],
                    scalar=tthr_sb[:, su * 4 + kc:su * 4 + kc + 1],
                    in1=es_scr[:, 0:512],
                    op0=ALU.is_ge, op1=ALU.mult).then_inc(gp.h, 1))
                dv_mask_b[bank] = gp.inc()
                pair += 1
        ac_exp_last = ac_exp
        dv_mask_b_last = dict(dv_mask_b)
        gp_mask_all = gp.n

        dv_oacc_prev = dict(dv_oacc)
        wait("tensor", gp, gp_mask_all)
        wait("tensor", dv, dv_lacc_prev)
        lbase = (su % 2) * 16
        pe_pv = {}
        for qs in range(16):
            buf = qs % 2
            ps_o = ps_pvA if buf == 0 else ps_pvB
            if qs >= 2:
                wait("tensor", dv, dv_oacc[qs - 2])
            elif qs == 1:
                wait("tensor", dv, dv_oacc_prev.get(15, 0))
            for kc in range(4):
                em("tensor", lambda E, qs=qs, kc=kc, ps=ps_o: E.matmul(
                    out=ps[:, 0:512],
                    lhsT=pt_sb[:, kc * TOWN + qs * 128:kc * TOWN + qs * 128 + 128],
                    rhs=v_sb[:, kc * D:kc * D + 512],
                    start=(kc == 0), stop=(kc == 3)).then_inc(pe.h, 1))
                pe.inc()
                em("tensor", lambda E, qs=qs, kc=kc, ps=ps_o: E.matmul(
                    out=ps[:, 512:1024],
                    lhsT=pt_sb[:, kc * TOWN + qs * 128:kc * TOWN + qs * 128 + 128],
                    rhs=v_sb[:, kc * D + 512:kc * D + 1024],
                    start=(kc == 0), stop=(kc == 3)).then_inc(pe.h, 1))
                pe.inc()
                em("tensor", lambda E, qs=qs, kc=kc, lc=lbase + qs: E.matmul(
                    out=ps_l[:, lc:lc + 1],
                    lhsT=pt_sb[:, kc * TOWN + qs * 128:kc * TOWN + qs * 128 + 128],
                    rhs=ones1_bf[:, 0:1],
                    start=(kc == 0), stop=(kc == 3)).then_inc(pe.h, 1))
                pe.inc()
            pe_pv[qs] = pe.n
            wait("vector", pe, pe_pv[qs])
            if su == 0 and qs == 0:
                wait("vector", dv, dv_zero)
            em("vector", lambda E, qs=qs, ps=ps_o: E.tensor_tensor(
                out=o_sb[:, qs * D:(qs + 1) * D], in0=o_sb[:, qs * D:(qs + 1) * D],
                in1=ps[:, 0:1024], op=ALU.add).then_inc(dv.h, 1))
            dv_oacc[qs] = dv.inc()
        pe_pv_all_prev = pe.n
        wait("vector", pe, pe_pv[15])
        em("vector", lambda E, lb=lbase: E.tensor_tensor(
            out=l_sb[:, 0:16], in0=l_sb[:, 0:16], in1=ps_l[:, lb:lb + 16],
            op=ALU.add).then_inc(dv.h, 1))
        dv_lacc_prev = dv.inc()

    if isinstance(limit, int):
        finish_debug([o_sb[:, 0:15 * D], l_sb[:], stat_sb[:]])
        return finalize()
    emit_wchunks(len(wchunks))
    dv_w_final = max(wslot[0], wslot[1])
    sh_w_final = wst_state["sh_last"]


    wait("vector", dv, dv_lacc_prev)
    em("vector", lambda E: E.reciprocal(out=rinv_sb[:], in_=l_sb[:]).then_inc(dv.h, 1))
    dv_rinv = dv.inc()
    wait("vector", dv, dv_rinv)
    hbf = hb[:].bitcast(F32)
    sh_xnew_prev = sh_hstore_prev
    dv_fold = {}
    for r in range(16):
        if r == 0:
            wait("sync", dv, dv_w_final)
            wait("vector", sh, sh_w_final)
        else:
            wait("sync", dv, dv_fold.get(r - 1, 0))
        n_x = dma("sync", xb[:], xq_in[r * 128:(r + 1) * 128, :], ld)
        wait("vector", ld, n_x)
        wait("vector", dv, dv_rinv)
        wait("vector", sh, sh_xnew_prev)
        em("vector", lambda E, r=r: E.scalar_tensor_tensor(
            out=hbf, in0=o_sb[:, r * D:(r + 1) * D],
            scalar=rinv_sb[:, r:r + 1], in1=xb[:],
            op0=ALU.mult, op1=ALU.add).then_inc(dv.h, 1))
        dv_fold[r] = dv.inc()
        if r == 15:
            em("vector", lambda E, r=r: E.reduce_sum(
                out=stat_sb[:, r:r + 1], in_=hbf, axis=AX.X).then_inc(dv.h, 1))
            wait("vector", dv, dv.inc())
        else:
            em("vector", lambda E, r=r: E.reduce_sum(
                out=stat_sb[:, r:r + 1], in_=hbf, axis=AX.X))
        wait("scalar", dv, dv_fold[r])
        em("scalar", lambda E, r=r: E.memzero(stat_sb[:, 16 + r:17 + r]))
        em("scalar", lambda E, r=r: E.memzero(stat_sb[:, 32 + r:33 + r]))
        em("scalar", lambda E, r=r: E.activation(
            out=ps_sq[:], in_=hbf[:, 0:512], func=AF.Square, bias=0.0, scale=1.0,
            accum_out=stat_sb[:, 16 + r:17 + r]))
        em("scalar", lambda E, r=r: E.activation(
            out=ps_sq[:], in_=hbf[:, 512:1024], func=AF.Square, bias=0.0, scale=1.0,
            accum_out=stat_sb[:, 32 + r:33 + r]).then_inc(ac.h, 1))
        ac.inc()
        sh_xnew_prev = dma("scalar", xnew_scr[r * 128:(r + 1) * 128, :], hbf, sh)
    ac_sq_done = ac.n
    wait("vector", ac, ac_sq_done)

    def _stepf(fn):
        em("vector", lambda E, f=fn: f(E).then_inc(dv.h, 1))
        n = dv.inc()
        wait("vector", dv, n)
        return n
    _stepf(lambda E: E.tensor_scalar(
        out=stat_sb[:, 48:64], in0=stat_sb[:, 0:16], scalar1=1.0 / D, scalar2=None,
        op0=ALU.mult))
    _stepf(lambda E: E.tensor_tensor(
        out=stat_sb[:, 16:32], in0=stat_sb[:, 16:32], in1=stat_sb[:, 32:48], op=ALU.add))
    _stepf(lambda E: E.tensor_scalar(
        out=stat_sb[:, 16:32], in0=stat_sb[:, 16:32], scalar1=1.0 / D, scalar2=None,
        op0=ALU.mult))
    _stepf(lambda E: E.tensor_tensor(
        out=stat_sb[:, 32:48], in0=stat_sb[:, 48:64], in1=stat_sb[:, 48:64], op=ALU.mult))
    dv_varf = _stepf(lambda E: E.tensor_tensor(
        out=stat_sb[:, 16:32], in0=stat_sb[:, 16:32], in1=stat_sb[:, 32:48],
        op=ALU.subtract))
    wait("scalar", dv, dv_varf)
    em("scalar", lambda E: E.activation(
        out=stat_sb[:, 32:48], in_=stat_sb[:, 16:32], func=AF.Ln, bias=np.float32(1e-5), scale=1.0))
    em("scalar", lambda E: E.activation(
        out=rstdf_sb[:], in_=stat_sb[:, 32:48], func=AF.Exp, bias=0.0, scale=np.float32(-0.5))
       .then_inc(ac.h, 1))
    ac_rstdf = ac.inc()
    wait("vector", ac, ac_rstdf)
    _stepf(lambda E: E.tensor_tensor(
        out=negmf_sb[:], in0=stat_sb[:, 48:64], in1=rstdf_sb[:], op=ALU.mult))
    dv_negmf = _stepf(lambda E: E.tensor_scalar(
        out=negmf_sb[:], in0=negmf_sb[:], scalar1=-1.0, scalar2=None,
        op0=ALU.mult))
    dv_fold_done = dv.n
    ac_fold_done = ac.n

    if limit == "fold":
        finish_debug([rstdf_sb[:], negmf_sb[:], rinv_sb[:], l_sb[:]])
        return finalize()
    ps_scope.close()
    bstack.close()

    cstack = ExitStack()
    C = cstack.enter_context
    wgq_f = C(nc.sbuf_tensor("wgq_f", [128, ECH * II], BF16))
    wuq_f = C(nc.sbuf_tensor("wuq_f", [128, ECH * II], BF16))
    wdq_f = C(nc.sbuf_tensor("wdq_f", [128, ICH * D], BF16))
    qt_xn = C(nc.sbuf_tensor("qt_xn", [128, D], F32))
    qt_i1 = C(nc.sbuf_tensor("qt_i1", [128, D], I32))
    qt_acc = C(nc.sbuf_tensor("qt_acc", [128, D], F32))
    qt_t = C(nc.sbuf_tensor("qt_t", [128, D], F32))
    qt_sg = C(nc.sbuf_tensor("qt_sg", [128, D], BF16))
    qt_bm = C(nc.sbuf_tensor("qt_bm", [128, 64], F32))
    qt_sc = C(nc.sbuf_tensor("qt_sc", [128, 64], F32))
    qt_rx = C(nc.sbuf_tensor("qt_rx", [128, 64], F32))
    wq_bf_sb = C(nc.sbuf_tensor("wq_bf_sb", [128, 2 * D], BF16))
    wst = C(nc.sbuf_tensor("wst", [128, 2 * D], F32))
    xc = C(nc.sbuf_tensor("xc", [128, 2 * D], F32))
    h2_sb = C(nc.sbuf_tensor("h2_sb", [128, D], F32))
    xq_bf = C(nc.sbuf_tensor("xq_bf", [128, D], BF16))
    xqf_sb = C(nc.sbuf_tensor("xqf_sb", [128, ECH * 128], BF16))
    hq_bf = C(nc.sbuf_tensor("hq_bf", [128, 512], BF16))
    hqf_sb = C(nc.sbuf_tensor("hqf_sb", [128, 8 * 128], BF16))
    sg_sb = C(nc.sbuf_tensor("sg_sb", [128, 512], F32))
    outb_sb = C(nc.sbuf_tensor("outb_sb", [128, D], F32))

    ps_g = C(nc.psum_tensor("ps_g", [128, 512], F32))
    ps_g2 = C(nc.psum_tensor("ps_g2", [128, 512], F32))
    ps_u = C(nc.psum_tensor("ps_u", [128, 512], F32))
    ps_u2 = C(nc.psum_tensor("ps_u2", [128, 512], F32))
    ps_d = C(nc.psum_tensor("ps_d", [128, D], F32))
    ps_tA = C(nc.psum_tensor("ps_tA", [128, 128], BF16))
    ps_tB = C(nc.psum_tensor("ps_tB", [128, 128], BF16))


    qstate = {"dst_dv": 0}

    def quant(src, dst, F, blocks, wait_pe_n=0, wait_src_dv=0, wait_src_ac=0,
              wait_src_ld=0):
        g16 = src.rearrange("p (g k) -> p g k", k=16)
        if wait_pe_n:
            wait("vector", pe, wait_pe_n)
        if wait_src_ac:
            wait("vector", ac, wait_src_ac)
        if wait_src_dv:
            wait("vector", dv, wait_src_dv)
        if wait_src_ld:
            wait("vector", ld, wait_src_ld)
        em("vector", lambda E: E.reduce_max(
            out=qt_bm[:, 0:blocks], in_=g16, axis=AX.X,
            apply_absolute_value=True).then_inc(dv.h, 1))
        dvq = dv.inc()
        wait("vector", dv, dvq)
        em("vector", lambda E: E.tensor_scalar(
            out=qt_sc[:, 0:blocks], in0=qt_bm[:, 0:blocks], scalar1=1.0 / 6.0,
            scalar2=1e-12, op0=ALU.mult, op1=ALU.max).then_inc(dv.h, 1))
        dvq = dv.inc()
        wait("vector", dv, dvq)
        em("vector", lambda E: E.reciprocal(
            out=qt_rx[:, 0:blocks], in_=qt_sc[:, 0:blocks]).then_inc(dv.h, 1))
        dvq = dv.inc()
        wait("vector", dv, dvq)
        rxb = qt_rx[:, 0:blocks].unsqueeze(2).broadcast_to([128, blocks, 16])
        scb = qt_sc[:, 0:blocks].unsqueeze(2).broadcast_to([128, blocks, 16])
        wait("scalar", dv, qstate["dst_dv"])
        if wait_src_dv:
            wait("scalar", dv, wait_src_dv)
        if wait_src_ac:
            wait("scalar", ac, wait_src_ac)
        if wait_src_ld:
            wait("scalar", ld, wait_src_ld)
        em("scalar", lambda E: E.activation(
            out=qt_t[:, 0:F], in_=src, func=AF.Abs, bias=0.0, scale=1.0))
        em("scalar", lambda E: E.activation(
            out=qt_sg[:, 0:F], in_=src, func=AF.Sign, bias=0.0, scale=1.0).then_inc(ac.h, 1))
        acsg = ac.inc()
        wait("vector", ac, acsg)
        em("vector", lambda E: E.tensor_tensor(
            out=qt_xn[:, 0:F].rearrange("p (g k) -> p g k", k=16),
            in0=qt_t[:, 0:F].rearrange("p (g k) -> p g k", k=16), in1=rxb,
            op=ALU.mult).then_inc(dv.h, 1))
        dv.inc()
        em("vector", lambda E: E.tensor_scalar(
            out=qt_i1[:, 0:F], in0=qt_xn[:, 0:F], scalar1=2.0, scalar2=None,
            op0=ALU.mult))
        em("vector", lambda E: E.tensor_scalar(
            out=qt_acc[:, 0:F], in0=qt_i1[:, 0:F], scalar1=4.0, scalar2=0.5,
            op0=ALU.min, op1=ALU.mult))
        em("vector", lambda E: E.tensor_scalar(
            out=qt_i1[:, 0:F], in0=qt_xn[:, 0:F], scalar1=1.0, scalar2=None,
            op0=ALU.mult))
        em("vector", lambda E: E.tensor_scalar(
            out=qt_t[:, 0:F], in0=qt_i1[:, 0:F], scalar1=2.0, scalar2=2.0,
            op0=ALU.subtract, op1=ALU.min))
        em("vector", lambda E: E.scalar_tensor_tensor(
            out=qt_acc[:, 0:F], in0=qt_t[:, 0:F], scalar=0.0, in1=qt_acc[:, 0:F],
            op0=ALU.max, op1=ALU.add))
        em("vector", lambda E: E.tensor_scalar(
            out=qt_i1[:, 0:F], in0=qt_xn[:, 0:F], scalar1=0.5, scalar2=None,
            op0=ALU.mult))
        em("vector", lambda E: E.tensor_scalar(
            out=qt_t[:, 0:F], in0=qt_i1[:, 0:F], scalar1=2.0, scalar2=4.0,
            op0=ALU.mult, op1=ALU.subtract))
        em("vector", lambda E: E.scalar_tensor_tensor(
            out=qt_acc[:, 0:F], in0=qt_t[:, 0:F], scalar=0.0, in1=qt_acc[:, 0:F],
            op0=ALU.max, op1=ALU.add))
        em("vector", lambda E: E.tensor_tensor(
            out=qt_t[:, 0:F].rearrange("p (g k) -> p g k", k=16),
            in0=qt_acc[:, 0:F].rearrange("p (g k) -> p g k", k=16), in1=scb, op=ALU.mult))
        wait("vector", ac, acsg)
        em("vector", lambda E, dst=dst: E.tensor_tensor(
            out=dst, in0=qt_t[:, 0:F], in1=qt_sg[:, 0:F], op=ALU.mult).then_inc(dv.h, 1))
        qstate["dst_dv"] = dv.inc()
        return qstate["dst_dv"]

    wait("sync", sh, sh_w_final)
    for dc in range(ECH):
        dma("sync", wgq_f[:, dc * II:(dc + 1) * II],
            wgq_scr[:, dc * 128:(dc + 1) * 128], ld, transpose=True)
        dma("sync", wuq_f[:, dc * II:(dc + 1) * II],
            wuq_scr[:, dc * 128:(dc + 1) * 128], ld, transpose=True)
    n_wf = 0
    for icc in range(ICH):
        n_wf = dma("sync", wdq_f[:, icc * D:(icc + 1) * D],
                   wdq_scr[:, icc * 128:(icc + 1) * 128], ld, transpose=True)

    NCC = [512, 512, 512, 512, 512, 256]
    dv_out_prev = 0
    pe_xt_prev = 0
    pe_hqt_prev = 0
    dv_tc = {"A": 0, "B": 0}
    h2x = wst
    ac_h2 = {}
    dv_xq = {}
    n_xc = {}

    def emit_ln_tile(r):
        if r >= 16:
            return
        xsel = r % 2
        if r >= 2:
            wait("sync", dv, dv_out_prev)
        n_xc[r] = dma("sync", xc[:, xsel * D:(xsel + 1) * D],
                      xnew_scr[r * 128:(r + 1) * 128, :], ld)
        wait("scalar", ld, n_xc[r])
        wait("scalar", dv, dv_negmf)
        em("scalar", lambda E, r=r, xs=xsel: E.activation(
            out=h2x[:, xs * D:(xs + 1) * D], in_=xc[:, xs * D:(xs + 1) * D],
            func=AF.Identity,
            bias=negmf_sb[:, r:r + 1], scale=rstdf_sb[:, r:r + 1]).then_inc(ac.h, 1))
        ac_h2[r] = ac.inc()

    def emit_xquant_tile(r):
        if r >= 16:
            return
        xsel = r % 2
        dv_xq[r] = quant(h2x[:, xsel * D:(xsel + 1) * D], xq_bf[:], D, 64,
                         wait_pe_n=pe_xt_prev, wait_src_ac=ac_h2[r])

    emit_ln_tile(0)
    emit_xquant_tile(0)
    for r in range(16):
        xsel = r % 2
        wait("tensor", dv, dv_xq[r])
        for dc in range(ECH):
            key = "A" if dc % 2 == 0 else "B"
            pst = ps_tA if dc % 2 == 0 else ps_tB
            wait("tensor", dv, dv_tc[key])
            em("tensor", lambda E, dc=dc, pst=pst: E.transpose(
                out=pst[:], in_=xq_bf[:, dc * 128:(dc + 1) * 128], identity=ident_sb[:])
               .then_inc(pe.h, 1))
            pe_t = pe.inc()
            wait("vector", pe, pe_t)
            em("vector", lambda E, dc=dc, pst=pst: E.tensor_copy(
                out=xqf_sb[:, dc * 128:(dc + 1) * 128], in_=pst[:]).then_inc(dv.h, 1))
            dv_tc[key] = dv.inc()
        pe_xt_prev = pe.n
        dv_xt = dv_tc["B"]

        def emit_gu(ci):
            csz = NCC[ci]
            off = 512 * ci
            gsel = ci % 2
            psg = ps_g if gsel == 0 else ps_g2
            psu = ps_u if gsel == 0 else ps_u2
            wait("tensor", ac, ac_sil_prev[gsel])
            wait("tensor", dv, dv_h_prev[gsel])
            if ci == 0:
                wait("tensor", dv, dv_xt)
                wait("tensor", ld, n_wf)
            for dc in range(ECH):
                em("tensor", lambda E, dc=dc, off=off, csz=csz, psg=psg: E.matmul(
                    out=psg[:, 0:csz], lhsT=xqf_sb[:, dc * 128:(dc + 1) * 128],
                    rhs=wgq_f[:, dc * II + off:dc * II + off + csz],
                    start=(dc == 0), stop=(dc == ECH - 1)).then_inc(pe.h, 1))
                pe.inc()
            for dc in range(ECH):
                em("tensor", lambda E, dc=dc, off=off, csz=csz, psu=psu: E.matmul(
                    out=psu[:, 0:csz], lhsT=xqf_sb[:, dc * 128:(dc + 1) * 128],
                    rhs=wuq_f[:, dc * II + off:dc * II + off + csz],
                    start=(dc == 0), stop=(dc == ECH - 1)).then_inc(pe.h, 1))
                pe.inc()
            return pe.n

        ac_sil_prev = {0: 0, 1: 0}
        dv_h_prev = {0: 0, 1: 0}
        dv_h_last = 0
        icc0 = 0
        pe_g = {0: emit_gu(0)}
        for ci, csz in enumerate(NCC):
            gsel = ci % 2
            psg = ps_g if gsel == 0 else ps_g2
            psu = ps_u if gsel == 0 else ps_u2
            wait("scalar", pe, pe_g[ci])
            wait("scalar", dv, dv_h_last)
            em("scalar", lambda E, csz=csz, psg=psg: E.activation(
                out=sg_sb[:, 0:csz], in_=psg[:, 0:csz], func=AF.Silu, bias=0.0,
                scale=1.0).then_inc(ac.h, 1))
            ac_sil = ac.inc()
            ac_sil_prev[gsel] = ac_sil
            wait("vector", ac, ac_sil)
            wait("vector", pe, pe_g[ci])
            em("vector", lambda E, csz=csz, psu=psu: E.tensor_tensor(
                out=h2_sb[:, 0:csz], in0=sg_sb[:, 0:csz], in1=psu[:, 0:csz],
                op=ALU.mult).then_inc(dv.h, 1))
            dv_h = dv.inc()
            dv_h_prev[gsel] = dv_h
            dv_h_last = dv_h
            dv_hq = quant(h2_sb[:, 0:csz], hq_bf[:, 0:csz], csz, csz // 16,
                          wait_pe_n=pe_hqt_prev, wait_src_dv=dv_h)
            if ci == 1:
                emit_ln_tile(r + 1)
            if ci == 2:
                emit_xquant_tile(r + 1)
            if ci + 1 < len(NCC):
                pe_g[ci + 1] = emit_gu(ci + 1)
            wait("tensor", dv, dv_hq)
            nic = csz // 128
            for j in range(nic):
                icc = icc0 + j
                key = "A" if icc % 2 == 0 else "B"
                pst = ps_tA if icc % 2 == 0 else ps_tB
                wait("tensor", dv, dv_tc[key])
                em("tensor", lambda E, j=j, pst=pst: E.transpose(
                    out=pst[:], in_=hq_bf[:, j * 128:(j + 1) * 128], identity=ident_sb[:])
                   .then_inc(pe.h, 1))
                pe_t = pe.inc()
                wait("vector", pe, pe_t)
                em("vector", lambda E, sl=icc % 8, pst=pst: E.tensor_copy(
                    out=hqf_sb[:, sl * 128:(sl + 1) * 128], in_=pst[:])
                   .then_inc(dv.h, 1))
                dv_tc[key] = dv.inc()
            pe_hqt_prev = pe.n
            wait("tensor", dv, dv_tc["A"])
            wait("tensor", dv, dv_tc["B"])
            if ci == 0 and r >= 1:
                wait("tensor", dv, dv_out_prev)
            for j in range(nic):
                icc = icc0 + j
                for hh2 in range(2):
                    em("tensor", lambda E, icc=icc, sl=icc % 8, hh=hh2: E.matmul(
                        out=ps_d[:, hh * 512:(hh + 1) * 512],
                        lhsT=hqf_sb[:, sl * 128:(sl + 1) * 128],
                        rhs=wdq_f[:, icc * D + hh * 512:icc * D + (hh + 1) * 512],
                        start=(icc == 0), stop=(icc == ICH - 1)).then_inc(pe.h, 1))
                    pe.inc()
            icc0 += nic
        pe_down = pe.n
        wait("vector", pe, pe_down)
        if r >= 1:
            wait("vector", so, r * 16)
        em("vector", lambda E, r=r, xs=xsel: E.tensor_tensor(
            out=outb_sb[:, 0:D], in0=ps_d[:, 0:D],
            in1=xc[:, xs * D:(xs + 1) * D], op=ALU.add).then_inc(dv.h, 1))
        dv_out_prev = dv.inc()
        wait("scalar", dv, dv_out_prev)
        dma("scalar", out_t[r * 128:(r + 1) * 128, :], outb_sb[:, 0:D], so)

    wait("sync", so, so.n)
    return finalize()


def _host_tables(j):
    tt = np.zeros((128, 64), dtype=np.float32)
    p = np.arange(128, dtype=np.float32)
    for su in range(NSU):
        kt, half = su // 2, su % 2
        for kc in range(4):
            t = kt * 1024 + half * 512 + kc * 128 - j * TOWN
            t = max(-128, min(2176, t))
            tt[:, su * 4 + kc] = p + t
    return tt


def kernel(L0_attn_X, L0_attn_Wq, L0_attn_bq, L0_attn_Wk, L0_attn_bk,
           L0_attn_Wv, L0_attn_bv, L0_ffn_w_gate, L0_ffn_w_up, L0_ffn_w_down):
    X = np.ascontiguousarray(np.asarray(L0_attn_X, dtype=np.float32))
    wqT = np.ascontiguousarray(np.asarray(L0_attn_Wq, np.float32).T)
    wkT = np.ascontiguousarray(np.asarray(L0_attn_Wk, np.float32).T)
    wvT = np.ascontiguousarray(np.asarray(L0_attn_Wv, np.float32).T)
    bq = np.ascontiguousarray(np.asarray(L0_attn_bq, np.float32))
    bk = np.ascontiguousarray(np.asarray(L0_attn_bk, np.float32))
    bv = np.ascontiguousarray(np.asarray(L0_attn_bv, np.float32))
    wg = np.ascontiguousarray(np.asarray(L0_ffn_w_gate, np.float32))
    wu = np.ascontiguousarray(np.asarray(L0_ffn_w_up, np.float32))
    wd = np.ascontiguousarray(np.asarray(L0_ffn_w_down, np.float32))

    ramp = np.broadcast_to(np.arange(TOWN, dtype=np.float32)[None, :], (128, TOWN)).copy()
    ident = np.eye(128, dtype=ml_dtypes.bfloat16)

    if "nc" not in _CACHE:
        _CACHE["nc"] = build_program()
    nc = _CACHE["nc"]

    in_maps = []
    for c in range(8):
        b, j = c // 4, c % 4
        in_maps.append({
            "xq": np.ascontiguousarray(X[b, j * TOWN:(j + 1) * TOWN, :]),
            "xkv": np.ascontiguousarray(X[b]),
            "wqT": wqT, "wkT": wkT, "wvT": wvT,
            "bq": bq, "bk": bk, "bv": bv,
            "wg": wg, "wu": wu, "wd": wd,
            "ramp": ramp, "tthr": _host_tables(j), "ident": ident,
        })
    res = run_bass_kernel_spmd(nc, in_maps, core_ids=list(range(8)))
    out = np.empty((2, NKV, D), dtype=np.float32)
    for c in range(8):
        b, j = c // 4, c % 4
        out[b, j * TOWN:(j + 1) * TOWN, :] = np.asarray(res.results[c]["out"])
    return out


# version 23
# speedup vs baseline: 1.1927x; 1.0307x over previous
import numpy as np
import ml_dtypes
from contextlib import ExitStack

import concourse.bass as bass
import concourse.mybir as mybir
from concourse.bass_utils import run_bass_kernel_spmd

F32 = mybir.dt.float32
BF16 = mybir.dt.bfloat16
I32 = mybir.dt.int32
AF = mybir.ActivationFunctionType
ALU = mybir.AluOpType
AX = mybir.AxisListType

D = 1024
ECH = D // 128
TOWN = 2048
NKV = 8192
NSU = 16
II = 2816
ICH = II // 128
NEG = -8.0

_CACHE = {}


class Ctr:
    def __init__(self, handle):
        self.h = handle
        self.n = 0

    def inc(self, by=1):
        self.n += by
        return self.n


def build_program(limit=None):
    nc = bass.Bass("TRN2", target_bir_lowering=False)

    xq_in = nc.dram_tensor("xq", [TOWN, D], F32, kind="ExternalInput").ap()
    xkv_in = nc.dram_tensor("xkv", [NKV, D], F32, kind="ExternalInput").ap()
    wqT_in = nc.dram_tensor("wqT", [D, D], F32, kind="ExternalInput").ap()
    wkT_in = nc.dram_tensor("wkT", [D, D], F32, kind="ExternalInput").ap()
    wvT_in = nc.dram_tensor("wvT", [D, D], F32, kind="ExternalInput").ap()
    bq_in = nc.dram_tensor("bq", [D], F32, kind="ExternalInput").ap()
    bk_in = nc.dram_tensor("bk", [D], F32, kind="ExternalInput").ap()
    bv_in = nc.dram_tensor("bv", [D], F32, kind="ExternalInput").ap()
    wg_in = nc.dram_tensor("wg", [II, D], F32, kind="ExternalInput").ap()
    wu_in = nc.dram_tensor("wu", [II, D], F32, kind="ExternalInput").ap()
    wd_in = nc.dram_tensor("wd", [D, II], F32, kind="ExternalInput").ap()
    ramp_in = nc.dram_tensor("ramp", [128, TOWN], F32, kind="ExternalInput").ap()
    tthr_in = nc.dram_tensor("tthr", [128, 64], F32, kind="ExternalInput").ap()
    ident_in = nc.dram_tensor("ident", [128, 128], BF16, kind="ExternalInput").ap()
    out_t = nc.dram_tensor("out", [TOWN, D], F32, kind="ExternalOutput").ap()

    h_scr = nc.dram_tensor("h_scr", [NKV, D], BF16, kind="Internal").ap()
    hq_scr = nc.dram_tensor("hq_scr", [TOWN, D], BF16, kind="Internal").ap()
    xnew_scr = nc.dram_tensor("xnew_scr", [TOWN, D], F32, kind="Internal").ap()
    wq_mine = nc.dram_tensor("wq_mine", [1152, D], BF16, kind="Internal").ap()
    wq_all1d = nc.dram_tensor("wq_all", [9216 * D], BF16, kind="Internal").ap()
    wq_all = wq_all1d.rearrange("(r c) -> r c", c=D)
    wgq_scr = wq_all[0:II, :]
    wuq_scr = wq_all[II:2 * II, :]
    wdq_scr = wq_all1d[2 * II * D:(2 * II + D) * II + 0].rearrange(
        "(r c) -> r c", c=II) if False else         wq_all1d[2 * II * D:2 * II * D + D * II].rearrange("(r c) -> r c", c=II)

    ops = {k: [] for k in ("sync", "vector", "tensor", "scalar", "gpsimd")}

    def em(eng, fn):
        ops[eng].append(fn)

    def finalize():
        with nc.Block() as block:
            @block.sync
            def _(E):
                for f in ops["sync"]:
                    f(E)

            @block.vector
            def _(E):
                for f in ops["vector"]:
                    f(E)

            @block.tensor
            def _(E):
                for f in ops["tensor"]:
                    f(E)

            @block.scalar
            def _(E):
                for f in ops["scalar"]:
                    f(E)

            @block.gpsimd
            def _(E):
                for f in ops["gpsimd"]:
                    f(E)
        return nc

    pstack = ExitStack()
    bstack = ExitStack()
    P = pstack.enter_context
    B = bstack.enter_context

    ld = Ctr(P(nc.semaphore("ld")))
    sh = Ctr(P(nc.semaphore("sh")))
    so = Ctr(P(nc.semaphore("so")))
    pe = Ctr(P(nc.semaphore("pe")))
    dv = Ctr(P(nc.semaphore("dv")))
    gp = Ctr(P(nc.semaphore("gp")))
    wcol = Ctr(P(nc.semaphore("wcol")))
    ac = Ctr(P(nc.semaphore("ac")))

    def dma(eng, out, in_, ctr, transpose=False):
        n = ctr.inc(16)
        if transpose:
            em(eng, lambda E, o=out, i=in_, c=ctr.h:
               E.dma_start(out=o, in_=i, transpose=True).then_inc(c, 16))
        else:
            em(eng, lambda E, o=out, i=in_, c=ctr.h:
               E.dma_start(out=o, in_=i).then_inc(c, 16))
        return n

    def wait(eng, ctr, n):
        if n > 0:
            em(eng, lambda E, c=ctr.h, v=n: E.wait_ge(c, v))

    tthr_sb = P(nc.sbuf_tensor("tthr_sb", [128, 64], F32))
    stat_sb = P(nc.sbuf_tensor("stat_sb", [128, 256], F32))
    rstd_all = P(nc.sbuf_tensor("rstd_all", [128, 80], F32))
    negm_all = P(nc.sbuf_tensor("negm_all", [128, 80], F32))
    bqc_sb = P(nc.sbuf_tensor("bqc_sb", [128, ECH], F32))
    bkc_sb = P(nc.sbuf_tensor("bkc_sb", [128, ECH], F32))
    bvb_sb = P(nc.sbuf_tensor("bvb_sb", [128, D], F32))
    ones1_bf = P(nc.sbuf_tensor("ones1_bf", [128, 1], BF16))
    rstdf_sb = P(nc.sbuf_tensor("rstdf_sb", [128, 16], F32))
    negmf_sb = P(nc.sbuf_tensor("negmf_sb", [128, 16], F32))
    rinv_sb = P(nc.sbuf_tensor("rinv_sb", [128, 16], F32))
    brow = P(nc.sbuf_tensor("brow", [1, 3 * D], F32))
    onef = P(nc.sbuf_tensor("onef", [1, 1], F32))
    onerow = P(nc.sbuf_tensor("onerow", [1, 128], F32))
    ident_sb = P(nc.sbuf_tensor("ident_sb", [128, 128], BF16))

    wkT_bf = B(nc.sbuf_tensor("wkT_bf", [128, ECH * D], BF16))
    wvT_bf = B(nc.sbuf_tensor("wvT_bf", [128, ECH * D], BF16))
    q_sb = B(nc.sbuf_tensor("q_sb", [128, ECH * TOWN], BF16))
    o_sb = B(nc.sbuf_tensor("o_sb", [128, 16 * D], F32))
    l_sb = B(nc.sbuf_tensor("l_sb", [128, 16], F32))
    xb = B(nc.sbuf_tensor("xb", [128, D], F32))
    hb = B(nc.sbuf_tensor("hb", [128, 2 * D], BF16))
    hf = B(nc.sbuf_tensor("hf", [128, ECH * 512], BF16))
    k_sb = B(nc.sbuf_tensor("k_sb", [128, ECH * 512], BF16))
    v_sb = B(nc.sbuf_tensor("v_sb", [128, 4 * D], BF16))
    pt_sb = B(nc.sbuf_tensor("pt_sb", [128, 4 * TOWN], BF16))
    ramp_sb = B(nc.sbuf_tensor("ramp_sb", [128, TOWN], F32))
    ev_scr = B(nc.sbuf_tensor("ev_scr", [128, 512], F32))
    es_scr = B(nc.sbuf_tensor("es_scr", [128, 512], F32))

    wstage = o_sb[:, 0:ECH * D]
    wq_bf = pt_sb

    ps_scope = ExitStack()
    pp = ps_scope.enter_context
    ps_pvA = pp(nc.psum_tensor("ps_pvA", [128, D], F32))
    ps_pvB = pp(nc.psum_tensor("ps_pvB", [128, D], F32))
    ps_s0 = pp(nc.psum_tensor("ps_s0", [128, 512], F32))
    ps_s1 = pp(nc.psum_tensor("ps_s1", [128, 512], F32))
    ps_l = pp(nc.psum_tensor("ps_l", [128, 512], F32))
    ps_sq = pp(nc.psum_tensor("ps_sq", [128, 512], F32))

    def _load_act(idx):
        def f(E):
            E.add_instruction(mybir.InstLoadActFuncSet(
                name=nc.get_next_instruction_name(),
                act_func_set_id=idx, ins=[], outs=[]))
        return f

    dma("sync", ramp_sb[:], ramp_in, ld)
    dma("sync", tthr_sb[:], tthr_in, ld)
    dma("sync", ident_sb[:], ident_in, ld)
    dma("sync", brow[0:1, 0:D], bq_in[None, :], ld)
    dma("sync", brow[0:1, D:2 * D], bk_in[None, :], ld)
    n_brow = dma("sync", brow[0:1, 2 * D:3 * D], bv_in[None, :], ld)

    em("vector", lambda E: E.memset(onef[:], 1.0))
    em("vector", lambda E: E.memset(onerow[:], 1.0))
    em("vector", lambda E: E.memset(ones1_bf[:], 1.0).then_inc(dv.h, 1))
    dv_ones = dv.inc()

    dv_wcast = 0
    for name, win, dst in (("q", wqT_in, None), ("k", wkT_in, wkT_bf), ("v", wvT_in, wvT_bf)):
        n_w = dma("sync", wstage.rearrange("p (c e) -> p c e", e=D),
                  win.rearrange("(c p) e -> p c e", p=128), ld)
        wait("vector", ld, n_w)
        if name == "q":
            em("vector", lambda E: E.tensor_copy(out=wq_bf[:, 0:ECH * D], in_=wstage)
               .then_inc(dv.h, 1))
        else:
            em("vector", lambda E, d=dst: E.tensor_copy(out=d[:], in_=wstage)
               .then_inc(dv.h, 1))
        dv_wcast = dv.inc()
        wait("sync", dv, dv_wcast)
    em("vector", lambda E: E.memset(o_sb[:], 0.0))
    em("vector", lambda E: E.memset(l_sb[:], 0.0).then_inc(dv.h, 1))
    dv_zero = dv.inc()

    wait("tensor", ld, n_brow)
    wait("tensor", dv, dv_ones)
    for bi in range(2):
        for ecj in range(ECH):
            em("tensor", lambda E, b=bi, c=ecj: E.matmul(
                out=ps_sq[:, b * 8 + c:b * 8 + c + 1],
                lhsT=brow[0:1, b * D + c * 128:b * D + (c + 1) * 128],
                rhs=onef[0:1, 0:1], start=True, stop=True).then_inc(pe.h, 1))
            n = pe.inc()
            wait("tensor", pe, n)
    pe_bias = pe.n
    wait("scalar", pe, pe_bias)
    em("scalar", lambda E: E.activation(out=bqc_sb[:], in_=ps_sq[:, 0:8],
                                        func=AF.Identity, bias=0.0, scale=np.float32(1.0 / 32.0)))
    em("scalar", lambda E: E.activation(out=bkc_sb[:], in_=ps_sq[:, 8:16],
                                        func=AF.Identity, bias=0.0, scale=1.0)
       .then_inc(ac.h, 1))
    ac_bias = ac.inc()
    for hh in range(2):
        wait("tensor", ac, ac_bias)
        em("tensor", lambda E, h=hh: E.matmul(
            out=ps_sq[:, 0:512],
            lhsT=onerow[0:1, :],
            rhs=brow[0:1, 2 * D + h * 512:2 * D + (h + 1) * 512],
            start=True, stop=True).then_inc(pe.h, 1))
        pe_bv = pe.inc()
        wait("scalar", pe, pe_bv)
        em("scalar", lambda E, h=hh: E.activation(
            out=bvb_sb[:, h * 512:(h + 1) * 512], in_=ps_sq[:, 0:512],
            func=AF.Identity, bias=0.0, scale=1.0).then_inc(ac.h, 1))
        ac_bias = ac.inc()
    ac_bvb = ac_bias

    def ln_tiles(x_ap_list, h_out_list, wait_loads, slot0=0):
        nt = len(x_ap_list)
        wait("scalar", ld, wait_loads)
        wait("scalar", ac, ac_bvb)
        em("scalar", lambda E: E.memzero(stat_sb[:, 8 + slot0:8 + slot0 + nt]))
        em("scalar", lambda E: E.memzero(stat_sb[:, 16 + slot0:16 + slot0 + nt]))
        for i, x in enumerate(x_ap_list):
            s = slot0 + i
            em("scalar", lambda E, x=x, s=s: E.activation(
                out=ps_sq[:], in_=x[:, 0:512], func=AF.Square, bias=0.0, scale=1.0,
                accum_out=stat_sb[:, 8 + s:9 + s]))
            em("scalar", lambda E, x=x, s=s: E.activation(
                out=ps_sq[:], in_=x[:, 512:1024], func=AF.Square, bias=0.0, scale=1.0,
                accum_out=stat_sb[:, 16 + s:17 + s]).then_inc(ac.h, 1))
            ac.inc()
        ac_sq = ac.n
        if limit == "ln0":
            finish_debug([stat_sb[:]])
            return "STOP"
        wait("vector", ld, wait_loads)
        for i, x in enumerate(x_ap_list):
            s = slot0 + i
            last = (i == len(x_ap_list) - 1)
            if last:
                em("vector", lambda E, x=x, s=s: E.reduce_sum(
                    out=stat_sb[:, s:s + 1], in_=x, axis=AX.X).then_inc(dv.h, 1))
                wait("vector", dv, dv.inc())
            else:
                em("vector", lambda E, x=x, s=s: E.reduce_sum(
                    out=stat_sb[:, s:s + 1], in_=x, axis=AX.X))
        wait("vector", ac, ac_sq)
        sl = slice(slot0, slot0 + nt)
        s2a = slice(8 + slot0, 8 + slot0 + nt)
        s2b = slice(16 + slot0, 16 + slot0 + nt)
        mu = slice(24 + slot0, 24 + slot0 + nt)
        var = slice(32 + slot0, 32 + slot0 + nt)
        rstd = slice(40 + slot0, 40 + slot0 + nt)
        ng = slice(48 + slot0, 48 + slot0 + nt)

        def _step(fn):
            em("vector", lambda E, f=fn: f(E).then_inc(dv.h, 1))
            n = dv.inc()
            wait("vector", dv, n)
            return n
        _step(lambda E: E.tensor_scalar(
            out=stat_sb[:, mu], in0=stat_sb[:, sl], scalar1=1.0 / D, scalar2=None,
            op0=ALU.mult))
        _step(lambda E: E.tensor_tensor(
            out=stat_sb[:, var], in0=stat_sb[:, s2a], in1=stat_sb[:, s2b], op=ALU.add))
        _step(lambda E: E.tensor_scalar(
            out=stat_sb[:, var], in0=stat_sb[:, var], scalar1=1.0 / D, scalar2=None,
            op0=ALU.mult))
        _step(lambda E: E.tensor_tensor(
            out=stat_sb[:, s2a], in0=stat_sb[:, mu], in1=stat_sb[:, mu], op=ALU.mult))
        dv_var = _step(lambda E: E.tensor_tensor(
            out=stat_sb[:, var], in0=stat_sb[:, var], in1=stat_sb[:, s2a],
            op=ALU.subtract))
        wait("scalar", dv, dv_var)
        em("scalar", lambda E: E.activation(
            out=stat_sb[:, s2b], in_=stat_sb[:, var], func=AF.Ln, bias=np.float32(1e-5), scale=1.0))
        em("scalar", lambda E: E.activation(
            out=stat_sb[:, rstd], in_=stat_sb[:, s2b], func=AF.Exp, bias=0.0,
            scale=np.float32(-0.5)).then_inc(ac.h, 1))
        ac_rstd = ac.inc()
        wait("vector", ac, ac_rstd)
        _step(lambda E: E.tensor_tensor(
            out=stat_sb[:, ng], in0=stat_sb[:, mu], in1=stat_sb[:, rstd], op=ALU.mult))
        dv_ng = _step(lambda E: E.tensor_scalar(
            out=stat_sb[:, ng], in0=stat_sb[:, ng], scalar1=-1.0, scalar2=None,
            op0=ALU.mult))
        wait("scalar", dv, dv_ng)
        for i, (x, h) in enumerate(zip(x_ap_list, h_out_list)):
            s = slot0 + i
            em("scalar", lambda E, x=x, h=h, s=s: E.activation(
                out=h, in_=x, func=AF.Identity,
                bias=stat_sb[:, 48 + s:49 + s], scale=stat_sb[:, 40 + s:41 + s])
               .then_inc(ac.h, 1))
            ac.inc()
        return ac.n

    dbg_out = nc.dram_tensor("dbg", [128, 16 * D], F32, kind="ExternalOutput").ap() if limit else None

    def finish_debug(ap_list):
        col = 0
        wait("sync", ac, ac.n)
        wait("sync", dv, dv.n)
        wait("sync", pe, pe.n)
        nd = 0
        for ap_ in ap_list:
            ncol = ap_.shape[-1]
            nd = dma("sync", dbg_out[:, col:col + ncol], ap_, so)
            col += ncol
        wait("sync", so, nd)

    if limit == "init":
        finish_debug([bvb_sb[:], bqc_sb[:], bkc_sb[:]])
        return finalize()
    hbf0 = hb[:].bitcast(F32)
    evs0 = ev_scr[:, 0:512]
    evs1 = es_scr[:, 0:512]
    dv_red = {0: 0, 1: 0}
    ac_sq2 = {0: 0, 1: 0}
    for b8 in range(2):
        for t8 in range(8):
            t = b8 * 8 + t8
            sl = t % 2
            xbuf = xb[:] if sl == 0 else hbf0
            wait("sync", dv, dv_red[sl])
            wait("sync", ac, ac_sq2[sl])
            n_x = dma("sync", xb[:] if sl == 0 else hbf0,
                      xq_in[t * 128:(t + 1) * 128, :], ld)
            wait("vector", ld, n_x)
            em("vector", lambda E, x=xbuf, t=t: E.reduce_sum(
                out=stat_sb[:, t:t + 1], in_=x, axis=AX.X).then_inc(dv.h, 1))
            dv_red[sl] = dv.inc()
            wait("scalar", ld, n_x)
            em("scalar", lambda E, t=t: E.memzero(stat_sb[:, 64 + t:65 + t]))
            em("scalar", lambda E, t=t: E.memzero(stat_sb[:, 128 + t:129 + t]))
            em("scalar", lambda E, x=xbuf, t=t: E.activation(
                out=ps_sq[:], in_=x[:, 0:512], func=AF.Square, bias=0.0, scale=1.0,
                accum_out=stat_sb[:, 64 + t:65 + t]))
            em("scalar", lambda E, x=xbuf, t=t: E.activation(
                out=ps_sq[:], in_=x[:, 512:1024], func=AF.Square, bias=0.0, scale=1.0,
                accum_out=stat_sb[:, 128 + t:129 + t]).then_inc(ac.h, 1))
            ac_sq2[sl] = ac.inc()
        c0 = b8 * 8
        q0 = 64 + b8 * 8
        wait("vector", ac, ac_sq2[0])
        wait("vector", ac, ac_sq2[1])
        wait("vector", dv, max(dv_red[0], dv_red[1]))

        def _stq(fn):
            em("vector", lambda E, f=fn: f(E).then_inc(dv.h, 1))
            n = dv.inc()
            wait("vector", dv, n)
            return n
        _stq(lambda E, c0=c0: E.tensor_scalar(
            out=stat_sb[:, 192 + c0:200 + c0], in0=stat_sb[:, c0:c0 + 8],
            scalar1=1.0 / D, scalar2=None, op0=ALU.mult))
        _stq(lambda E, c0=c0: E.tensor_tensor(
            out=stat_sb[:, 64 + c0:72 + c0], in0=stat_sb[:, 64 + c0:72 + c0],
            in1=stat_sb[:, 128 + c0:136 + c0], op=ALU.add))
        _stq(lambda E, c0=c0: E.tensor_scalar(
            out=stat_sb[:, 64 + c0:72 + c0], in0=stat_sb[:, 64 + c0:72 + c0],
            scalar1=1.0 / D, scalar2=None, op0=ALU.mult))
        _stq(lambda E, c0=c0: E.tensor_tensor(
            out=stat_sb[:, 128 + c0:136 + c0], in0=stat_sb[:, 192 + c0:200 + c0],
            in1=stat_sb[:, 192 + c0:200 + c0], op=ALU.mult))
        dvv_ = _stq(lambda E, c0=c0: E.tensor_tensor(
            out=stat_sb[:, 64 + c0:72 + c0], in0=stat_sb[:, 64 + c0:72 + c0],
            in1=stat_sb[:, 128 + c0:136 + c0], op=ALU.subtract))
        wait("scalar", dv, dvv_)
        em("scalar", lambda E, c0=c0: E.activation(
            out=stat_sb[:, 128 + c0:136 + c0], in_=stat_sb[:, 64 + c0:72 + c0],
            func=AF.Ln, bias=np.float32(1e-5), scale=1.0))
        em("scalar", lambda E, c0=c0, q0=q0: E.activation(
            out=rstd_all[:, q0:q0 + 8], in_=stat_sb[:, 128 + c0:136 + c0],
            func=AF.Exp, bias=0.0, scale=np.float32(-0.5)).then_inc(ac.h, 1))
        acr_ = ac.inc()
        wait("vector", ac, acr_)
        _stq(lambda E, c0=c0, q0=q0: E.tensor_tensor(
            out=negm_all[:, q0:q0 + 8], in0=stat_sb[:, 192 + c0:200 + c0],
            in1=rstd_all[:, q0:q0 + 8], op=ALU.mult))
        _stq(lambda E, q0=q0: E.tensor_scalar(
            out=negm_all[:, q0:q0 + 8], in0=negm_all[:, q0:q0 + 8],
            scalar1=-1.0, scalar2=None, op0=ALU.mult))
    dv_statsq_done = dv.n
    ac_norm = {0: 0, 1: 0}
    sh_hstore_prev = 0
    sh_hstore_pp = 0
    sh_qtile = []
    wait("scalar", dv, dv_statsq_done)
    for t in range(16):
        sl = t % 2
        wait("sync", ac, ac_norm[sl])
        if sl == 0:
            n_x = dma("sync", xb[:], xq_in[t * 128:(t + 1) * 128, :], ld)
        else:
            n_x = dma("sync", evs0, xq_in[t * 128:(t + 1) * 128, 0:512], ld)
            n_x = dma("sync", evs1, xq_in[t * 128:(t + 1) * 128, 512:1024], ld)
        hsl = t % 2
        wait("scalar", ld, n_x)
        wait("scalar", sh, sh_hstore_pp)
        if sl == 0:
            em("scalar", lambda E, t=t, h=hsl: E.activation(
                out=hb[:, h * D:h * D + D], in_=xb[:], func=AF.Identity,
                bias=negm_all[:, 64 + t:65 + t], scale=rstd_all[:, 64 + t:65 + t])
               .then_inc(ac.h, 1))
        else:
            em("scalar", lambda E, t=t, h=hsl: E.activation(
                out=hb[:, h * D:h * D + 512], in_=evs0, func=AF.Identity,
                bias=negm_all[:, 64 + t:65 + t], scale=rstd_all[:, 64 + t:65 + t]))
            em("scalar", lambda E, t=t, h=hsl: E.activation(
                out=hb[:, h * D + 512:h * D + D], in_=evs1, func=AF.Identity,
                bias=negm_all[:, 64 + t:65 + t], scale=rstd_all[:, 64 + t:65 + t])
               .then_inc(ac.h, 1))
        ac_norm[sl] = ac.inc()
        wait("scalar", ac, ac_norm[sl])
        sh_hstore_pp = sh_hstore_prev
        sh_hstore_prev = dma("scalar", hq_scr[t * 128:(t + 1) * 128, :],
                             hb[:, hsl * D:(hsl + 1) * D], sh)
        sh_qtile.append(sh_hstore_prev)
    sh_hdone_q = sh_hstore_prev

    pe_hf_prev = 0
    ac_evacA = {0: 0, 1: 0}
    dv_evA = {0: 0, 1: 0}
    for g in range(1 if limit in ("A1", "A0") else 4):
        wait("sync", pe, pe_hf_prev)
        wait("sync", sh, sh_qtile[g * 4 + 3])
        n_hf = 0
        for dc in range(ECH):
            n_hf = dma("sync", hf[:, dc * 512:(dc + 1) * 512],
                       hq_scr[g * 512:(g + 1) * 512, dc * 128:(dc + 1) * 128],
                       ld, transpose=True)
        if limit == "hf" and g == 0:
            wait("sync", ld, n_hf)
            finish_debug([hf[:].bitcast(F32)])
            return finalize()
        wait("tensor", ld, n_hf)
        wait("tensor", dv, dv_wcast)
        for ec in range(1 if limit == "A0" else ECH):
            hsel = ec % 2
            wait("tensor", ac, ac_evacA[hsel])
            for dc in range(ECH):
                em("tensor", lambda E, ec=ec, dc=dc, h=hsel: E.matmul(
                    out=ps_pvA[:, h * 512:(h + 1) * 512],
                    lhsT=wq_bf[:, dc * D + ec * 128:dc * D + ec * 128 + 128],
                    rhs=hf[:, dc * 512:(dc + 1) * 512],
                    start=(dc == 0), stop=(dc == ECH - 1)).then_inc(pe.h, 1))
                pe.inc()
            pe_q = pe.n
            wait("scalar", pe, pe_q)
            wait("scalar", gp, max(dv_evA[0], dv_evA[1]))
            em("scalar", lambda E, ec=ec, h=hsel: E.activation(
                out=ev_scr[:, 0:512],
                in_=ps_pvA[:, h * 512:(h + 1) * 512], func=AF.Identity,
                bias=bqc_sb[:, ec:ec + 1], scale=np.float32(1.0 / 32.0)).then_inc(ac.h, 1))
            ac_evacA[hsel] = ac.inc()
            wait("vector", ac, ac_evacA[hsel])
            em("vector", lambda E, ec=ec, g=g: E.tensor_copy(
                out=q_sb[:, ec * TOWN + g * 512:ec * TOWN + (g + 1) * 512],
                in_=ev_scr[:, 0:512]).then_inc(gp.h, 1))
            dv_evA[hsel] = gp.inc()
        pe_hf_prev = pe.n

    if limit in ("A", "A1", "A0"):
        finish_debug([q_sb[:].bitcast(F32)])
        return finalize()
    wsrc = [xb[:, 0:256], xb[:, 256:512]]
    wxn = xb[:, 512:768]
    wtt = xb[:, 768:1024]
    wacc = hb[:, 0:512].bitcast(F32)
    wi1 = hb[:, 512:1024].bitcast(I32)
    wsg = hb[:, 1024:1280]
    wdst = [hb[:, 1280:1536], hb[:, 1536:1792]]
    wsmall = hb[:, 1792:2048].bitcast(F32)
    wbm = wsmall[:, 0:16]
    wsc = wsmall[:, 16:32]
    wrx = wsmall[:, 32:48]
    wchunks = []
    for (w_in_t, w_scr, nrow) in ((wg_in, wgq_scr, II), (wu_in, wuq_scr, II),
                                  (wd_in, wdq_scr, D)):
        ncol = D if nrow == II else II
        for t in range(nrow // 128):
            for coff in range(0, ncol, 256):
                wchunks.append((w_in_t, w_scr, t, coff))
    wpos = {"i": 0}
    wslot = {0: 0, 1: 0}
    wsh = {0: 0, 1: 0}
    wst_state = {"sh_last": 0}

    def emit_wchunks(n, cap=128):
        for _ in range(n):
            i = wpos["i"]
            if i >= min(cap, len(wchunks)):
                return
            wpos["i"] += 1
            w_in_t, w_scr, t, coff = wchunks[i]
            slot = i % 2
            wait("sync", dv, wslot[slot])
            n_w = dma("sync", wsrc[slot], w_in_t[t * 128:(t + 1) * 128, coff:coff + 256], ld)
            wait("scalar", dv, max(wslot[0], wslot[1]))
            wait("scalar", ld, n_w)
            em("scalar", lambda E, s=slot: E.activation(
                out=wtt[:], in_=wsrc[s], func=AF.Abs, bias=0.0, scale=1.0))
            em("scalar", lambda E, s=slot: E.activation(
                out=wsg[:], in_=wsrc[s], func=AF.Sign, bias=0.0, scale=1.0)
               .then_inc(ac.h, 1))
            acsg = ac.inc()
            wait("vector", ac, acsg)
            wait("vector", sh, wsh[slot])

            def _w(fn):
                em("vector", lambda E, f=fn: f(E).then_inc(dv.h, 1))
                n = dv.inc()
                wait("vector", dv, n)
                return n
            g16 = wtt.rearrange("p (g k) -> p g k", k=16)
            _w(lambda E: E.reduce_max(out=wbm, in_=g16, axis=AX.X))
            _w(lambda E: E.reciprocal(out=wrx, in_=wbm))
            rxb = wrx.unsqueeze(2).broadcast_to([128, 16, 16])
            scb = wsc.unsqueeze(2).broadcast_to([128, 16, 16])
            em("vector", lambda E: E.tensor_tensor(
                out=wxn.rearrange("p (g k) -> p g k", k=16), in0=g16, in1=rxb,
                op=ALU.mult))
            em("vector", lambda E: E.tensor_scalar(
                out=wi1[:], in0=wxn, scalar1=12.0, scalar2=None, op0=ALU.mult))
            em("vector", lambda E: E.tensor_scalar(
                out=wacc[:], in0=wi1[:], scalar1=4.0, scalar2=0.5,
                op0=ALU.min, op1=ALU.mult))
            em("vector", lambda E: E.tensor_scalar(
                out=wi1[:], in0=wxn, scalar1=6.0, scalar2=None, op0=ALU.mult))
            em("vector", lambda E: E.tensor_scalar(
                out=wtt[:], in0=wi1[:], scalar1=2.0, scalar2=2.0,
                op0=ALU.subtract, op1=ALU.min))
            em("vector", lambda E: E.scalar_tensor_tensor(
                out=wacc[:], in0=wtt[:], scalar=0.0, in1=wacc[:],
                op0=ALU.max, op1=ALU.add))
            em("vector", lambda E: E.tensor_scalar(
                out=wi1[:], in0=wxn, scalar1=3.0, scalar2=None, op0=ALU.mult))
            em("vector", lambda E: E.tensor_scalar(
                out=wsc, in0=wbm, scalar1=1.0 / 6.0, scalar2=1e-12,
                op0=ALU.mult, op1=ALU.max))
            em("vector", lambda E: E.tensor_scalar(
                out=wtt[:], in0=wi1[:], scalar1=2.0, scalar2=4.0,
                op0=ALU.mult, op1=ALU.subtract))
            em("vector", lambda E: E.scalar_tensor_tensor(
                out=wacc[:], in0=wtt[:], scalar=0.0, in1=wacc[:],
                op0=ALU.max, op1=ALU.add))
            em("vector", lambda E: E.tensor_tensor(
                out=wtt.rearrange("p (g k) -> p g k", k=16),
                in0=wacc[:].rearrange("p (g k) -> p g k", k=16), in1=scb,
                op=ALU.mult))
            em("vector", lambda E, s=slot: E.tensor_tensor(
                out=wdst[s], in0=wtt[:], in1=wsg[:], op=ALU.mult).then_inc(dv.h, 1))
            wslot[slot] = dv.inc()
            wait("scalar", dv, wslot[slot])
            wsh[slot] = dma("scalar", w_scr[t * 128:(t + 1) * 128, coff:coff + 256],
                            wdst[slot], sh)
            wst_state["sh_last"] = wsh[slot]

    ac_exp_last = {0: 0, 1: 0}
    dv_mask_b_last = {0: 0, 1: 0}
    ac_kevac = {0: 0, 1: 0}
    dv_vevac = {0: 0, 1: 0}
    dv_oacc = {}
    dv_lacc_prev = 0
    pe_pv_all_prev = 0
    nsu_run = limit if isinstance(limit, int) else NSU

    def emit_su(su):
        nonlocal ac_exp_last, dv_mask_b_last, dv_lacc_prev, pe_pv_all_prev, \
            pe_hf_prev
        kt, half = su // 2, su % 2
        key0 = kt * 1024 + half * 512
        wait("sync", pe, pe_hf_prev)
        wait("sync", sh, sh_tile[key0 // 128 + 3])
        n_hf = 0
        for dc in range(ECH):
            n_hf = dma("sync", hf[:, dc * 512:(dc + 1) * 512],
                       h_scr[key0:key0 + 512, dc * 128:(dc + 1) * 128],
                       ld, transpose=True)
        if su >= 8:
            if su == 8:
                wait("sync", ac, ac_norm_seq)
                wait("sync", ac, max(ac_sq2[0], ac_sq2[1]))
                wait("sync", dv, max(dv_red[0], dv_red[1]))
                wait("sync", sh, sh_hstore_prev)
            emit_wchunks(16)
        wait("tensor", ld, n_hf)
        wait("tensor", dv, dv_oacc.get(14, 0))
        for ec in range(ECH):
            hsel = ec % 2
            wait("tensor", ac, ac_kevac[hsel])
            wait("tensor", dv, dv_vevac[hsel])
            for dc in range(ECH):
                em("tensor", lambda E, ec=ec, dc=dc, h=hsel: E.matmul(
                    out=ps_pvA[:, h * 512:(h + 1) * 512],
                    lhsT=wkT_bf[:, dc * D + ec * 128:dc * D + ec * 128 + 128],
                    rhs=hf[:, dc * 512:(dc + 1) * 512],
                    start=(dc == 0), stop=(dc == ECH - 1)).then_inc(pe.h, 1))
                pe.inc()
            pe_k = pe.n
            wait("scalar", pe, pe_k)
            wait("scalar", gp, max(dv_vevac[0], dv_vevac[1]))
            em("scalar", lambda E, ec=ec, h=hsel: E.activation(
                out=ev_scr[:, 0:512],
                in_=ps_pvA[:, h * 512:(h + 1) * 512], func=AF.Identity,
                bias=bkc_sb[:, ec:ec + 1], scale=1.0).then_inc(ac.h, 1))
            ac_kevac[hsel] = ac.inc()
            wait("vector", ac, ac_kevac[hsel])
            em("vector", lambda E, ec=ec: E.tensor_copy(
                out=k_sb[:, ec * 512:(ec + 1) * 512],
                in_=ev_scr[:, 0:512]).then_inc(gp.h, 1))
            dv_vevac[hsel] = gp.inc()
        for kc in range(4):
            for vh in range(2):
                hsel = (kc * 2 + vh) % 2
                wait("tensor", ac, ac_kevac[hsel])
                wait("tensor", dv, dv_vevac[hsel])
                for dc in range(ECH):
                    em("tensor", lambda E, kc=kc, vh=vh, dc=dc, h=hsel: E.matmul(
                        out=ps_pvA[:, h * 512:(h + 1) * 512],
                        lhsT=hf[:, dc * 512 + kc * 128:dc * 512 + kc * 128 + 128],
                        rhs=wvT_bf[:, dc * D + vh * 512:dc * D + (vh + 1) * 512],
                        start=(dc == 0), stop=(dc == ECH - 1)).then_inc(pe.h, 1))
                    pe.inc()
                pe_v = pe.n
                wait("scalar", pe, pe_v)
                wait("scalar", gp, max(dv_vevac[0], dv_vevac[1]))
                em("scalar", lambda E, h=hsel: E.activation(
                    out=ev_scr[:, 0:512],
                    in_=ps_pvA[:, h * 512:(h + 1) * 512], func=AF.Identity,
                    bias=0.0, scale=1.0).then_inc(ac.h, 1))
                ac_kevac[hsel] = ac.inc()
                wait("vector", ac, ac_kevac[hsel])
                em("vector", lambda E, kc=kc, vh=vh: E.tensor_tensor(
                    out=v_sb[:, kc * D + vh * 512:kc * D + (vh + 1) * 512],
                    in0=ev_scr[:, 0:512],
                    in1=bvb_sb[:, vh * 512:(vh + 1) * 512], op=ALU.add).then_inc(gp.h, 1))
                dv_vevac[hsel] = gp.inc()
        pe_hf_prev = pe.n

        gp_kv_done = gp.n
        wait("tensor", gp, gp_kv_done)
        wait("tensor", gp, max(dv_evA[0], dv_evA[1]))
        ac_exp = dict(ac_exp_last)
        pair = 0
        dv_mask_b = dict(dv_mask_b_last)
        for kc in range(4):
            for qh in range(4):
                bank = pair % 2
                ps_s = ps_s0 if bank == 0 else ps_s1
                wait("tensor", ac, ac_exp[bank])
                for ec in range(ECH):
                    em("tensor", lambda E, kc=kc, qh=qh, ec=ec, ps=ps_s: E.matmul(
                        out=ps[:, 0:512],
                        lhsT=k_sb[:, ec * 512 + kc * 128:ec * 512 + kc * 128 + 128],
                        rhs=q_sb[:, ec * TOWN + qh * 512:ec * TOWN + (qh + 1) * 512],
                        start=(ec == 0), stop=(ec == ECH - 1)).then_inc(pe.h, 1))
                    pe.inc()
                pe_s = pe.n
                wait("scalar", pe, pe_s)
                wait("scalar", gp, max(dv_mask_b[0], dv_mask_b[1]))
                em("scalar", lambda E, b=bank, ps=ps_s: E.activation(
                    out=es_scr[:, 0:512],
                    in_=ps[:, 0:512], func=AF.Exp, bias=np.float32(NEG),
                    scale=1.0).then_inc(ac.h, 1))
                ac_exp[bank] = ac.inc()
                wait("vector", ac, ac_exp[bank])
                wait("vector", pe, pe_pv_all_prev)
                em("vector", lambda E, su=su, kc=kc, qh=qh: E.scalar_tensor_tensor(
                    out=pt_sb[:, kc * TOWN + qh * 512:kc * TOWN + (qh + 1) * 512],
                    in0=ramp_sb[:, qh * 512:(qh + 1) * 512],
                    scalar=tthr_sb[:, su * 4 + kc:su * 4 + kc + 1],
                    in1=es_scr[:, 0:512],
                    op0=ALU.is_ge, op1=ALU.mult).then_inc(gp.h, 1))
                dv_mask_b[bank] = gp.inc()
                pair += 1
        ac_exp_last = ac_exp
        dv_mask_b_last = dict(dv_mask_b)
        gp_mask_all = gp.n

        dv_oacc_prev = dict(dv_oacc)
        wait("tensor", gp, gp_mask_all)
        wait("tensor", dv, dv_lacc_prev)
        lbase = (su % 2) * 16
        pe_pv = {}
        for qs in range(16):
            buf = qs % 2
            ps_o = ps_pvA if buf == 0 else ps_pvB
            if qs >= 2:
                wait("tensor", dv, dv_oacc[qs - 2])
            elif qs == 1:
                wait("tensor", dv, dv_oacc_prev.get(15, 0))
            for kc in range(4):
                em("tensor", lambda E, qs=qs, kc=kc, ps=ps_o: E.matmul(
                    out=ps[:, 0:512],
                    lhsT=pt_sb[:, kc * TOWN + qs * 128:kc * TOWN + qs * 128 + 128],
                    rhs=v_sb[:, kc * D:kc * D + 512],
                    start=(kc == 0), stop=(kc == 3)).then_inc(pe.h, 1))
                pe.inc()
                em("tensor", lambda E, qs=qs, kc=kc, ps=ps_o: E.matmul(
                    out=ps[:, 512:1024],
                    lhsT=pt_sb[:, kc * TOWN + qs * 128:kc * TOWN + qs * 128 + 128],
                    rhs=v_sb[:, kc * D + 512:kc * D + 1024],
                    start=(kc == 0), stop=(kc == 3)).then_inc(pe.h, 1))
                pe.inc()
                em("tensor", lambda E, qs=qs, kc=kc, lc=lbase + qs: E.matmul(
                    out=ps_l[:, lc:lc + 1],
                    lhsT=pt_sb[:, kc * TOWN + qs * 128:kc * TOWN + qs * 128 + 128],
                    rhs=ones1_bf[:, 0:1],
                    start=(kc == 0), stop=(kc == 3)).then_inc(pe.h, 1))
                pe.inc()
            pe_pv[qs] = pe.n
            wait("vector", pe, pe_pv[qs])
            if su == 0 and qs == 0:
                wait("vector", dv, dv_zero)
            em("vector", lambda E, qs=qs, ps=ps_o: E.tensor_tensor(
                out=o_sb[:, qs * D:(qs + 1) * D], in0=o_sb[:, qs * D:(qs + 1) * D],
                in1=ps[:, 0:1024], op=ALU.add).then_inc(dv.h, 1))
            dv_oacc[qs] = dv.inc()
        pe_pv_all_prev = pe.n
        wait("vector", pe, pe_pv[15])
        em("vector", lambda E, lb=lbase: E.tensor_tensor(
            out=l_sb[:, 0:16], in0=l_sb[:, 0:16], in1=ps_l[:, lb:lb + 16],
            op=ALU.add).then_inc(dv.h, 1))
        dv_lacc_prev = dv.inc()


    dv_chain_prev = 0
    chain_done = []
    sh_tile = []
    ac_norm_seq = max(ac_norm[0], ac_norm[1])
    emitted_su = set()
    for b8 in range(8):
        for t8 in range(8):
            t = b8 * 8 + t8
            sl = t % 2
            xbuf = xb[:] if sl == 0 else hbf0
            wait("sync", dv, dv_red[sl])
            wait("sync", ac, ac_sq2[sl])
            wait("sync", ac, ac_norm_seq)
            if sl == 1:
                wait("sync", sh, sh_hstore_prev)
            n_x = dma("sync", xbuf, xkv_in[t * 128:(t + 1) * 128, :], ld)
            wait("vector", ld, n_x)
            em("vector", lambda E, x=xbuf, t=t: E.reduce_sum(
                out=stat_sb[:, t:t + 1], in_=x, axis=AX.X).then_inc(dv.h, 1))
            dv_red[sl] = dv.inc()
            wait("scalar", ld, n_x)
            em("scalar", lambda E, x=xbuf, t=t: E.memzero(stat_sb[:, 64 + t:65 + t]))
            em("scalar", lambda E, x=xbuf, t=t: E.memzero(stat_sb[:, 128 + t:129 + t]))
            em("scalar", lambda E, x=xbuf, t=t: E.activation(
                out=ps_sq[:], in_=x[:, 0:512], func=AF.Square, bias=0.0, scale=1.0,
                accum_out=stat_sb[:, 64 + t:65 + t]))
            em("scalar", lambda E, x=xbuf, t=t: E.activation(
                out=ps_sq[:], in_=x[:, 512:1024], func=AF.Square, bias=0.0, scale=1.0,
                accum_out=stat_sb[:, 128 + t:129 + t]).then_inc(ac.h, 1))
            ac_sq2[sl] = ac.inc()
        c0 = b8 * 8
        wait("vector", ac, ac_sq2[0])
        wait("vector", ac, ac_sq2[1])
        wait("vector", dv, max(dv_red[0], dv_red[1]))

        def _stp(fn):
            em("vector", lambda E, f=fn: f(E).then_inc(dv.h, 1))
            n = dv.inc()
            wait("vector", dv, n)
            return n
        _stp(lambda E, c0=c0: E.tensor_scalar(
            out=stat_sb[:, 192 + c0:200 + c0], in0=stat_sb[:, c0:c0 + 8],
            scalar1=1.0 / D, scalar2=None, op0=ALU.mult))
        _stp(lambda E, c0=c0: E.tensor_tensor(
            out=stat_sb[:, 64 + c0:72 + c0], in0=stat_sb[:, 64 + c0:72 + c0],
            in1=stat_sb[:, 128 + c0:136 + c0], op=ALU.add))
        _stp(lambda E, c0=c0: E.tensor_scalar(
            out=stat_sb[:, 64 + c0:72 + c0], in0=stat_sb[:, 64 + c0:72 + c0],
            scalar1=1.0 / D, scalar2=None, op0=ALU.mult))
        _stp(lambda E, c0=c0: E.tensor_tensor(
            out=stat_sb[:, 128 + c0:136 + c0], in0=stat_sb[:, 192 + c0:200 + c0],
            in1=stat_sb[:, 192 + c0:200 + c0], op=ALU.mult))
        dvv_ = _stp(lambda E, c0=c0: E.tensor_tensor(
            out=stat_sb[:, 64 + c0:72 + c0], in0=stat_sb[:, 64 + c0:72 + c0],
            in1=stat_sb[:, 128 + c0:136 + c0], op=ALU.subtract))
        wait("scalar", dv, dvv_)
        em("scalar", lambda E, c0=c0: E.activation(
            out=stat_sb[:, 128 + c0:136 + c0], in_=stat_sb[:, 64 + c0:72 + c0],
            func=AF.Ln, bias=np.float32(1e-5), scale=1.0))
        em("scalar", lambda E, c0=c0: E.activation(
            out=rstd_all[:, c0:c0 + 8], in_=stat_sb[:, 128 + c0:136 + c0],
            func=AF.Exp, bias=0.0, scale=np.float32(-0.5)).then_inc(ac.h, 1))
        acr_ = ac.inc()
        wait("vector", ac, acr_)
        _stp(lambda E, c0=c0: E.tensor_tensor(
            out=negm_all[:, c0:c0 + 8], in0=stat_sb[:, 192 + c0:200 + c0],
            in1=rstd_all[:, c0:c0 + 8], op=ALU.mult))
        dv_chain_prev = _stp(lambda E, c0=c0: E.tensor_scalar(
            out=negm_all[:, c0:c0 + 8], in0=negm_all[:, c0:c0 + 8],
            scalar1=-1.0, scalar2=None, op0=ALU.mult))
        chain_done.append(dv_chain_prev)
        wait("scalar", dv, dv_chain_prev)
        for t8 in range(8):
            t = b8 * 8 + t8
            sl = t % 2
            wait("sync", ac, ac_norm_seq)
            wait("sync", dv, dv_red[0])
            wait("sync", ac, ac_sq2[0])
            n_x = dma("sync", xb[:], xkv_in[t * 128:(t + 1) * 128, :], ld)
            hsl = t % 2
            wait("scalar", ld, n_x)
            wait("scalar", sh, sh_hstore_pp)
            em("scalar", lambda E, t=t, h=hsl: E.activation(
                out=hb[:, h * D:h * D + D], in_=xb[:], func=AF.Identity,
                bias=negm_all[:, t:t + 1], scale=rstd_all[:, t:t + 1])
               .then_inc(ac.h, 1))
            ac_norm_seq = ac.inc()
            wait("scalar", ac, ac_norm_seq)
            sh_hstore_pp = sh_hstore_prev
            sh_hstore_prev = dma("scalar", h_scr[t * 128:(t + 1) * 128, :],
                                 hb[:, hsl * D:(hsl + 1) * D], sh)
            sh_tile.append(sh_hstore_prev)
        if b8 < nsu_run:
            emit_su(b8)
            emitted_su.add(b8)
    dv_stats_done = dv.n
    sh_hdone = sh_hstore_prev

    for su in range(nsu_run):
        if su not in emitted_su:
            emit_su(su)

    if isinstance(limit, int):
        finish_debug([o_sb[:, 0:15 * D], l_sb[:], stat_sb[:]])
        return finalize()
    emit_wchunks(128)
    dv_w_final = max(wslot[0], wslot[1])
    sh_w_fine = wst_state["sh_last"]


    wait("vector", dv, dv_lacc_prev)
    em("vector", lambda E: E.reciprocal(out=rinv_sb[:], in_=l_sb[:]).then_inc(dv.h, 1))
    dv_rinv = dv.inc()
    wait("vector", dv, dv_rinv)
    hbf = hb[:].bitcast(F32)
    sh_xnew_prev = sh_hstore_prev
    dv_fold = {}
    for r in range(16):
        if r == 0:
            wait("sync", dv, dv_w_final)
            wait("vector", sh, sh_w_fine)
        else:
            wait("sync", dv, dv_fold.get(r - 1, 0))
        n_x = dma("sync", xb[:], xq_in[r * 128:(r + 1) * 128, :], ld)
        wait("vector", ld, n_x)
        wait("vector", dv, dv_rinv)
        wait("vector", sh, sh_xnew_prev)
        em("vector", lambda E, r=r: E.scalar_tensor_tensor(
            out=hbf, in0=o_sb[:, r * D:(r + 1) * D],
            scalar=rinv_sb[:, r:r + 1], in1=xb[:],
            op0=ALU.mult, op1=ALU.add).then_inc(dv.h, 1))
        dv_fold[r] = dv.inc()
        if r == 15:
            em("vector", lambda E, r=r: E.reduce_sum(
                out=stat_sb[:, r:r + 1], in_=hbf, axis=AX.X).then_inc(dv.h, 1))
            wait("vector", dv, dv.inc())
        else:
            em("vector", lambda E, r=r: E.reduce_sum(
                out=stat_sb[:, r:r + 1], in_=hbf, axis=AX.X))
        wait("scalar", dv, dv_fold[r])
        em("scalar", lambda E, r=r: E.memzero(stat_sb[:, 16 + r:17 + r]))
        em("scalar", lambda E, r=r: E.memzero(stat_sb[:, 32 + r:33 + r]))
        em("scalar", lambda E, r=r: E.activation(
            out=ps_sq[:], in_=hbf[:, 0:512], func=AF.Square, bias=0.0, scale=1.0,
            accum_out=stat_sb[:, 16 + r:17 + r]))
        em("scalar", lambda E, r=r: E.activation(
            out=ps_sq[:], in_=hbf[:, 512:1024], func=AF.Square, bias=0.0, scale=1.0,
            accum_out=stat_sb[:, 32 + r:33 + r]).then_inc(ac.h, 1))
        ac.inc()
        sh_xnew_prev = dma("scalar", xnew_scr[r * 128:(r + 1) * 128, :], hbf, sh)
    ac_sq_done = ac.n
    wait("vector", ac, ac_sq_done)

    def _stepf(fn):
        em("vector", lambda E, f=fn: f(E).then_inc(dv.h, 1))
        n = dv.inc()
        wait("vector", dv, n)
        return n
    _stepf(lambda E: E.tensor_scalar(
        out=stat_sb[:, 48:64], in0=stat_sb[:, 0:16], scalar1=1.0 / D, scalar2=None,
        op0=ALU.mult))
    _stepf(lambda E: E.tensor_tensor(
        out=stat_sb[:, 16:32], in0=stat_sb[:, 16:32], in1=stat_sb[:, 32:48], op=ALU.add))
    _stepf(lambda E: E.tensor_scalar(
        out=stat_sb[:, 16:32], in0=stat_sb[:, 16:32], scalar1=1.0 / D, scalar2=None,
        op0=ALU.mult))
    _stepf(lambda E: E.tensor_tensor(
        out=stat_sb[:, 32:48], in0=stat_sb[:, 48:64], in1=stat_sb[:, 48:64], op=ALU.mult))
    dv_varf = _stepf(lambda E: E.tensor_tensor(
        out=stat_sb[:, 16:32], in0=stat_sb[:, 16:32], in1=stat_sb[:, 32:48],
        op=ALU.subtract))
    wait("scalar", dv, dv_varf)
    em("scalar", lambda E: E.activation(
        out=stat_sb[:, 32:48], in_=stat_sb[:, 16:32], func=AF.Ln, bias=np.float32(1e-5), scale=1.0))
    em("scalar", lambda E: E.activation(
        out=rstdf_sb[:], in_=stat_sb[:, 32:48], func=AF.Exp, bias=0.0, scale=np.float32(-0.5))
       .then_inc(ac.h, 1))
    ac_rstdf = ac.inc()
    wait("vector", ac, ac_rstdf)
    _stepf(lambda E: E.tensor_tensor(
        out=negmf_sb[:], in0=stat_sb[:, 48:64], in1=rstdf_sb[:], op=ALU.mult))
    dv_negmf = _stepf(lambda E: E.tensor_scalar(
        out=negmf_sb[:], in0=negmf_sb[:], scalar1=-1.0, scalar2=None,
        op0=ALU.mult))
    dv_fold_done = dv.n
    ac_fold_done = ac.n

    if limit == "fold":
        finish_debug([rstdf_sb[:], negmf_sb[:], rinv_sb[:], l_sb[:]])
        return finalize()
    ps_scope.close()
    bstack.close()

    cstack = ExitStack()
    C = cstack.enter_context
    wgq_f = C(nc.sbuf_tensor("wgq_f", [128, ECH * II], BF16))
    wuq_f = C(nc.sbuf_tensor("wuq_f", [128, ECH * II], BF16))
    wdq_f = C(nc.sbuf_tensor("wdq_f", [128, ICH * D], BF16))
    qt_xn = C(nc.sbuf_tensor("qt_xn", [128, D], F32))
    qt_i1 = C(nc.sbuf_tensor("qt_i1", [128, D], I32))
    qt_acc = C(nc.sbuf_tensor("qt_acc", [128, D], F32))
    qt_t = C(nc.sbuf_tensor("qt_t", [128, D], F32))
    qt_sg = C(nc.sbuf_tensor("qt_sg", [128, D], BF16))
    qt_bm = C(nc.sbuf_tensor("qt_bm", [128, 64], F32))
    qt_sc = C(nc.sbuf_tensor("qt_sc", [128, 64], F32))
    qt_rx = C(nc.sbuf_tensor("qt_rx", [128, 64], F32))
    wq_bf_sb = C(nc.sbuf_tensor("wq_bf_sb", [128, 2 * D], BF16))
    wst = C(nc.sbuf_tensor("wst", [128, 2 * D], F32))
    xc = C(nc.sbuf_tensor("xc", [128, 2 * D], F32))
    h2_sb = C(nc.sbuf_tensor("h2_sb", [128, D], F32))
    xq_bf = C(nc.sbuf_tensor("xq_bf", [128, D], BF16))
    xqf_sb = C(nc.sbuf_tensor("xqf_sb", [128, ECH * 128], BF16))
    hq_bf = C(nc.sbuf_tensor("hq_bf", [128, 512], BF16))
    hqf_sb = C(nc.sbuf_tensor("hqf_sb", [128, 8 * 128], BF16))
    sg_sb = C(nc.sbuf_tensor("sg_sb", [128, 512], F32))
    outb_sb = C(nc.sbuf_tensor("outb_sb", [128, D], F32))

    ps_g = C(nc.psum_tensor("ps_g", [128, 512], F32))
    ps_g2 = C(nc.psum_tensor("ps_g2", [128, 512], F32))
    ps_u = C(nc.psum_tensor("ps_u", [128, 512], F32))
    ps_u2 = C(nc.psum_tensor("ps_u2", [128, 512], F32))
    ps_d = C(nc.psum_tensor("ps_d", [128, D], F32))
    ps_tA = C(nc.psum_tensor("ps_tA", [128, 128], BF16))
    ps_tB = C(nc.psum_tensor("ps_tB", [128, 128], BF16))


    qstate = {"dst_dv": 0}

    def quant(src, dst, F, blocks, wait_pe_n=0, wait_src_dv=0, wait_src_ac=0,
              wait_src_ld=0):
        g16 = src.rearrange("p (g k) -> p g k", k=16)
        if wait_pe_n:
            wait("vector", pe, wait_pe_n)
        if wait_src_ac:
            wait("vector", ac, wait_src_ac)
        if wait_src_dv:
            wait("vector", dv, wait_src_dv)
        if wait_src_ld:
            wait("vector", ld, wait_src_ld)
        em("vector", lambda E: E.reduce_max(
            out=qt_bm[:, 0:blocks], in_=g16, axis=AX.X,
            apply_absolute_value=True).then_inc(dv.h, 1))
        dvq = dv.inc()
        wait("vector", dv, dvq)
        em("vector", lambda E: E.tensor_scalar(
            out=qt_sc[:, 0:blocks], in0=qt_bm[:, 0:blocks], scalar1=1.0 / 6.0,
            scalar2=1e-12, op0=ALU.mult, op1=ALU.max).then_inc(dv.h, 1))
        dvq = dv.inc()
        wait("vector", dv, dvq)
        em("vector", lambda E: E.reciprocal(
            out=qt_rx[:, 0:blocks], in_=qt_sc[:, 0:blocks]).then_inc(dv.h, 1))
        dvq = dv.inc()
        wait("vector", dv, dvq)
        rxb = qt_rx[:, 0:blocks].unsqueeze(2).broadcast_to([128, blocks, 16])
        scb = qt_sc[:, 0:blocks].unsqueeze(2).broadcast_to([128, blocks, 16])
        wait("scalar", dv, qstate["dst_dv"])
        if wait_src_dv:
            wait("scalar", dv, wait_src_dv)
        if wait_src_ac:
            wait("scalar", ac, wait_src_ac)
        if wait_src_ld:
            wait("scalar", ld, wait_src_ld)
        em("scalar", lambda E: E.activation(
            out=qt_t[:, 0:F], in_=src, func=AF.Abs, bias=0.0, scale=1.0))
        em("scalar", lambda E: E.activation(
            out=qt_sg[:, 0:F], in_=src, func=AF.Sign, bias=0.0, scale=1.0).then_inc(ac.h, 1))
        acsg = ac.inc()
        wait("vector", ac, acsg)
        em("vector", lambda E: E.tensor_tensor(
            out=qt_xn[:, 0:F].rearrange("p (g k) -> p g k", k=16),
            in0=qt_t[:, 0:F].rearrange("p (g k) -> p g k", k=16), in1=rxb,
            op=ALU.mult).then_inc(dv.h, 1))
        dv.inc()
        em("vector", lambda E: E.tensor_scalar(
            out=qt_i1[:, 0:F], in0=qt_xn[:, 0:F], scalar1=2.0, scalar2=None,
            op0=ALU.mult))
        em("vector", lambda E: E.tensor_scalar(
            out=qt_acc[:, 0:F], in0=qt_i1[:, 0:F], scalar1=4.0, scalar2=0.5,
            op0=ALU.min, op1=ALU.mult))
        em("vector", lambda E: E.tensor_scalar(
            out=qt_i1[:, 0:F], in0=qt_xn[:, 0:F], scalar1=1.0, scalar2=None,
            op0=ALU.mult))
        em("vector", lambda E: E.tensor_scalar(
            out=qt_t[:, 0:F], in0=qt_i1[:, 0:F], scalar1=2.0, scalar2=2.0,
            op0=ALU.subtract, op1=ALU.min))
        em("vector", lambda E: E.scalar_tensor_tensor(
            out=qt_acc[:, 0:F], in0=qt_t[:, 0:F], scalar=0.0, in1=qt_acc[:, 0:F],
            op0=ALU.max, op1=ALU.add))
        em("vector", lambda E: E.tensor_scalar(
            out=qt_i1[:, 0:F], in0=qt_xn[:, 0:F], scalar1=0.5, scalar2=None,
            op0=ALU.mult))
        em("vector", lambda E: E.tensor_scalar(
            out=qt_t[:, 0:F], in0=qt_i1[:, 0:F], scalar1=2.0, scalar2=4.0,
            op0=ALU.mult, op1=ALU.subtract))
        em("vector", lambda E: E.scalar_tensor_tensor(
            out=qt_acc[:, 0:F], in0=qt_t[:, 0:F], scalar=0.0, in1=qt_acc[:, 0:F],
            op0=ALU.max, op1=ALU.add))
        em("vector", lambda E: E.tensor_tensor(
            out=qt_t[:, 0:F].rearrange("p (g k) -> p g k", k=16),
            in0=qt_acc[:, 0:F].rearrange("p (g k) -> p g k", k=16), in1=scb, op=ALU.mult))
        wait("vector", ac, acsg)
        em("vector", lambda E, dst=dst: E.tensor_tensor(
            out=dst, in0=qt_t[:, 0:F], in1=qt_sg[:, 0:F], op=ALU.mult).then_inc(dv.h, 1))
        qstate["dst_dv"] = dv.inc()
        return qstate["dst_dv"]

    sh_w_prev = 0
    dv_q_prev = 0
    first_w = True
    wslot2 = {"i": 0, 0: 0, 1: 0}
    wsh2 = {0: 0, 1: 0}
    tail = [(wu_in, wuq_scr, t, 0, D) for t in range(10, 22)]
    for t in range(8):
        for (coff, csz) in ((0, D), (D, D), (2 * D, 768)):
            tail.append((wd_in, wdq_scr, t, coff, csz))
    for (w_in_t, w_scr, t, coff, csz) in tail:
        slot = wslot2["i"] % 2
        wslot2["i"] += 1
        if first_w:
            wait("sync", dv, dv_fold_done)
            wait("sync", ac, ac_fold_done)
            first_w = False
        else:
            wait("sync", dv, wslot2[slot])
        n_w = dma("sync", wst[:, slot * D:slot * D + csz],
                  w_in_t[t * 128:(t + 1) * 128, coff:coff + csz], ld)
        wait("vector", sh, wsh2[slot])
        dvq_last = quant(wst[:, slot * D:slot * D + csz],
                         wq_bf_sb[:, slot * D:slot * D + csz], csz, csz // 16,
                         wait_src_ld=n_w)
        wslot2[slot] = dvq_last
        dv_q_prev = dvq_last
        wait("scalar", dv, dvq_last)
        wsh2[slot] = dma("scalar", w_scr[t * 128:(t + 1) * 128, coff:coff + csz],
                         wq_bf_sb[:, slot * D:slot * D + csz], sh)
    sh_w_final = max(wsh2[0], wsh2[1], sh_w_fine)
    wait("sync", sh, sh_w_final)
    for dc in range(ECH):
        dma("sync", wgq_f[:, dc * II:(dc + 1) * II],
            wgq_scr[:, dc * 128:(dc + 1) * 128], ld, transpose=True)
        dma("sync", wuq_f[:, dc * II:(dc + 1) * II],
            wuq_scr[:, dc * 128:(dc + 1) * 128], ld, transpose=True)
    n_wf = 0
    for icc in range(ICH):
        n_wf = dma("sync", wdq_f[:, icc * D:(icc + 1) * D],
                   wdq_scr[:, icc * 128:(icc + 1) * 128], ld, transpose=True)

    NCC = [512, 512, 512, 512, 512, 256]
    dv_out_prev = 0
    pe_xt_prev = 0
    pe_hqt_prev = 0
    dv_tc = {"A": 0, "B": 0}
    h2x = wst
    ac_h2 = {}
    dv_xq = {}
    n_xc = {}

    def emit_ln_tile(r):
        if r >= 16:
            return
        xsel = r % 2
        if r >= 2:
            wait("sync", dv, dv_out_prev)
        n_xc[r] = dma("sync", xc[:, xsel * D:(xsel + 1) * D],
                      xnew_scr[r * 128:(r + 1) * 128, :], ld)
        wait("scalar", ld, n_xc[r])
        wait("scalar", dv, dv_negmf)
        em("scalar", lambda E, r=r, xs=xsel: E.activation(
            out=h2x[:, xs * D:(xs + 1) * D], in_=xc[:, xs * D:(xs + 1) * D],
            func=AF.Identity,
            bias=negmf_sb[:, r:r + 1], scale=rstdf_sb[:, r:r + 1]).then_inc(ac.h, 1))
        ac_h2[r] = ac.inc()

    def emit_xquant_tile(r):
        if r >= 16:
            return
        xsel = r % 2
        dv_xq[r] = quant(h2x[:, xsel * D:(xsel + 1) * D], xq_bf[:], D, 64,
                         wait_pe_n=pe_xt_prev, wait_src_ac=ac_h2[r])

    emit_ln_tile(0)
    emit_xquant_tile(0)
    for r in range(16):
        xsel = r % 2
        wait("tensor", dv, dv_xq[r])
        for dc in range(ECH):
            key = "A" if dc % 2 == 0 else "B"
            pst = ps_tA if dc % 2 == 0 else ps_tB
            wait("tensor", dv, dv_tc[key])
            em("tensor", lambda E, dc=dc, pst=pst: E.transpose(
                out=pst[:], in_=xq_bf[:, dc * 128:(dc + 1) * 128], identity=ident_sb[:])
               .then_inc(pe.h, 1))
            pe_t = pe.inc()
            wait("vector", pe, pe_t)
            em("vector", lambda E, dc=dc, pst=pst: E.tensor_copy(
                out=xqf_sb[:, dc * 128:(dc + 1) * 128], in_=pst[:]).then_inc(dv.h, 1))
            dv_tc[key] = dv.inc()
        pe_xt_prev = pe.n
        dv_xt = dv_tc["B"]

        def emit_gu(ci):
            csz = NCC[ci]
            off = 512 * ci
            gsel = ci % 2
            psg = ps_g if gsel == 0 else ps_g2
            psu = ps_u if gsel == 0 else ps_u2
            wait("tensor", ac, ac_sil_prev[gsel])
            wait("tensor", dv, dv_h_prev[gsel])
            if ci == 0:
                wait("tensor", dv, dv_xt)
                wait("tensor", ld, n_wf)
            for dc in range(ECH):
                em("tensor", lambda E, dc=dc, off=off, csz=csz, psg=psg: E.matmul(
                    out=psg[:, 0:csz], lhsT=xqf_sb[:, dc * 128:(dc + 1) * 128],
                    rhs=wgq_f[:, dc * II + off:dc * II + off + csz],
                    start=(dc == 0), stop=(dc == ECH - 1)).then_inc(pe.h, 1))
                pe.inc()
            for dc in range(ECH):
                em("tensor", lambda E, dc=dc, off=off, csz=csz, psu=psu: E.matmul(
                    out=psu[:, 0:csz], lhsT=xqf_sb[:, dc * 128:(dc + 1) * 128],
                    rhs=wuq_f[:, dc * II + off:dc * II + off + csz],
                    start=(dc == 0), stop=(dc == ECH - 1)).then_inc(pe.h, 1))
                pe.inc()
            return pe.n

        ac_sil_prev = {0: 0, 1: 0}
        dv_h_prev = {0: 0, 1: 0}
        dv_h_last = 0
        icc0 = 0
        pe_g = {0: emit_gu(0)}
        for ci, csz in enumerate(NCC):
            gsel = ci % 2
            psg = ps_g if gsel == 0 else ps_g2
            psu = ps_u if gsel == 0 else ps_u2
            wait("scalar", pe, pe_g[ci])
            wait("scalar", dv, dv_h_last)
            em("scalar", lambda E, csz=csz, psg=psg: E.activation(
                out=sg_sb[:, 0:csz], in_=psg[:, 0:csz], func=AF.Silu, bias=0.0,
                scale=1.0).then_inc(ac.h, 1))
            ac_sil = ac.inc()
            ac_sil_prev[gsel] = ac_sil
            wait("vector", ac, ac_sil)
            wait("vector", pe, pe_g[ci])
            em("vector", lambda E, csz=csz, psu=psu: E.tensor_tensor(
                out=h2_sb[:, 0:csz], in0=sg_sb[:, 0:csz], in1=psu[:, 0:csz],
                op=ALU.mult).then_inc(dv.h, 1))
            dv_h = dv.inc()
            dv_h_prev[gsel] = dv_h
            dv_h_last = dv_h
            dv_hq = quant(h2_sb[:, 0:csz], hq_bf[:, 0:csz], csz, csz // 16,
                          wait_pe_n=pe_hqt_prev, wait_src_dv=dv_h)
            if ci == 1:
                emit_ln_tile(r + 1)
            if ci == 2:
                emit_xquant_tile(r + 1)
            if ci + 1 < len(NCC):
                pe_g[ci + 1] = emit_gu(ci + 1)
            wait("tensor", dv, dv_hq)
            nic = csz // 128
            for j in range(nic):
                icc = icc0 + j
                key = "A" if icc % 2 == 0 else "B"
                pst = ps_tA if icc % 2 == 0 else ps_tB
                wait("tensor", dv, dv_tc[key])
                em("tensor", lambda E, j=j, pst=pst: E.transpose(
                    out=pst[:], in_=hq_bf[:, j * 128:(j + 1) * 128], identity=ident_sb[:])
                   .then_inc(pe.h, 1))
                pe_t = pe.inc()
                wait("vector", pe, pe_t)
                em("vector", lambda E, sl=icc % 8, pst=pst: E.tensor_copy(
                    out=hqf_sb[:, sl * 128:(sl + 1) * 128], in_=pst[:])
                   .then_inc(dv.h, 1))
                dv_tc[key] = dv.inc()
            pe_hqt_prev = pe.n
            wait("tensor", dv, dv_tc["A"])
            wait("tensor", dv, dv_tc["B"])
            if ci == 0 and r >= 1:
                wait("tensor", dv, dv_out_prev)
            for j in range(nic):
                icc = icc0 + j
                for hh2 in range(2):
                    em("tensor", lambda E, icc=icc, sl=icc % 8, hh=hh2: E.matmul(
                        out=ps_d[:, hh * 512:(hh + 1) * 512],
                        lhsT=hqf_sb[:, sl * 128:(sl + 1) * 128],
                        rhs=wdq_f[:, icc * D + hh * 512:icc * D + (hh + 1) * 512],
                        start=(icc == 0), stop=(icc == ICH - 1)).then_inc(pe.h, 1))
                    pe.inc()
            icc0 += nic
        pe_down = pe.n
        wait("vector", pe, pe_down)
        if r >= 1:
            wait("vector", so, r * 16)
        em("vector", lambda E, r=r, xs=xsel: E.tensor_tensor(
            out=outb_sb[:, 0:D], in0=ps_d[:, 0:D],
            in1=xc[:, xs * D:(xs + 1) * D], op=ALU.add).then_inc(dv.h, 1))
        dv_out_prev = dv.inc()
        wait("scalar", dv, dv_out_prev)
        dma("scalar", out_t[r * 128:(r + 1) * 128, :], outb_sb[:, 0:D], so)

    wait("sync", so, so.n)
    return finalize()


def _host_tables(j):
    tt = np.zeros((128, 64), dtype=np.float32)
    p = np.arange(128, dtype=np.float32)
    for su in range(NSU):
        kt, half = su // 2, su % 2
        for kc in range(4):
            t = kt * 1024 + half * 512 + kc * 128 - j * TOWN
            t = max(-128, min(2176, t))
            tt[:, su * 4 + kc] = p + t
    return tt


def kernel(L0_attn_X, L0_attn_Wq, L0_attn_bq, L0_attn_Wk, L0_attn_bk,
           L0_attn_Wv, L0_attn_bv, L0_ffn_w_gate, L0_ffn_w_up, L0_ffn_w_down):
    X = np.ascontiguousarray(np.asarray(L0_attn_X, dtype=np.float32))
    wqT = np.ascontiguousarray(np.asarray(L0_attn_Wq, np.float32).T)
    wkT = np.ascontiguousarray(np.asarray(L0_attn_Wk, np.float32).T)
    wvT = np.ascontiguousarray(np.asarray(L0_attn_Wv, np.float32).T)
    bq = np.ascontiguousarray(np.asarray(L0_attn_bq, np.float32))
    bk = np.ascontiguousarray(np.asarray(L0_attn_bk, np.float32))
    bv = np.ascontiguousarray(np.asarray(L0_attn_bv, np.float32))
    wg = np.ascontiguousarray(np.asarray(L0_ffn_w_gate, np.float32))
    wu = np.ascontiguousarray(np.asarray(L0_ffn_w_up, np.float32))
    wd = np.ascontiguousarray(np.asarray(L0_ffn_w_down, np.float32))

    ramp = np.broadcast_to(np.arange(TOWN, dtype=np.float32)[None, :], (128, TOWN)).copy()
    ident = np.eye(128, dtype=ml_dtypes.bfloat16)

    if "nc" not in _CACHE:
        _CACHE["nc"] = build_program()
    nc = _CACHE["nc"]

    in_maps = []
    for c in range(8):
        b, j = c // 4, c % 4
        in_maps.append({
            "xq": np.ascontiguousarray(X[b, j * TOWN:(j + 1) * TOWN, :]),
            "xkv": np.ascontiguousarray(X[b]),
            "wqT": wqT, "wkT": wkT, "wvT": wvT,
            "bq": bq, "bk": bk, "bv": bv,
            "wg": wg, "wu": wu, "wd": wd,
            "ramp": ramp, "tthr": _host_tables(j), "ident": ident,
        })
    res = run_bass_kernel_spmd(nc, in_maps, core_ids=list(range(8)))
    out = np.empty((2, NKV, D), dtype=np.float32)
    for c in range(8):
        b, j = c // 4, c % 4
        out[b, j * TOWN:(j + 1) * TOWN, :] = np.asarray(res.results[c]["out"])
    return out
